# version 02
# baseline (speedup 1.0000x reference)
import numpy as np
import ml_dtypes
import concourse.bass as bass
import concourse.mybir as mybir
from concourse.bass import IndirectOffsetOnAxis
from concourse.bass_utils import run_bass_kernel_spmd
from concourse.aot_env import allow_ici_all_to_all

F32 = mybir.dt.float32
BF16 = mybir.dt.bfloat16
I32 = mybir.dt.int32
AF = mybir.ActivationFunctionType
ALU = mybir.AluOpType
AX = mybir.AxisListType

NCORE = 8
T = 2048
NTOK = 16384
DM = 1024
DS = 128
FF = 4096
KSEL = 4096
CAP = 640
NPR = NCORE * CAP
BIG = 1 << 28
NBIS = 42
RATE_EPS = 1e-4
LN_EPS = 1e-5


class Ctr:
    __slots__ = ("v",)
    def __init__(self):
        self.v = 0
    def add(self, n=1):
        self.v += n
        return self.v


def build_program(debug=False, stop_after=5):
    nc = bass.Bass("TRN2", target_bir_lowering=False, debug=False)
    D = lambda n, s, dt, k="ExternalInput": nc.dram_tensor(n, s, dt, kind=k)

    xts = D("xts", [DM, T], F32)
    xs = D("xs", [T, DM], F32)
    xfull = D("xfull", [NTOK, DM], F32)
    w_inT = D("w_inT", [DM, 2 * DM], F32)
    w_stT = D("w_stT", [DM, DS], F32)
    w_dtT = D("w_dtT", [DM, DS], F32)
    w_cT = D("w_cT", [DS, DM], F32)
    w_outT = D("w_outT", [DM, DM], F32)
    wg_in = D("wg_in", [DM, 8], F32)
    b_dt_in = D("b_dt_in", [DS, 1], F32)
    a_log_in = D("a_log_in", [DS, 1], F32)
    dmat_in = D("dmat_in", [128, 8], F32)
    w1e = D("w1e", [DM, FF], F32)
    w2e = D("w2e", [FF, DM], F32)
    b1m_in = D("b1m_in", [128, 32], F32)
    b2bc_in = D("b2bc_in", [128, DM], F32)
    identbf_in = D("identbf_in", [128, 128], BF16)
    identf_in = D("identf_in", [128, 128], F32)
    onescol_in = D("onescol_in", [128, 1], F32)
    onesrow_in = D("onesrow_in", [1, 128], F32)
    onestt_in = D("onestt_in", [128, 128], F32)
    ltstrict_in = D("ltstrict_in", [128, 128], F32)
    lt16_in = D("lt16_in", [128, 128], F32)
    tokid_in = D("tokid_in", [128, 128], I32)
    jbase_in = D("jbase_in", [128, 1], F32)
    selv_in = D("selv_in", [1, 8], F32)
    emask_in = D("emask_in", [128, 1024], F32)

    sc_in = nc.dram_tensor("sc_in", [T, 8], F32)
    sc_out = nc.dram_tensor("sc_out", [NTOK, 8], F32)
    sum_in = nc.dram_tensor("sum_in", [2, 128], F32)
    sum_out = nc.dram_tensor("sum_out", [16, 128], F32)
    w1bfd = nc.dram_tensor("w1bfd", [DM, FF], BF16)
    w2bfd = nc.dram_tensor("w2bfd", [FF, DM], BF16)
    idxg = nc.dram_tensor("idxg", [KSEL, 5], I32)
    a2as = nc.dram_tensor("a2as", [NPR, DM], BF16)
    a2ar = nc.dram_tensor("a2ar", [NPR, DM], BF16)
    a2ais = nc.dram_tensor("a2ais", [NPR, 1], I32)
    a2air = nc.dram_tensor("a2air", [NPR, 1], I32)
    gdense = nc.dram_tensor("gdense", [T, DM], BF16)
    out_t = D("out", [T, DM], F32, "ExternalOutput")
    if debug:
        hh_dbg = D("hh_dbg", [128, T], F32, "ExternalOutput")
        thr_dbg = D("thr_dbg", [1, 8], F32, "ExternalOutput")
        s_dbg = D("s_dbg", [T, DM], F32, "ExternalOutput")
        g_dbg = D("g_dbg", [T, DM], F32, "ExternalOutput")
        sc_dbg = D("sc_dbg", [T, 8], F32, "ExternalOutput")
        ve_dbg = D("ve_dbg", [128, 128], F32, "ExternalOutput")

    from contextlib import ExitStack
    ctx = ExitStack()
    sb = lambda n, s, dt: ctx.enter_context(nc.sbuf_tensor(n, s, dt))
    ps = lambda n, s, dt: ctx.enter_context(nc.psum_tensor(n, s, dt))
    sem = lambda n: ctx.enter_context(nc.semaphore(n))

    xnbf = sb("xnbf", [128, 8 * T], BF16)
    ubf = sb("ubf", [128, 8 * T], BF16)
    arena1 = sb("arena1", [128, 8 * T], BF16)
    arena2 = sb("arena2", [128, 8 * T], BF16)
    arena3 = sb("arena3", [128, 8 * T], BF16)
    a1f = arena1.bitcast(F32)
    a2f = arena2.bitcast(F32)
    a3f = arena3.bitcast(F32)

    wstage = sb("wstage", [128, 4096], BF16)
    wg_sb = sb("wg_sb", [128, 64], F32)
    wstb = sb("wstb", [128, 1024], BF16)
    wdtb = sb("wdtb", [128, 1024], BF16)
    wctb = sb("wctb", [128, 1024], BF16)
    identbf = sb("identbf", [128, 128], BF16)
    identf = sb("identf", [128, 128], F32)
    onescol = sb("onescol", [128, 1], F32)
    onesrow = sb("onesrow", [1, 128], F32)
    onestt = sb("onestt", [128, 128], F32)
    ltstrict = sb("ltstrict", [128, 128], F32)
    lt16 = sb("lt16", [128, 128], F32)
    tokid = sb("tokid", [128, 128], I32)
    jbase = sb("jbase", [128, 1], F32)
    selv = sb("selv", [1, 8], F32)
    b_dt = sb("b_dt", [128, 1], F32)
    a_log = sb("a_log", [128, 1], F32)
    dmat = sb("dmat", [128, 8], F32)
    b1m = sb("b1m", [128, 32], F32)
    b2bc = sb("b2bc", [128, DM], F32)
    negrate = sb("negrate", [128, 1], F32)
    rates = sb("rates", [128, 4], F32)
    epst = sb("epst", [128, 1], F32)
    mhalf = sb("mhalf", [128, 1], F32)
    meanr = sb("meanr", [1, 512], F32)
    rstdr = sb("rstdr", [1, 512], F32)
    stat1 = sb("stat1", [1, 512], F32)
    stat3 = sb("stat3", [1, 512], F32)
    scsb = sb("scsb", [8, 512], F32)
    sctm = sb("sctm", [128, 32], F32)
    dts1 = sb("dts1", [128, 512], F32)
    dts2 = sb("dts2", [128, 512], F32)
    dts3 = sb("dts3", [128, 512], F32)
    dtsum = sb("dtsum", [128, 4], F32)
    hin = sb("hin", [1, 128], F32)
    hf1 = sb("hf1", [1, 128], F32)
    hf2 = sb("hf2", [1, 128], F32)
    h0col = sb("h0col", [128, 1], F32)
    vE = sb("vE", [128, 128], F32)
    bis = sb("bis", [1, 8], F32)
    thrcol = sb("thrcol", [128, 1], F32)
    cntcol = sb("cntcol", [128, 1], F32)
    exc = sb("exc", [128, 1], F32)
    excm1 = sb("excm1", [128, 1], F32)
    bnd = sb("bnd", [128, 1], F32)
    idxg_sb = sb("idxg_sb", [128, 160], I32)
    senti = sb("senti", [128, 40], I32)
    gmask = sb("gmask", [128, 32], F32)
    mcol = sb("mcol", [128, 1], F32)
    s2col = sb("s2col", [128, 1], F32)
    vcol = sb("vcol", [128, 1], F32)
    rcol = sb("rcol", [128, 1], F32)
    ncol = sb("ncol", [128, 1], F32)
    gtile = sb("gtile", [128, DM], BF16)
    recvd = sb("recvd", [128, DM], BF16)
    zerot = sb("zerot", [128, DM], BF16)
    ridl = sb("ridl", [128, 40], I32)

    p0 = ps("p0", [128, 512], F32)
    p1 = ps("p1", [128, 512], F32)
    p2 = ps("p2", [128, 512], F32)
    p3 = ps("p3", [128, 512], F32)
    pW = ps("pW", [128, 1024], F32)
    pS = ps("pS", [8, 512], F32)
    pT = ps("pT", [128, 128], BF16)

    pe = sem("pe"); c_pe = Ctr()
    dv = sem("dv"); c_dv = Ctr()
    ac = sem("ac"); c_ac = Ctr()
    cc1 = sem("cc1"); cc2 = sem("cc2"); cc3 = sem("cc3"); cc4 = sem("cc4")
    junk = sem("junk")

    class Q:
        def __init__(self, name, chained=False):
            self.s = sem(name)
            self.c = Ctr()
            self.chained = chained

    prog = {k: [] for k in ("sync", "scalar", "vector", "tensor", "gpsimd")}
    def em(engine, fn):
        prog[engine].append(fn)
    def W(engine, s, v):
        if v > 0:
            em(engine, lambda q, s=s, v=v: q.wait_ge(s, v))
    def EP(fn):
        em("tensor", lambda q, fn=fn: fn(q).then_inc(pe, 1))
        return c_pe.add(1)
    def EV(fn):
        if c_dv.v > 0:
            em("vector", lambda q, v=c_dv.v: q.wait_ge(dv, v))
        em("vector", lambda q, fn=fn: fn(q).then_inc(dv, 1))
        return c_dv.add(1)
    def EA(fn):
        if c_ac.v > 0:
            em("scalar", lambda q, v=c_ac.v: q.wait_ge(ac, v))
        em("scalar", lambda q, fn=fn: fn(q).then_inc(ac, 1))
        return c_ac.add(1)
    def DMA(engine, grp, fn):
        if grp.chained and grp.c.v > 0:
            em(engine, lambda q, s=grp.s, v=grp.c.v: q.wait_ge(s, v))
        em(engine, lambda q, fn=fn, s=grp.s: fn(q).then_inc(s, 16))
        return grp.c.add(16)
    g_const = Q("g_const")
    g_wssm = Q("g_wssm")
    g_stgld = Q("g_stgld", chained=True)
    g_stgst = Q("g_stgst", chained=True)
    g_pref = Q("g_pref")
    g_gz = Q("g_gz")
    g_az = Q("g_az")
    g_x = Q("g_x", chained=True)
    g_sc = Q("g_sc")
    g_sum = Q("g_sum")
    g_sumg = Q("g_sumg")
    g_em = Q("g_em")
    g_scg = Q("g_scg")
    g_idxl = Q("g_idxl")
    g_sq1 = Q("g_sq1")
    g_sq2 = Q("g_sq2")
    g_osb = [Q(f"g_osb{i}") for i in range(8)]
    g_ma = [Q(f"g_ma{i}") for i in range(8)]
    g_gath = Q("g_gath", chained=True)
    g_w1 = [Q("g_w1a", chained=True), Q("g_w1b", chained=True)]
    g_w2 = [Q("g_w2a", chained=True), Q("g_w2b", chained=True)]
    g_wo = Q("g_wo")
    g_gt = Q("g_gt", chained=True)
    g_xs = Q("g_xs", chained=True)
    g_mr = Q("g_mr", chained=True)
    g_out = Q("g_out", chained=True)
    g_dbq = Q("g_dbq", chained=True)
    def MACC(psv, pairs, waits=()):
        for sw, vw in waits:
            W("tensor", sw, vw)
        n = len(pairs)
        for i, (lf, rf) in enumerate(pairs):
            if i < n - 1:
                em("tensor", lambda q, lf=lf, rf=rf, st=(i == 0), psv=psv:
                   q.matmul(psv, lf, rf, start=st, stop=False))
            else:
                em("tensor", lambda q, lf=lf, rf=rf, st=(i == 0), psv=psv:
                   q.matmul(psv, lf, rf, start=st, stop=True).then_inc(pe, 1))
        return c_pe.add(1)

    def _final_stub():
        W("sync", dv, c_dv.v)
        W("sync", ac, c_ac.v)
        W("sync", pe, c_pe.v)
        for tt in range(16):
            for hb in range(2):
                DMA("sync", g_out, lambda q, tt=tt, hb=hb: q.dma_start(
                    out=out_t[128 * tt:128 * (tt + 1), :].bitcast(BF16)[:, 1024 * hb:1024 * (hb + 1)],
                    in_=zerot[:, :]))
        W("sync", g_out.s, g_out.c.v)

    def _finish():
        with nc.Block() as block:
            @block.sync
            def _(q):
                for f in prog["sync"]:
                    f(q)
            @block.scalar
            def _(q):
                for f in prog["scalar"]:
                    f(q)
            @block.vector
            def _(q):
                for f in prog["vector"]:
                    f(q)
            @block.tensor
            def _(q):
                for f in prog["tensor"]:
                    f(q)
            @block.gpsimd
            def _(q):
                for f in prog["gpsimd"]:
                    f(q)
        ctx.close()
        return nc

    constpairs = [
        (wg_sb.rearrange("p (k e) -> p k e", e=8), wg_in.rearrange("(k p) e -> p k e", p=128)),
        (identbf, identbf_in[:, :]), (identf, identf_in[:, :]),
        (onescol, onescol_in[:, :]), (onesrow, onesrow_in[:, :]),
        (onestt, onestt_in[:, :]),
        (ltstrict, ltstrict_in[:, :]), (lt16, lt16_in[:, :]),
        (tokid, tokid_in[:, :]), (jbase, jbase_in[:, :]), (selv, selv_in[:, :]),
        (b_dt, b_dt_in[:, :]), (a_log, a_log_in[:, :]), (dmat, dmat_in[:, :]),
        (b1m, b1m_in[:, :]), (b2bc, b2bc_in[:, :]),
    ]
    for dst, src in constpairs:
        dap = dst if isinstance(dst, bass.AP) else dst[:, :]
        DMA("sync", g_const, lambda q, d=dap, s=src: q.dma_start(out=d, in_=s))
    V_CONST = g_const.c.v

    for dst, src in [
        (wstb.rearrange("p (uk n) -> p uk n", uk=8), w_stT.rearrange("(uk p) n -> p uk n", p=128)),
        (wdtb.rearrange("p (uk n) -> p uk n", uk=8), w_dtT.rearrange("(uk p) n -> p uk n", p=128)),
        (wctb[:, :], w_cT[:, :]),
    ]:
        DMA("gpsimd", g_wssm, lambda q, d=dst, s=src: q.dma_start(out=d, in_=s))
    DMA("gpsimd", g_wssm,
        lambda q: q.dma_start(out=arena2[:, :].rearrange("p (k u) -> p k u", k=8),
                              in_=w_inT.rearrange("(k p) u -> p k u", p=128)))
    V_WSSM = g_wssm.c.v

    wst_free = [0, 0]
    for i in range(16):
        h = i % 2
        r, cblk = divmod(i, 2)
        W("gpsimd", g_stgst.s, wst_free[h])
        v = DMA("gpsimd", g_stgld, lambda q, r=r, cblk=cblk, h=h: q.dma_start(
            out=wstage[:, h * 2048:h * 2048 + 2048],
            in_=w1e[128 * r:128 * (r + 1), 2048 * cblk:2048 * (cblk + 1)]))
        W("sync", g_stgld.s, v)
        wst_free[h] = DMA("sync", g_stgst, lambda q, r=r, cblk=cblk, h=h: q.dma_start(
            out=w1bfd[128 * r:128 * (r + 1), 2048 * cblk:2048 * (cblk + 1)],
            in_=wstage[:, h * 2048:h * 2048 + 2048]))
    for i in range(32):
        h = i % 2
        W("gpsimd", g_stgst.s, wst_free[h])
        v = DMA("gpsimd", g_stgld, lambda q, i=i, h=h: q.dma_start(
            out=wstage[:, h * 2048:h * 2048 + 1024],
            in_=w2e[128 * i:128 * (i + 1), :]))
        W("sync", g_stgld.s, v)
        wst_free[h] = DMA("sync", g_stgst, lambda q, i=i, h=h: q.dma_start(
            out=w2bfd[128 * i:128 * (i + 1), :],
            in_=wstage[:, h * 2048:h * 2048 + 1024]))
    V_WBF = g_stgst.c.v

    V_SENTI = EV(lambda q: q.memset(senti[:, :], BIG))
    V_ZEROT = EV(lambda q: q.memset(zerot[:, :], 0))
    W("sync", dv, V_ZEROT)
    for qd in range(4):
        DMA("sync", g_pref, lambda q, qd=qd: q.dma_start(
            out=idxg[1024 * qd:1024 * (qd + 1), :].rearrange("(p w) c -> p w c", p=128),
            in_=senti[:, 0:40].rearrange("p (w c) -> p w c", c=5)))
    DMA("sync", g_pref, lambda q: q.dma_start(
        out=a2ais[:, :].rearrange("(p w) c -> p w c", p=128),
        in_=senti[:, 0:40].rearrange("p (w c) -> p w c", c=1)))
    V_PREFILL = g_pref.c.v
    for tt in range(16):
        DMA("sync", g_gz, lambda q, tt=tt: q.dma_start(
            out=gdense[128 * tt:128 * (tt + 1), :], in_=zerot[:, :]))
    V_GZERO = g_gz.c.v
    for m in range(40):
        DMA("sync", g_az, lambda q, m=m: q.dma_start(
            out=a2as[128 * m:128 * (m + 1), :], in_=zerot[:, :]))
    V_AZERO = g_az.c.v

    W("scalar", g_const.s, V_CONST)
    EA(lambda q: q.activation(rates[:, 0:1], a_log[:, :], AF.Relu))
    v = EA(lambda q: q.activation(rates[:, 1:2], a_log[:, :], AF.Abs))
    W("scalar", ac, v)
    v = EA(lambda q: q.activation(rates[:, 2:3], rates[:, 1:2], AF.Exp, scale=-1.0))
    W("scalar", ac, v)
    v = EA(lambda q: q.activation(rates[:, 3:4], rates[:, 2:3], AF.Ln, bias=1.0))
    W("vector", ac, v)
    v = EV(lambda q: q.tensor_add(negrate[:, :], rates[:, 0:1], rates[:, 3:4]))
    W("vector", dv, v)
    V_NEGRATE = EV(lambda q: q.tensor_scalar(negrate[:, :], negrate[:, :],
                                             RATE_EPS, -1.0, ALU.add, ALU.mult))
    EV(lambda q: q.memset(epst[:, :], LN_EPS))
    V_EPST = EV(lambda q: q.memset(mhalf[:, :], -0.5))

    W("tensor", g_const.s, V_CONST)
    W("vector", g_const.s, V_CONST)
    W("gpsimd", g_const.s, V_CONST)

    decay = a1f[:, 0:2048]
    vvn = a1f[:, 2048:4096]
    hh = a1f[:, 4096:6144]
    sumg = a1f[0:1, 6144:8192]
    hcast = arena1[:, 0:2048]
    mb_sb = a3f[:, 0:512]
    rb_sb = a3f[:, 512:1024]
    a2i = arena2.bitcast(I32)
    scg2 = a2f[:, 2048:3072]
    cmpscr = a2f[:, 0:128]
    msk = a2f[:, 128:256]
    pscan = a2f[:, 256:384]
    slotf = a2f[:, 384:512]
    fscr = a2f[:, 512:640]
    fscr2 = a2f[:, 640:768]
    gtv = a2f[:, 768:896]
    iscr = a2i[:, 896:1024]
    iscr2 = a2i[:, 1024:1152]
    desti = a2i[:, 1152:1280]
    pdesti = a2i[:, 1280:1408]
    prs5 = a2i[:, 1408:2048]
    xf32 = a1f[:, 0:4096]
    xnf = a1f[:, 4096:8192]
    v_xncast = [0, 0, 0, 0]
    for tc in range(4):
        pe_prev = c_pe.v
        if tc > 0:
            W("sync", pe, pe_prev)
        v_xf = DMA("sync", g_x, lambda q, tc=tc: q.dma_start(
            out=xf32.rearrange("p (k t) -> p k t", k=8),
            in_=xts[:, 512 * tc:512 * (tc + 1)].rearrange("(k p) t -> p k t", p=128)))
        W("tensor", g_x.s, v_xf)
        v_sum = MACC(pS[0:1, :],
                     [(onescol[:, :], xf32[:, 512 * k:512 * (k + 1)]) for k in range(8)])
        W("scalar", g_x.s, v_xf)
        if tc > 0:
            W("scalar", pe, pe_prev)
        for k in range(8):
            EA(lambda q, k=k: q.activation(
                xnf[:, 512 * k:512 * (k + 1)], xf32[:, 512 * k:512 * (k + 1)], AF.Square))
        v_sq = c_ac.v
        v_sumsq = MACC(p3[0:1, :],
                       [(onescol[:, :], xnf[:, 512 * k:512 * (k + 1)]) for k in range(8)],
                       waits=[(ac, v_sq)])
        mean_s = meanr[0:1, :]
        rstd_s = rstdr[0:1, :]
        var_s = stat1[0:1, :]
        sc3 = stat3[0:1, :]
        W("vector", pe, v_sumsq)
        EV(lambda q, m=mean_s: q.tensor_scalar(m, pS[0:1, :], 1.0 / DM, None, ALU.mult))
        EV(lambda q, v_=var_s: q.tensor_scalar(v_, p3[0:1, :], 1.0 / DM, None, ALU.mult))
        v = EV(lambda q, m=mean_s, s3=sc3: q.tensor_mul(s3, m, m))
        W("vector", dv, v)
        v = EV(lambda q, v_=var_s, s3=sc3: q.tensor_sub(v_, v_, s3))
        W("scalar", dv, v)
        if tc == 0:
            W("scalar", dv, V_EPST)
        v = EA(lambda q, v_=var_s, s3=sc3: q.activation(s3, v_, AF.Ln, bias=epst[0:1, 0:1]))
        W("scalar", ac, v)
        v = EA(lambda q, r=rstd_s, s3=sc3: q.activation(r, s3, AF.Exp, scale=mhalf[0:1, 0:1]))
        W("vector", ac, v)
        v = EV(lambda q, v_=var_s: q.tensor_scalar(v_, v_, LN_EPS, None, ALU.add))
        W("vector", dv, v)
        v = EV(lambda q, r=rstd_s, s3=sc3: q.tensor_mul(s3, r, r))
        W("vector", dv, v)
        v = EV(lambda q, v_=var_s, s3=sc3: q.tensor_mul(s3, s3, v_))
        W("vector", dv, v)
        v = EV(lambda q, s3=sc3: q.tensor_scalar(s3, s3, -0.5, 1.5, ALU.mult, ALU.add))
        W("vector", dv, v)
        v_newt = EV(lambda q, r=rstd_s, s3=sc3: q.tensor_mul(r, r, s3))
        v_bc = MACC(pW[:, 0:512], [(onesrow[:, :], mean_s)], waits=[(dv, v_newt)])
        v_bc2 = MACC(pW[:, 512:1024], [(onesrow[:, :], rstd_s)])
        W("scalar", pe, v_bc2)
        EA(lambda q: q.copy(mb_sb, pW[:, 0:512]))
        v_ev = EA(lambda q: q.copy(rb_sb, pW[:, 512:1024]))
        W("vector", ac, v_ev)
        W("vector", pe, v_sumsq)
        for k in range(8):
            EV(lambda q, k=k: q.tensor_sub(
                xnf[:, 512 * k:512 * (k + 1)], xf32[:, 512 * k:512 * (k + 1)], mb_sb))
            W("vector", dv, c_dv.v)
            EV(lambda q, k=k: q.tensor_mul(
                xnf[:, 512 * k:512 * (k + 1)], xnf[:, 512 * k:512 * (k + 1)], rb_sb))
        v_xn = c_dv.v
        W("scalar", dv, v_xn)
        for k in range(8):
            EA(lambda q, k=k, tc=tc: q.copy(
                xnbf[:, 2048 * k + 512 * tc:2048 * k + 512 * (tc + 1)],
                xnf[:, 512 * k:512 * (k + 1)]))
        v_xncast[tc] = c_ac.v
        v_sc = MACC(pS[0:8, :],
                    [(wg_sb[:, 8 * k:8 * (k + 1)], xnf[:, 512 * k:512 * (k + 1)])
                     for k in range(8)],
                    waits=[(dv, v_xn)])
        W("scalar", pe, v_sc)
        v_scsb = EA(lambda q: q.copy(scsb[:, :], pS[0:8, :]))
        W("tensor", ac, v_scsb)
        for j in range(4):
            EP(lambda q, j=j: q.transpose(
                pW[:, 8 * j:8 * (j + 1)], scsb[0:8, 128 * j:128 * (j + 1)], identf[0:8, 0:8]))
        v_tr = c_pe.v
        W("scalar", pe, v_tr)
        W("scalar", g_sc.s, g_sc.c.v)
        v_sctm = EA(lambda q: q.copy(sctm[:, :], pW[:, 0:32]))
        W("sync", ac, v_sctm)
        DMA("sync", g_sc, lambda q, tc=tc: q.dma_start(
            out=sc_in[512 * tc:512 * (tc + 1), :].rearrange("(j p) e -> p j e", p=128),
            in_=sctm[:, :].rearrange("p (j e) -> p j e", e=8)))
    V_SCIN = g_sc.c.v
    V_P1_PE = c_pe.v

    if stop_after < 2:
        _final_stub()
        return _finish()
    v_evac_ps = [0, 0]
    v_vvn_prev = 0
    v_dtabs_prev = 0
    W("tensor", g_wssm.s, V_WSSM)
    for tcc in range(4):
        W("tensor", ac, v_xncast[tcc])
        if tcc == 0:
            W("tensor", ac, v_xncast[3])
            W("tensor", g_sc.s, V_SCIN)
        for uk in range(16):
            pbuf = [p0, p1][uk % 2]
            W("tensor", ac, v_evac_ps[uk % 2])
            vm = MACC(pbuf[:, :],
                      [(arena2[:, 2048 * k + 128 * uk:2048 * k + 128 * (uk + 1)],
                        xnbf[:, 2048 * k + 512 * tcc:2048 * k + 512 * (tcc + 1)])
                       for k in range(8)])
            dstb = ubf if uk < 8 else arena3
            W("scalar", pe, vm)
            v_evac_ps[uk % 2] = EA(lambda q, d=dstb, uk=uk, tcc=tcc, pbuf=pbuf: q.copy(
                d[:, 2048 * (uk % 8) + 512 * tcc:2048 * (uk % 8) + 512 * (tcc + 1)],
                pbuf[:, :]))
        v_u_done = c_ac.v
        v_b = MACC(p2[:, :],
                   [(wstb[:, 128 * uk:128 * (uk + 1)],
                     ubf[:, 2048 * uk + 512 * tcc:2048 * uk + 512 * (tcc + 1)])
                    for uk in range(8)],
                   waits=[(ac, v_u_done), (dv, v_vvn_prev)])
        v_dt = MACC(p3[:, :],
                    [(wdtb[:, 128 * uk:128 * (uk + 1)],
                      ubf[:, 2048 * uk + 512 * tcc:2048 * uk + 512 * (tcc + 1)])
                     for uk in range(8)],
                    waits=[(ac, v_dtabs_prev)])
        W("scalar", pe, v_dt)
        W("scalar", dv, c_dv.v)
        EA(lambda q: q.activation(dts1[:, :], p3[:, :], AF.Relu, bias=b_dt[:, 0:1]))
        v = EA(lambda q: q.activation(dts2[:, :], p3[:, :], AF.Abs, bias=b_dt[:, 0:1]))
        v_dtabs_prev = v
        W("scalar", ac, v)
        v = EA(lambda q: q.activation(dts3[:, :], dts2[:, :], AF.Exp, scale=-1.0))
        W("scalar", ac, v)
        v_l = EA(lambda q: q.activation(dts2[:, :], dts3[:, :], AF.Ln, bias=1.0))
        W("vector", ac, v_l)
        v = EV(lambda q: q.tensor_add(dts1[:, :], dts1[:, :], dts2[:, :]))
        W("vector", dv, v)
        v_dtv = EV(lambda q: q.tensor_scalar_min(dts1[:, :], dts1[:, :], 10.0))
        W("vector", dv, v_dtv)
        EV(lambda q, tcc=tcc: q.reduce_sum(dtsum[:, tcc:tcc + 1], dts1[:, :], axis=AX.X))
        W("scalar", dv, v_dtv)
        if tcc == 0:
            W("scalar", dv, V_NEGRATE)
        v_dec = EA(lambda q, tcc=tcc: q.activation(
            decay[:, 512 * tcc:512 * (tcc + 1)], dts1[:, :], AF.Exp, scale=negrate[:, 0:1]))
        W("vector", ac, v_dec)
        W("vector", pe, v_b)
        v_vvn_prev = EV(lambda q, tcc=tcc: q.scalar_tensor_tensor(
            vvn[:, 512 * tcc:512 * (tcc + 1)], decay[:, 512 * tcc:512 * (tcc + 1)],
            1.0, p2[:, :], ALU.subtract, ALU.mult))
    W("vector", dv, c_dv.v)
    v = EV(lambda q: q.tensor_tensor_scan(hh, decay, vvn, 0.0, ALU.mult, ALU.subtract))
    W("vector", dv, v)
    v = EV(lambda q: q.tensor_add(dtsum[:, 0:1], dtsum[:, 0:1], dtsum[:, 1:2]))
    W("vector", dv, v)
    v = EV(lambda q: q.tensor_add(dtsum[:, 2:3], dtsum[:, 2:3], dtsum[:, 3:4]))
    W("vector", dv, v)
    v_dtt = EV(lambda q: q.tensor_add(dtsum[:, 0:1], dtsum[:, 0:1], dtsum[:, 2:3]))
    W("scalar", dv, v_dtt)
    v_at = EA(lambda q: q.activation(dtsum[:, 1:2], dtsum[:, 0:1], AF.Exp,
                                     scale=negrate[:, 0:1]))
    W("sync", ac, v_at)
    W("sync", dv, c_dv.v)
    DMA("sync", g_sum, lambda q: q.dma_start(out=sum_in[0:1, :].rearrange("o p -> p o"), in_=dtsum[:, 1:2]))
    v_sumdma = DMA("sync", g_sum, lambda q: q.dma_start(out=sum_in[1:2, :].rearrange("o p -> p o"), in_=hh[:, 2047:2048]))
    W("gpsimd", g_sum.s, v_sumdma)
    em("gpsimd", lambda q: q.collective_compute(
        "AllGather", ALU.bypass, [list(range(NCORE))],
        ins=[sum_in.ap()], outs=[sum_out.ap()]).then_inc(cc2, 1))
    W("gpsimd", cc2, 1)
    v_sumg = DMA("gpsimd", g_sumg, lambda q: q.dma_start(out=sumg[0:1, :], in_=sum_out.rearrange("a b -> (a b)")[None, :]))
    W("vector", g_sumg.s, v_sumg)
    EV(lambda q: q.memset(hin[:, :], 0.0))
    for j in range(NCORE):
        W("vector", dv, c_dv.v)
        EV(lambda q, j=j: q.tensor_scalar(hf1[:, :], sumg[0:1, 256 * j:256 * j + 128],
                                          -1.0, None, ALU.add))
        W("vector", dv, c_dv.v)
        EV(lambda q, j=j: q.tensor_scalar(hf1[:, :], hf1[:, :],
                                          selv[0:1, j:j + 1], 1.0, ALU.mult, ALU.add))
        W("vector", dv, c_dv.v)
        EV(lambda q: q.tensor_mul(hf2[:, :], hin[:, :], hf1[:, :]))
        W("vector", dv, c_dv.v)
        EV(lambda q, j=j: q.tensor_scalar(hf1[:, :], sumg[0:1, 256 * j + 128:256 * j + 256],
                                          selv[0:1, j:j + 1], None, ALU.mult))
        W("vector", dv, c_dv.v)
        EV(lambda q: q.tensor_add(hin[:, :], hf2[:, :], hf1[:, :]))
    v_hin = c_dv.v
    v_htr = MACC(pW[:, 0:1], [(hin[:, :], identf[0:1, 0:1])], waits=[(dv, v_hin)])
    W("vector", pe, v_htr)
    v = EV(lambda q: q.tensor_copy(h0col[:, :], pW[:, 0:1]))
    W("vector", dv, v)
    v_scan2 = EV(lambda q: q.tensor_tensor_scan(hh, decay, vvn, h0col[:, 0:1],
                                                ALU.mult, ALU.subtract))
    W("vector", dv, v_scan2)
    v_hcast = EV(lambda q: q.tensor_copy(hcast[:, :], hh))
    if debug:
        W("sync", dv, v_scan2)
        DMA("sync", g_dbq, lambda q: q.dma_start(out=hh_dbg[:, :], in_=hh))

    v_z_evac = [0, 0]
    for tcc in range(4):
        for k in range(8):
            pbuf = [p0, p1][k % 2]
            vy = MACC(pbuf[:, :],
                      [(wctb[:, 128 * k:128 * (k + 1)],
                        hcast[:, 512 * tcc:512 * (tcc + 1)])],
                      waits=[(dv, v_hcast), (dv, v_z_evac[k % 2])])
            W("scalar", pe, vy)
            W("scalar", dv, c_dv.v)
            v_sg = EA(lambda q, k=k, tcc=tcc: q.activation(
                dts1[:, :], arena3[:, 2048 * k + 512 * tcc:2048 * k + 512 * (tcc + 1)],
                AF.Silu))
            W("vector", ac, v_sg)
            W("vector", pe, vy)
            EV(lambda q, pbuf=pbuf: q.tensor_mul(dts2[:, :], pbuf[:, :], dts1[:, :]))
            W("vector", dv, c_dv.v)
            EV(lambda q, k=k, tcc=tcc: q.tensor_scalar(
                dts3[:, :], xnbf[:, 2048 * k + 512 * tcc:2048 * k + 512 * (tcc + 1)],
                dmat[:, k:k + 1], None, ALU.mult))
            W("vector", dv, c_dv.v)
            v_z_evac[k % 2] = EV(lambda q, k=k, tcc=tcc: q.tensor_add(
                ubf[:, 2048 * k + 512 * tcc:2048 * k + 512 * (tcc + 1)],
                dts2[:, :], dts3[:, :]))
    V_Z = c_dv.v
    V_P3_PE = c_pe.v

    if stop_after < 3:
        _final_stub()
        return _finish()
    W("gpsimd", g_sc.s, V_SCIN)
    em("gpsimd", lambda q: q.collective_compute(
        "AllGather", ALU.bypass, [list(range(NCORE))],
        ins=[sc_in.ap()], outs=[sc_out.ap()]).then_inc(cc1, 1))
    W("gpsimd", cc1, 1)
    v_scg = DMA("gpsimd", g_scg, lambda q: q.dma_start(
        out=scg2.rearrange("p (tl e) -> p tl e", e=8),
        in_=sc_out.rearrange("(p tl) e -> p tl e", p=128)))
    emask = a1f[:, 0:1024]
    W("sync", dv, V_Z)
    v_eml = DMA("sync", g_em, lambda q: q.dma_start(out=emask, in_=emask_in[:, :]))
    W("vector", g_scg.s, v_scg)
    W("vector", g_em.s, v_eml)
    W("vector", dv, V_Z)
    v = EV(lambda q: q.tensor_mul(a1f[:, 1024:2048], scg2, emask))
    W("vector", dv, v)
    v_vE = EV(lambda q: q.tensor_reduce(
        vE[:, :], a1f[:, 1024:2048].rearrange("p (tl e) -> p tl e", e=8),
        axis=AX.X, op=ALU.add))
    W("vector", dv, v_vE)
    EV(lambda q: q.memset(bis[0:1, :], 0.0))
    EV(lambda q: q.memset(bis[0:1, 0:1], -64.0))
    W("vector", dv, c_dv.v)
    EV(lambda q: q.memset(bis[0:1, 1:2], 64.0))
    W("vector", dv, c_dv.v)
    EV(lambda q: q.memset(bis[0:1, 2:3], 0.0))
    v_mid = c_dv.v
    for it in range(NBIS):
        v_bcm = MACC(p2[:, 0:1], [(onesrow[:, :], bis[0:1, 2:3])],
                     waits=[(dv, v_mid)])
        W("scalar", pe, v_bcm)
        v_thr = EA(lambda q: q.copy(thrcol[:, :], p2[:, 0:1]))
        W("vector", ac, v_thr)
        v_cmp = EV(lambda q: q.tensor_scalar(cmpscr, vE[:, :], thrcol[:, 0:1],
                                             0.0, ALU.is_ge, ALU.add,
                                             accum_out=cntcol[:, 0:1]))
        v_cnt = MACC(p2[0:1, 1:2], [(onescol[:, :], cntcol[:, :])],
                     waits=[(dv, v_cmp)])
        W("vector", pe, v_cnt)
        EV(lambda q: q.tensor_scalar(bis[0:1, 3:4], p2[0:1, 1:2], KSEL - 0.5,
                                     None, ALU.is_ge))
        W("vector", dv, c_dv.v)
        EV(lambda q: q.tensor_sub(bis[0:1, 4:5], bis[0:1, 2:3], bis[0:1, 0:1]))
        W("vector", dv, c_dv.v)
        EV(lambda q: q.tensor_mul(bis[0:1, 5:6], bis[0:1, 3:4], bis[0:1, 4:5]))
        W("vector", dv, c_dv.v)
        EV(lambda q: q.tensor_add(bis[0:1, 0:1], bis[0:1, 0:1], bis[0:1, 5:6]))
        W("vector", dv, c_dv.v)
        EV(lambda q: q.tensor_sub(bis[0:1, 4:5], bis[0:1, 1:2], bis[0:1, 2:3]))
        W("vector", dv, c_dv.v)
        EV(lambda q: q.tensor_mul(bis[0:1, 5:6], bis[0:1, 3:4], bis[0:1, 4:5]))
        W("vector", dv, c_dv.v)
        EV(lambda q: q.tensor_add(bis[0:1, 1:2], bis[0:1, 2:3], bis[0:1, 5:6]))
        W("vector", dv, c_dv.v)
        EV(lambda q: q.tensor_add(bis[0:1, 2:3], bis[0:1, 0:1], bis[0:1, 1:2]))
        W("vector", dv, c_dv.v)
        v_mid = EV(lambda q: q.tensor_scalar(bis[0:1, 2:3], bis[0:1, 2:3], 0.5,
                                             None, ALU.mult))
    v_bcf = MACC(p2[:, 0:1], [(onesrow[:, :], bis[0:1, 0:1])], waits=[(dv, v_mid)])
    W("scalar", pe, v_bcf)
    v_thrf = EA(lambda q: q.copy(thrcol[:, :], p2[:, 0:1]))
    if debug:
        W("sync", dv, v_mid)
        DMA("sync", g_dbq, lambda q: q.dma_start(out=thr_dbg[0:1, :], in_=bis[0:1, :]))
    W("vector", ac, v_thrf)
    v_msk = EV(lambda q: q.tensor_scalar(msk, vE[:, :], thrcol[:, 0:1],
                                         None, ALU.is_ge))
    W("vector", dv, v_msk)
    v_ps = EV(lambda q: q.tensor_tensor_scan(pscan, onestt[:, :], msk,
                                             0.0, ALU.mult, ALU.add))
    v_exm = MACC(pW[:, 0:1], [(ltstrict[:, :], pscan[:, 127:128])],
                 waits=[(dv, v_ps)])
    v_bnm = MACC(pW[:, 1:2], [(lt16[:, :], pscan[:, 127:128])])
    W("vector", pe, v_bnm)
    EV(lambda q: q.tensor_copy(exc[:, :], pW[:, 0:1]))
    W("vector", dv, c_dv.v)
    EV(lambda q: q.tensor_copy(bnd[:, :], pW[:, 1:2]))
    W("vector", dv, c_dv.v)
    EV(lambda q: q.tensor_scalar(excm1[:, :], exc[:, :], -1.0, None, ALU.add))
    W("vector", dv, c_dv.v)
    EV(lambda q: q.tensor_scalar(slotf, pscan, excm1[:, 0:1],
                                 None, ALU.add))
    W("vector", dv, c_dv.v)
    EV(lambda q: q.tensor_scalar(fscr[:, :], slotf, bnd[:, 0:1],
                                 None, ALU.subtract))
    W("vector", dv, c_dv.v)
    EV(lambda q: q.tensor_scalar(fscr2, slotf, KSEL - 0.5, None, ALU.is_lt))
    W("vector", dv, c_dv.v)
    EV(lambda q: q.tensor_mul(fscr2, fscr2, msk))
    W("vector", dv, c_dv.v)
    EV(lambda q: q.tensor_scalar(gtv, fscr[:, :], CAP - 0.5, None, ALU.is_lt))
    W("vector", dv, c_dv.v)
    EV(lambda q: q.tensor_mul(gtv, gtv, msk))
    W("vector", dv, c_dv.v)
    EV(lambda q: q.tensor_copy(iscr, slotf))
    W("vector", dv, c_dv.v)
    EV(lambda q: q.tensor_scalar(iscr2, iscr, 127, None, ALU.bitwise_and))
    W("vector", dv, c_dv.v)
    EV(lambda q: q.tensor_scalar(iscr2, iscr2, 5, None, ALU.logical_shift_left))
    W("vector", dv, c_dv.v)
    EV(lambda q: q.tensor_scalar(iscr, iscr, 7, None, ALU.logical_shift_right))
    W("vector", dv, c_dv.v)
    EV(lambda q: q.tensor_add(iscr2, iscr2, iscr))
    W("vector", dv, c_dv.v)
    EV(lambda q: q.memset(desti, BIG))
    W("vector", dv, c_dv.v)
    EV(lambda q: q.tensor_copy(iscr, fscr2))
    W("vector", dv, c_dv.v)
    EV(lambda q: q.copy_predicated(desti, iscr, iscr2))
    W("vector", dv, c_dv.v)
    EV(lambda q: q.tensor_copy(iscr2, fscr[:, :]))
    W("vector", dv, c_dv.v)
    EV(lambda q: q.tensor_scalar(iscr2, iscr2, jbase[:, 0:1], None, ALU.add))
    W("vector", dv, c_dv.v)
    EV(lambda q: q.memset(pdesti, BIG))
    W("vector", dv, c_dv.v)
    EV(lambda q: q.tensor_copy(iscr, gtv))
    W("vector", dv, c_dv.v)
    EV(lambda q: q.copy_predicated(pdesti, iscr, iscr2))
    W("vector", dv, c_dv.v)
    W("scalar", dv, c_dv.v)
    v_gtv = EA(lambda q: q.activation(gtv, vE[:, :], AF.Sigmoid))
    EV(lambda q: q.tensor_scalar(iscr, tokid[:, :], 2047, None, ALU.bitwise_and))
    v_lid = c_dv.v
    W("vector", dv, v_lid)
    EV(lambda q: q.tensor_copy(prs5[:, 0:640:5], tokid[:, :]))
    W("vector", ac, v_gtv)
    EV(lambda q: q.tensor_copy(prs5[:, 1:640:5], gtv.bitcast(I32)))
    EV(lambda q: q.tensor_copy(prs5[:, 2:640:5], pdesti))
    EV(lambda q: q.tensor_copy(prs5[:, 3:640:5], iscr))
    EV(lambda q: q.tensor_copy(iscr2, fscr[:, :]))
    EV(lambda q: q.tensor_scalar(msk.bitcast(I32), iscr2, 7, None, ALU.logical_shift_right))
    EV(lambda q: q.tensor_scalar(iscr2, iscr2, 127, None, ALU.bitwise_and))
    EV(lambda q: q.tensor_scalar(pscan.bitcast(I32), iscr2, 2, None, ALU.logical_shift_left))
    EV(lambda q: q.tensor_add(iscr2, iscr2, pscan.bitcast(I32)))
    EV(lambda q: q.tensor_add(iscr2, iscr2, msk.bitcast(I32)))
    EV(lambda q: q.tensor_copy(fscr2, iscr2))
    EV(lambda q: q.tensor_scalar(fscr2, fscr2, jbase[:, 0:1], None, ALU.add))
    EV(lambda q: q.tensor_copy(iscr2, fscr2))
    EV(lambda q: q.memset(msk.bitcast(I32), BIG))
    EV(lambda q: q.tensor_copy(fscr2, pdesti))
    EV(lambda q: q.tensor_scalar(fscr2, fscr2, 1.0e8, None, ALU.is_lt))
    EV(lambda q: q.tensor_copy(iscr, fscr2))
    EV(lambda q: q.copy_predicated(msk.bitcast(I32), iscr, iscr2))
    v_quad = EV(lambda q: q.tensor_copy(prs5[:, 4:640:5], msk.bitcast(I32)))
    W("gpsimd", dv, v_quad)
    W("gpsimd", g_pref.s, V_PREFILL)
    def _mk_scat(tg):
        def f(q):
            q.indirect_dma_start(
                out=idxg[:, :],
                out_offset=IndirectOffsetOnAxis(desti[:, tg:tg + 1], axis=0),
                in_=prs5[:, 5 * tg:5 * (tg + 1)], in_offset=None,
                bounds_check=bc4095[0], oob_is_err=False).then_inc(g_sq1.s, 16)
        return f
    bc4095 = [None]
    def _set_regs(q):
        bc4095[0] = q.to_reg(KSEL - 1)
        bcNPR[0] = q.to_reg(NPR - 1)
        bcNTOK[0] = q.to_reg(NTOK - 1)
        bcT[0] = q.to_reg(T - 1)
        return q.engine_nop()
    bcNPR = [None]; bcNTOK = [None]; bcT = [None]
    em("gpsimd", lambda q: _set_regs(q))
    for tg in range(128):
        em("gpsimd", _mk_scat(tg))
        g_sq1.c.add(16)
    W("gpsimd", g_sq1.s, g_sq1.c.v)
    v_idxl = DMA("gpsimd", g_idxl, lambda q: q.dma_start(
        out=idxg_sb[:, :].rearrange("p (w c) -> p w c", c=5),
        in_=idxg.rearrange("(p w) c -> p w c", p=128)))
    W("vector", g_idxl.s, v_idxl)
    for g in range(32):
        EV(lambda q, g=g: q.tensor_copy(fscr[:, g:g + 1], idxg_sb[:, 5 * g:5 * g + 1]))
    W("vector", dv, c_dv.v)
    v = EV(lambda q: q.tensor_scalar(fscr2[:, 0:32], fscr[:, 0:32], NTOK - 0.5,
                                     None, ALU.is_lt))
    W("vector", dv, v)
    v_gm = EV(lambda q: q.tensor_mul(
        gmask[:, :], fscr2[:, 0:32],
        idxg_sb.bitcast(F32)[:, :].rearrange("p (w c) -> p w c", c=5)[:, :, 1]))
    W("gpsimd", dv, v_gm)
    def _mk_idscat(g):
        def f(q):
            q.indirect_dma_start(
                out=a2ais[:, :],
                out_offset=IndirectOffsetOnAxis(idxg_sb[:, 5 * g + 4:5 * g + 5], axis=0),
                in_=idxg_sb[:, 5 * g + 3:5 * g + 4], in_offset=None,
                bounds_check=bcNPR[0], oob_is_err=False).then_inc(g_sq2.s, 16)
        return f
    for g in range(32):
        W("gpsimd", g_sq2.s, g_sq2.c.v)
        em("gpsimd", _mk_idscat(g))
        g_sq2.c.add(16)
    v_idsf = g_sq2.c.v

    if stop_after < 4:
        _final_stub()
        return _finish()
    xg = a2f[:, 0:4096]
    xnr = a2f[:, 6656:7680]
    sqs = xnr
    xnrb = arena2[:, 12288:13312]
    exT = arena2[:, 8192:12288]
    v_w1l = [0, 0]
    v_w2l = [0, 0]
    v_gelu_all = 0
    v_epi_dv = 0
    v_scat_prev = 0
    v_tpc = [0]
    for rc in range(8):
        pe_mark_rc = c_pe.v
        for gg in range(4):
            g = 4 * rc + gg
            if rc == 0 and gg == 0:
                W("gpsimd", pe, V_P3_PE)
                W("gpsimd", g_sq2.s, v_idsf)
            if rc > 0:
                W("gpsimd", dv, v_epi_dv)
            DMA("gpsimd", g_gath, lambda q, g=g, gg=gg: q.indirect_dma_start(
                out=xg.rearrange("p (b f) -> p b f", b=4)[:, gg, :],
                out_offset=None,
                in_=xfull[:, :],
                in_offset=IndirectOffsetOnAxis(idxg_sb[:, 5 * g:5 * g + 1], axis=0),
                bounds_check=bcNTOK[0], oob_is_err=False))
        v_gath = g_gath.c.v
        for gg in range(4):
            g = 4 * rc + gg
            xgb = a2f[:, 1024 * gg:1024 * (gg + 1)]
            W("vector", g_gath.s, v_gath)
            EV(lambda q, xgb=xgb: q.reduce_sum(mcol[:, :], xgb, axis=AX.X))
            W("scalar", g_gath.s, v_gath)
            if gg == 0 and rc == 0:
                W("scalar", pe, V_P3_PE)
            W("vector", dv, c_dv.v)
            EV(lambda q, xgb=xgb: q.scalar_tensor_tensor(
                sqs, xgb, 1.0, xgb, ALU.mult, ALU.mult, accum_out=s2col[:, :]))
            W("vector", dv, c_dv.v)
            EV(lambda q: q.tensor_scalar(mcol[:, :], mcol[:, :], 1.0 / DM, None, ALU.mult))
            W("vector", dv, c_dv.v)
            EV(lambda q: q.tensor_scalar(s2col[:, :], s2col[:, :], 1.0 / DM, None, ALU.mult))
            W("vector", dv, c_dv.v)
            EV(lambda q: q.tensor_mul(vcol[:, :], mcol[:, :], mcol[:, :]))
            W("vector", dv, c_dv.v)
            EV(lambda q: q.tensor_sub(vcol[:, :], s2col[:, :], vcol[:, :]))
            W("scalar", dv, c_dv.v)
            v = EA(lambda q: q.activation(ncol[:, :], vcol[:, :], AF.Ln, bias=epst[:, 0:1]))
            W("scalar", ac, v)
            v_r0 = EA(lambda q: q.activation(rcol[:, :], ncol[:, :], AF.Exp,
                                             scale=mhalf[:, 0:1]))
            W("vector", ac, v_r0)
            EV(lambda q: q.tensor_scalar(vcol[:, :], vcol[:, :], LN_EPS, None, ALU.add))
            W("vector", dv, c_dv.v)
            EV(lambda q: q.tensor_mul(ncol[:, :], rcol[:, :], rcol[:, :]))
            W("vector", dv, c_dv.v)
            EV(lambda q: q.tensor_mul(ncol[:, :], ncol[:, :], vcol[:, :]))
            W("vector", dv, c_dv.v)
            EV(lambda q: q.tensor_scalar(ncol[:, :], ncol[:, :], -0.5, 1.5, ALU.mult, ALU.add))
            W("vector", dv, c_dv.v)
            EV(lambda q: q.tensor_mul(rcol[:, :], rcol[:, :], ncol[:, :]))
            W("vector", dv, c_dv.v)
            EV(lambda q, xgb=xgb: q.tensor_scalar(xnr, xgb, mcol[:, 0:1], None, ALU.subtract))
            W("vector", dv, c_dv.v)
            W("vector", ac, v_tpc[0])
            v_xnr = EV(lambda q: q.tensor_scalar(xnrb[:, :], xnr, rcol[:, 0:1], None, ALU.mult))
            W("tensor", dv, v_xnr)
            for k in range(8):
                W("tensor", ac, v_tpc[0])
                v_tp = EP(lambda q, k=k: q.transpose(
                    pT[:, :], xnrb[:, 128 * k:128 * (k + 1)], identbf[:, :]))
                W("scalar", pe, v_tp)
                v_tpc[0] = EA(lambda q, k=k, gg=gg: q.copy(
                    exT[:, 512 * k + 128 * gg:512 * k + 128 * (gg + 1)], pT[:, :]))
        v_ext = c_ac.v
        v_gelu_pb = [0, 0]
        for ffc in range(32):
            hbuf = ffc % 2
            w1c = arena3[:, 1024 * hbuf:1024 * (hbuf + 1)]
            if rc == 0 and ffc < 2:
                W("sync", g_stgst.s, V_WBF)
                W("sync", pe, V_P3_PE)
            else:
                W("sync", pe, v_w1l[hbuf])
            vl = DMA("sync", g_w1[hbuf], lambda q, ffc=ffc, w1c=w1c: q.dma_start(
                out=w1c.rearrange("p (k f) -> p k f", k=8),
                in_=w1bfd[:, 128 * ffc:128 * (ffc + 1)].rearrange("(k p) f -> p k f", p=128)))
            pb = [p0, p1][ffc % 2]
            v_mm1 = MACC(pb[:, :],
                         [(w1c[:, 128 * k:128 * (k + 1)], exT[:, 512 * k:512 * (k + 1)])
                          for k in range(8)],
                         waits=[(g_w1[hbuf].s, vl), (ac, v_ext),
                                (ac, v_gelu_pb[ffc % 2]),
                                (dv, v_epi_dv if ffc < 2 else 0)])
            v_w1l[hbuf] = v_mm1
            W("scalar", pe, v_mm1)
            v_gelu_pb[ffc % 2] = EA(lambda q, ffc=ffc, pb=pb: q.activation(
                arena1[:, 512 * ffc:512 * (ffc + 1)], pb[:, :], AF.Gelu,
                bias=b1m[:, ffc:ffc + 1]))
        v_gelu_all = c_ac.v
        for hh2 in range(2):
            for ffc in range(32):
                wbuf = ffc % 2
                w2c = arena3[:, 2048 + 1024 * wbuf:2048 + 1024 * wbuf + 512]
                W("sync", pe, v_w2l[wbuf])
                vl = DMA("sync", g_w2[wbuf], lambda q, ffc=ffc, hh2=hh2, w2c=w2c: q.dma_start(
                    out=w2c, in_=w2bfd[128 * ffc:128 * (ffc + 1), 512 * hh2:512 * (hh2 + 1)]))
                for sub in range(4):
                    pbs = [p0, p1, p2, p3][sub]
                    lastk = (ffc == 31)
                    wts = [(g_w2[wbuf].s, vl)]
                    if ffc == 0:
                        wts.append((ac, v_gelu_all))
                        if hh2 == 1:
                            wts.append((dv, v_epi_half0))
                        else:
                            wts.append((dv, v_epi_dv))
                    vm = MACC(pbs[:, :],
                              [(arena1[:, 512 * ffc + 128 * sub:512 * ffc + 128 * (sub + 1)],
                                w2c)],
                              waits=wts) if False else None
                    for sw, vw in wts:
                        W("tensor", sw, vw)
                    certify = (sub == 3) or lastk
                    em("tensor", lambda q, ffc=ffc, sub=sub, pbs=pbs, w2c=w2c,
                       lastk=lastk, certify=certify:
                       (q.matmul(pbs[:, :],
                                 arena1[:, 512 * ffc + 128 * sub:512 * ffc + 128 * (sub + 1)],
                                 w2c, start=(ffc == 0), stop=lastk).then_inc(pe, 1)
                        if certify else
                        q.matmul(pbs[:, :],
                                 arena1[:, 512 * ffc + 128 * sub:512 * ffc + 128 * (sub + 1)],
                                 w2c, start=(ffc == 0), stop=lastk)))
                    if certify:
                        c_pe.add(1)
                v_w2l[wbuf] = c_pe.v
            v_mm2h = c_pe.v
            if hh2 == 0:
                pass
            for sub in range(4):
                pbs = [p0, p1, p2, p3][sub]
                g = 4 * rc + sub
                oev = a3f[:, 4096 + 512 * sub:4096 + 512 * (sub + 1)][:, 0:512]
                osb = arena3[:, 4096 + 1024 * sub:4096 + 1024 * (sub + 1)]
                W("vector", pe, v_mm2h)
                if rc > 0 and hh2 == 0:
                    W("vector", g_osb[rc - 1].s, 64)
                EV(lambda q, pbs=pbs, hh2=hh2, oev=oev: q.tensor_add(
                    oev, pbs[:, :], b2bc[:, 512 * hh2:512 * (hh2 + 1)]))
                W("vector", dv, c_dv.v)
                v_epi_dv = EV(lambda q, g=g, hh2=hh2, osb=osb, oev=oev: q.tensor_scalar(
                    osb[:, 512 * hh2:512 * (hh2 + 1)], oev, gmask[:, g:g + 1],
                    None, ALU.mult))
            if hh2 == 0:
                v_epi_half0 = v_epi_dv
        if rc == 0:
            W("gpsimd", g_az.s, V_AZERO)
        for sub in range(4):
            g = 4 * rc + sub
            osb = arena3[:, 4096 + 1024 * sub:4096 + 1024 * (sub + 1)]
            W("gpsimd", dv, v_epi_dv)
            em("gpsimd", lambda q, g=g, osb=osb, rc=rc: q.indirect_dma_start(
                out=a2as[:, :],
                out_offset=IndirectOffsetOnAxis(idxg_sb[:, 5 * g + 2:5 * g + 3], axis=0),
                in_=osb, in_offset=None,
                bounds_check=bcNPR[0], oob_is_err=False).then_inc(g_osb[rc].s, 16))
            g_osb[rc].c.add(16)
        v_scat_prev = rc
    if stop_after < 5:
        _final_stub()
        return _finish()
    for rc in range(8):
        W("gpsimd", g_osb[rc].s, 64)
    em("gpsimd", lambda q: q.collective_compute(
        "AllToAll", ALU.bypass, [list(range(NCORE))],
        ins=[a2as.ap()], outs=[a2ar.ap()]).then_inc(cc3, 1))
    em("gpsimd", lambda q: q.collective_compute(
        "AllToAll", ALU.bypass, [list(range(NCORE))],
        ins=[a2ais.ap()], outs=[a2air.ap()]).then_inc(cc4, 1))
    W("gpsimd", cc3, 1)
    W("gpsimd", cc4, 1)
    if stop_after < 4.3:
        _final_stub()
        return _finish()
    v_wol = DMA("gpsimd", g_wo, lambda q: q.dma_start(
        out=arena2[:, 0:8192].rearrange("p (k e) -> p k e", k=8),
        in_=w_outT.rearrange("(k p) e -> p k e", p=128)))
    W("gpsimd", g_gz.s, V_GZERO)
    v_ridl = DMA("gpsimd", g_mr, lambda q: q.dma_start(
        out=ridl[:, :].rearrange("p (j w) -> p j w", w=5),
        in_=a2air.rearrange("(j p w) o -> p j (w o)", p=128, w=5)))
    W("gpsimd", g_mr.s, v_ridl)
    for j in range(8):
        vd = DMA("gpsimd", g_mr, lambda q, j=j: q.dma_start(
            out=arena3[:, 0:5120].rearrange("p (c d) -> p c d", c=5),
            in_=a2ar[CAP * j:CAP * (j + 1), :].rearrange("(c p) d -> p c d", p=128)))
        W("gpsimd", g_mr.s, vd)
        for c5 in range(5):
            m = 5 * j + c5
            W("gpsimd", g_ma[j].s, g_ma[j].c.v)
            em("gpsimd", lambda q, j=j, c5=c5, m=m: q.indirect_dma_start(
                out=gdense[:, :],
                out_offset=IndirectOffsetOnAxis(ridl[:, m:m + 1], axis=0),
                in_=arena3[:, 1024 * c5:1024 * c5 + 1024], in_offset=None,
                bounds_check=bcT[0], oob_is_err=False,
                compute_op=ALU.add).then_inc(g_ma[j].s, 16))
            g_ma[j].c.add(16)
        W("gpsimd", g_ma[j].s, 80)
    if stop_after < 4.7:
        _final_stub()
        return _finish()
    gsil = a1f[:, 0:1024]
    xstile = a1f[:, 1024:2048]
    outtile = a1f[:, 2048:3072]
    v_sdbg_prev = [0]
    v_gl = 0
    v_out_ev = 0
    n_tt = 16 if stop_after >= 4.9 else (4 if stop_after >= 4.85 else 1)
    for tt in range(n_tt):
        if tt == 0:
            for jj in range(8):
                W("sync", g_ma[jj].s, 80)
        else:
            W("sync", dv, v_out_ev)
        vg = DMA("sync", g_gt, lambda q, tt=tt: q.dma_start(
            out=gtile[:, :], in_=gdense[128 * tt:128 * (tt + 1), :]))
        vx = DMA("sync", g_xs, lambda q, tt=tt: q.dma_start(
            out=xstile, in_=xs[128 * tt:128 * (tt + 1), :]))
        W("scalar", g_gt.s, vg)
        if debug:
            W("scalar", g_dbq.s, g_dbq.c.v)
        v_sil = EA(lambda q: q.activation(gsil, gtile[:, :], AF.Silu))
        for half in range(2):
            wts = []
            if half == 0:
                if tt == 0:
                    wts += [(g_wo.s, v_wol), (dv, V_Z)]
                else:
                    wts.append((dv, v_out_ev))
                if debug and tt > 0:
                    wts.append((ac, v_sdbg_prev[0]))
            v_mmo = MACC(pW[:, 512 * half:512 * (half + 1)],
                         [(ubf[:, 2048 * k + 128 * tt:2048 * k + 128 * (tt + 1)],
                           arena2[:, 1024 * k + 512 * half:1024 * k + 512 * (half + 1)])
                          for k in range(8)],
                         waits=wts)
        v_mmo2 = c_pe.v
        if debug:
            W("scalar", pe, v_mmo2)
            v_sdbg = EA(lambda q: q.copy(a1f[:, 3072:4096], pW[:, :]))
            v_sdbg_prev[0] = v_sdbg
            W("sync", ac, v_sdbg)
            DMA("sync", g_dbq, lambda q, tt=tt: q.dma_start(out=s_dbg[128 * tt:128 * (tt + 1), :],
                                                            in_=a1f[:, 3072:4096]))
        W("vector", pe, v_mmo2)
        W("vector", ac, v_sil)
        W("vector", g_xs.s, vx)
        W("vector", g_out.s, g_out.c.v)
        EV(lambda q: q.tensor_mul(outtile[:, 0:512], pW[:, 0:512], gsil[:, 0:512]))
        EV(lambda q: q.tensor_mul(outtile[:, 512:1024], pW[:, 512:1024], gsil[:, 512:1024]))
        EV(lambda q: q.tensor_add(outtile[:, 0:512], outtile[:, 0:512], xstile[:, 0:512]))
        v_out_ev = EV(lambda q: q.tensor_add(outtile[:, 512:1024], outtile[:, 512:1024], xstile[:, 512:1024]))
        W("sync", dv, v_out_ev)
        DMA("sync", g_out, lambda q, tt=tt: q.dma_start(out=out_t[128 * tt:128 * (tt + 1), :],
                                                        in_=outtile))
        if debug:
            W("sync", ac, v_sil)
            DMA("sync", g_dbq, lambda q, tt=tt: q.dma_start(out=g_dbg[128 * tt:128 * (tt + 1), :],
                                                            in_=gsil))
    if debug:
        W("sync", cc1, 1)
        DMA("sync", g_dbq, lambda q: q.dma_start(out=sc_dbg[:, :], in_=sc_in[:, :]))
        W("vector", dv, c_dv.v)
        v = EV(lambda q: q.tensor_copy(fscr[:, :], vE[:, :]))
        W("sync", dv, v)
        DMA("sync", g_dbq, lambda q: q.dma_start(out=ve_dbg[:, :], in_=fscr[:, :]))
    if n_tt < 16 or stop_after < 5:
        _final_stub()
        return _finish()
    W("sync", g_out.s, g_out.c.v)
    if debug:
        W("sync", g_dbq.s, g_dbq.c.v)
    return _finish()


def _host_inputs(inputs, debug=False):
    bf = ml_dtypes.bfloat16
    x = np.ascontiguousarray(np.asarray(inputs["L0_attn_x"], np.float32).reshape(NTOK, DM))
    W_in = np.asarray(inputs["L0_attn_W_in"], np.float32)
    W_state = np.asarray(inputs["L0_attn_W_state"], np.float32)
    W_dt = np.asarray(inputs["L0_attn_W_dt"], np.float32)
    b_dtv = np.asarray(inputs["L0_attn_b_dt"], np.float32)
    A_log = np.asarray(inputs["L0_attn_A_log"], np.float32)
    W_C = np.asarray(inputs["L0_attn_W_C"], np.float32)
    Dv = np.asarray(inputs["L0_attn_D"], np.float32)
    W_out = np.asarray(inputs["L0_attn_W_out"], np.float32)
    Wg = np.asarray(inputs["L0_ffn_Wg"], np.float32)
    W1 = np.asarray(inputs["L0_ffn_W1"], np.float32)
    b1 = np.asarray(inputs["L0_ffn_b1"], np.float32)
    W2 = np.asarray(inputs["L0_ffn_W2"], np.float32)
    b2 = np.asarray(inputs["L0_ffn_b2"], np.float32)

    com = dict(
        xfull=x,
        w_inT=np.ascontiguousarray(W_in.T),
        w_stT=np.ascontiguousarray(W_state.T),
        w_dtT=np.ascontiguousarray(W_dt.T),
        w_cT=np.ascontiguousarray(W_C.T),
        w_outT=np.ascontiguousarray(W_out.T),
        wg_in=np.ascontiguousarray(Wg),
        b_dt_in=np.ascontiguousarray(b_dtv.reshape(DS, 1)),
        a_log_in=np.ascontiguousarray(A_log.reshape(DS, 1)),
        dmat_in=np.ascontiguousarray(Dv.reshape(8, 128).T),
        identbf_in=np.eye(128, dtype=bf),
        identf_in=np.eye(128, dtype=np.float32),
        onescol_in=np.ones((128, 1), np.float32),
        onesrow_in=np.ones((1, 128), np.float32),
        onestt_in=np.ones((128, 128), np.float32),
        ltstrict_in=np.ascontiguousarray(
            (np.arange(128)[:, None] < np.arange(128)[None, :]).astype(np.float32)),
        lt16_in=np.ascontiguousarray(
            (np.arange(128)[:, None] < (16 * (np.arange(128)[None, :] // 16)))
            .astype(np.float32)),
        tokid_in=np.ascontiguousarray(
            (np.arange(128)[:, None] * 128 + np.arange(128)[None, :]).astype(np.int32)),
        jbase_in=np.ascontiguousarray(
            ((np.arange(128) // 16) * CAP).astype(np.float32).reshape(128, 1)),
    )
    in_maps = []
    for c in range(NCORE):
        m = dict(com)
        sl = slice(T * c, T * (c + 1))
        m["xs"] = np.ascontiguousarray(x[sl])
        m["xts"] = np.ascontiguousarray(x[sl].T)
        m["w1e"] = np.ascontiguousarray(W1[c])
        m["w2e"] = np.ascontiguousarray(W2[c])
        m["b1m_in"] = np.ascontiguousarray(b1[c].reshape(32, 128).T)
        m["b2bc_in"] = np.ascontiguousarray(np.broadcast_to(b2[c], (128, DM)))
        sv = np.zeros((1, 8), np.float32)
        for j in range(NCORE):
            if j // 4 == c // 4 and j < c:
                sv[0, j] = 1.0
        m["selv_in"] = sv
        em_ = np.zeros((128, 1024), np.float32)
        em_[:, c::8] = 1.0
        m["emask_in"] = em_
        in_maps.append(m)
    return in_maps


def kernel(**inputs):
    in_maps = _host_inputs(inputs)
    with allow_ici_all_to_all():
        nc = build_program(debug=False)
        res = run_bass_kernel_spmd(nc, in_maps, core_ids=list(range(NCORE)))
    out = np.concatenate([np.asarray(res.results[c]["out"]) for c in range(NCORE)], 0)
    return out.reshape(2, 8192, DM)


# version 05
# speedup vs baseline: 1.1179x; 1.1179x over previous
import numpy as np
import ml_dtypes
import concourse.bass as bass
import concourse.mybir as mybir
from concourse.bass import IndirectOffsetOnAxis
from concourse.bass_utils import run_bass_kernel_spmd
from concourse.aot_env import allow_ici_all_to_all

F32 = mybir.dt.float32
BF16 = mybir.dt.bfloat16
I32 = mybir.dt.int32
AF = mybir.ActivationFunctionType
ALU = mybir.AluOpType
AX = mybir.AxisListType

NCORE = 8
T = 2048
NTOK = 16384
DM = 1024
DS = 128
FF = 4096
KSEL = 4096
CAP = 640
NPR = NCORE * CAP
BIG = 1 << 28
NBIS = 42
RATE_EPS = 1e-4
LN_EPS = 1e-5


class Ctr:
    __slots__ = ("v",)
    def __init__(self):
        self.v = 0
    def add(self, n=1):
        self.v += n
        return self.v


def build_program(debug=False, stop_after=5):
    nc = bass.Bass("TRN2", target_bir_lowering=False, debug=False)
    D = lambda n, s, dt, k="ExternalInput": nc.dram_tensor(n, s, dt, kind=k)

    xts = D("xts", [DM, T], F32)
    xs = D("xs", [T, DM], F32)
    xfull = D("xfull", [NTOK, DM], F32)
    w_inT = D("w_inT", [DM, 2 * DM], F32)
    w_stT = D("w_stT", [DM, DS], F32)
    w_dtT = D("w_dtT", [DM, DS], F32)
    w_cT = D("w_cT", [DS, DM], F32)
    w_outT = D("w_outT", [DM, DM], F32)
    wg_in = D("wg_in", [DM, 8], F32)
    b_dt_in = D("b_dt_in", [DS, 1], F32)
    a_log_in = D("a_log_in", [DS, 1], F32)
    dmat_in = D("dmat_in", [128, 8], F32)
    w1e = D("w1e", [DM, FF], F32)
    w2e = D("w2e", [FF, DM], F32)
    b1m_in = D("b1m_in", [128, 32], F32)
    b2bc_in = D("b2bc_in", [128, DM], F32)
    identbf_in = D("identbf_in", [128, 128], BF16)
    identf_in = D("identf_in", [128, 128], F32)
    onescol_in = D("onescol_in", [128, 1], F32)
    onesrow_in = D("onesrow_in", [1, 128], F32)
    onestt_in = D("onestt_in", [128, 128], F32)
    ltstrict_in = D("ltstrict_in", [128, 128], F32)
    lt16_in = D("lt16_in", [128, 128], F32)
    tokid_in = D("tokid_in", [128, 128], I32)
    jbase_in = D("jbase_in", [128, 1], F32)
    selv_in = D("selv_in", [1, 8], F32)
    emask_in = D("emask_in", [128, 1024], F32)

    sc_in = nc.dram_tensor("sc_in", [T, 8], F32)
    sc_out = nc.dram_tensor("sc_out", [NTOK, 8], F32)
    sum_in = nc.dram_tensor("sum_in", [2, 128], F32)
    sum_out = nc.dram_tensor("sum_out", [16, 128], F32)
    w1bfd = nc.dram_tensor("w1bfd", [DM, FF], BF16)
    w2bfd = nc.dram_tensor("w2bfd", [FF, DM], BF16)
    idxg = nc.dram_tensor("idxg", [KSEL, 5], I32)
    a2as = nc.dram_tensor("a2as", [NPR, DM], BF16)
    a2ar = nc.dram_tensor("a2ar", [NPR, DM], BF16)
    a2ais = nc.dram_tensor("a2ais", [NPR, 1], I32)
    a2air = nc.dram_tensor("a2air", [NPR, 1], I32)
    gdense = nc.dram_tensor("gdense", [T, DM], BF16)
    out_t = D("out", [T, DM], F32, "ExternalOutput")
    if debug:
        hh_dbg = D("hh_dbg", [128, T], F32, "ExternalOutput")
        thr_dbg = D("thr_dbg", [1, 8], F32, "ExternalOutput")
        s_dbg = D("s_dbg", [T, DM], F32, "ExternalOutput")
        g_dbg = D("g_dbg", [T, DM], F32, "ExternalOutput")
        sc_dbg = D("sc_dbg", [T, 8], F32, "ExternalOutput")
        ve_dbg = D("ve_dbg", [128, 128], F32, "ExternalOutput")

    from contextlib import ExitStack
    ctx = ExitStack()
    sb = lambda n, s, dt: ctx.enter_context(nc.sbuf_tensor(n, s, dt))
    ps = lambda n, s, dt: ctx.enter_context(nc.psum_tensor(n, s, dt))
    sem = lambda n: ctx.enter_context(nc.semaphore(n))

    xnbf = sb("xnbf", [128, 8 * T], BF16)
    ubf = sb("ubf", [128, 8 * T], BF16)
    arena1 = sb("arena1", [128, 8 * T], BF16)
    arena2 = sb("arena2", [128, 8 * T], BF16)
    arena3 = sb("arena3", [128, 8 * T], BF16)
    a1f = arena1.bitcast(F32)
    a2f = arena2.bitcast(F32)
    a3f = arena3.bitcast(F32)

    wstage = sb("wstage", [128, 4096], BF16)
    wg_sb = sb("wg_sb", [128, 64], F32)
    wstb = sb("wstb", [128, 1024], BF16)
    wdtb = sb("wdtb", [128, 1024], BF16)
    wctb = sb("wctb", [128, 1024], BF16)
    identbf = sb("identbf", [128, 128], BF16)
    identf = sb("identf", [128, 128], F32)
    onescol = sb("onescol", [128, 1], F32)
    onesrow = sb("onesrow", [1, 128], F32)
    onestt = sb("onestt", [128, 128], F32)
    ltstrict = sb("ltstrict", [128, 128], F32)
    lt16 = sb("lt16", [128, 128], F32)
    tokid = sb("tokid", [128, 128], I32)
    jbase = sb("jbase", [128, 1], F32)
    selv = sb("selv", [1, 8], F32)
    b_dt = sb("b_dt", [128, 1], F32)
    a_log = sb("a_log", [128, 1], F32)
    dmat = sb("dmat", [128, 8], F32)
    b1m = sb("b1m", [128, 32], F32)
    b2bc = sb("b2bc", [128, DM], F32)
    negrate = sb("negrate", [128, 1], F32)
    rates = sb("rates", [128, 4], F32)
    epst = sb("epst", [128, 1], F32)
    mhalf = sb("mhalf", [128, 1], F32)
    meanr = sb("meanr", [1, 512], F32)
    rstdr = sb("rstdr", [1, 512], F32)
    stat1 = sb("stat1", [1, 512], F32)
    stat3 = sb("stat3", [1, 512], F32)
    scsb = sb("scsb", [8, 512], F32)
    sctm = sb("sctm", [128, 32], F32)
    dts1 = sb("dts1", [128, 512], F32)
    dts2 = sb("dts2", [128, 512], F32)
    dts3 = sb("dts3", [128, 512], F32)
    dtsum = sb("dtsum", [128, 4], F32)
    hin = sb("hin", [1, 128], F32)
    hf1 = sb("hf1", [1, 128], F32)
    hf2 = sb("hf2", [1, 128], F32)
    h0col = sb("h0col", [128, 1], F32)
    vE = sb("vE", [128, 128], F32)
    bis = sb("bis", [1, 8], F32)
    thrcol = sb("thrcol", [128, 1], F32)
    cntcol = sb("cntcol", [128, 1], F32)
    exc = sb("exc", [128, 1], F32)
    excm1 = sb("excm1", [128, 1], F32)
    bnd = sb("bnd", [128, 1], F32)
    idxg_sb = sb("idxg_sb", [128, 160], I32)
    senti = sb("senti", [128, 40], I32)
    gmask = sb("gmask", [128, 32], F32)
    mcol = sb("mcol", [128, 1], F32)
    s2col = sb("s2col", [128, 1], F32)
    vcol = sb("vcol", [128, 1], F32)
    rcol = sb("rcol", [128, 1], F32)
    ncol = sb("ncol", [128, 1], F32)
    gtile = sb("gtile", [128, DM], BF16)
    recvd = sb("recvd", [128, DM], BF16)
    zerot = sb("zerot", [128, DM], BF16)
    ridl = sb("ridl", [128, 40], I32)

    p0 = ps("p0", [128, 512], F32)
    p1 = ps("p1", [128, 512], F32)
    p2 = ps("p2", [128, 512], F32)
    p3 = ps("p3", [128, 512], F32)
    pW = ps("pW", [128, 1024], F32)
    pS = ps("pS", [8, 512], F32)
    pT = ps("pT", [128, 512], BF16)

    pe = sem("pe"); c_pe = Ctr()
    dv = sem("dv"); c_dv = Ctr()
    ac = sem("ac"); c_ac = Ctr()
    cc1 = sem("cc1"); cc2 = sem("cc2"); cc3 = sem("cc3"); cc4 = sem("cc4")
    junk = sem("junk")

    class Q:
        def __init__(self, name, chained=False):
            self.s = sem(name)
            self.c = Ctr()
            self.chained = chained

    prog = {k: [] for k in ("sync", "scalar", "vector", "tensor", "gpsimd")}
    def em(engine, fn):
        prog[engine].append(fn)
    def W(engine, s, v):
        if v > 0 and not (engine == "tensor" and s is pe):
            em(engine, lambda q, s=s, v=v: q.wait_ge(s, v))
    def EP(fn):
        em("tensor", lambda q, fn=fn: fn(q).then_inc(pe, 1))
        return c_pe.add(1)
    def EV(fn):
        if c_dv.v > 0:
            em("vector", lambda q, v=c_dv.v: q.wait_ge(dv, v))
        em("vector", lambda q, fn=fn: fn(q).then_inc(dv, 1))
        return c_dv.add(1)
    def EA(fn):
        if c_ac.v > 0:
            em("scalar", lambda q, v=c_ac.v: q.wait_ge(ac, v))
        em("scalar", lambda q, fn=fn: fn(q).then_inc(ac, 1))
        return c_ac.add(1)
    def DMA(engine, grp, fn):
        if grp.chained and grp.c.v > 0:
            em(engine, lambda q, s=grp.s, v=grp.c.v: q.wait_ge(s, v))
        em(engine, lambda q, fn=fn, s=grp.s: fn(q).then_inc(s, 16))
        return grp.c.add(16)
    g_const = Q("g_const")
    g_wssm = Q("g_wssm")
    g_stgld = Q("g_stgld", chained=True)
    g_stgst = Q("g_stgst", chained=True)
    g_pref = Q("g_pref")
    g_gz = Q("g_gz")
    g_az = Q("g_az")
    g_x = Q("g_x", chained=True)
    g_sc = Q("g_sc")
    g_sum = Q("g_sum")
    g_sumg = Q("g_sumg")
    g_em = Q("g_em")
    g_scg = Q("g_scg")
    g_idxl = Q("g_idxl")
    g_sq1 = Q("g_sq1")
    g_sq2 = Q("g_sq2")
    g_osb = [Q(f"g_osb{i}") for i in range(8)]
    g_ma = [Q(f"g_ma{i}") for i in range(8)]
    g_gath = Q("g_gath", chained=True)
    g_w1 = [Q("g_w1a", chained=True), Q("g_w1b", chained=True)]
    g_w2 = [Q("g_w2a", chained=True), Q("g_w2b", chained=True)]
    g_wo = Q("g_wo")
    g_gt = Q("g_gt", chained=True)
    g_xs = Q("g_xs", chained=True)
    g_mr = Q("g_mr", chained=True)
    g_out = Q("g_out", chained=True)
    g_dbq = Q("g_dbq", chained=True)
    def MACC(psv, pairs, waits=()):
        for sw, vw in waits:
            W("tensor", sw, vw)
        n = len(pairs)
        for i, (lf, rf) in enumerate(pairs):
            if i < n - 1:
                em("tensor", lambda q, lf=lf, rf=rf, st=(i == 0), psv=psv:
                   q.matmul(psv, lf, rf, start=st, stop=False))
            else:
                em("tensor", lambda q, lf=lf, rf=rf, st=(i == 0), psv=psv:
                   q.matmul(psv, lf, rf, start=st, stop=True).then_inc(pe, 1))
        return c_pe.add(1)

    def _final_stub():
        W("sync", dv, c_dv.v)
        W("sync", ac, c_ac.v)
        W("sync", pe, c_pe.v)
        for tt in range(16):
            for hb in range(2):
                DMA("sync", g_out, lambda q, tt=tt, hb=hb: q.dma_start(
                    out=out_t[128 * tt:128 * (tt + 1), :].bitcast(BF16)[:, 1024 * hb:1024 * (hb + 1)],
                    in_=zerot[:, :]))
        W("sync", g_out.s, g_out.c.v)

    def _finish():
        with nc.Block() as block:
            @block.sync
            def _(q):
                for f in prog["sync"]:
                    f(q)
            @block.scalar
            def _(q):
                for f in prog["scalar"]:
                    f(q)
            @block.vector
            def _(q):
                for f in prog["vector"]:
                    f(q)
            @block.tensor
            def _(q):
                for f in prog["tensor"]:
                    f(q)
            @block.gpsimd
            def _(q):
                for f in prog["gpsimd"]:
                    f(q)
        ctx.close()
        return nc

    constpairs = [
        (wg_sb.rearrange("p (k e) -> p k e", e=8), wg_in.rearrange("(k p) e -> p k e", p=128)),
        (identbf, identbf_in[:, :]), (identf, identf_in[:, :]),
        (onescol, onescol_in[:, :]), (onesrow, onesrow_in[:, :]),
        (onestt, onestt_in[:, :]),
        (ltstrict, ltstrict_in[:, :]), (lt16, lt16_in[:, :]),
        (tokid, tokid_in[:, :]), (jbase, jbase_in[:, :]), (selv, selv_in[:, :]),
        (b_dt, b_dt_in[:, :]), (a_log, a_log_in[:, :]), (dmat, dmat_in[:, :]),
        (b1m, b1m_in[:, :]), (b2bc, b2bc_in[:, :]),
    ]
    for dst, src in constpairs:
        dap = dst if isinstance(dst, bass.AP) else dst[:, :]
        DMA("sync", g_const, lambda q, d=dap, s=src: q.dma_start(out=d, in_=s))
    V_CONST = g_const.c.v

    for dst, src in [
        (wstb.rearrange("p (uk n) -> p uk n", uk=8), w_stT.rearrange("(uk p) n -> p uk n", p=128)),
        (wdtb.rearrange("p (uk n) -> p uk n", uk=8), w_dtT.rearrange("(uk p) n -> p uk n", p=128)),
        (wctb[:, :], w_cT[:, :]),
    ]:
        DMA("gpsimd", g_wssm, lambda q, d=dst, s=src: q.dma_start(out=d, in_=s))
    DMA("gpsimd", g_wssm,
        lambda q: q.dma_start(out=arena2[:, :].rearrange("p (k u) -> p k u", k=8),
                              in_=w_inT.rearrange("(k p) u -> p k u", p=128)))
    V_WSSM = g_wssm.c.v

    V_SENTI = EV(lambda q: q.memset(senti[:, :], BIG))
    V_ZEROT = EV(lambda q: q.memset(zerot[:, :], 0))
    W("gpsimd", dv, V_ZEROT)
    for qd in range(4):
        DMA("gpsimd", g_pref, lambda q, qd=qd: q.dma_start(
            out=idxg[1024 * qd:1024 * (qd + 1), :].rearrange("(p w) c -> p w c", p=128),
            in_=senti[:, 0:40].rearrange("p (w c) -> p w c", c=5)))
    DMA("gpsimd", g_pref, lambda q: q.dma_start(
        out=a2ais[:, :].rearrange("(p w) c -> p w c", p=128),
        in_=senti[:, 0:40].rearrange("p (w c) -> p w c", c=1)))
    V_PREFILL = g_pref.c.v
    for tt in range(16):
        DMA("gpsimd", g_gz, lambda q, tt=tt: q.dma_start(
            out=gdense[128 * tt:128 * (tt + 1), :], in_=zerot[:, :]))
    V_GZERO = g_gz.c.v
    for m in range(40):
        DMA("gpsimd", g_az, lambda q, m=m: q.dma_start(
            out=a2as[128 * m:128 * (m + 1), :], in_=zerot[:, :]))
    V_AZERO = g_az.c.v

    wst_free = [0, 0]
    for i in range(16):
        h = i % 2
        r, cblk = divmod(i, 2)
        W("gpsimd", g_stgst.s, wst_free[h])
        v = DMA("gpsimd", g_stgld, lambda q, r=r, cblk=cblk, h=h: q.dma_start(
            out=wstage[:, h * 2048:h * 2048 + 2048],
            in_=w1e[128 * r:128 * (r + 1), 2048 * cblk:2048 * (cblk + 1)]))
        W("gpsimd", g_stgld.s, v)
        wst_free[h] = DMA("gpsimd", g_stgst, lambda q, r=r, cblk=cblk, h=h: q.dma_start(
            out=w1bfd[128 * r:128 * (r + 1), 2048 * cblk:2048 * (cblk + 1)],
            in_=wstage[:, h * 2048:h * 2048 + 2048]))
    for i in range(32):
        h = i % 2
        W("gpsimd", g_stgst.s, wst_free[h])
        v = DMA("gpsimd", g_stgld, lambda q, i=i, h=h: q.dma_start(
            out=wstage[:, h * 2048:h * 2048 + 1024],
            in_=w2e[128 * i:128 * (i + 1), :]))
        W("gpsimd", g_stgld.s, v)
        wst_free[h] = DMA("gpsimd", g_stgst, lambda q, i=i, h=h: q.dma_start(
            out=w2bfd[128 * i:128 * (i + 1), :],
            in_=wstage[:, h * 2048:h * 2048 + 1024]))
    V_WBF = g_stgst.c.v

    W("scalar", g_const.s, V_CONST)
    EA(lambda q: q.activation(rates[:, 0:1], a_log[:, :], AF.Relu))
    v = EA(lambda q: q.activation(rates[:, 1:2], a_log[:, :], AF.Abs))
    W("scalar", ac, v)
    v = EA(lambda q: q.activation(rates[:, 2:3], rates[:, 1:2], AF.Exp, scale=-1.0))
    W("scalar", ac, v)
    v = EA(lambda q: q.activation(rates[:, 3:4], rates[:, 2:3], AF.Ln, bias=1.0))
    W("vector", ac, v)
    v = EV(lambda q: q.tensor_add(negrate[:, :], rates[:, 0:1], rates[:, 3:4]))
    W("vector", dv, v)
    V_NEGRATE = EV(lambda q: q.tensor_scalar(negrate[:, :], negrate[:, :],
                                             RATE_EPS, -1.0, ALU.add, ALU.mult))
    EV(lambda q: q.memset(epst[:, :], LN_EPS))
    V_EPST = EV(lambda q: q.memset(mhalf[:, :], -0.5))

    W("tensor", g_const.s, V_CONST)
    W("vector", g_const.s, V_CONST)
    W("gpsimd", g_const.s, V_CONST)

    decay = a1f[:, 0:2048]
    vvn = a1f[:, 2048:4096]
    hh = a1f[:, 4096:6144]
    sumg = a1f[0:1, 6144:8192]
    hcast = arena1[:, 0:2048]
    mb_sb = a3f[:, 0:512]
    rb_sb = a3f[:, 512:1024]
    a2i = arena2.bitcast(I32)
    scg2 = a2f[:, 2048:3072]
    cmpscr = a2f[:, 0:128]
    msk = a2f[:, 128:256]
    pscan = a2f[:, 256:384]
    slotf = a2f[:, 384:512]
    fscr = a2f[:, 512:640]
    fscr2 = a2f[:, 640:768]
    gtv = a2f[:, 768:896]
    iscr = a2i[:, 896:1024]
    iscr2 = a2i[:, 1024:1152]
    desti = a2i[:, 1152:1280]
    pdesti = a2i[:, 1280:1408]
    prs5 = a2i[:, 1408:2048]
    xf32 = a1f[:, 0:4096]
    xnf = a1f[:, 4096:8192]
    v_xncast = [0, 0, 0, 0]
    for tc in range(4):
        pe_prev = c_pe.v
        if tc > 0:
            W("sync", pe, pe_prev)
        v_xf = DMA("sync", g_x, lambda q, tc=tc: q.dma_start(
            out=xf32.rearrange("p (k t) -> p k t", k=8),
            in_=xts[:, 512 * tc:512 * (tc + 1)].rearrange("(k p) t -> p k t", p=128)))
        W("tensor", g_x.s, v_xf)
        v_sum = MACC(pS[0:1, :],
                     [(onescol[:, :], xf32[:, 512 * k:512 * (k + 1)]) for k in range(8)])
        W("scalar", g_x.s, v_xf)
        if tc > 0:
            W("scalar", pe, pe_prev)
        for k in range(8):
            EA(lambda q, k=k: q.activation(
                xnf[:, 512 * k:512 * (k + 1)], xf32[:, 512 * k:512 * (k + 1)], AF.Square))
        v_sq = c_ac.v
        v_sumsq = MACC(p3[0:1, :],
                       [(onescol[:, :], xnf[:, 512 * k:512 * (k + 1)]) for k in range(8)],
                       waits=[(ac, v_sq)])
        mean_s = meanr[0:1, :]
        rstd_s = rstdr[0:1, :]
        var_s = stat1[0:1, :]
        sc3 = stat3[0:1, :]
        W("vector", pe, v_sumsq)
        EV(lambda q, m=mean_s: q.tensor_scalar(m, pS[0:1, :], 1.0 / DM, None, ALU.mult))
        EV(lambda q, v_=var_s: q.tensor_scalar(v_, p3[0:1, :], 1.0 / DM, None, ALU.mult))
        v = EV(lambda q, m=mean_s, s3=sc3: q.tensor_mul(s3, m, m))
        W("vector", dv, v)
        v = EV(lambda q, v_=var_s, s3=sc3: q.tensor_sub(v_, v_, s3))
        W("scalar", dv, v)
        if tc == 0:
            W("scalar", dv, V_EPST)
        v = EA(lambda q, v_=var_s, s3=sc3: q.activation(s3, v_, AF.Ln, bias=epst[0:1, 0:1]))
        W("scalar", ac, v)
        v = EA(lambda q, r=rstd_s, s3=sc3: q.activation(r, s3, AF.Exp, scale=mhalf[0:1, 0:1]))
        W("vector", ac, v)
        v = EV(lambda q, v_=var_s: q.tensor_scalar(v_, v_, LN_EPS, None, ALU.add))
        W("vector", dv, v)
        v = EV(lambda q, r=rstd_s, s3=sc3: q.tensor_mul(s3, r, r))
        W("vector", dv, v)
        v = EV(lambda q, v_=var_s, s3=sc3: q.tensor_mul(s3, s3, v_))
        W("vector", dv, v)
        v = EV(lambda q, s3=sc3: q.tensor_scalar(s3, s3, -0.5, 1.5, ALU.mult, ALU.add))
        W("vector", dv, v)
        v_newt = EV(lambda q, r=rstd_s, s3=sc3: q.tensor_mul(r, r, s3))
        v_bc = MACC(pW[:, 0:512], [(onesrow[:, :], mean_s)], waits=[(dv, v_newt)])
        v_bc2 = MACC(pW[:, 512:1024], [(onesrow[:, :], rstd_s)])
        W("scalar", pe, v_bc2)
        EA(lambda q: q.copy(mb_sb, pW[:, 0:512]))
        v_ev = EA(lambda q: q.copy(rb_sb, pW[:, 512:1024]))
        W("vector", ac, v_ev)
        W("vector", pe, v_sumsq)
        for k in range(8):
            EV(lambda q, k=k: q.tensor_sub(
                xnf[:, 512 * k:512 * (k + 1)], xf32[:, 512 * k:512 * (k + 1)], mb_sb))
            W("vector", dv, c_dv.v)
            EV(lambda q, k=k: q.tensor_mul(
                xnf[:, 512 * k:512 * (k + 1)], xnf[:, 512 * k:512 * (k + 1)], rb_sb))
        v_xn = c_dv.v
        W("scalar", dv, v_xn)
        for k in range(8):
            EA(lambda q, k=k, tc=tc: q.copy(
                xnbf[:, 2048 * k + 512 * tc:2048 * k + 512 * (tc + 1)],
                xnf[:, 512 * k:512 * (k + 1)]))
        v_xncast[tc] = c_ac.v
        v_sc = MACC(pS[0:8, :],
                    [(wg_sb[:, 8 * k:8 * (k + 1)], xnf[:, 512 * k:512 * (k + 1)])
                     for k in range(8)],
                    waits=[(dv, v_xn)])
        W("scalar", pe, v_sc)
        v_scsb = EA(lambda q: q.copy(scsb[:, :], pS[0:8, :]))
        W("tensor", ac, v_scsb)
        for j in range(4):
            EP(lambda q, j=j: q.transpose(
                pW[:, 8 * j:8 * (j + 1)], scsb[0:8, 128 * j:128 * (j + 1)], identf[0:8, 0:8]))
        v_tr = c_pe.v
        W("scalar", pe, v_tr)
        W("scalar", g_sc.s, g_sc.c.v)
        v_sctm = EA(lambda q: q.copy(sctm[:, :], pW[:, 0:32]))
        W("sync", ac, v_sctm)
        DMA("sync", g_sc, lambda q, tc=tc: q.dma_start(
            out=sc_in[512 * tc:512 * (tc + 1), :].rearrange("(j p) e -> p j e", p=128),
            in_=sctm[:, :].rearrange("p (j e) -> p j e", e=8)))
    V_SCIN = g_sc.c.v
    V_P1_PE = c_pe.v

    if stop_after < 2:
        _final_stub()
        return _finish()
    v_evac_ps = [0, 0]
    v_vvn_prev = 0
    v_dtabs_prev = 0
    W("tensor", g_wssm.s, V_WSSM)
    for tcc in range(4):
        W("tensor", ac, v_xncast[tcc])
        if tcc == 0:
            W("tensor", ac, v_xncast[3])
            W("tensor", g_sc.s, V_SCIN)
        for uk in range(16):
            pbuf = [p0, p1][uk % 2]
            W("tensor", ac, v_evac_ps[uk % 2])
            vm = MACC(pbuf[:, :],
                      [(arena2[:, 2048 * k + 128 * uk:2048 * k + 128 * (uk + 1)],
                        xnbf[:, 2048 * k + 512 * tcc:2048 * k + 512 * (tcc + 1)])
                       for k in range(8)])
            dstb = ubf if uk < 8 else arena3
            W("scalar", pe, vm)
            v_evac_ps[uk % 2] = EA(lambda q, d=dstb, uk=uk, tcc=tcc, pbuf=pbuf: q.copy(
                d[:, 2048 * (uk % 8) + 512 * tcc:2048 * (uk % 8) + 512 * (tcc + 1)],
                pbuf[:, :]))
        v_u_done = c_ac.v
        v_b = MACC(p2[:, :],
                   [(wstb[:, 128 * uk:128 * (uk + 1)],
                     ubf[:, 2048 * uk + 512 * tcc:2048 * uk + 512 * (tcc + 1)])
                    for uk in range(8)],
                   waits=[(ac, v_u_done), (dv, v_vvn_prev)])
        v_dt = MACC(p3[:, :],
                    [(wdtb[:, 128 * uk:128 * (uk + 1)],
                      ubf[:, 2048 * uk + 512 * tcc:2048 * uk + 512 * (tcc + 1)])
                     for uk in range(8)],
                    waits=[(ac, v_dtabs_prev)])
        W("scalar", pe, v_dt)
        W("scalar", dv, c_dv.v)
        EA(lambda q: q.activation(dts1[:, :], p3[:, :], AF.Relu, bias=b_dt[:, 0:1]))
        v = EA(lambda q: q.activation(dts2[:, :], p3[:, :], AF.Abs, bias=b_dt[:, 0:1]))
        v_dtabs_prev = v
        W("scalar", ac, v)
        v = EA(lambda q: q.activation(dts3[:, :], dts2[:, :], AF.Exp, scale=-1.0))
        W("scalar", ac, v)
        v_l = EA(lambda q: q.activation(dts2[:, :], dts3[:, :], AF.Ln, bias=1.0))
        W("vector", ac, v_l)
        v = EV(lambda q: q.tensor_add(dts1[:, :], dts1[:, :], dts2[:, :]))
        W("vector", dv, v)
        v_dtv = EV(lambda q: q.tensor_scalar_min(dts1[:, :], dts1[:, :], 10.0))
        W("vector", dv, v_dtv)
        EV(lambda q, tcc=tcc: q.reduce_sum(dtsum[:, tcc:tcc + 1], dts1[:, :], axis=AX.X))
        W("scalar", dv, v_dtv)
        if tcc == 0:
            W("scalar", dv, V_NEGRATE)
        v_dec = EA(lambda q, tcc=tcc: q.activation(
            decay[:, 512 * tcc:512 * (tcc + 1)], dts1[:, :], AF.Exp, scale=negrate[:, 0:1]))
        W("vector", ac, v_dec)
        W("vector", pe, v_b)
        v_vvn_prev = EV(lambda q, tcc=tcc: q.scalar_tensor_tensor(
            vvn[:, 512 * tcc:512 * (tcc + 1)], decay[:, 512 * tcc:512 * (tcc + 1)],
            1.0, p2[:, :], ALU.subtract, ALU.mult))
    W("vector", dv, c_dv.v)
    v = EV(lambda q: q.tensor_tensor_scan(hh, decay, vvn, 0.0, ALU.mult, ALU.subtract))
    W("vector", dv, v)
    v = EV(lambda q: q.tensor_add(dtsum[:, 0:1], dtsum[:, 0:1], dtsum[:, 1:2]))
    W("vector", dv, v)
    v = EV(lambda q: q.tensor_add(dtsum[:, 2:3], dtsum[:, 2:3], dtsum[:, 3:4]))
    W("vector", dv, v)
    v_dtt = EV(lambda q: q.tensor_add(dtsum[:, 0:1], dtsum[:, 0:1], dtsum[:, 2:3]))
    W("scalar", dv, v_dtt)
    v_at = EA(lambda q: q.activation(dtsum[:, 1:2], dtsum[:, 0:1], AF.Exp,
                                     scale=negrate[:, 0:1]))
    W("sync", ac, v_at)
    W("sync", dv, c_dv.v)
    DMA("sync", g_sum, lambda q: q.dma_start(out=sum_in[0:1, :].rearrange("o p -> p o"), in_=dtsum[:, 1:2]))
    v_sumdma = DMA("sync", g_sum, lambda q: q.dma_start(out=sum_in[1:2, :].rearrange("o p -> p o"), in_=hh[:, 2047:2048]))
    W("gpsimd", g_sum.s, v_sumdma)
    em("gpsimd", lambda q: q.collective_compute(
        "AllGather", ALU.bypass, [list(range(NCORE))],
        ins=[sum_in.ap()], outs=[sum_out.ap()]).then_inc(cc2, 1))
    W("gpsimd", cc2, 1)
    v_sumg = DMA("gpsimd", g_sumg, lambda q: q.dma_start(out=sumg[0:1, :], in_=sum_out.rearrange("a b -> (a b)")[None, :]))
    W("vector", g_sumg.s, v_sumg)
    EV(lambda q: q.memset(hin[:, :], 0.0))
    for j in range(NCORE):
        W("vector", dv, c_dv.v)
        EV(lambda q, j=j: q.tensor_scalar(hf1[:, :], sumg[0:1, 256 * j:256 * j + 128],
                                          -1.0, None, ALU.add))
        W("vector", dv, c_dv.v)
        EV(lambda q, j=j: q.tensor_scalar(hf1[:, :], hf1[:, :],
                                          selv[0:1, j:j + 1], 1.0, ALU.mult, ALU.add))
        W("vector", dv, c_dv.v)
        EV(lambda q: q.tensor_mul(hf2[:, :], hin[:, :], hf1[:, :]))
        W("vector", dv, c_dv.v)
        EV(lambda q, j=j: q.tensor_scalar(hf1[:, :], sumg[0:1, 256 * j + 128:256 * j + 256],
                                          selv[0:1, j:j + 1], None, ALU.mult))
        W("vector", dv, c_dv.v)
        EV(lambda q: q.tensor_add(hin[:, :], hf2[:, :], hf1[:, :]))
    v_hin = c_dv.v
    v_htr = MACC(pW[:, 0:1], [(hin[:, :], identf[0:1, 0:1])], waits=[(dv, v_hin)])
    W("vector", pe, v_htr)
    v = EV(lambda q: q.tensor_copy(h0col[:, :], pW[:, 0:1]))
    W("vector", dv, v)
    v_scan2 = EV(lambda q: q.tensor_tensor_scan(hh, decay, vvn, h0col[:, 0:1],
                                                ALU.mult, ALU.subtract))
    W("vector", dv, v_scan2)
    v_hcast = EV(lambda q: q.tensor_copy(hcast[:, :], hh))
    if debug:
        W("sync", dv, v_scan2)
        DMA("sync", g_dbq, lambda q: q.dma_start(out=hh_dbg[:, :], in_=hh))

    v_z_evac = [0, 0]
    for tcc in range(4):
        for k in range(8):
            pbuf = [p0, p1][k % 2]
            vy = MACC(pbuf[:, :],
                      [(wctb[:, 128 * k:128 * (k + 1)],
                        hcast[:, 512 * tcc:512 * (tcc + 1)])],
                      waits=[(dv, v_hcast), (dv, v_z_evac[k % 2])])
            W("scalar", pe, vy)
            W("scalar", dv, c_dv.v)
            v_sg = EA(lambda q, k=k, tcc=tcc: q.activation(
                dts1[:, :], arena3[:, 2048 * k + 512 * tcc:2048 * k + 512 * (tcc + 1)],
                AF.Silu))
            W("vector", ac, v_sg)
            W("vector", pe, vy)
            EV(lambda q, pbuf=pbuf: q.tensor_mul(dts2[:, :], pbuf[:, :], dts1[:, :]))
            W("vector", dv, c_dv.v)
            EV(lambda q, k=k, tcc=tcc: q.tensor_scalar(
                dts3[:, :], xnbf[:, 2048 * k + 512 * tcc:2048 * k + 512 * (tcc + 1)],
                dmat[:, k:k + 1], None, ALU.mult))
            W("vector", dv, c_dv.v)
            v_z_evac[k % 2] = EV(lambda q, k=k, tcc=tcc: q.tensor_add(
                ubf[:, 2048 * k + 512 * tcc:2048 * k + 512 * (tcc + 1)],
                dts2[:, :], dts3[:, :]))
    V_Z = c_dv.v
    V_P3_PE = c_pe.v

    if stop_after < 3:
        _final_stub()
        return _finish()
    W("gpsimd", g_sc.s, V_SCIN)
    em("gpsimd", lambda q: q.collective_compute(
        "AllGather", ALU.bypass, [list(range(NCORE))],
        ins=[sc_in.ap()], outs=[sc_out.ap()]).then_inc(cc1, 1))
    W("gpsimd", cc1, 1)
    v_scg = DMA("gpsimd", g_scg, lambda q: q.dma_start(
        out=scg2.rearrange("p (tl e) -> p tl e", e=8),
        in_=sc_out.rearrange("(p tl) e -> p tl e", p=128)))
    emask = a1f[:, 0:1024]
    W("sync", dv, V_Z)
    v_eml = DMA("sync", g_em, lambda q: q.dma_start(out=emask, in_=emask_in[:, :]))
    W("vector", g_scg.s, v_scg)
    W("vector", g_em.s, v_eml)
    W("vector", dv, V_Z)
    v = EV(lambda q: q.tensor_mul(a1f[:, 1024:2048], scg2, emask))
    W("vector", dv, v)
    v_vE = EV(lambda q: q.tensor_reduce(
        vE[:, :], a1f[:, 1024:2048].rearrange("p (tl e) -> p tl e", e=8),
        axis=AX.X, op=ALU.add))
    W("vector", dv, v_vE)
    EV(lambda q: q.memset(bis[0:1, :], 0.0))
    EV(lambda q: q.memset(bis[0:1, 0:1], -64.0))
    W("vector", dv, c_dv.v)
    EV(lambda q: q.memset(bis[0:1, 1:2], 64.0))
    W("vector", dv, c_dv.v)
    EV(lambda q: q.memset(bis[0:1, 2:3], 0.0))
    v_mid = c_dv.v
    for it in range(NBIS):
        v_bcm = MACC(p2[:, 0:1], [(onesrow[:, :], bis[0:1, 2:3])],
                     waits=[(dv, v_mid)])
        W("scalar", pe, v_bcm)
        v_thr = EA(lambda q: q.copy(thrcol[:, :], p2[:, 0:1]))
        W("vector", ac, v_thr)
        v_cmp = EV(lambda q: q.tensor_scalar(cmpscr, vE[:, :], thrcol[:, 0:1],
                                             0.0, ALU.is_ge, ALU.add,
                                             accum_out=cntcol[:, 0:1]))
        v_cnt = MACC(p2[0:1, 1:2], [(onescol[:, :], cntcol[:, :])],
                     waits=[(dv, v_cmp)])
        W("vector", pe, v_cnt)
        EV(lambda q: q.tensor_scalar(bis[0:1, 3:4], p2[0:1, 1:2], KSEL - 0.5,
                                     None, ALU.is_ge))
        W("vector", dv, c_dv.v)
        EV(lambda q: q.tensor_sub(bis[0:1, 4:5], bis[0:1, 2:3], bis[0:1, 0:1]))
        W("vector", dv, c_dv.v)
        EV(lambda q: q.tensor_mul(bis[0:1, 5:6], bis[0:1, 3:4], bis[0:1, 4:5]))
        W("vector", dv, c_dv.v)
        EV(lambda q: q.tensor_add(bis[0:1, 0:1], bis[0:1, 0:1], bis[0:1, 5:6]))
        W("vector", dv, c_dv.v)
        EV(lambda q: q.tensor_sub(bis[0:1, 4:5], bis[0:1, 1:2], bis[0:1, 2:3]))
        W("vector", dv, c_dv.v)
        EV(lambda q: q.tensor_mul(bis[0:1, 5:6], bis[0:1, 3:4], bis[0:1, 4:5]))
        W("vector", dv, c_dv.v)
        EV(lambda q: q.tensor_add(bis[0:1, 1:2], bis[0:1, 2:3], bis[0:1, 5:6]))
        W("vector", dv, c_dv.v)
        EV(lambda q: q.tensor_add(bis[0:1, 2:3], bis[0:1, 0:1], bis[0:1, 1:2]))
        W("vector", dv, c_dv.v)
        v_mid = EV(lambda q: q.tensor_scalar(bis[0:1, 2:3], bis[0:1, 2:3], 0.5,
                                             None, ALU.mult))
    v_bcf = MACC(p2[:, 0:1], [(onesrow[:, :], bis[0:1, 0:1])], waits=[(dv, v_mid)])
    W("scalar", pe, v_bcf)
    v_thrf = EA(lambda q: q.copy(thrcol[:, :], p2[:, 0:1]))
    if debug:
        W("sync", dv, v_mid)
        DMA("sync", g_dbq, lambda q: q.dma_start(out=thr_dbg[0:1, :], in_=bis[0:1, :]))
    W("vector", ac, v_thrf)
    v_msk = EV(lambda q: q.tensor_scalar(msk, vE[:, :], thrcol[:, 0:1],
                                         None, ALU.is_ge))
    W("vector", dv, v_msk)
    v_ps = EV(lambda q: q.tensor_tensor_scan(pscan, onestt[:, :], msk,
                                             0.0, ALU.mult, ALU.add))
    v_exm = MACC(pW[:, 0:1], [(ltstrict[:, :], pscan[:, 127:128])],
                 waits=[(dv, v_ps)])
    v_bnm = MACC(pW[:, 1:2], [(lt16[:, :], pscan[:, 127:128])])
    W("vector", pe, v_bnm)
    EV(lambda q: q.tensor_copy(exc[:, :], pW[:, 0:1]))
    W("vector", dv, c_dv.v)
    EV(lambda q: q.tensor_copy(bnd[:, :], pW[:, 1:2]))
    W("vector", dv, c_dv.v)
    EV(lambda q: q.tensor_scalar(excm1[:, :], exc[:, :], -1.0, None, ALU.add))
    W("vector", dv, c_dv.v)
    EV(lambda q: q.tensor_scalar(slotf, pscan, excm1[:, 0:1],
                                 None, ALU.add))
    W("vector", dv, c_dv.v)
    EV(lambda q: q.tensor_scalar(fscr[:, :], slotf, bnd[:, 0:1],
                                 None, ALU.subtract))
    W("vector", dv, c_dv.v)
    EV(lambda q: q.tensor_scalar(fscr2, slotf, KSEL - 0.5, None, ALU.is_lt))
    W("vector", dv, c_dv.v)
    EV(lambda q: q.tensor_mul(fscr2, fscr2, msk))
    W("vector", dv, c_dv.v)
    EV(lambda q: q.tensor_scalar(gtv, fscr[:, :], CAP - 0.5, None, ALU.is_lt))
    W("vector", dv, c_dv.v)
    EV(lambda q: q.tensor_mul(gtv, gtv, msk))
    W("vector", dv, c_dv.v)
    EV(lambda q: q.tensor_copy(iscr, slotf))
    W("vector", dv, c_dv.v)
    EV(lambda q: q.tensor_scalar(iscr2, iscr, 127, None, ALU.bitwise_and))
    W("vector", dv, c_dv.v)
    EV(lambda q: q.tensor_scalar(iscr2, iscr2, 5, None, ALU.logical_shift_left))
    W("vector", dv, c_dv.v)
    EV(lambda q: q.tensor_scalar(iscr, iscr, 7, None, ALU.logical_shift_right))
    W("vector", dv, c_dv.v)
    EV(lambda q: q.tensor_add(iscr2, iscr2, iscr))
    W("vector", dv, c_dv.v)
    EV(lambda q: q.memset(desti, BIG))
    W("vector", dv, c_dv.v)
    EV(lambda q: q.tensor_copy(iscr, fscr2))
    W("vector", dv, c_dv.v)
    EV(lambda q: q.copy_predicated(desti, iscr, iscr2))
    W("vector", dv, c_dv.v)
    EV(lambda q: q.tensor_copy(iscr2, fscr[:, :]))
    W("vector", dv, c_dv.v)
    EV(lambda q: q.tensor_scalar(iscr2, iscr2, jbase[:, 0:1], None, ALU.add))
    W("vector", dv, c_dv.v)
    EV(lambda q: q.memset(pdesti, BIG))
    W("vector", dv, c_dv.v)
    EV(lambda q: q.tensor_copy(iscr, gtv))
    W("vector", dv, c_dv.v)
    EV(lambda q: q.copy_predicated(pdesti, iscr, iscr2))
    W("vector", dv, c_dv.v)
    W("scalar", dv, c_dv.v)
    v_gtv = EA(lambda q: q.activation(gtv, vE[:, :], AF.Sigmoid))
    EV(lambda q: q.tensor_scalar(iscr, tokid[:, :], 2047, None, ALU.bitwise_and))
    v_lid = c_dv.v
    W("vector", dv, v_lid)
    EV(lambda q: q.tensor_copy(prs5[:, 0:640:5], tokid[:, :]))
    W("vector", ac, v_gtv)
    EV(lambda q: q.tensor_copy(prs5[:, 1:640:5], gtv.bitcast(I32)))
    EV(lambda q: q.tensor_copy(prs5[:, 2:640:5], pdesti))
    EV(lambda q: q.tensor_copy(prs5[:, 3:640:5], iscr))
    EV(lambda q: q.tensor_copy(iscr2, fscr[:, :]))
    EV(lambda q: q.tensor_scalar(msk.bitcast(I32), iscr2, 7, None, ALU.logical_shift_right))
    EV(lambda q: q.tensor_scalar(iscr2, iscr2, 127, None, ALU.bitwise_and))
    EV(lambda q: q.tensor_scalar(pscan.bitcast(I32), iscr2, 2, None, ALU.logical_shift_left))
    EV(lambda q: q.tensor_add(iscr2, iscr2, pscan.bitcast(I32)))
    EV(lambda q: q.tensor_add(iscr2, iscr2, msk.bitcast(I32)))
    EV(lambda q: q.tensor_copy(fscr2, iscr2))
    EV(lambda q: q.tensor_scalar(fscr2, fscr2, jbase[:, 0:1], None, ALU.add))
    EV(lambda q: q.tensor_copy(iscr2, fscr2))
    EV(lambda q: q.memset(msk.bitcast(I32), BIG))
    EV(lambda q: q.tensor_copy(fscr2, pdesti))
    EV(lambda q: q.tensor_scalar(fscr2, fscr2, 1.0e8, None, ALU.is_lt))
    EV(lambda q: q.tensor_copy(iscr, fscr2))
    EV(lambda q: q.copy_predicated(msk.bitcast(I32), iscr, iscr2))
    v_quad = EV(lambda q: q.tensor_copy(prs5[:, 4:640:5], msk.bitcast(I32)))
    W("gpsimd", dv, v_quad)
    W("gpsimd", g_pref.s, V_PREFILL)
    def _mk_scat(tg):
        def f(q):
            q.indirect_dma_start(
                out=idxg[:, :],
                out_offset=IndirectOffsetOnAxis(desti[:, tg:tg + 1], axis=0),
                in_=prs5[:, 5 * tg:5 * (tg + 1)], in_offset=None,
                bounds_check=bc4095[0], oob_is_err=False).then_inc(g_sq1.s, 16)
        return f
    bc4095 = [None]
    def _set_regs(q):
        bc4095[0] = q.to_reg(KSEL - 1)
        bcNPR[0] = q.to_reg(NPR - 1)
        bcNTOK[0] = q.to_reg(NTOK - 1)
        bcT[0] = q.to_reg(T - 1)
        return q.engine_nop()
    bcNPR = [None]; bcNTOK = [None]; bcT = [None]
    em("gpsimd", lambda q: _set_regs(q))
    for tg in range(128):
        em("gpsimd", _mk_scat(tg))
        g_sq1.c.add(16)
    W("gpsimd", g_sq1.s, g_sq1.c.v)
    v_idxl = DMA("gpsimd", g_idxl, lambda q: q.dma_start(
        out=idxg_sb[:, :].rearrange("p (w c) -> p w c", c=5),
        in_=idxg.rearrange("(p w) c -> p w c", p=128)))
    W("vector", g_idxl.s, v_idxl)
    for g in range(32):
        EV(lambda q, g=g: q.tensor_copy(fscr[:, g:g + 1], idxg_sb[:, 5 * g:5 * g + 1]))
    W("vector", dv, c_dv.v)
    v = EV(lambda q: q.tensor_scalar(fscr2[:, 0:32], fscr[:, 0:32], NTOK - 0.5,
                                     None, ALU.is_lt))
    W("vector", dv, v)
    v_gm = EV(lambda q: q.tensor_mul(
        gmask[:, :], fscr2[:, 0:32],
        idxg_sb.bitcast(F32)[:, :].rearrange("p (w c) -> p w c", c=5)[:, :, 1]))
    W("gpsimd", dv, v_gm)
    def _mk_idscat(g):
        def f(q):
            q.indirect_dma_start(
                out=a2ais[:, :],
                out_offset=IndirectOffsetOnAxis(idxg_sb[:, 5 * g + 4:5 * g + 5], axis=0),
                in_=idxg_sb[:, 5 * g + 3:5 * g + 4], in_offset=None,
                bounds_check=bcNPR[0], oob_is_err=False).then_inc(g_sq2.s, 16)
        return f
    for g in range(32):
        W("gpsimd", g_sq2.s, g_sq2.c.v)
        em("gpsimd", _mk_idscat(g))
        g_sq2.c.add(16)
    v_idsf = g_sq2.c.v

    if stop_after < 4:
        _final_stub()
        return _finish()
    xg_bufs = [a2f[:, 0:4096], xnbf.bitcast(F32)[:, 0:4096]]
    exT_bufs = [arena2[:, 8192:12288], xnbf[:, 8192:12288]]
    xnr = a2f[:, 6656:7680]
    sqs = xnr
    xnrb4 = xnbf[:, 12288:16384]

    v_w1l = [0, 0]
    v_w2l = [0, 0]
    v_epi_dv = 0
    v_epi_half0 = 0
    v_tev = [0, 0]
    v_ext = [0, 0]
    v_xg_read = [0, 0]
    v_T_done = [0] * 9
    v_xnrb = {}
    v_gath_v = {}

    def emit_gather(rcn):
        buf = rcn % 2
        xgv = xg_bufs[buf].rearrange("p (b f) -> p b f", b=4)
        if rcn == 0:
            W("gpsimd", pe, V_P3_PE)
            W("gpsimd", g_sq2.s, v_idsf)
        elif rcn == 1:
            W("gpsimd", dv, V_Z)
        else:
            W("gpsimd", dv, v_xg_read[buf])
        for gg in range(4):
            g = 4 * rcn + gg
            DMA("gpsimd", g_gath, lambda q, g=g, gg=gg, xgv=xgv: q.indirect_dma_start(
                out=xgv[:, gg, :], out_offset=None, in_=xfull[:, :],
                in_offset=IndirectOffsetOnAxis(idxg_sb[:, 5 * g:5 * g + 1], axis=0),
                bounds_check=bcNTOK[0], oob_is_err=False))
        v_gath_v[rcn] = g_gath.c.v

    def emit_ln(rcn):
        buf = rcn % 2
        v_gath = v_gath_v[rcn]
        if rcn >= 1:
            W("vector", pe, v_T_done[rcn - 1])
        v_last_read = 0
        for gg in range(4):
            xgb = xg_bufs[buf].rearrange("p (b f) -> p b f", b=4)[:, gg, :]
            W("vector", g_gath.s, v_gath)
            EV(lambda q, xgb=xgb: q.reduce_sum(mcol[:, :], xgb, axis=AX.X))
            W("scalar", g_gath.s, v_gath)
            if gg == 0 and rcn == 0:
                W("scalar", pe, V_P3_PE)
            EV(lambda q, xgb=xgb: q.scalar_tensor_tensor(
                sqs, xgb, 1.0, xgb, ALU.mult, ALU.mult, accum_out=s2col[:, :]))
            EV(lambda q: q.tensor_scalar(mcol[:, :], mcol[:, :], 1.0 / DM, None, ALU.mult))
            EV(lambda q: q.tensor_scalar(s2col[:, :], s2col[:, :], 1.0 / DM, None, ALU.mult))
            EV(lambda q: q.tensor_mul(vcol[:, :], mcol[:, :], mcol[:, :]))
            v = EV(lambda q: q.tensor_sub(vcol[:, :], s2col[:, :], vcol[:, :]))
            W("scalar", dv, v)
            EA(lambda q: q.activation(ncol[:, :], vcol[:, :], AF.Ln, bias=epst[:, 0:1]))
            v_r0 = EA(lambda q: q.activation(rcol[:, :], ncol[:, :], AF.Exp,
                                             scale=mhalf[:, 0:1]))
            W("vector", ac, v_r0)
            EV(lambda q: q.tensor_scalar(vcol[:, :], vcol[:, :], LN_EPS, None, ALU.add))
            EV(lambda q: q.tensor_mul(ncol[:, :], rcol[:, :], rcol[:, :]))
            EV(lambda q: q.tensor_mul(ncol[:, :], ncol[:, :], vcol[:, :]))
            EV(lambda q: q.tensor_scalar(ncol[:, :], ncol[:, :], -0.5, 1.5, ALU.mult, ALU.add))
            EV(lambda q: q.tensor_mul(rcol[:, :], rcol[:, :], ncol[:, :]))
            v_last_read = EV(lambda q, xgb=xgb: q.tensor_scalar(
                xnr, xgb, mcol[:, 0:1], None, ALU.subtract))
            v = EV(lambda q, gg=gg: q.tensor_scalar(
                xnrb4[:, 1024 * gg:1024 * (gg + 1)], xnr, rcol[:, 0:1], None, ALU.mult))
            v_xnrb[(rcn, gg)] = v
        v_xg_read[buf] = v_last_read

    def emit_T(rcn):
        buf = rcn % 2
        for gg in range(4):
            W("tensor", dv, v_xnrb[(rcn, gg)])
            for half in range(2):
                W("tensor", ac, v_tev[0])
                for k4 in range(4):
                    k = 4 * half + k4
                    EP(lambda q, gg=gg, k=k, k4=k4: q.transpose(
                        pT[:, 128 * k4:128 * (k4 + 1)],
                        xnrb4[:, 1024 * gg + 128 * k:1024 * gg + 128 * (k + 1)],
                        identbf[:, :]))
                v_tp = c_pe.v
                W("scalar", pe, v_tp)
                v_tev[0] = EA(lambda q, gg=gg, half=half, buf=buf: q.copy(
                    exT_bufs[buf].rearrange("p (k c) -> p k c", c=512)
                        [:, 4 * half:4 * half + 4, 128 * gg:128 * (gg + 1)],
                    pT[:, :].rearrange("p (k f) -> p k f", f=128)))
        v_ext[buf] = c_ac.v
        v_T_done[rcn] = c_pe.v

    emit_gather(0)
    emit_ln(0)
    emit_T(0)
    for rc in range(8):
        if rc < 7:
            emit_gather(rc + 1)
            emit_ln(rc + 1)
        exT = exT_bufs[rc % 2]
        v_gelu_pb = [0, 0]
        for ffc in range(32):
            hbuf = ffc % 2
            w1c = arena3[:, 1024 * hbuf:1024 * (hbuf + 1)]
            if rc == 0 and ffc < 2:
                W("sync", g_stgst.s, V_WBF)
                W("sync", pe, V_P3_PE)
            else:
                W("sync", pe, v_w1l[hbuf])
            vl = DMA("sync", g_w1[hbuf], lambda q, ffc=ffc, w1c=w1c: q.dma_start(
                out=w1c.rearrange("p (k f) -> p k f", k=8),
                in_=w1bfd[:, 128 * ffc:128 * (ffc + 1)].rearrange("(k p) f -> p k f", p=128)))
            pb = [p0, p1][ffc % 2]
            v_mm1 = MACC(pb[:, :],
                         [(w1c[:, 128 * k:128 * (k + 1)], exT[:, 512 * k:512 * (k + 1)])
                          for k in range(8)],
                         waits=[(g_w1[hbuf].s, vl), (ac, v_ext[rc % 2]),
                                (ac, v_gelu_pb[ffc % 2]),
                                (dv, v_epi_dv if ffc < 2 else 0)])
            v_w1l[hbuf] = v_mm1
            W("scalar", pe, v_mm1)
            v_gelu_pb[ffc % 2] = EA(lambda q, ffc=ffc, pb=pb: q.activation(
                arena1[:, 512 * ffc:512 * (ffc + 1)], pb[:, :], AF.Gelu,
                bias=b1m[:, ffc:ffc + 1]))
        v_gelu_all = c_ac.v
        if rc < 7:
            emit_T(rc + 1)
        for hh2 in range(2):
            for ffc in range(32):
                wbuf = ffc % 2
                w2c = arena3[:, 2048 + 1024 * wbuf:2048 + 1024 * wbuf + 512]
                W("sync", pe, v_w2l[wbuf])
                vl = DMA("sync", g_w2[wbuf], lambda q, ffc=ffc, hh2=hh2, w2c=w2c: q.dma_start(
                    out=w2c, in_=w2bfd[128 * ffc:128 * (ffc + 1), 512 * hh2:512 * (hh2 + 1)]))
                for sub in range(4):
                    pbs = [p0, p1, p2, p3][sub]
                    lastk = (ffc == 31)
                    wts = [(g_w2[wbuf].s, vl)]
                    if ffc == 0:
                        wts.append((ac, v_gelu_all))
                        if hh2 == 1:
                            wts.append((dv, v_epi_half0))
                        else:
                            wts.append((dv, v_epi_dv))
                    for sw, vw in wts:
                        W("tensor", sw, vw)
                    certify = (sub == 3) or lastk
                    em("tensor", lambda q, ffc=ffc, sub=sub, pbs=pbs, w2c=w2c,
                       lastk=lastk, certify=certify:
                       (q.matmul(pbs[:, :],
                                 arena1[:, 512 * ffc + 128 * sub:512 * ffc + 128 * (sub + 1)],
                                 w2c, start=(ffc == 0), stop=lastk).then_inc(pe, 1)
                        if certify else
                        q.matmul(pbs[:, :],
                                 arena1[:, 512 * ffc + 128 * sub:512 * ffc + 128 * (sub + 1)],
                                 w2c, start=(ffc == 0), stop=lastk)))
                    if certify:
                        c_pe.add(1)
                v_w2l[wbuf] = c_pe.v
            v_mm2h = c_pe.v
            for sub in range(4):
                pbs = [p0, p1, p2, p3][sub]
                g = 4 * rc + sub
                oev = a3f[:, 4096 + 512 * sub:4096 + 512 * (sub + 1)][:, 0:512]
                osb = arena3[:, 4096 + 1024 * sub:4096 + 1024 * (sub + 1)]
                W("vector", pe, v_mm2h)
                if rc > 0 and hh2 == 0:
                    W("vector", g_osb[rc - 1].s, 64)
                EV(lambda q, pbs=pbs, hh2=hh2, oev=oev: q.tensor_add(
                    oev, pbs[:, :], b2bc[:, 512 * hh2:512 * (hh2 + 1)]))
                v_epi_dv = EV(lambda q, g=g, hh2=hh2, osb=osb, oev=oev: q.tensor_scalar(
                    osb[:, 512 * hh2:512 * (hh2 + 1)], oev, gmask[:, g:g + 1],
                    None, ALU.mult))
            if hh2 == 0:
                v_epi_half0 = v_epi_dv
        if rc == 0:
            W("gpsimd", g_az.s, V_AZERO)
        for sub in range(4):
            g = 4 * rc + sub
            osb = arena3[:, 4096 + 1024 * sub:4096 + 1024 * (sub + 1)]
            W("gpsimd", dv, v_epi_dv)
            em("gpsimd", lambda q, g=g, osb=osb, rc=rc: q.indirect_dma_start(
                out=a2as[:, :],
                out_offset=IndirectOffsetOnAxis(idxg_sb[:, 5 * g + 2:5 * g + 3], axis=0),
                in_=osb, in_offset=None,
                bounds_check=bcNPR[0], oob_is_err=False).then_inc(g_osb[rc].s, 16))
            g_osb[rc].c.add(16)
    if stop_after < 5:
        _final_stub()
        return _finish()
    for rc in range(8):
        W("gpsimd", g_osb[rc].s, 64)
    em("gpsimd", lambda q: q.collective_compute(
        "AllToAll", ALU.bypass, [list(range(NCORE))],
        ins=[a2as.ap()], outs=[a2ar.ap()]).then_inc(cc3, 1))
    em("gpsimd", lambda q: q.collective_compute(
        "AllToAll", ALU.bypass, [list(range(NCORE))],
        ins=[a2ais.ap()], outs=[a2air.ap()]).then_inc(cc4, 1))
    W("gpsimd", cc3, 1)
    W("gpsimd", cc4, 1)
    if stop_after < 4.3:
        _final_stub()
        return _finish()
    v_wol = DMA("gpsimd", g_wo, lambda q: q.dma_start(
        out=arena2[:, 0:8192].rearrange("p (k e) -> p k e", k=8),
        in_=w_outT.rearrange("(k p) e -> p k e", p=128)))
    W("gpsimd", g_gz.s, V_GZERO)
    v_ridl = DMA("gpsimd", g_mr, lambda q: q.dma_start(
        out=ridl[:, :].rearrange("p (j w) -> p j w", w=5),
        in_=a2air.rearrange("(j p w) o -> p j (w o)", p=128, w=5)))
    W("gpsimd", g_mr.s, v_ridl)
    for j in range(8):
        vd = DMA("gpsimd", g_mr, lambda q, j=j: q.dma_start(
            out=arena3[:, 0:5120].rearrange("p (c d) -> p c d", c=5),
            in_=a2ar[CAP * j:CAP * (j + 1), :].rearrange("(c p) d -> p c d", p=128)))
        W("gpsimd", g_mr.s, vd)
        for c5 in range(5):
            m = 5 * j + c5
            W("gpsimd", g_ma[j].s, g_ma[j].c.v)
            em("gpsimd", lambda q, j=j, c5=c5, m=m: q.indirect_dma_start(
                out=gdense[:, :],
                out_offset=IndirectOffsetOnAxis(ridl[:, m:m + 1], axis=0),
                in_=arena3[:, 1024 * c5:1024 * c5 + 1024], in_offset=None,
                bounds_check=bcT[0], oob_is_err=False,
                compute_op=ALU.add).then_inc(g_ma[j].s, 16))
            g_ma[j].c.add(16)
        W("gpsimd", g_ma[j].s, 80)
    if stop_after < 4.7:
        _final_stub()
        return _finish()
    gsil = a1f[:, 0:1024]
    xstile = a1f[:, 1024:2048]
    outtile = a1f[:, 2048:3072]
    v_sdbg_prev = [0]
    v_gl = 0
    v_out_ev = 0
    n_tt = 16 if stop_after >= 4.9 else (4 if stop_after >= 4.85 else 1)
    for tt in range(n_tt):
        if tt == 0:
            for jj in range(8):
                W("sync", g_ma[jj].s, 80)
        else:
            W("sync", dv, v_out_ev)
        vg = DMA("sync", g_gt, lambda q, tt=tt: q.dma_start(
            out=gtile[:, :], in_=gdense[128 * tt:128 * (tt + 1), :]))
        vx = DMA("sync", g_xs, lambda q, tt=tt: q.dma_start(
            out=xstile, in_=xs[128 * tt:128 * (tt + 1), :]))
        W("scalar", g_gt.s, vg)
        if debug:
            W("scalar", g_dbq.s, g_dbq.c.v)
        v_sil = EA(lambda q: q.activation(gsil, gtile[:, :], AF.Silu))
        for half in range(2):
            wts = []
            if half == 0:
                if tt == 0:
                    wts += [(g_wo.s, v_wol), (dv, V_Z)]
                else:
                    wts.append((dv, v_out_ev))
                if debug and tt > 0:
                    wts.append((ac, v_sdbg_prev[0]))
            v_mmo = MACC(pW[:, 512 * half:512 * (half + 1)],
                         [(ubf[:, 2048 * k + 128 * tt:2048 * k + 128 * (tt + 1)],
                           arena2[:, 1024 * k + 512 * half:1024 * k + 512 * (half + 1)])
                          for k in range(8)],
                         waits=wts)
        v_mmo2 = c_pe.v
        if debug:
            W("scalar", pe, v_mmo2)
            v_sdbg = EA(lambda q: q.copy(a1f[:, 3072:4096], pW[:, :]))
            v_sdbg_prev[0] = v_sdbg
            W("sync", ac, v_sdbg)
            DMA("sync", g_dbq, lambda q, tt=tt: q.dma_start(out=s_dbg[128 * tt:128 * (tt + 1), :],
                                                            in_=a1f[:, 3072:4096]))
        W("vector", pe, v_mmo2)
        W("vector", ac, v_sil)
        W("vector", g_xs.s, vx)
        W("vector", g_out.s, g_out.c.v)
        EV(lambda q: q.tensor_mul(outtile[:, 0:512], pW[:, 0:512], gsil[:, 0:512]))
        EV(lambda q: q.tensor_mul(outtile[:, 512:1024], pW[:, 512:1024], gsil[:, 512:1024]))
        EV(lambda q: q.tensor_add(outtile[:, 0:512], outtile[:, 0:512], xstile[:, 0:512]))
        v_out_ev = EV(lambda q: q.tensor_add(outtile[:, 512:1024], outtile[:, 512:1024], xstile[:, 512:1024]))
        W("sync", dv, v_out_ev)
        DMA("sync", g_out, lambda q, tt=tt: q.dma_start(out=out_t[128 * tt:128 * (tt + 1), :],
                                                        in_=outtile))
        if debug:
            W("sync", ac, v_sil)
            DMA("sync", g_dbq, lambda q, tt=tt: q.dma_start(out=g_dbg[128 * tt:128 * (tt + 1), :],
                                                            in_=gsil))
    if debug:
        W("sync", cc1, 1)
        DMA("sync", g_dbq, lambda q: q.dma_start(out=sc_dbg[:, :], in_=sc_in[:, :]))
        W("vector", dv, c_dv.v)
        v = EV(lambda q: q.tensor_copy(fscr[:, :], vE[:, :]))
        W("sync", dv, v)
        DMA("sync", g_dbq, lambda q: q.dma_start(out=ve_dbg[:, :], in_=fscr[:, :]))
    if n_tt < 16 or stop_after < 5:
        _final_stub()
        return _finish()
    W("sync", g_out.s, g_out.c.v)
    if debug:
        W("sync", g_dbq.s, g_dbq.c.v)
    return _finish()


def _host_inputs(inputs, debug=False):
    bf = ml_dtypes.bfloat16
    x = np.ascontiguousarray(np.asarray(inputs["L0_attn_x"], np.float32).reshape(NTOK, DM))
    W_in = np.asarray(inputs["L0_attn_W_in"], np.float32)
    W_state = np.asarray(inputs["L0_attn_W_state"], np.float32)
    W_dt = np.asarray(inputs["L0_attn_W_dt"], np.float32)
    b_dtv = np.asarray(inputs["L0_attn_b_dt"], np.float32)
    A_log = np.asarray(inputs["L0_attn_A_log"], np.float32)
    W_C = np.asarray(inputs["L0_attn_W_C"], np.float32)
    Dv = np.asarray(inputs["L0_attn_D"], np.float32)
    W_out = np.asarray(inputs["L0_attn_W_out"], np.float32)
    Wg = np.asarray(inputs["L0_ffn_Wg"], np.float32)
    W1 = np.asarray(inputs["L0_ffn_W1"], np.float32)
    b1 = np.asarray(inputs["L0_ffn_b1"], np.float32)
    W2 = np.asarray(inputs["L0_ffn_W2"], np.float32)
    b2 = np.asarray(inputs["L0_ffn_b2"], np.float32)

    com = dict(
        xfull=x,
        w_inT=np.ascontiguousarray(W_in.T),
        w_stT=np.ascontiguousarray(W_state.T),
        w_dtT=np.ascontiguousarray(W_dt.T),
        w_cT=np.ascontiguousarray(W_C.T),
        w_outT=np.ascontiguousarray(W_out.T),
        wg_in=np.ascontiguousarray(Wg),
        b_dt_in=np.ascontiguousarray(b_dtv.reshape(DS, 1)),
        a_log_in=np.ascontiguousarray(A_log.reshape(DS, 1)),
        dmat_in=np.ascontiguousarray(Dv.reshape(8, 128).T),
        identbf_in=np.eye(128, dtype=bf),
        identf_in=np.eye(128, dtype=np.float32),
        onescol_in=np.ones((128, 1), np.float32),
        onesrow_in=np.ones((1, 128), np.float32),
        onestt_in=np.ones((128, 128), np.float32),
        ltstrict_in=np.ascontiguousarray(
            (np.arange(128)[:, None] < np.arange(128)[None, :]).astype(np.float32)),
        lt16_in=np.ascontiguousarray(
            (np.arange(128)[:, None] < (16 * (np.arange(128)[None, :] // 16)))
            .astype(np.float32)),
        tokid_in=np.ascontiguousarray(
            (np.arange(128)[:, None] * 128 + np.arange(128)[None, :]).astype(np.int32)),
        jbase_in=np.ascontiguousarray(
            ((np.arange(128) // 16) * CAP).astype(np.float32).reshape(128, 1)),
    )
    in_maps = []
    for c in range(NCORE):
        m = dict(com)
        sl = slice(T * c, T * (c + 1))
        m["xs"] = np.ascontiguousarray(x[sl])
        m["xts"] = np.ascontiguousarray(x[sl].T)
        m["w1e"] = np.ascontiguousarray(W1[c])
        m["w2e"] = np.ascontiguousarray(W2[c])
        m["b1m_in"] = np.ascontiguousarray(b1[c].reshape(32, 128).T)
        m["b2bc_in"] = np.ascontiguousarray(np.broadcast_to(b2[c], (128, DM)))
        sv = np.zeros((1, 8), np.float32)
        for j in range(NCORE):
            if j // 4 == c // 4 and j < c:
                sv[0, j] = 1.0
        m["selv_in"] = sv
        em_ = np.zeros((128, 1024), np.float32)
        em_[:, c::8] = 1.0
        m["emask_in"] = em_
        in_maps.append(m)
    return in_maps


def kernel(**inputs):
    in_maps = _host_inputs(inputs)
    with allow_ici_all_to_all():
        nc = build_program(debug=False)
        res = run_bass_kernel_spmd(nc, in_maps, core_ids=list(range(NCORE)))
    out = np.concatenate([np.asarray(res.results[c]["out"]) for c in range(NCORE)], 0)
    return out.reshape(2, 8192, DM)


# version 06
# speedup vs baseline: 1.2232x; 1.0942x over previous
import numpy as np
import ml_dtypes
import concourse.bass as bass
import concourse.mybir as mybir
from concourse.bass import IndirectOffsetOnAxis
from concourse.bass_utils import run_bass_kernel_spmd
from concourse.aot_env import allow_ici_all_to_all

F32 = mybir.dt.float32
BF16 = mybir.dt.bfloat16
I32 = mybir.dt.int32
AF = mybir.ActivationFunctionType
ALU = mybir.AluOpType
AX = mybir.AxisListType

NCORE = 8
T = 2048
NTOK = 16384
DM = 1024
DS = 128
FF = 4096
KSEL = 4096
CAP = 640
NPR = NCORE * CAP
BIG = 1 << 28
NBIS = 42
RATE_EPS = 1e-4
LN_EPS = 1e-5


class Ctr:
    __slots__ = ("v",)
    def __init__(self):
        self.v = 0
    def add(self, n=1):
        self.v += n
        return self.v


def build_program(debug=False, stop_after=5):
    nc = bass.Bass("TRN2", target_bir_lowering=False, debug=False)
    D = lambda n, s, dt, k="ExternalInput": nc.dram_tensor(n, s, dt, kind=k)

    xts = D("xts", [DM, T], F32)
    xs = D("xs", [T, DM], F32)
    xfull = D("xfull", [NTOK, DM], F32)
    w_inT = D("w_inT", [DM, 2 * DM], F32)
    w_stT = D("w_stT", [DM, DS], F32)
    w_dtT = D("w_dtT", [DM, DS], F32)
    w_cT = D("w_cT", [DS, DM], F32)
    w_outT = D("w_outT", [DM, DM], F32)
    wg_in = D("wg_in", [DM, 8], F32)
    b_dt_in = D("b_dt_in", [DS, 1], F32)
    a_log_in = D("a_log_in", [DS, 1], F32)
    dmat_in = D("dmat_in", [128, 8], F32)
    w1e = D("w1e", [DM, FF], F32)
    w2e = D("w2e", [FF, DM], F32)
    b1m_in = D("b1m_in", [128, 32], F32)
    b2bc_in = D("b2bc_in", [128, DM], F32)
    identbf_in = D("identbf_in", [128, 128], BF16)
    identf_in = D("identf_in", [128, 128], F32)
    onescol_in = D("onescol_in", [128, 1], F32)
    onesrow_in = D("onesrow_in", [1, 128], F32)
    onestt_in = D("onestt_in", [128, 128], F32)
    ltstrict_in = D("ltstrict_in", [128, 128], F32)
    lt16_in = D("lt16_in", [128, 128], F32)
    tokid_in = D("tokid_in", [128, 128], I32)
    jbase_in = D("jbase_in", [128, 1], F32)
    selv_in = D("selv_in", [1, 8], F32)
    emask_in = D("emask_in", [128, 1024], F32)

    sc_in = nc.dram_tensor("sc_in", [T, 8], F32)
    sc_out = nc.dram_tensor("sc_out", [NTOK, 8], F32)
    sum_in = nc.dram_tensor("sum_in", [2, 128], F32)
    sum_out = nc.dram_tensor("sum_out", [16, 128], F32)
    w1bfd = nc.dram_tensor("w1bfd", [DM, FF], BF16)
    w2bfd = nc.dram_tensor("w2bfd", [FF, DM], BF16)
    idxg = nc.dram_tensor("idxg", [KSEL, 5], I32)
    a2as = nc.dram_tensor("a2as", [NPR, DM], BF16)
    a2ar = nc.dram_tensor("a2ar", [NPR, DM], BF16)
    a2ais = nc.dram_tensor("a2ais", [NPR, 1], I32)
    a2air = nc.dram_tensor("a2air", [NPR, 1], I32)
    gdense = nc.dram_tensor("gdense", [T, DM], BF16)
    out_t = D("out", [T, DM], F32, "ExternalOutput")
    if debug:
        hh_dbg = D("hh_dbg", [128, T], F32, "ExternalOutput")
        thr_dbg = D("thr_dbg", [1, 8], F32, "ExternalOutput")
        s_dbg = D("s_dbg", [T, DM], F32, "ExternalOutput")
        g_dbg = D("g_dbg", [T, DM], F32, "ExternalOutput")
        sc_dbg = D("sc_dbg", [T, 8], F32, "ExternalOutput")
        ve_dbg = D("ve_dbg", [128, 128], F32, "ExternalOutput")

    from contextlib import ExitStack
    ctx = ExitStack()
    sb = lambda n, s, dt: ctx.enter_context(nc.sbuf_tensor(n, s, dt))
    ps = lambda n, s, dt: ctx.enter_context(nc.psum_tensor(n, s, dt))
    sem = lambda n: ctx.enter_context(nc.semaphore(n))

    xnbf = sb("xnbf", [128, 8 * T], BF16)
    ubf = sb("ubf", [128, 8 * T], BF16)
    arena1 = sb("arena1", [128, 8 * T], BF16)
    arena2 = sb("arena2", [128, 8 * T], BF16)
    arena3 = sb("arena3", [128, 8 * T], BF16)
    a1f = arena1.bitcast(F32)
    a2f = arena2.bitcast(F32)
    a3f = arena3.bitcast(F32)

    wstage = sb("wstage", [128, 4096], BF16)
    wg_sb = sb("wg_sb", [128, 64], F32)
    wstb = sb("wstb", [128, 1024], BF16)
    wdtb = sb("wdtb", [128, 1024], BF16)
    wctb = sb("wctb", [128, 1024], BF16)
    identbf = sb("identbf", [128, 128], BF16)
    identf = sb("identf", [128, 128], F32)
    onescol = sb("onescol", [128, 1], F32)
    onesrow = sb("onesrow", [1, 128], F32)
    onestt = sb("onestt", [128, 128], F32)
    ltstrict = sb("ltstrict", [128, 128], F32)
    lt16 = sb("lt16", [128, 128], F32)
    tokid = sb("tokid", [128, 128], I32)
    jbase = sb("jbase", [128, 1], F32)
    selv = sb("selv", [1, 8], F32)
    b_dt = sb("b_dt", [128, 1], F32)
    a_log = sb("a_log", [128, 1], F32)
    dmat = sb("dmat", [128, 8], F32)
    b1m = sb("b1m", [128, 32], F32)
    b2bc = sb("b2bc", [128, DM], F32)
    negrate = sb("negrate", [128, 1], F32)
    rates = sb("rates", [128, 4], F32)
    epst = sb("epst", [128, 1], F32)
    mhalf = sb("mhalf", [128, 1], F32)
    meanr = sb("meanr", [1, 512], F32)
    rstdr = sb("rstdr", [1, 512], F32)
    stat1 = sb("stat1", [1, 512], F32)
    stat3 = sb("stat3", [1, 512], F32)
    scsb = sb("scsb", [8, 512], F32)
    sctm = sb("sctm", [128, 32], F32)
    dts1 = sb("dts1", [128, 512], F32)
    dts2 = sb("dts2", [128, 512], F32)
    dts3 = sb("dts3", [128, 512], F32)
    dtsum = sb("dtsum", [128, 4], F32)
    hin = sb("hin", [1, 128], F32)
    hf1 = sb("hf1", [1, 128], F32)
    hf2 = sb("hf2", [1, 128], F32)
    h0col = sb("h0col", [128, 1], F32)
    vE = sb("vE", [128, 128], F32)
    bis = sb("bis", [1, 8], F32)
    thrcol = sb("thrcol", [128, 1], F32)
    cntcol = sb("cntcol", [128, 1], F32)
    exc = sb("exc", [128, 1], F32)
    excm1 = sb("excm1", [128, 1], F32)
    bnd = sb("bnd", [128, 1], F32)
    idxg_sb = sb("idxg_sb", [128, 160], I32)
    senti = sb("senti", [128, 40], I32)
    gmask = sb("gmask", [128, 32], F32)
    mcol = sb("mcol", [128, 1], F32)
    s2col = sb("s2col", [128, 1], F32)
    vcol = sb("vcol", [128, 1], F32)
    rcol = sb("rcol", [128, 1], F32)
    ncol = sb("ncol", [128, 1], F32)
    gtile = sb("gtile", [128, DM], BF16)
    recvd = sb("recvd", [128, DM], BF16)
    zerot = sb("zerot", [128, DM], BF16)
    ridl = sb("ridl", [128, 40], I32)

    p0 = ps("p0", [128, 512], F32)
    p1 = ps("p1", [128, 512], F32)
    p2 = ps("p2", [128, 512], F32)
    p3 = ps("p3", [128, 512], F32)
    pW = ps("pW", [128, 1024], F32)
    pS = ps("pS", [8, 512], F32)
    pT = ps("pT", [128, 512], BF16)

    pe = sem("pe"); c_pe = Ctr()
    dv = sem("dv"); c_dv = Ctr()
    ac = sem("ac"); c_ac = Ctr()
    cc1 = sem("cc1"); cc2 = sem("cc2"); cc3 = sem("cc3"); cc4 = sem("cc4")
    junk = sem("junk")

    class Q:
        def __init__(self, name, chained=False):
            self.s = sem(name)
            self.c = Ctr()
            self.chained = chained

    prog = {k: [] for k in ("sync", "scalar", "vector", "tensor", "gpsimd")}
    def em(engine, fn):
        prog[engine].append(fn)
    def W(engine, s, v):
        if v > 0 and not (engine == "tensor" and s is pe):
            em(engine, lambda q, s=s, v=v: q.wait_ge(s, v))
    def EP(fn):
        em("tensor", lambda q, fn=fn: fn(q).then_inc(pe, 1))
        return c_pe.add(1)
    def EV(fn):
        if c_dv.v > 0:
            em("vector", lambda q, v=c_dv.v: q.wait_ge(dv, v))
        em("vector", lambda q, fn=fn: fn(q).then_inc(dv, 1))
        return c_dv.add(1)
    def EA(fn):
        if c_ac.v > 0:
            em("scalar", lambda q, v=c_ac.v: q.wait_ge(ac, v))
        em("scalar", lambda q, fn=fn: fn(q).then_inc(ac, 1))
        return c_ac.add(1)
    def DMA(engine, grp, fn):
        if grp.chained and grp.c.v > 0:
            em(engine, lambda q, s=grp.s, v=grp.c.v: q.wait_ge(s, v))
        em(engine, lambda q, fn=fn, s=grp.s: fn(q).then_inc(s, 16))
        return grp.c.add(16)
    g_const = Q("g_const")
    g_wssm = Q("g_wssm")
    g_stgld = Q("g_stgld", chained=True)
    g_stgst = Q("g_stgst", chained=True)
    g_pref = Q("g_pref")
    g_gz = Q("g_gz")
    g_az = Q("g_az")
    g_x = Q("g_x", chained=True)
    g_sc = Q("g_sc")
    g_sum = Q("g_sum")
    g_sumg = Q("g_sumg")
    g_em = Q("g_em")
    g_scg = Q("g_scg")
    g_idxl = Q("g_idxl")
    g_sq1 = Q("g_sq1")
    g_sq2 = Q("g_sq2")
    g_osb = [Q(f"g_osb{i}") for i in range(8)]
    g_ma = [Q(f"g_ma{i}") for i in range(8)]
    g_gath = Q("g_gath", chained=True)
    g_w1 = [Q("g_w1a", chained=True), Q("g_w1b", chained=True)]
    g_w2 = [Q("g_w2a", chained=True), Q("g_w2b", chained=True)]
    g_wo = Q("g_wo")
    g_gt = Q("g_gt", chained=True)
    g_xs = Q("g_xs", chained=True)
    g_mr = Q("g_mr", chained=True)
    g_out = Q("g_out", chained=True)
    g_dbq = Q("g_dbq", chained=True)
    def MACC(psv, pairs, waits=()):
        for sw, vw in waits:
            W("tensor", sw, vw)
        n = len(pairs)
        for i, (lf, rf) in enumerate(pairs):
            if i < n - 1:
                em("tensor", lambda q, lf=lf, rf=rf, st=(i == 0), psv=psv:
                   q.matmul(psv, lf, rf, start=st, stop=False))
            else:
                em("tensor", lambda q, lf=lf, rf=rf, st=(i == 0), psv=psv:
                   q.matmul(psv, lf, rf, start=st, stop=True).then_inc(pe, 1))
        return c_pe.add(1)

    def _final_stub():
        W("sync", dv, c_dv.v)
        W("sync", ac, c_ac.v)
        W("sync", pe, c_pe.v)
        for tt in range(16):
            for hb in range(2):
                DMA("sync", g_out, lambda q, tt=tt, hb=hb: q.dma_start(
                    out=out_t[128 * tt:128 * (tt + 1), :].bitcast(BF16)[:, 1024 * hb:1024 * (hb + 1)],
                    in_=zerot[:, :]))
        W("sync", g_out.s, g_out.c.v)

    def _finish():
        with nc.Block() as block:
            @block.sync
            def _(q):
                for f in prog["sync"]:
                    f(q)
            @block.scalar
            def _(q):
                for f in prog["scalar"]:
                    f(q)
            @block.vector
            def _(q):
                for f in prog["vector"]:
                    f(q)
            @block.tensor
            def _(q):
                for f in prog["tensor"]:
                    f(q)
            @block.gpsimd
            def _(q):
                for f in prog["gpsimd"]:
                    f(q)
        ctx.close()
        return nc

    constpairs = [
        (wg_sb.rearrange("p (k e) -> p k e", e=8), wg_in.rearrange("(k p) e -> p k e", p=128)),
        (identbf, identbf_in[:, :]), (identf, identf_in[:, :]),
        (onescol, onescol_in[:, :]), (onesrow, onesrow_in[:, :]),
        (onestt, onestt_in[:, :]),
        (ltstrict, ltstrict_in[:, :]), (lt16, lt16_in[:, :]),
        (tokid, tokid_in[:, :]), (jbase, jbase_in[:, :]), (selv, selv_in[:, :]),
        (b_dt, b_dt_in[:, :]), (a_log, a_log_in[:, :]), (dmat, dmat_in[:, :]),
        (b1m, b1m_in[:, :]), (b2bc, b2bc_in[:, :]),
    ]
    for dst, src in constpairs:
        dap = dst if isinstance(dst, bass.AP) else dst[:, :]
        DMA("sync", g_const, lambda q, d=dap, s=src: q.dma_start(out=d, in_=s))
    V_CONST = g_const.c.v

    for dst, src in [
        (wstb.rearrange("p (uk n) -> p uk n", uk=8), w_stT.rearrange("(uk p) n -> p uk n", p=128)),
        (wdtb.rearrange("p (uk n) -> p uk n", uk=8), w_dtT.rearrange("(uk p) n -> p uk n", p=128)),
        (wctb[:, :], w_cT[:, :]),
    ]:
        DMA("gpsimd", g_wssm, lambda q, d=dst, s=src: q.dma_start(out=d, in_=s))
    DMA("gpsimd", g_wssm,
        lambda q: q.dma_start(out=arena2[:, :].rearrange("p (k u) -> p k u", k=8),
                              in_=w_inT.rearrange("(k p) u -> p k u", p=128)))
    V_WSSM = g_wssm.c.v

    V_SENTI = EV(lambda q: q.memset(senti[:, :], BIG))
    V_ZEROT = EV(lambda q: q.memset(zerot[:, :], 0))
    W("gpsimd", dv, V_ZEROT)
    for qd in range(4):
        DMA("gpsimd", g_pref, lambda q, qd=qd: q.dma_start(
            out=idxg[1024 * qd:1024 * (qd + 1), :].rearrange("(p w) c -> p w c", p=128),
            in_=senti[:, 0:40].rearrange("p (w c) -> p w c", c=5)))
    DMA("gpsimd", g_pref, lambda q: q.dma_start(
        out=a2ais[:, :].rearrange("(p w) c -> p w c", p=128),
        in_=senti[:, 0:40].rearrange("p (w c) -> p w c", c=1)))
    V_PREFILL = g_pref.c.v
    for tt in range(16):
        DMA("gpsimd", g_gz, lambda q, tt=tt: q.dma_start(
            out=gdense[128 * tt:128 * (tt + 1), :], in_=zerot[:, :]))
    V_GZERO = g_gz.c.v
    for m in range(40):
        DMA("gpsimd", g_az, lambda q, m=m: q.dma_start(
            out=a2as[128 * m:128 * (m + 1), :], in_=zerot[:, :]))
    V_AZERO = g_az.c.v

    wst_free = [0, 0]
    for i in range(16):
        h = i % 2
        r, cblk = divmod(i, 2)
        W("gpsimd", g_stgst.s, wst_free[h])
        v = DMA("gpsimd", g_stgld, lambda q, r=r, cblk=cblk, h=h: q.dma_start(
            out=wstage[:, h * 2048:h * 2048 + 2048],
            in_=w1e[128 * r:128 * (r + 1), 2048 * cblk:2048 * (cblk + 1)]))
        W("gpsimd", g_stgld.s, v)
        wst_free[h] = DMA("gpsimd", g_stgst, lambda q, r=r, cblk=cblk, h=h: q.dma_start(
            out=w1bfd[128 * r:128 * (r + 1), 2048 * cblk:2048 * (cblk + 1)],
            in_=wstage[:, h * 2048:h * 2048 + 2048]))
    for i in range(32):
        h = i % 2
        W("gpsimd", g_stgst.s, wst_free[h])
        v = DMA("gpsimd", g_stgld, lambda q, i=i, h=h: q.dma_start(
            out=wstage[:, h * 2048:h * 2048 + 1024],
            in_=w2e[128 * i:128 * (i + 1), :]))
        W("gpsimd", g_stgld.s, v)
        wst_free[h] = DMA("gpsimd", g_stgst, lambda q, i=i, h=h: q.dma_start(
            out=w2bfd[128 * i:128 * (i + 1), :],
            in_=wstage[:, h * 2048:h * 2048 + 1024]))
    V_WBF = g_stgst.c.v

    W("scalar", g_const.s, V_CONST)
    EA(lambda q: q.activation(rates[:, 0:1], a_log[:, :], AF.Relu))
    v = EA(lambda q: q.activation(rates[:, 1:2], a_log[:, :], AF.Abs))
    W("scalar", ac, v)
    v = EA(lambda q: q.activation(rates[:, 2:3], rates[:, 1:2], AF.Exp, scale=-1.0))
    W("scalar", ac, v)
    v = EA(lambda q: q.activation(rates[:, 3:4], rates[:, 2:3], AF.Ln, bias=1.0))
    W("vector", ac, v)
    v = EV(lambda q: q.tensor_add(negrate[:, :], rates[:, 0:1], rates[:, 3:4]))
    W("vector", dv, v)
    V_NEGRATE = EV(lambda q: q.tensor_scalar(negrate[:, :], negrate[:, :],
                                             RATE_EPS, -1.0, ALU.add, ALU.mult))
    EV(lambda q: q.memset(epst[:, :], LN_EPS))
    V_EPST = EV(lambda q: q.memset(mhalf[:, :], -0.5))

    W("tensor", g_const.s, V_CONST)
    W("vector", g_const.s, V_CONST)
    W("gpsimd", g_const.s, V_CONST)

    decay = a1f[:, 0:2048]
    vvn = a1f[:, 2048:4096]
    hh = a1f[:, 4096:6144]
    sumg = a1f[0:1, 6144:8192]
    hcast = arena1[:, 0:2048]
    mb_sb = a3f[:, 0:512]
    rb_sb = a3f[:, 512:1024]
    a2i = arena2.bitcast(I32)
    scg2 = a2f[:, 2048:3072]
    cmpscr = a2f[:, 0:128]
    msk = a2f[:, 128:256]
    pscan = a2f[:, 256:384]
    slotf = a2f[:, 384:512]
    fscr = a2f[:, 512:640]
    fscr2 = a2f[:, 640:768]
    gtv = a2f[:, 768:896]
    iscr = a2i[:, 896:1024]
    iscr2 = a2i[:, 1024:1152]
    desti = a2i[:, 1152:1280]
    pdesti = a2i[:, 1280:1408]
    prs5 = a2i[:, 1408:2048]
    xf32 = a1f[:, 0:4096]
    xnf = a1f[:, 4096:8192]
    v_xncast = [0, 0, 0, 0]
    for tc in range(4):
        pe_prev = c_pe.v
        if tc > 0:
            W("sync", pe, pe_prev)
        v_xf = DMA("sync", g_x, lambda q, tc=tc: q.dma_start(
            out=xf32.rearrange("p (k t) -> p k t", k=8),
            in_=xts[:, 512 * tc:512 * (tc + 1)].rearrange("(k p) t -> p k t", p=128)))
        W("tensor", g_x.s, v_xf)
        v_sum = MACC(pS[0:1, :],
                     [(onescol[:, :], xf32[:, 512 * k:512 * (k + 1)]) for k in range(8)])
        W("scalar", g_x.s, v_xf)
        if tc > 0:
            W("scalar", pe, pe_prev)
        for k in range(8):
            EA(lambda q, k=k: q.activation(
                xnf[:, 512 * k:512 * (k + 1)], xf32[:, 512 * k:512 * (k + 1)], AF.Square))
        v_sq = c_ac.v
        v_sumsq = MACC(p3[0:1, :],
                       [(onescol[:, :], xnf[:, 512 * k:512 * (k + 1)]) for k in range(8)],
                       waits=[(ac, v_sq)])
        mean_s = meanr[0:1, :]
        rstd_s = rstdr[0:1, :]
        var_s = stat1[0:1, :]
        sc3 = stat3[0:1, :]
        W("vector", pe, v_sumsq)
        EV(lambda q, m=mean_s: q.tensor_scalar(m, pS[0:1, :], 1.0 / DM, None, ALU.mult))
        EV(lambda q, v_=var_s: q.tensor_scalar(v_, p3[0:1, :], 1.0 / DM, None, ALU.mult))
        v = EV(lambda q, m=mean_s, s3=sc3: q.tensor_mul(s3, m, m))
        W("vector", dv, v)
        v = EV(lambda q, v_=var_s, s3=sc3: q.tensor_sub(v_, v_, s3))
        W("scalar", dv, v)
        if tc == 0:
            W("scalar", dv, V_EPST)
        v = EA(lambda q, v_=var_s, s3=sc3: q.activation(s3, v_, AF.Ln, bias=epst[0:1, 0:1]))
        W("scalar", ac, v)
        v = EA(lambda q, r=rstd_s, s3=sc3: q.activation(r, s3, AF.Exp, scale=mhalf[0:1, 0:1]))
        W("vector", ac, v)
        v = EV(lambda q, v_=var_s: q.tensor_scalar(v_, v_, LN_EPS, None, ALU.add))
        W("vector", dv, v)
        v = EV(lambda q, r=rstd_s, s3=sc3: q.tensor_mul(s3, r, r))
        W("vector", dv, v)
        v = EV(lambda q, v_=var_s, s3=sc3: q.tensor_mul(s3, s3, v_))
        W("vector", dv, v)
        v = EV(lambda q, s3=sc3: q.tensor_scalar(s3, s3, -0.5, 1.5, ALU.mult, ALU.add))
        W("vector", dv, v)
        v_newt = EV(lambda q, r=rstd_s, s3=sc3: q.tensor_mul(r, r, s3))
        v_bc = MACC(pW[:, 0:512], [(onesrow[:, :], mean_s)], waits=[(dv, v_newt)])
        v_bc2 = MACC(pW[:, 512:1024], [(onesrow[:, :], rstd_s)])
        W("scalar", pe, v_bc2)
        EA(lambda q: q.copy(mb_sb, pW[:, 0:512]))
        v_ev = EA(lambda q: q.copy(rb_sb, pW[:, 512:1024]))
        W("vector", ac, v_ev)
        W("vector", pe, v_sumsq)
        for k in range(8):
            EV(lambda q, k=k: q.tensor_sub(
                xnf[:, 512 * k:512 * (k + 1)], xf32[:, 512 * k:512 * (k + 1)], mb_sb))
            W("vector", dv, c_dv.v)
            EV(lambda q, k=k: q.tensor_mul(
                xnf[:, 512 * k:512 * (k + 1)], xnf[:, 512 * k:512 * (k + 1)], rb_sb))
        v_xn = c_dv.v
        W("scalar", dv, v_xn)
        for k in range(8):
            EA(lambda q, k=k, tc=tc: q.copy(
                xnbf[:, 2048 * k + 512 * tc:2048 * k + 512 * (tc + 1)],
                xnf[:, 512 * k:512 * (k + 1)]))
        v_xncast[tc] = c_ac.v
        v_sc = MACC(pS[0:8, :],
                    [(wg_sb[:, 8 * k:8 * (k + 1)], xnf[:, 512 * k:512 * (k + 1)])
                     for k in range(8)],
                    waits=[(dv, v_xn)])
        W("scalar", pe, v_sc)
        v_scsb = EA(lambda q: q.copy(scsb[:, :], pS[0:8, :]))
        W("tensor", ac, v_scsb)
        for j in range(4):
            EP(lambda q, j=j: q.transpose(
                pW[:, 8 * j:8 * (j + 1)], scsb[0:8, 128 * j:128 * (j + 1)], identf[0:8, 0:8]))
        v_tr = c_pe.v
        W("scalar", pe, v_tr)
        W("scalar", g_sc.s, g_sc.c.v)
        v_sctm = EA(lambda q: q.copy(sctm[:, :], pW[:, 0:32]))
        W("sync", ac, v_sctm)
        DMA("sync", g_sc, lambda q, tc=tc: q.dma_start(
            out=sc_in[512 * tc:512 * (tc + 1), :].rearrange("(j p) e -> p j e", p=128),
            in_=sctm[:, :].rearrange("p (j e) -> p j e", e=8)))
    V_SCIN = g_sc.c.v
    V_P1_PE = c_pe.v

    if stop_after < 2:
        _final_stub()
        return _finish()
    v_evac_ps = [0, 0]
    v_vvn_prev = 0
    v_dtabs_prev = 0
    W("tensor", g_wssm.s, V_WSSM)
    for tcc in range(4):
        W("tensor", ac, v_xncast[tcc])
        if tcc == 0:
            W("tensor", ac, v_xncast[3])
            W("tensor", g_sc.s, V_SCIN)
        for uk in range(16):
            pbuf = [p0, p1][uk % 2]
            W("tensor", ac, v_evac_ps[uk % 2])
            vm = MACC(pbuf[:, :],
                      [(arena2[:, 2048 * k + 128 * uk:2048 * k + 128 * (uk + 1)],
                        xnbf[:, 2048 * k + 512 * tcc:2048 * k + 512 * (tcc + 1)])
                       for k in range(8)])
            dstb = ubf if uk < 8 else arena3
            W("scalar", pe, vm)
            v_evac_ps[uk % 2] = EA(lambda q, d=dstb, uk=uk, tcc=tcc, pbuf=pbuf: q.copy(
                d[:, 2048 * (uk % 8) + 512 * tcc:2048 * (uk % 8) + 512 * (tcc + 1)],
                pbuf[:, :]))
        v_u_done = c_ac.v
        v_b = MACC(p2[:, :],
                   [(wstb[:, 128 * uk:128 * (uk + 1)],
                     ubf[:, 2048 * uk + 512 * tcc:2048 * uk + 512 * (tcc + 1)])
                    for uk in range(8)],
                   waits=[(ac, v_u_done), (dv, v_vvn_prev)])
        v_dt = MACC(p3[:, :],
                    [(wdtb[:, 128 * uk:128 * (uk + 1)],
                      ubf[:, 2048 * uk + 512 * tcc:2048 * uk + 512 * (tcc + 1)])
                     for uk in range(8)],
                    waits=[(ac, v_dtabs_prev)])
        W("scalar", pe, v_dt)
        W("scalar", dv, c_dv.v)
        EA(lambda q: q.activation(dts1[:, :], p3[:, :], AF.Relu, bias=b_dt[:, 0:1]))
        v = EA(lambda q: q.activation(dts2[:, :], p3[:, :], AF.Abs, bias=b_dt[:, 0:1]))
        v_dtabs_prev = v
        W("scalar", ac, v)
        v = EA(lambda q: q.activation(dts3[:, :], dts2[:, :], AF.Exp, scale=-1.0))
        W("scalar", ac, v)
        v_l = EA(lambda q: q.activation(dts2[:, :], dts3[:, :], AF.Ln, bias=1.0))
        W("vector", ac, v_l)
        v = EV(lambda q: q.tensor_add(dts1[:, :], dts1[:, :], dts2[:, :]))
        W("vector", dv, v)
        v_dtv = EV(lambda q: q.tensor_scalar_min(dts1[:, :], dts1[:, :], 10.0))
        W("vector", dv, v_dtv)
        EV(lambda q, tcc=tcc: q.reduce_sum(dtsum[:, tcc:tcc + 1], dts1[:, :], axis=AX.X))
        W("scalar", dv, v_dtv)
        if tcc == 0:
            W("scalar", dv, V_NEGRATE)
        v_dec = EA(lambda q, tcc=tcc: q.activation(
            decay[:, 512 * tcc:512 * (tcc + 1)], dts1[:, :], AF.Exp, scale=negrate[:, 0:1]))
        W("vector", ac, v_dec)
        W("vector", pe, v_b)
        v_vvn_prev = EV(lambda q, tcc=tcc: q.scalar_tensor_tensor(
            vvn[:, 512 * tcc:512 * (tcc + 1)], decay[:, 512 * tcc:512 * (tcc + 1)],
            1.0, p2[:, :], ALU.subtract, ALU.mult))
    W("vector", dv, c_dv.v)
    v = EV(lambda q: q.tensor_tensor_scan(hh, decay, vvn, 0.0, ALU.mult, ALU.subtract))
    W("vector", dv, v)
    v = EV(lambda q: q.tensor_add(dtsum[:, 0:1], dtsum[:, 0:1], dtsum[:, 1:2]))
    W("vector", dv, v)
    v = EV(lambda q: q.tensor_add(dtsum[:, 2:3], dtsum[:, 2:3], dtsum[:, 3:4]))
    W("vector", dv, v)
    v_dtt = EV(lambda q: q.tensor_add(dtsum[:, 0:1], dtsum[:, 0:1], dtsum[:, 2:3]))
    W("scalar", dv, v_dtt)
    v_at = EA(lambda q: q.activation(dtsum[:, 1:2], dtsum[:, 0:1], AF.Exp,
                                     scale=negrate[:, 0:1]))
    W("sync", ac, v_at)
    W("sync", dv, c_dv.v)
    DMA("sync", g_sum, lambda q: q.dma_start(out=sum_in[0:1, :].rearrange("o p -> p o"), in_=dtsum[:, 1:2]))
    v_sumdma = DMA("sync", g_sum, lambda q: q.dma_start(out=sum_in[1:2, :].rearrange("o p -> p o"), in_=hh[:, 2047:2048]))
    W("gpsimd", g_sum.s, v_sumdma)
    em("gpsimd", lambda q: q.collective_compute(
        "AllGather", ALU.bypass, [list(range(NCORE))],
        ins=[sum_in.ap()], outs=[sum_out.ap()]).then_inc(cc2, 1))
    W("gpsimd", cc2, 1)
    v_sumg = DMA("gpsimd", g_sumg, lambda q: q.dma_start(out=sumg[0:1, :], in_=sum_out.rearrange("a b -> (a b)")[None, :]))
    W("vector", g_sumg.s, v_sumg)
    EV(lambda q: q.memset(hin[:, :], 0.0))
    for j in range(NCORE):
        W("vector", dv, c_dv.v)
        EV(lambda q, j=j: q.tensor_scalar(hf1[:, :], sumg[0:1, 256 * j:256 * j + 128],
                                          -1.0, None, ALU.add))
        W("vector", dv, c_dv.v)
        EV(lambda q, j=j: q.tensor_scalar(hf1[:, :], hf1[:, :],
                                          selv[0:1, j:j + 1], 1.0, ALU.mult, ALU.add))
        W("vector", dv, c_dv.v)
        EV(lambda q: q.tensor_mul(hf2[:, :], hin[:, :], hf1[:, :]))
        W("vector", dv, c_dv.v)
        EV(lambda q, j=j: q.tensor_scalar(hf1[:, :], sumg[0:1, 256 * j + 128:256 * j + 256],
                                          selv[0:1, j:j + 1], None, ALU.mult))
        W("vector", dv, c_dv.v)
        EV(lambda q: q.tensor_add(hin[:, :], hf2[:, :], hf1[:, :]))
    v_hin = c_dv.v
    v_htr = MACC(pW[:, 0:1], [(hin[:, :], identf[0:1, 0:1])], waits=[(dv, v_hin)])
    W("vector", pe, v_htr)
    v = EV(lambda q: q.tensor_copy(h0col[:, :], pW[:, 0:1]))
    W("vector", dv, v)
    v_scan2 = EV(lambda q: q.tensor_tensor_scan(hh, decay, vvn, h0col[:, 0:1],
                                                ALU.mult, ALU.subtract))
    W("vector", dv, v_scan2)
    v_hcast = EV(lambda q: q.tensor_copy(hcast[:, :], hh))
    if debug:
        W("sync", dv, v_scan2)
        DMA("sync", g_dbq, lambda q: q.dma_start(out=hh_dbg[:, :], in_=hh))

    silu4 = a1f[:, 2048:6144]
    v_z_evac = [0, 0]
    v_mul_last = 0
    for tcc in range(4):
        gate3 = arena3.rearrange("p (k c) -> p k c", c=2048)[:, :, 512 * tcc:512 * (tcc + 1)]
        W("scalar", dv, v_hcast if tcc == 0 else v_mul_last)
        v_sg = EA(lambda q, gate3=gate3: q.activation(
            silu4.rearrange("p (k c) -> p k c", c=512), gate3, AF.Silu))
        for k in range(8):
            pbuf = [p0, p1][k % 2]
            vy = MACC(pbuf[:, :],
                      [(wctb[:, 128 * k:128 * (k + 1)],
                        hcast[:, 512 * tcc:512 * (tcc + 1)])],
                      waits=[(dv, v_hcast), (dv, v_z_evac[k % 2])])
            W("vector", ac, v_sg)
            W("vector", pe, vy)
            v_mul_last = EV(lambda q, pbuf=pbuf, k=k: q.tensor_mul(
                dts2[:, :], pbuf[:, :], silu4[:, 512 * k:512 * (k + 1)]))
            v_z_evac[k % 2] = EV(lambda q, k=k, tcc=tcc: q.scalar_tensor_tensor(
                ubf[:, 2048 * k + 512 * tcc:2048 * k + 512 * (tcc + 1)],
                xnbf[:, 2048 * k + 512 * tcc:2048 * k + 512 * (tcc + 1)],
                dmat[:, k:k + 1], dts2[:, :], ALU.mult, ALU.add))
    V_Z = c_dv.v
    V_P3_PE = c_pe.v

    if stop_after < 3:
        _final_stub()
        return _finish()
    W("gpsimd", g_sc.s, V_SCIN)
    em("gpsimd", lambda q: q.collective_compute(
        "AllGather", ALU.bypass, [list(range(NCORE))],
        ins=[sc_in.ap()], outs=[sc_out.ap()]).then_inc(cc1, 1))
    W("gpsimd", cc1, 1)
    v_scg = DMA("gpsimd", g_scg, lambda q: q.dma_start(
        out=scg2.rearrange("p (tl e) -> p tl e", e=8),
        in_=sc_out.rearrange("(p tl) e -> p tl e", p=128)))
    emask = a1f[:, 0:1024]
    W("sync", dv, V_Z)
    v_eml = DMA("sync", g_em, lambda q: q.dma_start(out=emask, in_=emask_in[:, :]))
    W("vector", g_scg.s, v_scg)
    W("vector", g_em.s, v_eml)
    W("vector", dv, V_Z)
    v = EV(lambda q: q.tensor_mul(a1f[:, 1024:2048], scg2, emask))
    W("vector", dv, v)
    v_vE = EV(lambda q: q.tensor_reduce(
        vE[:, :], a1f[:, 1024:2048].rearrange("p (tl e) -> p tl e", e=8),
        axis=AX.X, op=ALU.add))
    W("vector", dv, v_vE)
    EV(lambda q: q.memset(bis[0:1, :], 0.0))
    EV(lambda q: q.memset(bis[0:1, 0:1], -64.0))
    W("vector", dv, c_dv.v)
    EV(lambda q: q.memset(bis[0:1, 1:2], 64.0))
    W("vector", dv, c_dv.v)
    EV(lambda q: q.memset(bis[0:1, 2:3], 0.0))
    v_mid = c_dv.v
    for it in range(NBIS):
        v_bcm = MACC(p2[:, 0:1], [(onesrow[:, :], bis[0:1, 2:3])],
                     waits=[(dv, v_mid)])
        W("scalar", pe, v_bcm)
        v_thr = EA(lambda q: q.copy(thrcol[:, :], p2[:, 0:1]))
        W("vector", ac, v_thr)
        v_cmp = EV(lambda q: q.tensor_scalar(cmpscr, vE[:, :], thrcol[:, 0:1],
                                             0.0, ALU.is_ge, ALU.add,
                                             accum_out=cntcol[:, 0:1]))
        v_cnt = MACC(p2[0:1, 1:2], [(onescol[:, :], cntcol[:, :])],
                     waits=[(dv, v_cmp)])
        W("vector", pe, v_cnt)
        EV(lambda q: q.tensor_scalar(bis[0:1, 3:4], p2[0:1, 1:2], KSEL - 0.5,
                                     None, ALU.is_ge))
        W("vector", dv, c_dv.v)
        EV(lambda q: q.tensor_sub(bis[0:1, 4:5], bis[0:1, 2:3], bis[0:1, 0:1]))
        W("vector", dv, c_dv.v)
        EV(lambda q: q.tensor_mul(bis[0:1, 5:6], bis[0:1, 3:4], bis[0:1, 4:5]))
        W("vector", dv, c_dv.v)
        EV(lambda q: q.tensor_add(bis[0:1, 0:1], bis[0:1, 0:1], bis[0:1, 5:6]))
        W("vector", dv, c_dv.v)
        EV(lambda q: q.tensor_sub(bis[0:1, 4:5], bis[0:1, 1:2], bis[0:1, 2:3]))
        W("vector", dv, c_dv.v)
        EV(lambda q: q.tensor_mul(bis[0:1, 5:6], bis[0:1, 3:4], bis[0:1, 4:5]))
        W("vector", dv, c_dv.v)
        EV(lambda q: q.tensor_add(bis[0:1, 1:2], bis[0:1, 2:3], bis[0:1, 5:6]))
        W("vector", dv, c_dv.v)
        EV(lambda q: q.tensor_add(bis[0:1, 2:3], bis[0:1, 0:1], bis[0:1, 1:2]))
        W("vector", dv, c_dv.v)
        v_mid = EV(lambda q: q.tensor_scalar(bis[0:1, 2:3], bis[0:1, 2:3], 0.5,
                                             None, ALU.mult))
    v_bcf = MACC(p2[:, 0:1], [(onesrow[:, :], bis[0:1, 0:1])], waits=[(dv, v_mid)])
    W("scalar", pe, v_bcf)
    v_thrf = EA(lambda q: q.copy(thrcol[:, :], p2[:, 0:1]))
    if debug:
        W("sync", dv, v_mid)
        DMA("sync", g_dbq, lambda q: q.dma_start(out=thr_dbg[0:1, :], in_=bis[0:1, :]))
    W("vector", ac, v_thrf)
    v_msk = EV(lambda q: q.tensor_scalar(msk, vE[:, :], thrcol[:, 0:1],
                                         None, ALU.is_ge))
    W("vector", dv, v_msk)
    v_ps = EV(lambda q: q.tensor_tensor_scan(pscan, onestt[:, :], msk,
                                             0.0, ALU.mult, ALU.add))
    v_exm = MACC(pW[:, 0:1], [(ltstrict[:, :], pscan[:, 127:128])],
                 waits=[(dv, v_ps)])
    v_bnm = MACC(pW[:, 1:2], [(lt16[:, :], pscan[:, 127:128])])
    W("vector", pe, v_bnm)
    EV(lambda q: q.tensor_copy(exc[:, :], pW[:, 0:1]))
    W("vector", dv, c_dv.v)
    EV(lambda q: q.tensor_copy(bnd[:, :], pW[:, 1:2]))
    W("vector", dv, c_dv.v)
    EV(lambda q: q.tensor_scalar(excm1[:, :], exc[:, :], -1.0, None, ALU.add))
    W("vector", dv, c_dv.v)
    EV(lambda q: q.tensor_scalar(slotf, pscan, excm1[:, 0:1],
                                 None, ALU.add))
    W("vector", dv, c_dv.v)
    EV(lambda q: q.tensor_scalar(fscr[:, :], slotf, bnd[:, 0:1],
                                 None, ALU.subtract))
    W("vector", dv, c_dv.v)
    EV(lambda q: q.tensor_scalar(fscr2, slotf, KSEL - 0.5, None, ALU.is_lt))
    W("vector", dv, c_dv.v)
    EV(lambda q: q.tensor_mul(fscr2, fscr2, msk))
    W("vector", dv, c_dv.v)
    EV(lambda q: q.tensor_scalar(gtv, fscr[:, :], CAP - 0.5, None, ALU.is_lt))
    W("vector", dv, c_dv.v)
    EV(lambda q: q.tensor_mul(gtv, gtv, msk))
    W("vector", dv, c_dv.v)
    EV(lambda q: q.tensor_copy(iscr, slotf))
    W("vector", dv, c_dv.v)
    EV(lambda q: q.tensor_scalar(iscr2, iscr, 127, None, ALU.bitwise_and))
    W("vector", dv, c_dv.v)
    EV(lambda q: q.tensor_scalar(iscr2, iscr2, 5, None, ALU.logical_shift_left))
    W("vector", dv, c_dv.v)
    EV(lambda q: q.tensor_scalar(iscr, iscr, 7, None, ALU.logical_shift_right))
    W("vector", dv, c_dv.v)
    EV(lambda q: q.tensor_add(iscr2, iscr2, iscr))
    W("vector", dv, c_dv.v)
    EV(lambda q: q.memset(desti, BIG))
    W("vector", dv, c_dv.v)
    EV(lambda q: q.tensor_copy(iscr, fscr2))
    W("vector", dv, c_dv.v)
    EV(lambda q: q.copy_predicated(desti, iscr, iscr2))
    W("vector", dv, c_dv.v)
    EV(lambda q: q.tensor_copy(iscr2, fscr[:, :]))
    W("vector", dv, c_dv.v)
    EV(lambda q: q.tensor_scalar(iscr2, iscr2, jbase[:, 0:1], None, ALU.add))
    W("vector", dv, c_dv.v)
    EV(lambda q: q.memset(pdesti, BIG))
    W("vector", dv, c_dv.v)
    EV(lambda q: q.tensor_copy(iscr, gtv))
    W("vector", dv, c_dv.v)
    EV(lambda q: q.copy_predicated(pdesti, iscr, iscr2))
    W("vector", dv, c_dv.v)
    W("scalar", dv, c_dv.v)
    v_gtv = EA(lambda q: q.activation(gtv, vE[:, :], AF.Sigmoid))
    EV(lambda q: q.tensor_scalar(iscr, tokid[:, :], 2047, None, ALU.bitwise_and))
    v_lid = c_dv.v
    W("vector", dv, v_lid)
    EV(lambda q: q.tensor_copy(prs5[:, 0:640:5], tokid[:, :]))
    W("vector", ac, v_gtv)
    EV(lambda q: q.tensor_copy(prs5[:, 1:640:5], gtv.bitcast(I32)))
    EV(lambda q: q.tensor_copy(prs5[:, 2:640:5], pdesti))
    EV(lambda q: q.tensor_copy(prs5[:, 3:640:5], iscr))
    EV(lambda q: q.tensor_copy(iscr2, fscr[:, :]))
    EV(lambda q: q.tensor_scalar(msk.bitcast(I32), iscr2, 7, None, ALU.logical_shift_right))
    EV(lambda q: q.tensor_scalar(iscr2, iscr2, 127, None, ALU.bitwise_and))
    EV(lambda q: q.tensor_scalar(pscan.bitcast(I32), iscr2, 2, None, ALU.logical_shift_left))
    EV(lambda q: q.tensor_add(iscr2, iscr2, pscan.bitcast(I32)))
    EV(lambda q: q.tensor_add(iscr2, iscr2, msk.bitcast(I32)))
    EV(lambda q: q.tensor_copy(fscr2, iscr2))
    EV(lambda q: q.tensor_scalar(fscr2, fscr2, jbase[:, 0:1], None, ALU.add))
    EV(lambda q: q.tensor_copy(iscr2, fscr2))
    EV(lambda q: q.memset(msk.bitcast(I32), BIG))
    EV(lambda q: q.tensor_copy(fscr2, pdesti))
    EV(lambda q: q.tensor_scalar(fscr2, fscr2, 1.0e8, None, ALU.is_lt))
    EV(lambda q: q.tensor_copy(iscr, fscr2))
    EV(lambda q: q.copy_predicated(msk.bitcast(I32), iscr, iscr2))
    v_quad = EV(lambda q: q.tensor_copy(prs5[:, 4:640:5], msk.bitcast(I32)))
    W("gpsimd", dv, v_quad)
    W("gpsimd", g_pref.s, V_PREFILL)
    def _mk_scat(tg):
        def f(q):
            q.indirect_dma_start(
                out=idxg[:, :],
                out_offset=IndirectOffsetOnAxis(desti[:, tg:tg + 1], axis=0),
                in_=prs5[:, 5 * tg:5 * (tg + 1)], in_offset=None,
                bounds_check=bc4095[0], oob_is_err=False).then_inc(g_sq1.s, 16)
        return f
    bc4095 = [None]
    def _set_regs(q):
        bc4095[0] = q.to_reg(KSEL - 1)
        bcNPR[0] = q.to_reg(NPR - 1)
        bcNTOK[0] = q.to_reg(NTOK - 1)
        bcT[0] = q.to_reg(T - 1)
        return q.engine_nop()
    bcNPR = [None]; bcNTOK = [None]; bcT = [None]
    em("gpsimd", lambda q: _set_regs(q))
    for tg in range(128):
        em("gpsimd", _mk_scat(tg))
        g_sq1.c.add(16)
    W("gpsimd", g_sq1.s, g_sq1.c.v)
    v_idxl = DMA("gpsimd", g_idxl, lambda q: q.dma_start(
        out=idxg_sb[:, :].rearrange("p (w c) -> p w c", c=5),
        in_=idxg.rearrange("(p w) c -> p w c", p=128)))
    W("vector", g_idxl.s, v_idxl)
    for g in range(32):
        EV(lambda q, g=g: q.tensor_copy(fscr[:, g:g + 1], idxg_sb[:, 5 * g:5 * g + 1]))
    W("vector", dv, c_dv.v)
    v = EV(lambda q: q.tensor_scalar(fscr2[:, 0:32], fscr[:, 0:32], NTOK - 0.5,
                                     None, ALU.is_lt))
    W("vector", dv, v)
    v_gm = EV(lambda q: q.tensor_mul(
        gmask[:, :], fscr2[:, 0:32],
        idxg_sb.bitcast(F32)[:, :].rearrange("p (w c) -> p w c", c=5)[:, :, 1]))
    W("gpsimd", dv, v_gm)
    def _mk_idscat(g):
        def f(q):
            q.indirect_dma_start(
                out=a2ais[:, :],
                out_offset=IndirectOffsetOnAxis(idxg_sb[:, 5 * g + 4:5 * g + 5], axis=0),
                in_=idxg_sb[:, 5 * g + 3:5 * g + 4], in_offset=None,
                bounds_check=bcNPR[0], oob_is_err=False).then_inc(g_sq2.s, 16)
        return f
    for g in range(32):
        W("gpsimd", g_sq2.s, g_sq2.c.v)
        em("gpsimd", _mk_idscat(g))
        g_sq2.c.add(16)
    v_idsf = g_sq2.c.v

    if stop_after < 4:
        _final_stub()
        return _finish()
    xg_bufs = [a2f[:, 0:4096], xnbf.bitcast(F32)[:, 0:4096]]
    exT_bufs = [arena2[:, 8192:12288], xnbf[:, 8192:12288]]
    xnr = a2f[:, 6656:7680]
    sqs = xnr
    xnrb4 = xnbf[:, 12288:16384]

    v_w1l = [0, 0]
    v_w2l = [0, 0]
    v_epi_dv = 0
    v_epi_half0 = 0
    v_tev = [0, 0]
    v_ext = [0, 0]
    v_xg_read = [0, 0]
    v_T_done = [0] * 9
    v_xnrb = {}
    v_gath_v = {}

    def emit_gather(rcn):
        buf = rcn % 2
        xgv = xg_bufs[buf].rearrange("p (b f) -> p b f", b=4)
        if rcn == 0:
            W("gpsimd", pe, V_P3_PE)
            W("gpsimd", g_sq2.s, v_idsf)
        elif rcn == 1:
            W("gpsimd", dv, V_Z)
        else:
            W("gpsimd", dv, v_xg_read[buf])
        for gg in range(4):
            g = 4 * rcn + gg
            DMA("gpsimd", g_gath, lambda q, g=g, gg=gg, xgv=xgv: q.indirect_dma_start(
                out=xgv[:, gg, :], out_offset=None, in_=xfull[:, :],
                in_offset=IndirectOffsetOnAxis(idxg_sb[:, 5 * g:5 * g + 1], axis=0),
                bounds_check=bcNTOK[0], oob_is_err=False))
        v_gath_v[rcn] = g_gath.c.v

    mcol4 = dts1[:, 0:4]
    s2col4 = dts1[:, 4:8]
    vcol4 = dts1[:, 8:12]
    ncol4 = dts1[:, 12:16]
    rcol4 = dts1[:, 16:20]

    def emit_ln(rcn):
        buf = rcn % 2
        v_gath = v_gath_v[rcn]
        xgv = xg_bufs[buf].rearrange("p (b f) -> p b f", b=4)
        if rcn >= 1:
            W("vector", pe, v_T_done[rcn - 1])
        W("vector", g_gath.s, v_gath)
        EV(lambda q, xgv=xgv: q.tensor_reduce(mcol4, xgv, axis=AX.X, op=ALU.add))
        for gg in range(4):
            EV(lambda q, gg=gg, xgv=xgv: q.scalar_tensor_tensor(
                sqs, xgv[:, gg, :], 1.0, xgv[:, gg, :], ALU.mult, ALU.mult,
                accum_out=s2col4[:, gg:gg + 1]))
        EV(lambda q: q.tensor_scalar(mcol4, mcol4, 1.0 / DM, None, ALU.mult))
        EV(lambda q: q.tensor_scalar(s2col4, s2col4, 1.0 / DM, None, ALU.mult))
        EV(lambda q: q.tensor_mul(vcol4, mcol4, mcol4))
        v = EV(lambda q: q.tensor_sub(vcol4, s2col4, vcol4))
        W("scalar", dv, v)
        if rcn == 0:
            W("scalar", pe, V_P3_PE)
        EA(lambda q: q.activation(ncol4, vcol4, AF.Ln, bias=epst[:, 0:1]))
        v_r0 = EA(lambda q: q.activation(rcol4, ncol4, AF.Exp, scale=mhalf[:, 0:1]))
        W("vector", ac, v_r0)
        EV(lambda q: q.tensor_scalar(vcol4, vcol4, LN_EPS, None, ALU.add))
        EV(lambda q: q.tensor_mul(ncol4, rcol4, rcol4))
        EV(lambda q: q.tensor_mul(ncol4, ncol4, vcol4))
        EV(lambda q: q.tensor_scalar(ncol4, ncol4, -0.5, 1.5, ALU.mult, ALU.add))
        EV(lambda q: q.tensor_mul(rcol4, rcol4, ncol4))
        v_last_read = 0
        for gg in range(4):
            v_last_read = EV(lambda q, gg=gg, xgv=xgv: q.tensor_scalar(
                xnr, xgv[:, gg, :], mcol4[:, gg:gg + 1], None, ALU.subtract))
            v = EV(lambda q, gg=gg: q.tensor_scalar(
                xnrb4[:, 1024 * gg:1024 * (gg + 1)], xnr, rcol4[:, gg:gg + 1],
                None, ALU.mult))
            v_xnrb[(rcn, gg)] = v
        v_xg_read[buf] = v_last_read

    def emit_T(rcn):
        buf = rcn % 2
        for gg in range(4):
            W("tensor", dv, v_xnrb[(rcn, gg)])
            for half in range(2):
                W("tensor", ac, v_tev[0])
                for k4 in range(4):
                    k = 4 * half + k4
                    EP(lambda q, gg=gg, k=k, k4=k4: q.transpose(
                        pT[:, 128 * k4:128 * (k4 + 1)],
                        xnrb4[:, 1024 * gg + 128 * k:1024 * gg + 128 * (k + 1)],
                        identbf[:, :]))
                v_tp = c_pe.v
                W("scalar", pe, v_tp)
                v_tev[0] = EA(lambda q, gg=gg, half=half, buf=buf: q.copy(
                    exT_bufs[buf].rearrange("p (k c) -> p k c", c=512)
                        [:, 4 * half:4 * half + 4, 128 * gg:128 * (gg + 1)],
                    pT[:, :].rearrange("p (k f) -> p k f", f=128)))
        v_ext[buf] = c_ac.v
        v_T_done[rcn] = c_pe.v

    v_epi_by_rc = {}

    def emit_scatter(rcn):
        obase = 4096 if rcn % 2 == 0 else 12288
        if rcn == 0:
            W("gpsimd", g_az.s, V_AZERO)
        for sub in range(4):
            g = 4 * rcn + sub
            osb = arena3[:, obase + 1024 * sub:obase + 1024 * (sub + 1)]
            W("gpsimd", dv, v_epi_by_rc[rcn])
            em("gpsimd", lambda q, g=g, osb=osb, rcn=rcn: q.indirect_dma_start(
                out=a2as[:, :],
                out_offset=IndirectOffsetOnAxis(idxg_sb[:, 5 * g + 2:5 * g + 3], axis=0),
                in_=osb, in_offset=None,
                bounds_check=bcNPR[0], oob_is_err=False).then_inc(g_osb[rcn].s, 16))
            g_osb[rcn].c.add(16)

    emit_gather(0)
    emit_ln(0)
    emit_T(0)
    for rc in range(8):
        if rc < 7:
            emit_gather(rc + 1)
            emit_ln(rc + 1)
        if rc >= 1:
            emit_scatter(rc - 1)
        exT = exT_bufs[rc % 2]
        v_gelu_pb = [0, 0]
        for ffc in range(32):
            hbuf = ffc % 2
            w1c = arena3[:, 1024 * hbuf:1024 * (hbuf + 1)]
            if rc == 0 and ffc < 2:
                W("sync", g_stgst.s, V_WBF)
                W("sync", pe, V_P3_PE)
            else:
                W("sync", pe, v_w1l[hbuf])
            vl = DMA("sync", g_w1[hbuf], lambda q, ffc=ffc, w1c=w1c: q.dma_start(
                out=w1c.rearrange("p (k f) -> p k f", k=8),
                in_=w1bfd[:, 128 * ffc:128 * (ffc + 1)].rearrange("(k p) f -> p k f", p=128)))
            pb = [p0, p1][ffc % 2]
            v_mm1 = MACC(pb[:, :],
                         [(w1c[:, 128 * k:128 * (k + 1)], exT[:, 512 * k:512 * (k + 1)])
                          for k in range(8)],
                         waits=[(g_w1[hbuf].s, vl), (ac, v_ext[rc % 2]),
                                (ac, v_gelu_pb[ffc % 2]),
                                (dv, v_epi_dv if ffc < 2 else 0)])
            v_w1l[hbuf] = v_mm1
            W("scalar", pe, v_mm1)
            v_gelu_pb[ffc % 2] = EA(lambda q, ffc=ffc, pb=pb: q.activation(
                arena1[:, 512 * ffc:512 * (ffc + 1)], pb[:, :], AF.Gelu,
                bias=b1m[:, ffc:ffc + 1]))
        v_gelu_all = c_ac.v
        if rc < 7:
            emit_T(rc + 1)
        for hh2 in range(2):
            for ffc in range(32):
                wbuf = ffc % 2
                w2c = arena3[:, 2048 + 1024 * wbuf:2048 + 1024 * wbuf + 512]
                W("sync", pe, v_w2l[wbuf])
                vl = DMA("sync", g_w2[wbuf], lambda q, ffc=ffc, hh2=hh2, w2c=w2c: q.dma_start(
                    out=w2c, in_=w2bfd[128 * ffc:128 * (ffc + 1), 512 * hh2:512 * (hh2 + 1)]))
                for sub in range(4):
                    pbs = [p0, p1, p2, p3][sub]
                    lastk = (ffc == 31)
                    wts = [(g_w2[wbuf].s, vl)]
                    if ffc == 0:
                        wts.append((ac, v_gelu_all))
                        if hh2 == 1:
                            wts.append((dv, v_epi_half0))
                        else:
                            wts.append((dv, v_epi_dv))
                    for sw, vw in wts:
                        W("tensor", sw, vw)
                    certify = (sub == 3) or lastk
                    em("tensor", lambda q, ffc=ffc, sub=sub, pbs=pbs, w2c=w2c,
                       lastk=lastk, certify=certify:
                       (q.matmul(pbs[:, :],
                                 arena1[:, 512 * ffc + 128 * sub:512 * ffc + 128 * (sub + 1)],
                                 w2c, start=(ffc == 0), stop=lastk).then_inc(pe, 1)
                        if certify else
                        q.matmul(pbs[:, :],
                                 arena1[:, 512 * ffc + 128 * sub:512 * ffc + 128 * (sub + 1)],
                                 w2c, start=(ffc == 0), stop=lastk)))
                    if certify:
                        c_pe.add(1)
                v_w2l[wbuf] = c_pe.v
            v_mm2h = c_pe.v
            obase = 4096 if rc % 2 == 0 else 12288
            for sub in range(4):
                pbs = [p0, p1, p2, p3][sub]
                g = 4 * rc + sub
                oev = a3f[:, 4096 + 512 * sub:4096 + 512 * (sub + 1)][:, 0:512]
                osb = arena3[:, obase + 1024 * sub:obase + 1024 * (sub + 1)]
                W("vector", pe, v_mm2h)
                if rc > 1 and hh2 == 0:
                    W("vector", g_osb[rc - 2].s, 64)
                EV(lambda q, pbs=pbs, hh2=hh2, oev=oev: q.tensor_add(
                    oev, pbs[:, :], b2bc[:, 512 * hh2:512 * (hh2 + 1)]))
                v_epi_dv = EV(lambda q, g=g, hh2=hh2, osb=osb, oev=oev: q.tensor_scalar(
                    osb[:, 512 * hh2:512 * (hh2 + 1)], oev, gmask[:, g:g + 1],
                    None, ALU.mult))
            if hh2 == 0:
                v_epi_half0 = v_epi_dv
        v_epi_by_rc[rc] = v_epi_dv
    emit_scatter(7)
    if stop_after < 5:
        _final_stub()
        return _finish()
    for rc in range(8):
        W("gpsimd", g_osb[rc].s, 64)
    em("gpsimd", lambda q: q.collective_compute(
        "AllToAll", ALU.bypass, [list(range(NCORE))],
        ins=[a2as.ap()], outs=[a2ar.ap()]).then_inc(cc3, 1))
    em("gpsimd", lambda q: q.collective_compute(
        "AllToAll", ALU.bypass, [list(range(NCORE))],
        ins=[a2ais.ap()], outs=[a2air.ap()]).then_inc(cc4, 1))
    W("gpsimd", cc3, 1)
    W("gpsimd", cc4, 1)
    if stop_after < 4.3:
        _final_stub()
        return _finish()
    v_wol = DMA("gpsimd", g_wo, lambda q: q.dma_start(
        out=arena2[:, 0:8192].rearrange("p (k e) -> p k e", k=8),
        in_=w_outT.rearrange("(k p) e -> p k e", p=128)))
    W("gpsimd", g_gz.s, V_GZERO)
    v_ridl = DMA("gpsimd", g_mr, lambda q: q.dma_start(
        out=ridl[:, :].rearrange("p (j w) -> p j w", w=5),
        in_=a2air.rearrange("(j p w) o -> p j (w o)", p=128, w=5)))
    W("gpsimd", g_mr.s, v_ridl)
    for j in range(8):
        vd = DMA("gpsimd", g_mr, lambda q, j=j: q.dma_start(
            out=arena3[:, 0:5120].rearrange("p (c d) -> p c d", c=5),
            in_=a2ar[CAP * j:CAP * (j + 1), :].rearrange("(c p) d -> p c d", p=128)))
        W("gpsimd", g_mr.s, vd)
        for c5 in range(5):
            m = 5 * j + c5
            W("gpsimd", g_ma[j].s, g_ma[j].c.v)
            em("gpsimd", lambda q, j=j, c5=c5, m=m: q.indirect_dma_start(
                out=gdense[:, :],
                out_offset=IndirectOffsetOnAxis(ridl[:, m:m + 1], axis=0),
                in_=arena3[:, 1024 * c5:1024 * c5 + 1024], in_offset=None,
                bounds_check=bcT[0], oob_is_err=False,
                compute_op=ALU.add).then_inc(g_ma[j].s, 16))
            g_ma[j].c.add(16)
        W("gpsimd", g_ma[j].s, 80)
    if stop_after < 4.7:
        _final_stub()
        return _finish()
    gsil = a1f[:, 0:1024]
    xstile = a1f[:, 1024:2048]
    outtile = a1f[:, 2048:3072]
    v_sdbg_prev = [0]
    v_gl = 0
    v_out_ev = 0
    n_tt = 16 if stop_after >= 4.9 else (4 if stop_after >= 4.85 else 1)
    for tt in range(n_tt):
        if tt == 0:
            for jj in range(8):
                W("sync", g_ma[jj].s, 80)
        else:
            W("sync", dv, v_out_ev)
        vg = DMA("sync", g_gt, lambda q, tt=tt: q.dma_start(
            out=gtile[:, :], in_=gdense[128 * tt:128 * (tt + 1), :]))
        vx = DMA("sync", g_xs, lambda q, tt=tt: q.dma_start(
            out=xstile, in_=xs[128 * tt:128 * (tt + 1), :]))
        W("scalar", g_gt.s, vg)
        if debug:
            W("scalar", g_dbq.s, g_dbq.c.v)
        v_sil = EA(lambda q: q.activation(gsil, gtile[:, :], AF.Silu))
        for half in range(2):
            wts = []
            if half == 0:
                if tt == 0:
                    wts += [(g_wo.s, v_wol), (dv, V_Z)]
                else:
                    wts.append((dv, v_out_ev))
                if debug and tt > 0:
                    wts.append((ac, v_sdbg_prev[0]))
            v_mmo = MACC(pW[:, 512 * half:512 * (half + 1)],
                         [(ubf[:, 2048 * k + 128 * tt:2048 * k + 128 * (tt + 1)],
                           arena2[:, 1024 * k + 512 * half:1024 * k + 512 * (half + 1)])
                          for k in range(8)],
                         waits=wts)
        v_mmo2 = c_pe.v
        if debug:
            W("scalar", pe, v_mmo2)
            v_sdbg = EA(lambda q: q.copy(a1f[:, 3072:4096], pW[:, :]))
            v_sdbg_prev[0] = v_sdbg
            W("sync", ac, v_sdbg)
            DMA("sync", g_dbq, lambda q, tt=tt: q.dma_start(out=s_dbg[128 * tt:128 * (tt + 1), :],
                                                            in_=a1f[:, 3072:4096]))
        W("vector", pe, v_mmo2)
        W("vector", ac, v_sil)
        W("vector", g_xs.s, vx)
        W("vector", g_out.s, g_out.c.v)
        EV(lambda q: q.tensor_mul(outtile[:, 0:512], pW[:, 0:512], gsil[:, 0:512]))
        EV(lambda q: q.tensor_mul(outtile[:, 512:1024], pW[:, 512:1024], gsil[:, 512:1024]))
        EV(lambda q: q.tensor_add(outtile[:, 0:512], outtile[:, 0:512], xstile[:, 0:512]))
        v_out_ev = EV(lambda q: q.tensor_add(outtile[:, 512:1024], outtile[:, 512:1024], xstile[:, 512:1024]))
        W("sync", dv, v_out_ev)
        DMA("sync", g_out, lambda q, tt=tt: q.dma_start(out=out_t[128 * tt:128 * (tt + 1), :],
                                                        in_=outtile))
        if debug:
            W("sync", ac, v_sil)
            DMA("sync", g_dbq, lambda q, tt=tt: q.dma_start(out=g_dbg[128 * tt:128 * (tt + 1), :],
                                                            in_=gsil))
    if debug:
        W("sync", cc1, 1)
        DMA("sync", g_dbq, lambda q: q.dma_start(out=sc_dbg[:, :], in_=sc_in[:, :]))
        W("vector", dv, c_dv.v)
        v = EV(lambda q: q.tensor_copy(fscr[:, :], vE[:, :]))
        W("sync", dv, v)
        DMA("sync", g_dbq, lambda q: q.dma_start(out=ve_dbg[:, :], in_=fscr[:, :]))
    if n_tt < 16 or stop_after < 5:
        _final_stub()
        return _finish()
    W("sync", g_out.s, g_out.c.v)
    if debug:
        W("sync", g_dbq.s, g_dbq.c.v)
    return _finish()


def _host_inputs(inputs, debug=False):
    bf = ml_dtypes.bfloat16
    x = np.ascontiguousarray(np.asarray(inputs["L0_attn_x"], np.float32).reshape(NTOK, DM))
    W_in = np.asarray(inputs["L0_attn_W_in"], np.float32)
    W_state = np.asarray(inputs["L0_attn_W_state"], np.float32)
    W_dt = np.asarray(inputs["L0_attn_W_dt"], np.float32)
    b_dtv = np.asarray(inputs["L0_attn_b_dt"], np.float32)
    A_log = np.asarray(inputs["L0_attn_A_log"], np.float32)
    W_C = np.asarray(inputs["L0_attn_W_C"], np.float32)
    Dv = np.asarray(inputs["L0_attn_D"], np.float32)
    W_out = np.asarray(inputs["L0_attn_W_out"], np.float32)
    Wg = np.asarray(inputs["L0_ffn_Wg"], np.float32)
    W1 = np.asarray(inputs["L0_ffn_W1"], np.float32)
    b1 = np.asarray(inputs["L0_ffn_b1"], np.float32)
    W2 = np.asarray(inputs["L0_ffn_W2"], np.float32)
    b2 = np.asarray(inputs["L0_ffn_b2"], np.float32)

    com = dict(
        xfull=x,
        w_inT=np.ascontiguousarray(W_in.T),
        w_stT=np.ascontiguousarray(W_state.T),
        w_dtT=np.ascontiguousarray(W_dt.T),
        w_cT=np.ascontiguousarray(W_C.T),
        w_outT=np.ascontiguousarray(W_out.T),
        wg_in=np.ascontiguousarray(Wg),
        b_dt_in=np.ascontiguousarray(b_dtv.reshape(DS, 1)),
        a_log_in=np.ascontiguousarray(A_log.reshape(DS, 1)),
        dmat_in=np.ascontiguousarray(Dv.reshape(8, 128).T),
        identbf_in=np.eye(128, dtype=bf),
        identf_in=np.eye(128, dtype=np.float32),
        onescol_in=np.ones((128, 1), np.float32),
        onesrow_in=np.ones((1, 128), np.float32),
        onestt_in=np.ones((128, 128), np.float32),
        ltstrict_in=np.ascontiguousarray(
            (np.arange(128)[:, None] < np.arange(128)[None, :]).astype(np.float32)),
        lt16_in=np.ascontiguousarray(
            (np.arange(128)[:, None] < (16 * (np.arange(128)[None, :] // 16)))
            .astype(np.float32)),
        tokid_in=np.ascontiguousarray(
            (np.arange(128)[:, None] * 128 + np.arange(128)[None, :]).astype(np.int32)),
        jbase_in=np.ascontiguousarray(
            ((np.arange(128) // 16) * CAP).astype(np.float32).reshape(128, 1)),
    )
    in_maps = []
    for c in range(NCORE):
        m = dict(com)
        sl = slice(T * c, T * (c + 1))
        m["xs"] = np.ascontiguousarray(x[sl])
        m["xts"] = np.ascontiguousarray(x[sl].T)
        m["w1e"] = np.ascontiguousarray(W1[c])
        m["w2e"] = np.ascontiguousarray(W2[c])
        m["b1m_in"] = np.ascontiguousarray(b1[c].reshape(32, 128).T)
        m["b2bc_in"] = np.ascontiguousarray(np.broadcast_to(b2[c], (128, DM)))
        sv = np.zeros((1, 8), np.float32)
        for j in range(NCORE):
            if j // 4 == c // 4 and j < c:
                sv[0, j] = 1.0
        m["selv_in"] = sv
        em_ = np.zeros((128, 1024), np.float32)
        em_[:, c::8] = 1.0
        m["emask_in"] = em_
        in_maps.append(m)
    return in_maps


def kernel(**inputs):
    in_maps = _host_inputs(inputs)
    with allow_ici_all_to_all():
        nc = build_program(debug=False)
        res = run_bass_kernel_spmd(nc, in_maps, core_ids=list(range(NCORE)))
    out = np.concatenate([np.asarray(res.results[c]["out"]) for c in range(NCORE)], 0)
    return out.reshape(2, 8192, DM)


# version 07
# speedup vs baseline: 1.3611x; 1.1128x over previous
import numpy as np
import ml_dtypes
import concourse.bass as bass
import concourse.mybir as mybir
from concourse.bass import IndirectOffsetOnAxis
from concourse.bass_utils import run_bass_kernel_spmd
from concourse.aot_env import allow_ici_all_to_all

F32 = mybir.dt.float32
BF16 = mybir.dt.bfloat16
I32 = mybir.dt.int32
AF = mybir.ActivationFunctionType
ALU = mybir.AluOpType
AX = mybir.AxisListType

NCORE = 8
T = 2048
NTOK = 16384
DM = 1024
DS = 128
FF = 4096
KSEL = 4096
CAP = 640
NPR = NCORE * CAP
BIG = 1 << 28
NBIS = 42
RATE_EPS = 1e-4
LN_EPS = 1e-5


class Ctr:
    __slots__ = ("v",)
    def __init__(self):
        self.v = 0
    def add(self, n=1):
        self.v += n
        return self.v


def build_program(debug=False, stop_after=5):
    nc = bass.Bass("TRN2", target_bir_lowering=False, debug=False)
    D = lambda n, s, dt, k="ExternalInput": nc.dram_tensor(n, s, dt, kind=k)

    xts = D("xts", [DM, T], F32)
    xs = D("xs", [T, DM], F32)
    xfull = D("xfull", [NTOK, DM], F32)
    w_inT = D("w_inT", [DM, 2 * DM], F32)
    w_stT = D("w_stT", [DM, DS], F32)
    w_dtT = D("w_dtT", [DM, DS], F32)
    w_cT = D("w_cT", [DS, DM], F32)
    w_outT = D("w_outT", [DM, DM], F32)
    wg_in = D("wg_in", [DM, 8], F32)
    b_dt_in = D("b_dt_in", [DS, 1], F32)
    a_log_in = D("a_log_in", [DS, 1], F32)
    dmat_in = D("dmat_in", [128, 8], F32)
    w1e = D("w1e", [DM, FF], F32)
    w2e = D("w2e", [FF, DM], F32)
    b1m_in = D("b1m_in", [128, 32], F32)
    b2bc_in = D("b2bc_in", [128, DM], F32)
    identbf_in = D("identbf_in", [128, 128], BF16)
    identf_in = D("identf_in", [128, 128], F32)
    onescol_in = D("onescol_in", [128, 1], F32)
    onesrow_in = D("onesrow_in", [1, 128], F32)
    onestt_in = D("onestt_in", [128, 128], F32)
    ltstrict_in = D("ltstrict_in", [128, 128], F32)
    lt16_in = D("lt16_in", [128, 128], F32)
    tokid_in = D("tokid_in", [128, 128], I32)
    jbase_in = D("jbase_in", [128, 1], F32)
    selv_in = D("selv_in", [1, 8], F32)
    emask_in = D("emask_in", [128, 1024], F32)

    sc_in = nc.dram_tensor("sc_in", [T, 8], F32)
    sc_out = nc.dram_tensor("sc_out", [NTOK, 8], F32)
    sum_in = nc.dram_tensor("sum_in", [2, 128], F32)
    sum_out = nc.dram_tensor("sum_out", [16, 128], F32)
    w1bfd = nc.dram_tensor("w1bfd", [FF, DM], BF16)
    w2bfd = nc.dram_tensor("w2bfd", [2 * FF, DM // 2], BF16)
    idxg = nc.dram_tensor("idxg", [KSEL, 5], I32)
    a2as = nc.dram_tensor("a2as", [NPR, DM], BF16)
    a2ar = nc.dram_tensor("a2ar", [NPR, DM], BF16)
    a2ais = nc.dram_tensor("a2ais", [NPR, 1], I32)
    a2air = nc.dram_tensor("a2air", [NPR, 1], I32)
    gdense = nc.dram_tensor("gdense", [T, DM], BF16)
    out_t = D("out", [T, DM], F32, "ExternalOutput")
    if debug:
        hh_dbg = D("hh_dbg", [128, T], F32, "ExternalOutput")
        thr_dbg = D("thr_dbg", [1, 8], F32, "ExternalOutput")
        s_dbg = D("s_dbg", [T, DM], F32, "ExternalOutput")
        g_dbg = D("g_dbg", [T, DM], F32, "ExternalOutput")
        sc_dbg = D("sc_dbg", [T, 8], F32, "ExternalOutput")
        ve_dbg = D("ve_dbg", [128, 128], F32, "ExternalOutput")

    from contextlib import ExitStack
    ctx = ExitStack()
    sb = lambda n, s, dt: ctx.enter_context(nc.sbuf_tensor(n, s, dt))
    ps = lambda n, s, dt: ctx.enter_context(nc.psum_tensor(n, s, dt))
    sem = lambda n: ctx.enter_context(nc.semaphore(n))

    xnbf = sb("xnbf", [128, 8 * T], BF16)
    ubf = sb("ubf", [128, 8 * T], BF16)
    arena1 = sb("arena1", [128, 8 * T], BF16)
    arena2 = sb("arena2", [128, 8 * T], BF16)
    arena3 = sb("arena3", [128, 8 * T], BF16)
    a1f = arena1.bitcast(F32)
    a2f = arena2.bitcast(F32)
    a3f = arena3.bitcast(F32)

    wstage = sb("wstage", [128, 4096], BF16)
    wg_sb = sb("wg_sb", [128, 64], F32)
    wstb = sb("wstb", [128, 1024], BF16)
    wdtb = sb("wdtb", [128, 1024], BF16)
    wctb = sb("wctb", [128, 1024], BF16)
    identbf = sb("identbf", [128, 128], BF16)
    identf = sb("identf", [128, 128], F32)
    onescol = sb("onescol", [128, 1], F32)
    onesrow = sb("onesrow", [1, 128], F32)
    onestt = sb("onestt", [128, 128], F32)
    ltstrict = sb("ltstrict", [128, 128], F32)
    lt16 = sb("lt16", [128, 128], F32)
    tokid = sb("tokid", [128, 128], I32)
    jbase = sb("jbase", [128, 1], F32)
    selv = sb("selv", [1, 8], F32)
    b_dt = sb("b_dt", [128, 1], F32)
    a_log = sb("a_log", [128, 1], F32)
    dmat = sb("dmat", [128, 8], F32)
    b1m = sb("b1m", [128, 32], F32)
    b2bc = sb("b2bc", [128, DM], F32)
    negrate = sb("negrate", [128, 1], F32)
    rates = sb("rates", [128, 4], F32)
    epst = sb("epst", [128, 1], F32)
    mhalf = sb("mhalf", [128, 1], F32)
    meanr = sb("meanr", [1, 512], F32)
    rstdr = sb("rstdr", [1, 512], F32)
    stat1 = sb("stat1", [1, 512], F32)
    stat3 = sb("stat3", [1, 512], F32)
    scsb = sb("scsb", [8, 512], F32)
    sctm = sb("sctm", [128, 32], F32)
    dts1 = sb("dts1", [128, 512], F32)
    dts2 = sb("dts2", [128, 512], F32)
    dts3 = sb("dts3", [128, 512], F32)
    dtsum = sb("dtsum", [128, 4], F32)
    hin = sb("hin", [1, 128], F32)
    hf1 = sb("hf1", [1, 128], F32)
    hf2 = sb("hf2", [1, 128], F32)
    h0col = sb("h0col", [128, 1], F32)
    vE = sb("vE", [128, 128], F32)
    bis = sb("bis", [1, 8], F32)
    thrcol = sb("thrcol", [128, 1], F32)
    cntcol = sb("cntcol", [128, 1], F32)
    exc = sb("exc", [128, 1], F32)
    excm1 = sb("excm1", [128, 1], F32)
    bnd = sb("bnd", [128, 1], F32)
    idxg_sb = sb("idxg_sb", [128, 160], I32)
    senti = sb("senti", [128, 40], I32)
    gmask = sb("gmask", [128, 32], F32)
    mcol = sb("mcol", [128, 1], F32)
    s2col = sb("s2col", [128, 1], F32)
    vcol = sb("vcol", [128, 1], F32)
    rcol = sb("rcol", [128, 1], F32)
    ncol = sb("ncol", [128, 1], F32)
    gtile = sb("gtile", [128, DM], BF16)
    recvd = sb("recvd", [128, DM], BF16)
    zerot = sb("zerot", [128, DM], BF16)
    ridl = sb("ridl", [128, 40], I32)

    p0 = ps("p0", [128, 512], F32)
    p1 = ps("p1", [128, 512], F32)
    p2 = ps("p2", [128, 512], F32)
    p3 = ps("p3", [128, 512], F32)
    pW = ps("pW", [128, 1024], F32)
    pS = ps("pS", [8, 512], F32)
    pT = ps("pT", [128, 512], BF16)

    pe = sem("pe"); c_pe = Ctr()
    dv = sem("dv"); c_dv = Ctr()
    ac = sem("ac"); c_ac = Ctr()
    cc1 = sem("cc1"); cc2 = sem("cc2"); cc3 = sem("cc3"); cc4 = sem("cc4")
    junk = sem("junk")

    class Q:
        def __init__(self, name, chained=False):
            self.s = sem(name)
            self.c = Ctr()
            self.chained = chained

    prog = {k: [] for k in ("sync", "scalar", "vector", "tensor", "gpsimd")}
    def em(engine, fn):
        prog[engine].append(fn)
    def W(engine, s, v):
        if v > 0 and not (engine == "tensor" and s is pe):
            em(engine, lambda q, s=s, v=v: q.wait_ge(s, v))
    def EP(fn):
        em("tensor", lambda q, fn=fn: fn(q).then_inc(pe, 1))
        return c_pe.add(1)
    def EV(fn):
        if c_dv.v > 0:
            em("vector", lambda q, v=c_dv.v: q.wait_ge(dv, v))
        em("vector", lambda q, fn=fn: fn(q).then_inc(dv, 1))
        return c_dv.add(1)
    def EA(fn):
        if c_ac.v > 0:
            em("scalar", lambda q, v=c_ac.v: q.wait_ge(ac, v))
        em("scalar", lambda q, fn=fn: fn(q).then_inc(ac, 1))
        return c_ac.add(1)
    def DMA(engine, grp, fn):
        if grp.chained and grp.c.v > 0:
            em(engine, lambda q, s=grp.s, v=grp.c.v: q.wait_ge(s, v))
        em(engine, lambda q, fn=fn, s=grp.s: fn(q).then_inc(s, 16))
        return grp.c.add(16)
    g_const = Q("g_const")
    g_wssm = Q("g_wssm")
    g_stgld = Q("g_stgld", chained=True)
    g_stgst = Q("g_stgst", chained=True)
    g_pref = Q("g_pref")
    g_gz = Q("g_gz")
    g_az = Q("g_az")
    g_x = Q("g_x", chained=True)
    g_sc = Q("g_sc")
    g_sum = Q("g_sum")
    g_sumg = Q("g_sumg")
    g_em = Q("g_em")
    g_scg = Q("g_scg")
    g_idxl = Q("g_idxl")
    g_sq1 = Q("g_sq1")
    g_sq2 = Q("g_sq2")
    g_osb = [Q(f"g_osb{i}") for i in range(8)]
    g_ma = [Q(f"g_ma{i}") for i in range(8)]
    g_gath = Q("g_gath", chained=True)
    g_w1 = [Q("g_w1a", chained=True), Q("g_w1b", chained=True)]
    g_w2 = [Q("g_w2a", chained=True), Q("g_w2b", chained=True)]
    g_wo = Q("g_wo")
    g_gt = Q("g_gt", chained=True)
    g_xs = Q("g_xs", chained=True)
    g_mr = Q("g_mr", chained=True)
    g_out = Q("g_out", chained=True)
    g_dbq = Q("g_dbq", chained=True)
    def MACC(psv, pairs, waits=()):
        for sw, vw in waits:
            W("tensor", sw, vw)
        n = len(pairs)
        for i, (lf, rf) in enumerate(pairs):
            if i < n - 1:
                em("tensor", lambda q, lf=lf, rf=rf, st=(i == 0), psv=psv:
                   q.matmul(psv, lf, rf, start=st, stop=False))
            else:
                em("tensor", lambda q, lf=lf, rf=rf, st=(i == 0), psv=psv:
                   q.matmul(psv, lf, rf, start=st, stop=True).then_inc(pe, 1))
        return c_pe.add(1)

    def _final_stub():
        W("sync", dv, c_dv.v)
        W("sync", ac, c_ac.v)
        W("sync", pe, c_pe.v)
        for tt in range(16):
            for hb in range(2):
                DMA("sync", g_out, lambda q, tt=tt, hb=hb: q.dma_start(
                    out=out_t[128 * tt:128 * (tt + 1), :].bitcast(BF16)[:, 1024 * hb:1024 * (hb + 1)],
                    in_=zerot[:, :]))
        W("sync", g_out.s, g_out.c.v)

    def _finish():
        with nc.Block() as block:
            @block.sync
            def _(q):
                for f in prog["sync"]:
                    f(q)
            @block.scalar
            def _(q):
                for f in prog["scalar"]:
                    f(q)
            @block.vector
            def _(q):
                for f in prog["vector"]:
                    f(q)
            @block.tensor
            def _(q):
                for f in prog["tensor"]:
                    f(q)
            @block.gpsimd
            def _(q):
                for f in prog["gpsimd"]:
                    f(q)
        ctx.close()
        return nc

    constpairs = [
        (wg_sb.rearrange("p (k e) -> p k e", e=8), wg_in.rearrange("(k p) e -> p k e", p=128)),
        (identbf, identbf_in[:, :]), (identf, identf_in[:, :]),
        (onescol, onescol_in[:, :]), (onesrow, onesrow_in[:, :]),
        (onestt, onestt_in[:, :]),
        (ltstrict, ltstrict_in[:, :]), (lt16, lt16_in[:, :]),
        (tokid, tokid_in[:, :]), (jbase, jbase_in[:, :]), (selv, selv_in[:, :]),
        (b_dt, b_dt_in[:, :]), (a_log, a_log_in[:, :]), (dmat, dmat_in[:, :]),
        (b1m, b1m_in[:, :]), (b2bc, b2bc_in[:, :]),
    ]
    for dst, src in constpairs:
        dap = dst if isinstance(dst, bass.AP) else dst[:, :]
        DMA("sync", g_const, lambda q, d=dap, s=src: q.dma_start(out=d, in_=s))
    V_CONST = g_const.c.v

    for dst, src in [
        (wstb.rearrange("p (uk n) -> p uk n", uk=8), w_stT.rearrange("(uk p) n -> p uk n", p=128)),
        (wdtb.rearrange("p (uk n) -> p uk n", uk=8), w_dtT.rearrange("(uk p) n -> p uk n", p=128)),
        (wctb[:, :], w_cT[:, :]),
    ]:
        DMA("gpsimd", g_wssm, lambda q, d=dst, s=src: q.dma_start(out=d, in_=s))
    DMA("gpsimd", g_wssm,
        lambda q: q.dma_start(out=arena2[:, :].rearrange("p (k u) -> p k u", k=8),
                              in_=w_inT.rearrange("(k p) u -> p k u", p=128)))
    V_WSSM = g_wssm.c.v

    V_SENTI = EV(lambda q: q.memset(senti[:, :], BIG))
    V_ZEROT = EV(lambda q: q.memset(zerot[:, :], 0))
    W("gpsimd", dv, V_ZEROT)
    for qd in range(4):
        DMA("gpsimd", g_pref, lambda q, qd=qd: q.dma_start(
            out=idxg[1024 * qd:1024 * (qd + 1), :].rearrange("(p w) c -> p w c", p=128),
            in_=senti[:, 0:40].rearrange("p (w c) -> p w c", c=5)))
    DMA("gpsimd", g_pref, lambda q: q.dma_start(
        out=a2ais[:, :].rearrange("(p w) c -> p w c", p=128),
        in_=senti[:, 0:40].rearrange("p (w c) -> p w c", c=1)))
    V_PREFILL = g_pref.c.v
    for tt in range(16):
        DMA("gpsimd", g_gz, lambda q, tt=tt: q.dma_start(
            out=gdense[128 * tt:128 * (tt + 1), :], in_=zerot[:, :]))
    V_GZERO = g_gz.c.v
    for m in range(40):
        DMA("gpsimd", g_az, lambda q, m=m: q.dma_start(
            out=a2as[128 * m:128 * (m + 1), :], in_=zerot[:, :]))
    V_AZERO = g_az.c.v

    wst_free = [0, 0]
    for i in range(16):
        h = i % 2
        r, cblk = divmod(i, 2)
        W("gpsimd", g_stgst.s, wst_free[h])
        v = DMA("gpsimd", g_stgld, lambda q, r=r, cblk=cblk, h=h: q.dma_start(
            out=wstage[:, h * 2048:h * 2048 + 2048],
            in_=w1e[128 * r:128 * (r + 1), 2048 * cblk:2048 * (cblk + 1)]))
        W("gpsimd", g_stgld.s, v)
        wst_free[h] = DMA("gpsimd", g_stgst, lambda q, r=r, cblk=cblk, h=h: q.dma_start(
            out=w1bfd.rearrange("(a p) (k f) -> p k a f", p=128, f=128)
                [:, r, 16 * cblk:16 * (cblk + 1), :],
            in_=wstage[:, h * 2048:h * 2048 + 2048].rearrange(
                "p (a f) -> p a f", f=128)))
    for i in range(32):
        h = i % 2
        W("gpsimd", g_stgst.s, wst_free[h])
        v = DMA("gpsimd", g_stgld, lambda q, i=i, h=h: q.dma_start(
            out=wstage[:, h * 2048:h * 2048 + 1024],
            in_=w2e[128 * i:128 * (i + 1), :]))
        W("gpsimd", g_stgld.s, v)
        wst_free[h] = DMA("gpsimd", g_stgst, lambda q, i=i, h=h: q.dma_start(
            out=w2bfd.rearrange("(h2 a p) c -> p h2 a c", h2=2, p=128)[:, :, i, :],
            in_=wstage[:, h * 2048:h * 2048 + 1024].rearrange(
                "p (h2 c) -> p h2 c", h2=2)))
    V_WBF = g_stgst.c.v

    W("scalar", g_const.s, V_CONST)
    EA(lambda q: q.activation(rates[:, 0:1], a_log[:, :], AF.Relu))
    v = EA(lambda q: q.activation(rates[:, 1:2], a_log[:, :], AF.Abs))
    W("scalar", ac, v)
    v = EA(lambda q: q.activation(rates[:, 2:3], rates[:, 1:2], AF.Exp, scale=-1.0))
    W("scalar", ac, v)
    v = EA(lambda q: q.activation(rates[:, 3:4], rates[:, 2:3], AF.Ln, bias=1.0))
    W("vector", ac, v)
    v = EV(lambda q: q.tensor_add(negrate[:, :], rates[:, 0:1], rates[:, 3:4]))
    W("vector", dv, v)
    V_NEGRATE = EV(lambda q: q.tensor_scalar(negrate[:, :], negrate[:, :],
                                             RATE_EPS, -1.0, ALU.add, ALU.mult))
    EV(lambda q: q.memset(epst[:, :], LN_EPS))
    V_EPST = EV(lambda q: q.memset(mhalf[:, :], -0.5))

    W("tensor", g_const.s, V_CONST)
    W("vector", g_const.s, V_CONST)
    W("gpsimd", g_const.s, V_CONST)

    decay = a1f[:, 0:2048]
    vvn = a1f[:, 2048:4096]
    hh = a1f[:, 4096:6144]
    sumg = a1f[0:1, 6144:8192]
    hcast = arena1[:, 0:2048]
    mb_sb = a3f[:, 0:512]
    rb_sb = a3f[:, 512:1024]
    a2i = arena2.bitcast(I32)
    scg2 = a2f[:, 2048:3072]
    cmpscr = a2f[:, 0:128]
    msk = a2f[:, 128:256]
    pscan = a2f[:, 256:384]
    slotf = a2f[:, 384:512]
    fscr = a2f[:, 512:640]
    fscr2 = a2f[:, 640:768]
    gtv = a2f[:, 768:896]
    iscr = a2i[:, 896:1024]
    iscr2 = a2i[:, 1024:1152]
    desti = a2i[:, 1152:1280]
    pdesti = a2i[:, 1280:1408]
    prs5 = a2i[:, 1408:2048]
    xf32 = a1f[:, 0:4096]
    xnf = a1f[:, 4096:8192]
    v_xncast = [0, 0, 0, 0]
    for tc in range(4):
        pe_prev = c_pe.v
        if tc > 0:
            W("sync", pe, pe_prev)
        v_xf = DMA("sync", g_x, lambda q, tc=tc: q.dma_start(
            out=xf32.rearrange("p (k t) -> p k t", k=8),
            in_=xts[:, 512 * tc:512 * (tc + 1)].rearrange("(k p) t -> p k t", p=128)))
        W("tensor", g_x.s, v_xf)
        v_sum = MACC(pS[0:1, :],
                     [(onescol[:, :], xf32[:, 512 * k:512 * (k + 1)]) for k in range(8)])
        W("scalar", g_x.s, v_xf)
        if tc > 0:
            W("scalar", pe, pe_prev)
        for k in range(8):
            EA(lambda q, k=k: q.activation(
                xnf[:, 512 * k:512 * (k + 1)], xf32[:, 512 * k:512 * (k + 1)], AF.Square))
        v_sq = c_ac.v
        v_sumsq = MACC(p3[0:1, :],
                       [(onescol[:, :], xnf[:, 512 * k:512 * (k + 1)]) for k in range(8)],
                       waits=[(ac, v_sq)])
        mean_s = meanr[0:1, :]
        rstd_s = rstdr[0:1, :]
        var_s = stat1[0:1, :]
        sc3 = stat3[0:1, :]
        W("vector", pe, v_sumsq)
        EV(lambda q, m=mean_s: q.tensor_scalar(m, pS[0:1, :], 1.0 / DM, None, ALU.mult))
        EV(lambda q, v_=var_s: q.tensor_scalar(v_, p3[0:1, :], 1.0 / DM, None, ALU.mult))
        v = EV(lambda q, m=mean_s, s3=sc3: q.tensor_mul(s3, m, m))
        W("vector", dv, v)
        v = EV(lambda q, v_=var_s, s3=sc3: q.tensor_sub(v_, v_, s3))
        W("scalar", dv, v)
        if tc == 0:
            W("scalar", dv, V_EPST)
        v = EA(lambda q, v_=var_s, s3=sc3: q.activation(s3, v_, AF.Ln, bias=epst[0:1, 0:1]))
        W("scalar", ac, v)
        v = EA(lambda q, r=rstd_s, s3=sc3: q.activation(r, s3, AF.Exp, scale=mhalf[0:1, 0:1]))
        W("vector", ac, v)
        v = EV(lambda q, v_=var_s: q.tensor_scalar(v_, v_, LN_EPS, None, ALU.add))
        W("vector", dv, v)
        v = EV(lambda q, r=rstd_s, s3=sc3: q.tensor_mul(s3, r, r))
        W("vector", dv, v)
        v = EV(lambda q, v_=var_s, s3=sc3: q.tensor_mul(s3, s3, v_))
        W("vector", dv, v)
        v = EV(lambda q, s3=sc3: q.tensor_scalar(s3, s3, -0.5, 1.5, ALU.mult, ALU.add))
        W("vector", dv, v)
        v_newt = EV(lambda q, r=rstd_s, s3=sc3: q.tensor_mul(r, r, s3))
        v_bc = MACC(pW[:, 0:512], [(onesrow[:, :], mean_s)], waits=[(dv, v_newt)])
        v_bc2 = MACC(pW[:, 512:1024], [(onesrow[:, :], rstd_s)])
        W("scalar", pe, v_bc2)
        EA(lambda q: q.copy(mb_sb, pW[:, 0:512]))
        v_ev = EA(lambda q: q.copy(rb_sb, pW[:, 512:1024]))
        W("vector", ac, v_ev)
        W("vector", pe, v_sumsq)
        for k in range(8):
            EV(lambda q, k=k: q.tensor_sub(
                xnf[:, 512 * k:512 * (k + 1)], xf32[:, 512 * k:512 * (k + 1)], mb_sb))
            W("vector", dv, c_dv.v)
            EV(lambda q, k=k: q.tensor_mul(
                xnf[:, 512 * k:512 * (k + 1)], xnf[:, 512 * k:512 * (k + 1)], rb_sb))
        v_xn = c_dv.v
        W("scalar", dv, v_xn)
        for k in range(8):
            EA(lambda q, k=k, tc=tc: q.copy(
                xnbf[:, 2048 * k + 512 * tc:2048 * k + 512 * (tc + 1)],
                xnf[:, 512 * k:512 * (k + 1)]))
        v_xncast[tc] = c_ac.v
        v_sc = MACC(pS[0:8, :],
                    [(wg_sb[:, 8 * k:8 * (k + 1)], xnf[:, 512 * k:512 * (k + 1)])
                     for k in range(8)],
                    waits=[(dv, v_xn)])
        W("scalar", pe, v_sc)
        v_scsb = EA(lambda q: q.copy(scsb[:, :], pS[0:8, :]))
        W("tensor", ac, v_scsb)
        for j in range(4):
            EP(lambda q, j=j: q.transpose(
                pW[:, 8 * j:8 * (j + 1)], scsb[0:8, 128 * j:128 * (j + 1)], identf[0:8, 0:8]))
        v_tr = c_pe.v
        W("scalar", pe, v_tr)
        W("scalar", g_sc.s, g_sc.c.v)
        v_sctm = EA(lambda q: q.copy(sctm[:, :], pW[:, 0:32]))
        W("sync", ac, v_sctm)
        DMA("sync", g_sc, lambda q, tc=tc: q.dma_start(
            out=sc_in[512 * tc:512 * (tc + 1), :].rearrange("(j p) e -> p j e", p=128),
            in_=sctm[:, :].rearrange("p (j e) -> p j e", e=8)))
    V_SCIN = g_sc.c.v
    V_P1_PE = c_pe.v

    if stop_after < 2:
        _final_stub()
        return _finish()
    v_evac_ps = [0, 0]
    v_vvn_prev = 0
    v_dtabs_prev = 0
    W("tensor", g_wssm.s, V_WSSM)
    for tcc in range(4):
        W("tensor", ac, v_xncast[tcc])
        if tcc == 0:
            W("tensor", ac, v_xncast[3])
            W("tensor", g_sc.s, V_SCIN)
        for uk in range(16):
            pbuf = [p0, p1][uk % 2]
            W("tensor", ac, v_evac_ps[uk % 2])
            vm = MACC(pbuf[:, :],
                      [(arena2[:, 2048 * k + 128 * uk:2048 * k + 128 * (uk + 1)],
                        xnbf[:, 2048 * k + 512 * tcc:2048 * k + 512 * (tcc + 1)])
                       for k in range(8)])
            dstb = ubf if uk < 8 else arena3
            W("scalar", pe, vm)
            v_evac_ps[uk % 2] = EA(lambda q, d=dstb, uk=uk, tcc=tcc, pbuf=pbuf: q.copy(
                d[:, 2048 * (uk % 8) + 512 * tcc:2048 * (uk % 8) + 512 * (tcc + 1)],
                pbuf[:, :]))
        v_u_done = c_ac.v
        v_b = MACC(p2[:, :],
                   [(wstb[:, 128 * uk:128 * (uk + 1)],
                     ubf[:, 2048 * uk + 512 * tcc:2048 * uk + 512 * (tcc + 1)])
                    for uk in range(8)],
                   waits=[(ac, v_u_done), (dv, v_vvn_prev)])
        v_dt = MACC(p3[:, :],
                    [(wdtb[:, 128 * uk:128 * (uk + 1)],
                      ubf[:, 2048 * uk + 512 * tcc:2048 * uk + 512 * (tcc + 1)])
                     for uk in range(8)],
                    waits=[(ac, v_dtabs_prev)])
        W("scalar", pe, v_dt)
        W("scalar", dv, c_dv.v)
        EA(lambda q: q.activation(dts1[:, :], p3[:, :], AF.Relu, bias=b_dt[:, 0:1]))
        v = EA(lambda q: q.activation(dts2[:, :], p3[:, :], AF.Abs, bias=b_dt[:, 0:1]))
        v_dtabs_prev = v
        W("scalar", ac, v)
        v = EA(lambda q: q.activation(dts3[:, :], dts2[:, :], AF.Exp, scale=-1.0))
        W("scalar", ac, v)
        v_l = EA(lambda q: q.activation(dts2[:, :], dts3[:, :], AF.Ln, bias=1.0))
        W("vector", ac, v_l)
        v = EV(lambda q: q.tensor_add(dts1[:, :], dts1[:, :], dts2[:, :]))
        W("vector", dv, v)
        v_dtv = EV(lambda q: q.tensor_scalar_min(dts1[:, :], dts1[:, :], 10.0))
        W("vector", dv, v_dtv)
        EV(lambda q, tcc=tcc: q.reduce_sum(dtsum[:, tcc:tcc + 1], dts1[:, :], axis=AX.X))
        W("scalar", dv, v_dtv)
        if tcc == 0:
            W("scalar", dv, V_NEGRATE)
        v_dec = EA(lambda q, tcc=tcc: q.activation(
            decay[:, 512 * tcc:512 * (tcc + 1)], dts1[:, :], AF.Exp, scale=negrate[:, 0:1]))
        W("vector", ac, v_dec)
        W("vector", pe, v_b)
        v_vvn_prev = EV(lambda q, tcc=tcc: q.scalar_tensor_tensor(
            vvn[:, 512 * tcc:512 * (tcc + 1)], decay[:, 512 * tcc:512 * (tcc + 1)],
            1.0, p2[:, :], ALU.subtract, ALU.mult))
    W("vector", dv, c_dv.v)
    v = EV(lambda q: q.tensor_tensor_scan(hh, decay, vvn, 0.0, ALU.mult, ALU.subtract))
    W("vector", dv, v)
    v = EV(lambda q: q.tensor_add(dtsum[:, 0:1], dtsum[:, 0:1], dtsum[:, 1:2]))
    W("vector", dv, v)
    v = EV(lambda q: q.tensor_add(dtsum[:, 2:3], dtsum[:, 2:3], dtsum[:, 3:4]))
    W("vector", dv, v)
    v_dtt = EV(lambda q: q.tensor_add(dtsum[:, 0:1], dtsum[:, 0:1], dtsum[:, 2:3]))
    W("scalar", dv, v_dtt)
    v_at = EA(lambda q: q.activation(dtsum[:, 1:2], dtsum[:, 0:1], AF.Exp,
                                     scale=negrate[:, 0:1]))
    W("sync", ac, v_at)
    W("sync", dv, c_dv.v)
    DMA("sync", g_sum, lambda q: q.dma_start(out=sum_in[0:1, :].rearrange("o p -> p o"), in_=dtsum[:, 1:2]))
    v_sumdma = DMA("sync", g_sum, lambda q: q.dma_start(out=sum_in[1:2, :].rearrange("o p -> p o"), in_=hh[:, 2047:2048]))
    W("gpsimd", g_sum.s, v_sumdma)
    em("gpsimd", lambda q: q.collective_compute(
        "AllGather", ALU.bypass, [list(range(NCORE))],
        ins=[sum_in.ap()], outs=[sum_out.ap()]).then_inc(cc2, 1))
    W("gpsimd", cc2, 1)
    v_sumg = DMA("gpsimd", g_sumg, lambda q: q.dma_start(out=sumg[0:1, :], in_=sum_out.rearrange("a b -> (a b)")[None, :]))
    W("vector", g_sumg.s, v_sumg)
    EV(lambda q: q.memset(hin[:, :], 0.0))
    for j in range(NCORE):
        W("vector", dv, c_dv.v)
        EV(lambda q, j=j: q.tensor_scalar(hf1[:, :], sumg[0:1, 256 * j:256 * j + 128],
                                          -1.0, None, ALU.add))
        W("vector", dv, c_dv.v)
        EV(lambda q, j=j: q.tensor_scalar(hf1[:, :], hf1[:, :],
                                          selv[0:1, j:j + 1], 1.0, ALU.mult, ALU.add))
        W("vector", dv, c_dv.v)
        EV(lambda q: q.tensor_mul(hf2[:, :], hin[:, :], hf1[:, :]))
        W("vector", dv, c_dv.v)
        EV(lambda q, j=j: q.tensor_scalar(hf1[:, :], sumg[0:1, 256 * j + 128:256 * j + 256],
                                          selv[0:1, j:j + 1], None, ALU.mult))
        W("vector", dv, c_dv.v)
        EV(lambda q: q.tensor_add(hin[:, :], hf2[:, :], hf1[:, :]))
    v_hin = c_dv.v
    v_htr = MACC(pW[:, 0:1], [(hin[:, :], identf[0:1, 0:1])], waits=[(dv, v_hin)])
    W("vector", pe, v_htr)
    v = EV(lambda q: q.tensor_copy(h0col[:, :], pW[:, 0:1]))
    W("vector", dv, v)
    v_scan2 = EV(lambda q: q.tensor_tensor_scan(hh, decay, vvn, h0col[:, 0:1],
                                                ALU.mult, ALU.subtract))
    W("vector", dv, v_scan2)
    v_hcast = EV(lambda q: q.tensor_copy(hcast[:, :], hh))
    if debug:
        W("sync", dv, v_scan2)
        DMA("sync", g_dbq, lambda q: q.dma_start(out=hh_dbg[:, :], in_=hh))

    silu4 = a1f[:, 2048:6144]
    v_z_evac = [0, 0]
    v_mul_last = 0
    for tcc in range(4):
        gate3 = arena3.rearrange("p (k c) -> p k c", c=2048)[:, :, 512 * tcc:512 * (tcc + 1)]
        W("scalar", dv, v_hcast if tcc == 0 else v_mul_last)
        v_sg = EA(lambda q, gate3=gate3: q.activation(
            silu4.rearrange("p (k c) -> p k c", c=512), gate3, AF.Silu))
        for k in range(8):
            pbuf = [p0, p1][k % 2]
            vy = MACC(pbuf[:, :],
                      [(wctb[:, 128 * k:128 * (k + 1)],
                        hcast[:, 512 * tcc:512 * (tcc + 1)])],
                      waits=[(dv, v_hcast), (dv, v_z_evac[k % 2])])
            W("vector", ac, v_sg)
            W("vector", pe, vy)
            v_mul_last = EV(lambda q, pbuf=pbuf, k=k: q.tensor_mul(
                dts2[:, :], pbuf[:, :], silu4[:, 512 * k:512 * (k + 1)]))
            v_z_evac[k % 2] = EV(lambda q, k=k, tcc=tcc: q.scalar_tensor_tensor(
                ubf[:, 2048 * k + 512 * tcc:2048 * k + 512 * (tcc + 1)],
                xnbf[:, 2048 * k + 512 * tcc:2048 * k + 512 * (tcc + 1)],
                dmat[:, k:k + 1], dts2[:, :], ALU.mult, ALU.add))
    V_Z = c_dv.v
    V_P3_PE = c_pe.v

    if stop_after < 3:
        _final_stub()
        return _finish()
    W("gpsimd", g_sc.s, V_SCIN)
    em("gpsimd", lambda q: q.collective_compute(
        "AllGather", ALU.bypass, [list(range(NCORE))],
        ins=[sc_in.ap()], outs=[sc_out.ap()]).then_inc(cc1, 1))
    W("gpsimd", cc1, 1)
    v_scg = DMA("gpsimd", g_scg, lambda q: q.dma_start(
        out=scg2.rearrange("p (tl e) -> p tl e", e=8),
        in_=sc_out.rearrange("(p tl) e -> p tl e", p=128)))
    emask = a1f[:, 0:1024]
    W("sync", dv, V_Z)
    v_eml = DMA("sync", g_em, lambda q: q.dma_start(out=emask, in_=emask_in[:, :]))
    W("vector", g_scg.s, v_scg)
    W("vector", g_em.s, v_eml)
    W("vector", dv, V_Z)
    v = EV(lambda q: q.tensor_mul(a1f[:, 1024:2048], scg2, emask))
    W("vector", dv, v)
    v_vE = EV(lambda q: q.tensor_reduce(
        vE[:, :], a1f[:, 1024:2048].rearrange("p (tl e) -> p tl e", e=8),
        axis=AX.X, op=ALU.add))
    W("vector", dv, v_vE)
    EV(lambda q: q.memset(bis[0:1, :], 0.0))
    EV(lambda q: q.memset(bis[0:1, 0:1], -64.0))
    W("vector", dv, c_dv.v)
    EV(lambda q: q.memset(bis[0:1, 1:2], 64.0))
    W("vector", dv, c_dv.v)
    EV(lambda q: q.memset(bis[0:1, 2:3], 0.0))
    v_mid = c_dv.v
    for it in range(NBIS):
        v_bcm = MACC(p2[:, 0:1], [(onesrow[:, :], bis[0:1, 2:3])],
                     waits=[(dv, v_mid)])
        W("scalar", pe, v_bcm)
        v_thr = EA(lambda q: q.copy(thrcol[:, :], p2[:, 0:1]))
        W("vector", ac, v_thr)
        v_cmp = EV(lambda q: q.tensor_scalar(cmpscr, vE[:, :], thrcol[:, 0:1],
                                             0.0, ALU.is_ge, ALU.add,
                                             accum_out=cntcol[:, 0:1]))
        v_cnt = MACC(p2[0:1, 1:2], [(onescol[:, :], cntcol[:, :])],
                     waits=[(dv, v_cmp)])
        W("vector", pe, v_cnt)
        EV(lambda q: q.tensor_scalar(bis[0:1, 3:4], p2[0:1, 1:2], KSEL - 0.5,
                                     None, ALU.is_ge))
        W("vector", dv, c_dv.v)
        EV(lambda q: q.tensor_sub(bis[0:1, 4:5], bis[0:1, 2:3], bis[0:1, 0:1]))
        W("vector", dv, c_dv.v)
        EV(lambda q: q.tensor_mul(bis[0:1, 5:6], bis[0:1, 3:4], bis[0:1, 4:5]))
        W("vector", dv, c_dv.v)
        EV(lambda q: q.tensor_add(bis[0:1, 0:1], bis[0:1, 0:1], bis[0:1, 5:6]))
        W("vector", dv, c_dv.v)
        EV(lambda q: q.tensor_sub(bis[0:1, 4:5], bis[0:1, 1:2], bis[0:1, 2:3]))
        W("vector", dv, c_dv.v)
        EV(lambda q: q.tensor_mul(bis[0:1, 5:6], bis[0:1, 3:4], bis[0:1, 4:5]))
        W("vector", dv, c_dv.v)
        EV(lambda q: q.tensor_add(bis[0:1, 1:2], bis[0:1, 2:3], bis[0:1, 5:6]))
        W("vector", dv, c_dv.v)
        EV(lambda q: q.tensor_add(bis[0:1, 2:3], bis[0:1, 0:1], bis[0:1, 1:2]))
        W("vector", dv, c_dv.v)
        v_mid = EV(lambda q: q.tensor_scalar(bis[0:1, 2:3], bis[0:1, 2:3], 0.5,
                                             None, ALU.mult))
    v_bcf = MACC(p2[:, 0:1], [(onesrow[:, :], bis[0:1, 0:1])], waits=[(dv, v_mid)])
    W("scalar", pe, v_bcf)
    v_thrf = EA(lambda q: q.copy(thrcol[:, :], p2[:, 0:1]))
    if debug:
        W("sync", dv, v_mid)
        DMA("sync", g_dbq, lambda q: q.dma_start(out=thr_dbg[0:1, :], in_=bis[0:1, :]))
    W("vector", ac, v_thrf)
    v_msk = EV(lambda q: q.tensor_scalar(msk, vE[:, :], thrcol[:, 0:1],
                                         None, ALU.is_ge))
    W("vector", dv, v_msk)
    v_ps = EV(lambda q: q.tensor_tensor_scan(pscan, onestt[:, :], msk,
                                             0.0, ALU.mult, ALU.add))
    v_exm = MACC(pW[:, 0:1], [(ltstrict[:, :], pscan[:, 127:128])],
                 waits=[(dv, v_ps)])
    v_bnm = MACC(pW[:, 1:2], [(lt16[:, :], pscan[:, 127:128])])
    W("vector", pe, v_bnm)
    EV(lambda q: q.tensor_copy(exc[:, :], pW[:, 0:1]))
    W("vector", dv, c_dv.v)
    EV(lambda q: q.tensor_copy(bnd[:, :], pW[:, 1:2]))
    W("vector", dv, c_dv.v)
    EV(lambda q: q.tensor_scalar(excm1[:, :], exc[:, :], -1.0, None, ALU.add))
    W("vector", dv, c_dv.v)
    EV(lambda q: q.tensor_scalar(slotf, pscan, excm1[:, 0:1],
                                 None, ALU.add))
    W("vector", dv, c_dv.v)
    EV(lambda q: q.tensor_scalar(fscr[:, :], slotf, bnd[:, 0:1],
                                 None, ALU.subtract))
    W("vector", dv, c_dv.v)
    EV(lambda q: q.tensor_scalar(fscr2, slotf, KSEL - 0.5, None, ALU.is_lt))
    W("vector", dv, c_dv.v)
    EV(lambda q: q.tensor_mul(fscr2, fscr2, msk))
    W("vector", dv, c_dv.v)
    EV(lambda q: q.tensor_scalar(gtv, fscr[:, :], CAP - 0.5, None, ALU.is_lt))
    W("vector", dv, c_dv.v)
    EV(lambda q: q.tensor_mul(gtv, gtv, msk))
    W("vector", dv, c_dv.v)
    EV(lambda q: q.tensor_copy(iscr, slotf))
    W("vector", dv, c_dv.v)
    EV(lambda q: q.tensor_scalar(iscr2, iscr, 127, None, ALU.bitwise_and))
    W("vector", dv, c_dv.v)
    EV(lambda q: q.tensor_scalar(iscr2, iscr2, 5, None, ALU.logical_shift_left))
    W("vector", dv, c_dv.v)
    EV(lambda q: q.tensor_scalar(iscr, iscr, 7, None, ALU.logical_shift_right))
    W("vector", dv, c_dv.v)
    EV(lambda q: q.tensor_add(iscr2, iscr2, iscr))
    W("vector", dv, c_dv.v)
    EV(lambda q: q.memset(desti, BIG))
    W("vector", dv, c_dv.v)
    EV(lambda q: q.tensor_copy(iscr, fscr2))
    W("vector", dv, c_dv.v)
    EV(lambda q: q.copy_predicated(desti, iscr, iscr2))
    W("vector", dv, c_dv.v)
    EV(lambda q: q.tensor_copy(iscr2, fscr[:, :]))
    W("vector", dv, c_dv.v)
    EV(lambda q: q.tensor_scalar(iscr2, iscr2, jbase[:, 0:1], None, ALU.add))
    W("vector", dv, c_dv.v)
    EV(lambda q: q.memset(pdesti, BIG))
    W("vector", dv, c_dv.v)
    EV(lambda q: q.tensor_copy(iscr, gtv))
    W("vector", dv, c_dv.v)
    EV(lambda q: q.copy_predicated(pdesti, iscr, iscr2))
    W("vector", dv, c_dv.v)
    W("scalar", dv, c_dv.v)
    v_gtv = EA(lambda q: q.activation(gtv, vE[:, :], AF.Sigmoid))
    EV(lambda q: q.tensor_scalar(iscr, tokid[:, :], 2047, None, ALU.bitwise_and))
    v_lid = c_dv.v
    W("vector", dv, v_lid)
    EV(lambda q: q.tensor_copy(prs5[:, 0:640:5], tokid[:, :]))
    W("vector", ac, v_gtv)
    EV(lambda q: q.tensor_copy(prs5[:, 1:640:5], gtv.bitcast(I32)))
    EV(lambda q: q.tensor_copy(prs5[:, 2:640:5], pdesti))
    EV(lambda q: q.tensor_copy(prs5[:, 3:640:5], iscr))
    EV(lambda q: q.tensor_copy(iscr2, fscr[:, :]))
    EV(lambda q: q.tensor_scalar(msk.bitcast(I32), iscr2, 7, None, ALU.logical_shift_right))
    EV(lambda q: q.tensor_scalar(iscr2, iscr2, 127, None, ALU.bitwise_and))
    EV(lambda q: q.tensor_scalar(pscan.bitcast(I32), iscr2, 2, None, ALU.logical_shift_left))
    EV(lambda q: q.tensor_add(iscr2, iscr2, pscan.bitcast(I32)))
    EV(lambda q: q.tensor_add(iscr2, iscr2, msk.bitcast(I32)))
    EV(lambda q: q.tensor_copy(fscr2, iscr2))
    EV(lambda q: q.tensor_scalar(fscr2, fscr2, jbase[:, 0:1], None, ALU.add))
    EV(lambda q: q.tensor_copy(iscr2, fscr2))
    EV(lambda q: q.memset(msk.bitcast(I32), BIG))
    EV(lambda q: q.tensor_copy(fscr2, pdesti))
    EV(lambda q: q.tensor_scalar(fscr2, fscr2, 1.0e8, None, ALU.is_lt))
    EV(lambda q: q.tensor_copy(iscr, fscr2))
    EV(lambda q: q.copy_predicated(msk.bitcast(I32), iscr, iscr2))
    v_quad = EV(lambda q: q.tensor_copy(prs5[:, 4:640:5], msk.bitcast(I32)))
    W("gpsimd", dv, v_quad)
    W("gpsimd", g_pref.s, V_PREFILL)
    def _mk_scat(tg):
        def f(q):
            q.indirect_dma_start(
                out=idxg[:, :],
                out_offset=IndirectOffsetOnAxis(desti[:, tg:tg + 1], axis=0),
                in_=prs5[:, 5 * tg:5 * (tg + 1)], in_offset=None,
                bounds_check=bc4095[0], oob_is_err=False).then_inc(g_sq1.s, 16)
        return f
    bc4095 = [None]
    def _set_regs(q):
        bc4095[0] = q.to_reg(KSEL - 1)
        bcNPR[0] = q.to_reg(NPR - 1)
        bcNTOK[0] = q.to_reg(NTOK - 1)
        bcT[0] = q.to_reg(T - 1)
        return q.engine_nop()
    bcNPR = [None]; bcNTOK = [None]; bcT = [None]
    em("gpsimd", lambda q: _set_regs(q))
    for tg in range(128):
        em("gpsimd", _mk_scat(tg))
        g_sq1.c.add(16)
    W("gpsimd", g_sq1.s, g_sq1.c.v)
    v_idxl = DMA("gpsimd", g_idxl, lambda q: q.dma_start(
        out=idxg_sb[:, :].rearrange("p (w c) -> p w c", c=5),
        in_=idxg.rearrange("(p w) c -> p w c", p=128)))
    W("vector", g_idxl.s, v_idxl)
    for g in range(32):
        EV(lambda q, g=g: q.tensor_copy(fscr[:, g:g + 1], idxg_sb[:, 5 * g:5 * g + 1]))
    W("vector", dv, c_dv.v)
    v = EV(lambda q: q.tensor_scalar(fscr2[:, 0:32], fscr[:, 0:32], NTOK - 0.5,
                                     None, ALU.is_lt))
    W("vector", dv, v)
    v_gm = EV(lambda q: q.tensor_mul(
        gmask[:, :], fscr2[:, 0:32],
        idxg_sb.bitcast(F32)[:, :].rearrange("p (w c) -> p w c", c=5)[:, :, 1]))
    W("gpsimd", dv, v_gm)
    def _mk_idscat(g):
        def f(q):
            q.indirect_dma_start(
                out=a2ais[:, :],
                out_offset=IndirectOffsetOnAxis(idxg_sb[:, 5 * g + 4:5 * g + 5], axis=0),
                in_=idxg_sb[:, 5 * g + 3:5 * g + 4], in_offset=None,
                bounds_check=bcNPR[0], oob_is_err=False).then_inc(g_sq2.s, 16)
        return f
    for g in range(32):
        em("gpsimd", _mk_idscat(g))
        g_sq2.c.add(16)
    v_idsf = g_sq2.c.v

    if stop_after < 4:
        _final_stub()
        return _finish()
    xg_bufs = [a2f[:, 0:4096], xnbf.bitcast(F32)[:, 0:4096]]
    exT_bufs = [arena2[:, 8192:12288], xnbf[:, 8192:12288]]
    xnr = a2f[:, 6656:7680]
    sqs = xnr
    xnrb4 = xnbf[:, 12288:16384]

    v_w1l = [0, 0]
    v_w2l = [0, 0]
    v_epi_dv = 0
    v_epi_psum = 0
    v_epi_half0 = 0
    v_tev = [0, 0]
    v_ext = [0, 0]
    v_xg_read = [0, 0]
    v_T_done = [0] * 9
    v_xnrb = {}
    v_gath_v = {}

    def emit_gather(rcn):
        buf = rcn % 2
        xgv = xg_bufs[buf].rearrange("p (b f) -> p b f", b=4)
        if rcn == 0:
            W("gpsimd", pe, V_P3_PE)
            W("gpsimd", g_sq2.s, v_idsf)
        elif rcn == 1:
            W("gpsimd", dv, V_Z)
        else:
            W("gpsimd", dv, v_xg_read[buf])
        for gg in range(4):
            g = 4 * rcn + gg
            DMA("gpsimd", g_gath, lambda q, g=g, gg=gg, xgv=xgv: q.indirect_dma_start(
                out=xgv[:, gg, :], out_offset=None, in_=xfull[:, :],
                in_offset=IndirectOffsetOnAxis(idxg_sb[:, 5 * g:5 * g + 1], axis=0),
                bounds_check=bcNTOK[0], oob_is_err=False))
        v_gath_v[rcn] = g_gath.c.v

    mcol4 = dts1[:, 0:4]
    s2col4 = dts1[:, 4:8]
    vcol4 = dts1[:, 8:12]
    ncol4 = dts1[:, 12:16]
    rcol4 = dts1[:, 16:20]

    def emit_ln(rcn):
        buf = rcn % 2
        v_gath = v_gath_v[rcn]
        xgv = xg_bufs[buf].rearrange("p (b f) -> p b f", b=4)
        if rcn >= 1:
            W("vector", pe, v_T_done[rcn - 1])
        W("vector", g_gath.s, v_gath)
        EV(lambda q, xgv=xgv: q.tensor_reduce(mcol4, xgv, axis=AX.X, op=ALU.add))
        for gg in range(4):
            EV(lambda q, gg=gg, xgv=xgv: q.scalar_tensor_tensor(
                sqs, xgv[:, gg, :], 1.0, xgv[:, gg, :], ALU.mult, ALU.mult,
                accum_out=s2col4[:, gg:gg + 1]))
        EV(lambda q: q.tensor_scalar(mcol4, mcol4, 1.0 / DM, None, ALU.mult))
        EV(lambda q: q.tensor_scalar(s2col4, s2col4, 1.0 / DM, None, ALU.mult))
        EV(lambda q: q.tensor_mul(vcol4, mcol4, mcol4))
        v = EV(lambda q: q.tensor_sub(vcol4, s2col4, vcol4))
        W("scalar", dv, v)
        if rcn == 0:
            W("scalar", pe, V_P3_PE)
        EA(lambda q: q.activation(ncol4, vcol4, AF.Ln, bias=epst[:, 0:1]))
        v_r0 = EA(lambda q: q.activation(rcol4, ncol4, AF.Exp, scale=mhalf[:, 0:1]))
        W("vector", ac, v_r0)
        EV(lambda q: q.tensor_scalar(vcol4, vcol4, LN_EPS, None, ALU.add))
        EV(lambda q: q.tensor_mul(ncol4, rcol4, rcol4))
        EV(lambda q: q.tensor_mul(ncol4, ncol4, vcol4))
        EV(lambda q: q.tensor_scalar(ncol4, ncol4, -0.5, 1.5, ALU.mult, ALU.add))
        EV(lambda q: q.tensor_mul(rcol4, rcol4, ncol4))
        v_last_read = 0
        for gg in range(4):
            v_last_read = EV(lambda q, gg=gg, xgv=xgv: q.tensor_scalar(
                xnr, xgv[:, gg, :], mcol4[:, gg:gg + 1], None, ALU.subtract))
            v = EV(lambda q, gg=gg: q.tensor_scalar(
                xnrb4[:, 1024 * gg:1024 * (gg + 1)], xnr, rcol4[:, gg:gg + 1],
                None, ALU.mult))
            v_xnrb[(rcn, gg)] = v
        v_xg_read[buf] = v_last_read

    def emit_T(rcn):
        buf = rcn % 2
        for gg in range(4):
            W("tensor", dv, v_xnrb[(rcn, gg)])
            for half in range(2):
                W("tensor", ac, v_tev[0])
                for k4 in range(4):
                    k = 4 * half + k4
                    EP(lambda q, gg=gg, k=k, k4=k4: q.transpose(
                        pT[:, 128 * k4:128 * (k4 + 1)],
                        xnrb4[:, 1024 * gg + 128 * k:1024 * gg + 128 * (k + 1)],
                        identbf[:, :]))
                v_tp = c_pe.v
                W("scalar", pe, v_tp)
                v_tev[0] = EA(lambda q, gg=gg, half=half, buf=buf: q.copy(
                    exT_bufs[buf].rearrange("p (k c) -> p k c", c=512)
                        [:, 4 * half:4 * half + 4, 128 * gg:128 * (gg + 1)],
                    pT[:, :].rearrange("p (k f) -> p k f", f=128)))
        v_ext[buf] = c_ac.v
        v_T_done[rcn] = c_pe.v

    v_epi_by_rc = {}

    def emit_scatter(rcn):
        obase = 4096 if rcn % 2 == 0 else 12288
        if rcn == 0:
            W("gpsimd", g_az.s, V_AZERO)
        for sub in range(4):
            g = 4 * rcn + sub
            osb = arena3[:, obase + 1024 * sub:obase + 1024 * (sub + 1)]
            W("gpsimd", dv, v_epi_by_rc[rcn])
            em("gpsimd", lambda q, g=g, osb=osb, rcn=rcn: q.indirect_dma_start(
                out=a2as[:, :],
                out_offset=IndirectOffsetOnAxis(idxg_sb[:, 5 * g + 2:5 * g + 3], axis=0),
                in_=osb, in_offset=None,
                bounds_check=bcNPR[0], oob_is_err=False).then_inc(g_osb[rcn].s, 16))
            g_osb[rcn].c.add(16)

    emit_gather(0)
    emit_ln(0)
    emit_T(0)
    for rc in range(8):
        if rc < 7:
            emit_gather(rc + 1)
            emit_ln(rc + 1)
        if rc >= 1:
            emit_scatter(rc - 1)
        exT = exT_bufs[rc % 2]
        v_gelu_pb = [0, 0, 0, 0]
        for ffc in range(32):
            hbuf = ffc % 2
            w1c = arena3[:, 1024 * hbuf:1024 * (hbuf + 1)]
            if rc == 0 and ffc < 2:
                W("sync", g_stgst.s, V_WBF)
                W("sync", pe, V_P3_PE)
            else:
                W("sync", pe, v_w1l[hbuf])
            vl = DMA("sync", g_w1[hbuf], lambda q, ffc=ffc, w1c=w1c: q.dma_start(
                out=w1c, in_=w1bfd[128 * ffc:128 * (ffc + 1), :]))
            pb = [p0, p1, p2, p3][ffc % 4]
            v_mm1 = MACC(pb[:, :],
                         [(w1c[:, 128 * k:128 * (k + 1)], exT[:, 512 * k:512 * (k + 1)])
                          for k in range(8)],
                         waits=[(g_w1[hbuf].s, vl), (ac, v_ext[rc % 2]),
                                (ac, v_gelu_pb[ffc % 4]),
                                (dv, v_epi_psum if ffc < 4 else 0)])
            v_w1l[hbuf] = v_mm1
            W("scalar", pe, v_mm1)
            v_gelu_pb[ffc % 4] = EA(lambda q, ffc=ffc, pb=pb: q.activation(
                arena1[:, 512 * ffc:512 * (ffc + 1)], pb[:, :], AF.Gelu,
                bias=b1m[:, ffc:ffc + 1]))
        v_gelu_all = c_ac.v
        if rc < 7:
            emit_T(rc + 1)
        for hh2 in range(2):
            for ffc in range(32):
                wbuf = ffc % 2
                w2c = arena3[:, 2048 + 1024 * wbuf:2048 + 1024 * wbuf + 512]
                W("sync", pe, v_w2l[wbuf])
                vl = DMA("sync", g_w2[wbuf], lambda q, ffc=ffc, hh2=hh2, w2c=w2c: q.dma_start(
                    out=w2c,
                    in_=w2bfd[4096 * hh2 + 128 * ffc:4096 * hh2 + 128 * (ffc + 1), :]))
                for sub in range(4):
                    pbs = [p0, p1, p2, p3][sub]
                    lastk = (ffc == 31)
                    wts = [(g_w2[wbuf].s, vl)]
                    if ffc == 0:
                        wts.append((ac, v_gelu_all))
                        if hh2 == 1:
                            wts.append((dv, v_epi_half0))
                    for sw, vw in wts:
                        W("tensor", sw, vw)
                    certify = (sub == 3) or lastk
                    em("tensor", lambda q, ffc=ffc, sub=sub, pbs=pbs, w2c=w2c,
                       lastk=lastk, certify=certify:
                       (q.matmul(pbs[:, :],
                                 arena1[:, 512 * ffc + 128 * sub:512 * ffc + 128 * (sub + 1)],
                                 w2c, start=(ffc == 0), stop=lastk).then_inc(pe, 1)
                        if certify else
                        q.matmul(pbs[:, :],
                                 arena1[:, 512 * ffc + 128 * sub:512 * ffc + 128 * (sub + 1)],
                                 w2c, start=(ffc == 0), stop=lastk)))
                    if certify:
                        c_pe.add(1)
                v_w2l[wbuf] = c_pe.v
            v_mm2h = c_pe.v
            obase = 4096 if rc % 2 == 0 else 12288
            W("vector", pe, v_mm2h)
            for sub in range(4):
                pbs = [p0, p1, p2, p3][sub]
                oev = a3f[:, 4096 + 512 * sub:4096 + 512 * (sub + 1)][:, 0:512]
                EV(lambda q, pbs=pbs, hh2=hh2, oev=oev: q.tensor_add(
                    oev, pbs[:, :], b2bc[:, 512 * hh2:512 * (hh2 + 1)]))
            v_epi_psum = c_dv.v
            if hh2 == 0:
                v_epi_half0 = v_epi_psum
            if rc > 1 and hh2 == 0:
                W("vector", g_osb[rc - 2].s, 64)
            for sub in range(4):
                g = 4 * rc + sub
                oev = a3f[:, 4096 + 512 * sub:4096 + 512 * (sub + 1)][:, 0:512]
                osb = arena3[:, obase + 1024 * sub:obase + 1024 * (sub + 1)]
                v_epi_dv = EV(lambda q, g=g, hh2=hh2, osb=osb, oev=oev: q.tensor_scalar(
                    osb[:, 512 * hh2:512 * (hh2 + 1)], oev, gmask[:, g:g + 1],
                    None, ALU.mult))
        v_epi_by_rc[rc] = v_epi_dv
    emit_scatter(7)
    if stop_after < 5:
        _final_stub()
        return _finish()
    for rc in range(8):
        W("gpsimd", g_osb[rc].s, 64)
    em("gpsimd", lambda q: q.collective_compute(
        "AllToAll", ALU.bypass, [list(range(NCORE))],
        ins=[a2as.ap()], outs=[a2ar.ap()]).then_inc(cc3, 1))
    em("gpsimd", lambda q: q.collective_compute(
        "AllToAll", ALU.bypass, [list(range(NCORE))],
        ins=[a2ais.ap()], outs=[a2air.ap()]).then_inc(cc4, 1))
    W("gpsimd", cc3, 1)
    W("gpsimd", cc4, 1)
    if stop_after < 4.3:
        _final_stub()
        return _finish()
    v_wol = DMA("gpsimd", g_wo, lambda q: q.dma_start(
        out=arena2[:, 0:8192].rearrange("p (k e) -> p k e", k=8),
        in_=w_outT.rearrange("(k p) e -> p k e", p=128)))
    W("gpsimd", g_gz.s, V_GZERO)
    v_ridl = DMA("gpsimd", g_mr, lambda q: q.dma_start(
        out=ridl[:, :].rearrange("p (j w) -> p j w", w=5),
        in_=a2air.rearrange("(j p w) o -> p j (w o)", p=128, w=5)))
    g_mg = [Q("g_mg0"), Q("g_mg1")]
    mgbuf = [arena3[:, 0:5120], arena3[:, 5120:10240]]
    vload = {}

    def mload(j):
        b = j % 2
        return DMA("gpsimd", g_mg[b], lambda q, j=j, b=b: q.dma_start(
            out=mgbuf[b].rearrange("p (c d) -> p c d", c=5),
            in_=a2ar[CAP * j:CAP * (j + 1), :].rearrange("(c p) d -> p c d", p=128)))

    W("gpsimd", g_mr.s, v_ridl)
    vload[0] = mload(0)
    vload[1] = mload(1)
    for j in range(8):
        b = j % 2
        W("gpsimd", g_mg[b].s, vload[j])
        if j > 0:
            W("gpsimd", g_ma[j - 1].s, 80)
        for c5 in range(5):
            m = 5 * j + c5
            em("gpsimd", lambda q, j=j, c5=c5, m=m, b=b: q.indirect_dma_start(
                out=gdense[:, :],
                out_offset=IndirectOffsetOnAxis(ridl[:, m:m + 1], axis=0),
                in_=mgbuf[b][:, 1024 * c5:1024 * c5 + 1024], in_offset=None,
                bounds_check=bcT[0], oob_is_err=False,
                compute_op=ALU.add).then_inc(g_ma[j].s, 16))
            g_ma[j].c.add(16)
        if j + 2 < 8:
            W("gpsimd", g_ma[j].s, 80)
            vload[j + 2] = mload(j + 2)
    if stop_after < 4.7:
        _final_stub()
        return _finish()
    gsil = a1f[:, 0:1024]
    xstile = a1f[:, 1024:2048]
    outtile = a1f[:, 2048:3072]
    v_sdbg_prev = [0]
    v_gl = 0
    v_out_ev = 0
    n_tt = 16 if stop_after >= 4.9 else (4 if stop_after >= 4.85 else 1)
    for tt in range(n_tt):
        if tt == 0:
            for jj in range(8):
                W("sync", g_ma[jj].s, 80)
        else:
            W("sync", dv, v_out_ev)
        vg = DMA("sync", g_gt, lambda q, tt=tt: q.dma_start(
            out=gtile[:, :], in_=gdense[128 * tt:128 * (tt + 1), :]))
        vx = DMA("sync", g_xs, lambda q, tt=tt: q.dma_start(
            out=xstile, in_=xs[128 * tt:128 * (tt + 1), :]))
        W("scalar", g_gt.s, vg)
        if debug:
            W("scalar", g_dbq.s, g_dbq.c.v)
        v_sil = EA(lambda q: q.activation(gsil, gtile[:, :], AF.Silu))
        for half in range(2):
            wts = []
            if half == 0:
                if tt == 0:
                    wts += [(g_wo.s, v_wol), (dv, V_Z)]
                else:
                    wts.append((dv, v_out_ev))
                if debug and tt > 0:
                    wts.append((ac, v_sdbg_prev[0]))
            v_mmo = MACC(pW[:, 512 * half:512 * (half + 1)],
                         [(ubf[:, 2048 * k + 128 * tt:2048 * k + 128 * (tt + 1)],
                           arena2[:, 1024 * k + 512 * half:1024 * k + 512 * (half + 1)])
                          for k in range(8)],
                         waits=wts)
        v_mmo2 = c_pe.v
        if debug:
            W("scalar", pe, v_mmo2)
            v_sdbg = EA(lambda q: q.copy(a1f[:, 3072:4096], pW[:, :]))
            v_sdbg_prev[0] = v_sdbg
            W("sync", ac, v_sdbg)
            DMA("sync", g_dbq, lambda q, tt=tt: q.dma_start(out=s_dbg[128 * tt:128 * (tt + 1), :],
                                                            in_=a1f[:, 3072:4096]))
        W("vector", pe, v_mmo2)
        W("vector", ac, v_sil)
        W("vector", g_xs.s, vx)
        W("vector", g_out.s, g_out.c.v)
        EV(lambda q: q.tensor_mul(outtile[:, 0:512], pW[:, 0:512], gsil[:, 0:512]))
        EV(lambda q: q.tensor_mul(outtile[:, 512:1024], pW[:, 512:1024], gsil[:, 512:1024]))
        EV(lambda q: q.tensor_add(outtile[:, 0:512], outtile[:, 0:512], xstile[:, 0:512]))
        v_out_ev = EV(lambda q: q.tensor_add(outtile[:, 512:1024], outtile[:, 512:1024], xstile[:, 512:1024]))
        W("sync", dv, v_out_ev)
        DMA("sync", g_out, lambda q, tt=tt: q.dma_start(out=out_t[128 * tt:128 * (tt + 1), :],
                                                        in_=outtile))
        if debug:
            W("sync", ac, v_sil)
            DMA("sync", g_dbq, lambda q, tt=tt: q.dma_start(out=g_dbg[128 * tt:128 * (tt + 1), :],
                                                            in_=gsil))
    if debug:
        W("sync", cc1, 1)
        DMA("sync", g_dbq, lambda q: q.dma_start(out=sc_dbg[:, :], in_=sc_in[:, :]))
        W("vector", dv, c_dv.v)
        v = EV(lambda q: q.tensor_copy(fscr[:, :], vE[:, :]))
        W("sync", dv, v)
        DMA("sync", g_dbq, lambda q: q.dma_start(out=ve_dbg[:, :], in_=fscr[:, :]))
    if n_tt < 16 or stop_after < 5:
        _final_stub()
        return _finish()
    W("sync", g_out.s, g_out.c.v)
    if debug:
        W("sync", g_dbq.s, g_dbq.c.v)
    return _finish()


def _host_inputs(inputs, debug=False):
    bf = ml_dtypes.bfloat16
    x = np.ascontiguousarray(np.asarray(inputs["L0_attn_x"], np.float32).reshape(NTOK, DM))
    W_in = np.asarray(inputs["L0_attn_W_in"], np.float32)
    W_state = np.asarray(inputs["L0_attn_W_state"], np.float32)
    W_dt = np.asarray(inputs["L0_attn_W_dt"], np.float32)
    b_dtv = np.asarray(inputs["L0_attn_b_dt"], np.float32)
    A_log = np.asarray(inputs["L0_attn_A_log"], np.float32)
    W_C = np.asarray(inputs["L0_attn_W_C"], np.float32)
    Dv = np.asarray(inputs["L0_attn_D"], np.float32)
    W_out = np.asarray(inputs["L0_attn_W_out"], np.float32)
    Wg = np.asarray(inputs["L0_ffn_Wg"], np.float32)
    W1 = np.asarray(inputs["L0_ffn_W1"], np.float32)
    b1 = np.asarray(inputs["L0_ffn_b1"], np.float32)
    W2 = np.asarray(inputs["L0_ffn_W2"], np.float32)
    b2 = np.asarray(inputs["L0_ffn_b2"], np.float32)

    com = dict(
        xfull=x,
        w_inT=np.ascontiguousarray(W_in.T),
        w_stT=np.ascontiguousarray(W_state.T),
        w_dtT=np.ascontiguousarray(W_dt.T),
        w_cT=np.ascontiguousarray(W_C.T),
        w_outT=np.ascontiguousarray(W_out.T),
        wg_in=np.ascontiguousarray(Wg),
        b_dt_in=np.ascontiguousarray(b_dtv.reshape(DS, 1)),
        a_log_in=np.ascontiguousarray(A_log.reshape(DS, 1)),
        dmat_in=np.ascontiguousarray(Dv.reshape(8, 128).T),
        identbf_in=np.eye(128, dtype=bf),
        identf_in=np.eye(128, dtype=np.float32),
        onescol_in=np.ones((128, 1), np.float32),
        onesrow_in=np.ones((1, 128), np.float32),
        onestt_in=np.ones((128, 128), np.float32),
        ltstrict_in=np.ascontiguousarray(
            (np.arange(128)[:, None] < np.arange(128)[None, :]).astype(np.float32)),
        lt16_in=np.ascontiguousarray(
            (np.arange(128)[:, None] < (16 * (np.arange(128)[None, :] // 16)))
            .astype(np.float32)),
        tokid_in=np.ascontiguousarray(
            (np.arange(128)[:, None] * 128 + np.arange(128)[None, :]).astype(np.int32)),
        jbase_in=np.ascontiguousarray(
            ((np.arange(128) // 16) * CAP).astype(np.float32).reshape(128, 1)),
    )
    in_maps = []
    for c in range(NCORE):
        m = dict(com)
        sl = slice(T * c, T * (c + 1))
        m["xs"] = np.ascontiguousarray(x[sl])
        m["xts"] = np.ascontiguousarray(x[sl].T)
        m["w1e"] = np.ascontiguousarray(W1[c])
        m["w2e"] = np.ascontiguousarray(W2[c])
        m["b1m_in"] = np.ascontiguousarray(b1[c].reshape(32, 128).T)
        m["b2bc_in"] = np.ascontiguousarray(np.broadcast_to(b2[c], (128, DM)))
        sv = np.zeros((1, 8), np.float32)
        for j in range(NCORE):
            if j // 4 == c // 4 and j < c:
                sv[0, j] = 1.0
        m["selv_in"] = sv
        em_ = np.zeros((128, 1024), np.float32)
        em_[:, c::8] = 1.0
        m["emask_in"] = em_
        in_maps.append(m)
    return in_maps


def kernel(**inputs):
    in_maps = _host_inputs(inputs)
    with allow_ici_all_to_all():
        nc = build_program(debug=False)
        res = run_bass_kernel_spmd(nc, in_maps, core_ids=list(range(NCORE)))
    out = np.concatenate([np.asarray(res.results[c]["out"]) for c in range(NCORE)], 0)
    return out.reshape(2, 8192, DM)


# version 08
# speedup vs baseline: 1.3774x; 1.0120x over previous
import numpy as np
import ml_dtypes
import concourse.bass as bass
import concourse.mybir as mybir
from concourse.bass import IndirectOffsetOnAxis
from concourse.bass_utils import run_bass_kernel_spmd
from concourse.aot_env import allow_ici_all_to_all

F32 = mybir.dt.float32
BF16 = mybir.dt.bfloat16
I32 = mybir.dt.int32
AF = mybir.ActivationFunctionType
ALU = mybir.AluOpType
AX = mybir.AxisListType

NCORE = 8
T = 2048
NTOK = 16384
DM = 1024
DS = 128
FF = 4096
KSEL = 4096
CAP = 640
NPR = NCORE * CAP
BIG = 1 << 28
NBIS = 42
RATE_EPS = 1e-4
LN_EPS = 1e-5


class Ctr:
    __slots__ = ("v",)
    def __init__(self):
        self.v = 0
    def add(self, n=1):
        self.v += n
        return self.v


def build_program(debug=False, stop_after=5):
    nc = bass.Bass("TRN2", target_bir_lowering=False, debug=False)
    D = lambda n, s, dt, k="ExternalInput": nc.dram_tensor(n, s, dt, kind=k)

    xts = D("xts", [DM, T], F32)
    xs = D("xs", [T, DM], F32)
    xfull = D("xfull", [NTOK, DM], F32)
    w_inT = D("w_inT", [DM, 2 * DM], F32)
    w_stT = D("w_stT", [DM, DS], F32)
    w_dtT = D("w_dtT", [DM, DS], F32)
    w_cT = D("w_cT", [DS, DM], F32)
    w_outT = D("w_outT", [DM, DM], F32)
    wg_in = D("wg_in", [DM, 8], F32)
    b_dt_in = D("b_dt_in", [DS, 1], F32)
    a_log_in = D("a_log_in", [DS, 1], F32)
    dmat_in = D("dmat_in", [128, 8], F32)
    w1e = D("w1e", [DM, FF], F32)
    w2e = D("w2e", [FF, DM], F32)
    b1m_in = D("b1m_in", [128, 32], F32)
    b2bc_in = D("b2bc_in", [128, DM], F32)
    identbf_in = D("identbf_in", [128, 128], BF16)
    identf_in = D("identf_in", [128, 128], F32)
    onescol_in = D("onescol_in", [128, 1], F32)
    onesrow_in = D("onesrow_in", [1, 128], F32)
    onestt_in = D("onestt_in", [128, 128], F32)
    ltstrict_in = D("ltstrict_in", [128, 128], F32)
    lt16_in = D("lt16_in", [128, 128], F32)
    tokid_in = D("tokid_in", [128, 128], I32)
    jbase_in = D("jbase_in", [128, 1], F32)
    selv_in = D("selv_in", [1, 8], F32)
    jmat_in = D("jmat_in", [128, 8], F32)
    emask_in = D("emask_in", [128, 1024], F32)

    sc_in = nc.dram_tensor("sc_in", [T, 8], F32)
    sc_out = nc.dram_tensor("sc_out", [NTOK, 8], F32)
    sum_in = nc.dram_tensor("sum_in", [2, 128], F32)
    sum_out = nc.dram_tensor("sum_out", [16, 128], F32)
    w1bfd = nc.dram_tensor("w1bfd", [FF, DM], BF16)
    w2bfd = nc.dram_tensor("w2bfd", [2 * FF, DM // 2], BF16)
    idxg = nc.dram_tensor("idxg", [KSEL, 5], I32)
    a2as = nc.dram_tensor("a2as", [NPR, DM], BF16)
    a2ar = nc.dram_tensor("a2ar", [NPR, DM], BF16)
    a2ais = nc.dram_tensor("a2ais", [NPR, 1], I32)
    a2air = nc.dram_tensor("a2air", [NPR, 1], I32)
    gdense = nc.dram_tensor("gdense", [T, DM], BF16)
    out_t = D("out", [T, DM], F32, "ExternalOutput")
    if debug:
        hh_dbg = D("hh_dbg", [128, T], F32, "ExternalOutput")
        thr_dbg = D("thr_dbg", [1, 8], F32, "ExternalOutput")
        s_dbg = D("s_dbg", [T, DM], F32, "ExternalOutput")
        g_dbg = D("g_dbg", [T, DM], F32, "ExternalOutput")
        sc_dbg = D("sc_dbg", [T, 8], F32, "ExternalOutput")
        ve_dbg = D("ve_dbg", [128, 128], F32, "ExternalOutput")

    from contextlib import ExitStack
    ctx = ExitStack()
    sb = lambda n, s, dt: ctx.enter_context(nc.sbuf_tensor(n, s, dt))
    ps = lambda n, s, dt: ctx.enter_context(nc.psum_tensor(n, s, dt))
    sem = lambda n: ctx.enter_context(nc.semaphore(n))

    xnbf = sb("xnbf", [128, 8 * T], BF16)
    ubf = sb("ubf", [128, 8 * T], BF16)
    arena1 = sb("arena1", [128, 8 * T], BF16)
    arena2 = sb("arena2", [128, 8 * T], BF16)
    arena3 = sb("arena3", [128, 8 * T], BF16)
    a1f = arena1.bitcast(F32)
    a2f = arena2.bitcast(F32)
    a3f = arena3.bitcast(F32)

    wstage = sb("wstage", [128, 4096], BF16)
    wg_sb = sb("wg_sb", [128, 64], F32)
    wstb = sb("wstb", [128, 1024], BF16)
    wdtb = sb("wdtb", [128, 1024], BF16)
    wctb = sb("wctb", [128, 1024], BF16)
    identbf = sb("identbf", [128, 128], BF16)
    identf = sb("identf", [128, 128], F32)
    onescol = sb("onescol", [128, 1], F32)
    onesrow = sb("onesrow", [1, 128], F32)
    onestt = sb("onestt", [128, 128], F32)
    ltstrict = sb("ltstrict", [128, 128], F32)
    lt16 = sb("lt16", [128, 128], F32)
    tokid = sb("tokid", [128, 128], I32)
    jbase = sb("jbase", [128, 1], F32)
    selv = sb("selv", [1, 8], F32)
    jmat = sb("jmat", [128, 8], F32)
    b_dt = sb("b_dt", [128, 1], F32)
    a_log = sb("a_log", [128, 1], F32)
    dmat = sb("dmat", [128, 8], F32)
    b1m = sb("b1m", [128, 32], F32)
    b2bc = sb("b2bc", [128, DM], F32)
    negrate = sb("negrate", [128, 1], F32)
    rates = sb("rates", [128, 4], F32)
    epst = sb("epst", [128, 1], F32)
    mhalf = sb("mhalf", [128, 1], F32)
    meanr = sb("meanr", [1, 512], F32)
    rstdr = sb("rstdr", [1, 512], F32)
    stat1 = sb("stat1", [1, 512], F32)
    stat3 = sb("stat3", [1, 512], F32)
    scsb = sb("scsb", [8, 512], F32)
    sctm = sb("sctm", [128, 32], F32)
    dts1 = sb("dts1", [128, 512], F32)
    dts2 = sb("dts2", [128, 512], F32)
    dts3 = sb("dts3", [128, 512], F32)
    dtsum = sb("dtsum", [128, 4], F32)
    hin = sb("hin", [1, 128], F32)
    hf1 = sb("hf1", [1, 128], F32)
    hf2 = sb("hf2", [1, 128], F32)
    h0col = sb("h0col", [128, 1], F32)
    vE = sb("vE", [128, 128], F32)
    bis = sb("bis", [1, 8], F32)
    thrcol = sb("thrcol", [128, 1], F32)
    cntcol = sb("cntcol", [128, 1], F32)
    exc = sb("exc", [128, 1], F32)
    excm1 = sb("excm1", [128, 1], F32)
    bnd = sb("bnd", [128, 1], F32)
    idxg_sb = sb("idxg_sb", [128, 160], I32)
    senti = sb("senti", [128, 40], I32)
    gmask = sb("gmask", [128, 32], F32)
    mcol = sb("mcol", [128, 1], F32)
    s2col = sb("s2col", [128, 1], F32)
    vcol = sb("vcol", [128, 1], F32)
    rcol = sb("rcol", [128, 1], F32)
    ncol = sb("ncol", [128, 1], F32)
    gtile = sb("gtile", [128, DM], BF16)
    recvd = sb("recvd", [128, DM], BF16)
    zerot = sb("zerot", [128, DM], BF16)
    ridl = sb("ridl", [128, 40], I32)

    p0 = ps("p0", [128, 512], F32)
    p1 = ps("p1", [128, 512], F32)
    p2 = ps("p2", [128, 512], F32)
    p3 = ps("p3", [128, 512], F32)
    pW = ps("pW", [128, 1024], F32)
    pS = ps("pS", [8, 512], F32)
    pT = ps("pT", [128, 512], BF16)

    pe = sem("pe"); c_pe = Ctr()
    dv = sem("dv"); c_dv = Ctr()
    ac = sem("ac"); c_ac = Ctr()
    cc1 = sem("cc1"); cc2 = sem("cc2"); cc3 = sem("cc3"); cc4 = sem("cc4")
    junk = sem("junk")

    class Q:
        def __init__(self, name, chained=False):
            self.s = sem(name)
            self.c = Ctr()
            self.chained = chained

    prog = {k: [] for k in ("sync", "scalar", "vector", "tensor", "gpsimd")}
    def em(engine, fn):
        prog[engine].append(fn)
    def W(engine, s, v):
        if v > 0 and not (engine == "tensor" and s is pe):
            em(engine, lambda q, s=s, v=v: q.wait_ge(s, v))
    def EP(fn):
        em("tensor", lambda q, fn=fn: fn(q).then_inc(pe, 1))
        return c_pe.add(1)
    def EV(fn):
        if c_dv.v > 0:
            em("vector", lambda q, v=c_dv.v: q.wait_ge(dv, v))
        em("vector", lambda q, fn=fn: fn(q).then_inc(dv, 1))
        return c_dv.add(1)
    def EA(fn):
        if c_ac.v > 0:
            em("scalar", lambda q, v=c_ac.v: q.wait_ge(ac, v))
        em("scalar", lambda q, fn=fn: fn(q).then_inc(ac, 1))
        return c_ac.add(1)
    def DMA(engine, grp, fn):
        if grp.chained and grp.c.v > 0:
            em(engine, lambda q, s=grp.s, v=grp.c.v: q.wait_ge(s, v))
        em(engine, lambda q, fn=fn, s=grp.s: fn(q).then_inc(s, 16))
        return grp.c.add(16)
    g_const = Q("g_const")
    g_wssm = Q("g_wssm")
    g_stgld = Q("g_stgld", chained=True)
    g_stgst = Q("g_stgst", chained=True)
    g_pref = Q("g_pref")
    g_gz = Q("g_gz")
    g_az = Q("g_az")
    g_x = Q("g_x", chained=True)
    g_sc = Q("g_sc")
    g_sum = Q("g_sum")
    g_sumg = Q("g_sumg")
    g_em = Q("g_em")
    g_scg = Q("g_scg")
    g_idxl = Q("g_idxl")
    g_sq1 = Q("g_sq1")
    g_sq2 = Q("g_sq2")
    g_osb = [Q(f"g_osb{i}") for i in range(8)]
    g_ma = [Q(f"g_ma{i}") for i in range(8)]
    g_gath = Q("g_gath", chained=True)
    g_w1 = [Q("g_w1a", chained=True), Q("g_w1b", chained=True)]
    g_w2 = [Q("g_w2a", chained=True), Q("g_w2b", chained=True)]
    g_wo = Q("g_wo")
    g_gt = Q("g_gt", chained=True)
    g_xs = Q("g_xs", chained=True)
    g_mr = Q("g_mr", chained=True)
    g_out = Q("g_out", chained=True)
    g_dbq = Q("g_dbq", chained=True)
    def MACC(psv, pairs, waits=()):
        for sw, vw in waits:
            W("tensor", sw, vw)
        n = len(pairs)
        for i, (lf, rf) in enumerate(pairs):
            if i < n - 1:
                em("tensor", lambda q, lf=lf, rf=rf, st=(i == 0), psv=psv:
                   q.matmul(psv, lf, rf, start=st, stop=False))
            else:
                em("tensor", lambda q, lf=lf, rf=rf, st=(i == 0), psv=psv:
                   q.matmul(psv, lf, rf, start=st, stop=True).then_inc(pe, 1))
        return c_pe.add(1)

    def _final_stub():
        W("sync", dv, c_dv.v)
        W("sync", ac, c_ac.v)
        W("sync", pe, c_pe.v)
        for tt in range(16):
            for hb in range(2):
                DMA("sync", g_out, lambda q, tt=tt, hb=hb: q.dma_start(
                    out=out_t[128 * tt:128 * (tt + 1), :].bitcast(BF16)[:, 1024 * hb:1024 * (hb + 1)],
                    in_=zerot[:, :]))
        W("sync", g_out.s, g_out.c.v)

    def _finish():
        with nc.Block() as block:
            @block.sync
            def _(q):
                for f in prog["sync"]:
                    f(q)
            @block.scalar
            def _(q):
                for f in prog["scalar"]:
                    f(q)
            @block.vector
            def _(q):
                for f in prog["vector"]:
                    f(q)
            @block.tensor
            def _(q):
                for f in prog["tensor"]:
                    f(q)
            @block.gpsimd
            def _(q):
                for f in prog["gpsimd"]:
                    f(q)
        ctx.close()
        return nc

    constpairs = [
        (wg_sb.rearrange("p (k e) -> p k e", e=8), wg_in.rearrange("(k p) e -> p k e", p=128)),
        (identbf, identbf_in[:, :]), (identf, identf_in[:, :]),
        (onescol, onescol_in[:, :]), (onesrow, onesrow_in[:, :]),
        (onestt, onestt_in[:, :]),
        (ltstrict, ltstrict_in[:, :]), (lt16, lt16_in[:, :]),
        (tokid, tokid_in[:, :]), (jbase, jbase_in[:, :]), (selv, selv_in[:, :]),
        (jmat, jmat_in[:, :]),
        (b_dt, b_dt_in[:, :]), (a_log, a_log_in[:, :]), (dmat, dmat_in[:, :]),
        (b1m, b1m_in[:, :]), (b2bc, b2bc_in[:, :]),
    ]
    for dst, src in constpairs:
        dap = dst if isinstance(dst, bass.AP) else dst[:, :]
        DMA("sync", g_const, lambda q, d=dap, s=src: q.dma_start(out=d, in_=s))
    V_CONST = g_const.c.v

    for dst, src in [
        (wstb.rearrange("p (uk n) -> p uk n", uk=8), w_stT.rearrange("(uk p) n -> p uk n", p=128)),
        (wdtb.rearrange("p (uk n) -> p uk n", uk=8), w_dtT.rearrange("(uk p) n -> p uk n", p=128)),
        (wctb[:, :], w_cT[:, :]),
    ]:
        DMA("gpsimd", g_wssm, lambda q, d=dst, s=src: q.dma_start(out=d, in_=s))
    DMA("gpsimd", g_wssm,
        lambda q: q.dma_start(out=arena2[:, :].rearrange("p (k u) -> p k u", k=8),
                              in_=w_inT.rearrange("(k p) u -> p k u", p=128)))
    V_WSSM = g_wssm.c.v

    V_SENTI = EV(lambda q: q.memset(senti[:, :], BIG))
    V_ZEROT = EV(lambda q: q.memset(zerot[:, :], 0))
    W("gpsimd", dv, V_ZEROT)
    for qd in range(4):
        DMA("gpsimd", g_pref, lambda q, qd=qd: q.dma_start(
            out=idxg[1024 * qd:1024 * (qd + 1), :].rearrange("(p w) c -> p w c", p=128),
            in_=senti[:, 0:40].rearrange("p (w c) -> p w c", c=5)))
    DMA("gpsimd", g_pref, lambda q: q.dma_start(
        out=a2ais[:, :].rearrange("(p w) c -> p w c", p=128),
        in_=senti[:, 0:40].rearrange("p (w c) -> p w c", c=1)))
    V_PREFILL = g_pref.c.v
    for tt in range(16):
        DMA("gpsimd", g_gz, lambda q, tt=tt: q.dma_start(
            out=gdense[128 * tt:128 * (tt + 1), :], in_=zerot[:, :]))
    V_GZERO = g_gz.c.v
    for m in range(40):
        DMA("gpsimd", g_az, lambda q, m=m: q.dma_start(
            out=a2as[128 * m:128 * (m + 1), :], in_=zerot[:, :]))
    V_AZERO = g_az.c.v

    wst_free = [0, 0]
    for i in range(16):
        h = i % 2
        r, cblk = divmod(i, 2)
        W("gpsimd", g_stgst.s, wst_free[h])
        v = DMA("gpsimd", g_stgld, lambda q, r=r, cblk=cblk, h=h: q.dma_start(
            out=wstage[:, h * 2048:h * 2048 + 2048],
            in_=w1e[128 * r:128 * (r + 1), 2048 * cblk:2048 * (cblk + 1)]))
        W("gpsimd", g_stgld.s, v)
        wst_free[h] = DMA("gpsimd", g_stgst, lambda q, r=r, cblk=cblk, h=h: q.dma_start(
            out=w1bfd.rearrange("(a p) (k f) -> p k a f", p=128, f=128)
                [:, r, 16 * cblk:16 * (cblk + 1), :],
            in_=wstage[:, h * 2048:h * 2048 + 2048].rearrange(
                "p (a f) -> p a f", f=128)))
    for i in range(32):
        h = i % 2
        W("gpsimd", g_stgst.s, wst_free[h])
        v = DMA("gpsimd", g_stgld, lambda q, i=i, h=h: q.dma_start(
            out=wstage[:, h * 2048:h * 2048 + 1024],
            in_=w2e[128 * i:128 * (i + 1), :]))
        W("gpsimd", g_stgld.s, v)
        wst_free[h] = DMA("gpsimd", g_stgst, lambda q, i=i, h=h: q.dma_start(
            out=w2bfd.rearrange("(h2 a p) c -> p h2 a c", h2=2, p=128)[:, :, i, :],
            in_=wstage[:, h * 2048:h * 2048 + 1024].rearrange(
                "p (h2 c) -> p h2 c", h2=2)))
    V_WBF = g_stgst.c.v

    W("scalar", g_const.s, V_CONST)
    EA(lambda q: q.activation(rates[:, 0:1], a_log[:, :], AF.Relu))
    v = EA(lambda q: q.activation(rates[:, 1:2], a_log[:, :], AF.Abs))
    W("scalar", ac, v)
    v = EA(lambda q: q.activation(rates[:, 2:3], rates[:, 1:2], AF.Exp, scale=-1.0))
    W("scalar", ac, v)
    v = EA(lambda q: q.activation(rates[:, 3:4], rates[:, 2:3], AF.Ln, bias=1.0))
    W("vector", ac, v)
    v = EV(lambda q: q.tensor_add(negrate[:, :], rates[:, 0:1], rates[:, 3:4]))
    W("vector", dv, v)
    V_NEGRATE = EV(lambda q: q.tensor_scalar(negrate[:, :], negrate[:, :],
                                             RATE_EPS, -1.0, ALU.add, ALU.mult))
    EV(lambda q: q.memset(epst[:, :], LN_EPS))
    V_EPST = EV(lambda q: q.memset(mhalf[:, :], -0.5))

    W("tensor", g_const.s, V_CONST)
    W("vector", g_const.s, V_CONST)
    W("gpsimd", g_const.s, V_CONST)

    decay = a1f[:, 0:2048]
    vvn = a1f[:, 2048:4096]
    hh = a1f[:, 4096:6144]
    sumg = a1f[0:1, 6144:8192]
    hcast = arena1[:, 0:2048]
    mb_sb = a3f[:, 0:512]
    rb_sb = a3f[:, 512:1024]
    a2i = arena2.bitcast(I32)
    scg2 = a2f[:, 2048:3072]
    cmpscr = a2f[:, 0:128]
    msk = a2f[:, 128:256]
    pscan = a2f[:, 256:384]
    slotf = a2f[:, 384:512]
    fscr = a2f[:, 512:640]
    fscr2 = a2f[:, 640:768]
    gtv = a2f[:, 768:896]
    iscr = a2i[:, 896:1024]
    iscr2 = a2i[:, 1024:1152]
    desti = a2i[:, 1152:1280]
    pdesti = a2i[:, 1280:1408]
    prs5 = a2i[:, 1408:2048]
    xf32 = a1f[:, 0:4096]
    xnf = a1f[:, 4096:8192]
    v_xncast = [0, 0, 0, 0]
    for tc in range(4):
        pe_prev = c_pe.v
        if tc > 0:
            W("sync", pe, pe_prev)
        v_xf = DMA("sync", g_x, lambda q, tc=tc: q.dma_start(
            out=xf32.rearrange("p (k t) -> p k t", k=8),
            in_=xts[:, 512 * tc:512 * (tc + 1)].rearrange("(k p) t -> p k t", p=128)))
        W("tensor", g_x.s, v_xf)
        v_sum = MACC(pS[0:1, :],
                     [(onescol[:, :], xf32[:, 512 * k:512 * (k + 1)]) for k in range(8)])
        W("scalar", g_x.s, v_xf)
        if tc > 0:
            W("scalar", pe, pe_prev)
        for k in range(8):
            EA(lambda q, k=k: q.activation(
                xnf[:, 512 * k:512 * (k + 1)], xf32[:, 512 * k:512 * (k + 1)], AF.Square))
        v_sq = c_ac.v
        v_sumsq = MACC(p3[0:1, :],
                       [(onescol[:, :], xnf[:, 512 * k:512 * (k + 1)]) for k in range(8)],
                       waits=[(ac, v_sq)])
        mean_s = meanr[0:1, :]
        rstd_s = rstdr[0:1, :]
        var_s = stat1[0:1, :]
        sc3 = stat3[0:1, :]
        W("vector", pe, v_sumsq)
        EV(lambda q, m=mean_s: q.tensor_scalar(m, pS[0:1, :], 1.0 / DM, None, ALU.mult))
        EV(lambda q, v_=var_s: q.tensor_scalar(v_, p3[0:1, :], 1.0 / DM, None, ALU.mult))
        v = EV(lambda q, m=mean_s, s3=sc3: q.tensor_mul(s3, m, m))
        W("vector", dv, v)
        v = EV(lambda q, v_=var_s, s3=sc3: q.tensor_sub(v_, v_, s3))
        W("scalar", dv, v)
        if tc == 0:
            W("scalar", dv, V_EPST)
        v = EA(lambda q, v_=var_s, s3=sc3: q.activation(s3, v_, AF.Ln, bias=epst[0:1, 0:1]))
        W("scalar", ac, v)
        v = EA(lambda q, r=rstd_s, s3=sc3: q.activation(r, s3, AF.Exp, scale=mhalf[0:1, 0:1]))
        W("vector", ac, v)
        v = EV(lambda q, v_=var_s: q.tensor_scalar(v_, v_, LN_EPS, None, ALU.add))
        W("vector", dv, v)
        v = EV(lambda q, r=rstd_s, s3=sc3: q.tensor_mul(s3, r, r))
        W("vector", dv, v)
        v = EV(lambda q, v_=var_s, s3=sc3: q.tensor_mul(s3, s3, v_))
        W("vector", dv, v)
        v = EV(lambda q, s3=sc3: q.tensor_scalar(s3, s3, -0.5, 1.5, ALU.mult, ALU.add))
        W("vector", dv, v)
        v_newt = EV(lambda q, r=rstd_s, s3=sc3: q.tensor_mul(r, r, s3))
        v_bc = MACC(pW[:, 0:512], [(onesrow[:, :], mean_s)], waits=[(dv, v_newt)])
        v_bc2 = MACC(pW[:, 512:1024], [(onesrow[:, :], rstd_s)])
        W("scalar", pe, v_bc2)
        EA(lambda q: q.copy(mb_sb, pW[:, 0:512]))
        v_ev = EA(lambda q: q.copy(rb_sb, pW[:, 512:1024]))
        W("vector", ac, v_ev)
        W("vector", pe, v_sumsq)
        for k in range(8):
            EV(lambda q, k=k: q.tensor_sub(
                xnf[:, 512 * k:512 * (k + 1)], xf32[:, 512 * k:512 * (k + 1)], mb_sb))
            W("vector", dv, c_dv.v)
            EV(lambda q, k=k: q.tensor_mul(
                xnf[:, 512 * k:512 * (k + 1)], xnf[:, 512 * k:512 * (k + 1)], rb_sb))
        v_xn = c_dv.v
        W("scalar", dv, v_xn)
        for k in range(8):
            EA(lambda q, k=k, tc=tc: q.copy(
                xnbf[:, 2048 * k + 512 * tc:2048 * k + 512 * (tc + 1)],
                xnf[:, 512 * k:512 * (k + 1)]))
        v_xncast[tc] = c_ac.v
        v_sc = MACC(pS[0:8, :],
                    [(wg_sb[:, 8 * k:8 * (k + 1)], xnf[:, 512 * k:512 * (k + 1)])
                     for k in range(8)],
                    waits=[(dv, v_xn)])
        W("scalar", pe, v_sc)
        v_scsb = EA(lambda q: q.copy(scsb[:, :], pS[0:8, :]))
        W("tensor", ac, v_scsb)
        for j in range(4):
            EP(lambda q, j=j: q.transpose(
                pW[:, 8 * j:8 * (j + 1)], scsb[0:8, 128 * j:128 * (j + 1)], identf[0:8, 0:8]))
        v_tr = c_pe.v
        W("scalar", pe, v_tr)
        W("scalar", g_sc.s, g_sc.c.v)
        v_sctm = EA(lambda q: q.copy(sctm[:, :], pW[:, 0:32]))
        W("sync", ac, v_sctm)
        DMA("sync", g_sc, lambda q, tc=tc: q.dma_start(
            out=sc_in[512 * tc:512 * (tc + 1), :].rearrange("(j p) e -> p j e", p=128),
            in_=sctm[:, :].rearrange("p (j e) -> p j e", e=8)))
    V_SCIN = g_sc.c.v
    V_P1_PE = c_pe.v

    if stop_after < 2:
        _final_stub()
        return _finish()
    v_evac_ps = [0, 0]
    v_vvn_prev = 0
    v_dtabs_prev = 0
    W("tensor", g_wssm.s, V_WSSM)
    for tcc in range(4):
        W("tensor", ac, v_xncast[tcc])
        if tcc == 0:
            W("tensor", ac, v_xncast[3])
            W("tensor", g_sc.s, V_SCIN)
        for uk in range(16):
            pbuf = [p0, p1][uk % 2]
            W("tensor", ac, v_evac_ps[uk % 2])
            vm = MACC(pbuf[:, :],
                      [(arena2[:, 2048 * k + 128 * uk:2048 * k + 128 * (uk + 1)],
                        xnbf[:, 2048 * k + 512 * tcc:2048 * k + 512 * (tcc + 1)])
                       for k in range(8)])
            dstb = ubf if uk < 8 else arena3
            W("scalar", pe, vm)
            v_evac_ps[uk % 2] = EA(lambda q, d=dstb, uk=uk, tcc=tcc, pbuf=pbuf: q.copy(
                d[:, 2048 * (uk % 8) + 512 * tcc:2048 * (uk % 8) + 512 * (tcc + 1)],
                pbuf[:, :]))
        v_u_done = c_ac.v
        v_b = MACC(p2[:, :],
                   [(wstb[:, 128 * uk:128 * (uk + 1)],
                     ubf[:, 2048 * uk + 512 * tcc:2048 * uk + 512 * (tcc + 1)])
                    for uk in range(8)],
                   waits=[(ac, v_u_done), (dv, v_vvn_prev)])
        v_dt = MACC(p3[:, :],
                    [(wdtb[:, 128 * uk:128 * (uk + 1)],
                      ubf[:, 2048 * uk + 512 * tcc:2048 * uk + 512 * (tcc + 1)])
                     for uk in range(8)],
                    waits=[(ac, v_dtabs_prev)])
        W("scalar", pe, v_dt)
        W("scalar", dv, c_dv.v)
        EA(lambda q: q.activation(dts1[:, :], p3[:, :], AF.Relu, bias=b_dt[:, 0:1]))
        v = EA(lambda q: q.activation(dts2[:, :], p3[:, :], AF.Abs, bias=b_dt[:, 0:1]))
        v_dtabs_prev = v
        W("scalar", ac, v)
        v = EA(lambda q: q.activation(dts3[:, :], dts2[:, :], AF.Exp, scale=-1.0))
        W("scalar", ac, v)
        v_l = EA(lambda q: q.activation(dts2[:, :], dts3[:, :], AF.Ln, bias=1.0))
        W("vector", ac, v_l)
        v = EV(lambda q: q.tensor_add(dts1[:, :], dts1[:, :], dts2[:, :]))
        W("vector", dv, v)
        v_dtv = EV(lambda q: q.tensor_scalar_min(dts1[:, :], dts1[:, :], 10.0))
        W("vector", dv, v_dtv)
        EV(lambda q, tcc=tcc: q.reduce_sum(dtsum[:, tcc:tcc + 1], dts1[:, :], axis=AX.X))
        W("scalar", dv, v_dtv)
        if tcc == 0:
            W("scalar", dv, V_NEGRATE)
        v_dec = EA(lambda q, tcc=tcc: q.activation(
            decay[:, 512 * tcc:512 * (tcc + 1)], dts1[:, :], AF.Exp, scale=negrate[:, 0:1]))
        W("vector", ac, v_dec)
        W("vector", pe, v_b)
        v_vvn_prev = EV(lambda q, tcc=tcc: q.scalar_tensor_tensor(
            vvn[:, 512 * tcc:512 * (tcc + 1)], decay[:, 512 * tcc:512 * (tcc + 1)],
            1.0, p2[:, :], ALU.subtract, ALU.mult))
    W("vector", dv, c_dv.v)
    v = EV(lambda q: q.tensor_tensor_scan(hh, decay, vvn, 0.0, ALU.mult, ALU.subtract))
    W("vector", dv, v)
    v = EV(lambda q: q.tensor_add(dtsum[:, 0:1], dtsum[:, 0:1], dtsum[:, 1:2]))
    W("vector", dv, v)
    v = EV(lambda q: q.tensor_add(dtsum[:, 2:3], dtsum[:, 2:3], dtsum[:, 3:4]))
    W("vector", dv, v)
    v_dtt = EV(lambda q: q.tensor_add(dtsum[:, 0:1], dtsum[:, 0:1], dtsum[:, 2:3]))
    W("scalar", dv, v_dtt)
    v_at = EA(lambda q: q.activation(dtsum[:, 1:2], dtsum[:, 0:1], AF.Exp,
                                     scale=negrate[:, 0:1]))
    W("sync", ac, v_at)
    W("sync", dv, c_dv.v)
    DMA("sync", g_sum, lambda q: q.dma_start(out=sum_in[0:1, :].rearrange("o p -> p o"), in_=dtsum[:, 1:2]))
    v_sumdma = DMA("sync", g_sum, lambda q: q.dma_start(out=sum_in[1:2, :].rearrange("o p -> p o"), in_=hh[:, 2047:2048]))
    W("gpsimd", g_sum.s, v_sumdma)
    em("gpsimd", lambda q: q.collective_compute(
        "AllGather", ALU.bypass, [list(range(NCORE))],
        ins=[sum_in.ap()], outs=[sum_out.ap()]).then_inc(cc2, 1))
    W("gpsimd", cc2, 1)
    v_sumg = DMA("gpsimd", g_sumg, lambda q: q.dma_start(out=sumg[0:1, :], in_=sum_out.rearrange("a b -> (a b)")[None, :]))
    W("vector", g_sumg.s, v_sumg)
    EV(lambda q: q.memset(hin[:, :], 0.0))
    for j in range(NCORE):
        W("vector", dv, c_dv.v)
        EV(lambda q, j=j: q.tensor_scalar(hf1[:, :], sumg[0:1, 256 * j:256 * j + 128],
                                          -1.0, None, ALU.add))
        W("vector", dv, c_dv.v)
        EV(lambda q, j=j: q.tensor_scalar(hf1[:, :], hf1[:, :],
                                          selv[0:1, j:j + 1], 1.0, ALU.mult, ALU.add))
        W("vector", dv, c_dv.v)
        EV(lambda q: q.tensor_mul(hf2[:, :], hin[:, :], hf1[:, :]))
        W("vector", dv, c_dv.v)
        EV(lambda q, j=j: q.tensor_scalar(hf1[:, :], sumg[0:1, 256 * j + 128:256 * j + 256],
                                          selv[0:1, j:j + 1], None, ALU.mult))
        W("vector", dv, c_dv.v)
        EV(lambda q: q.tensor_add(hin[:, :], hf2[:, :], hf1[:, :]))
    v_hin = c_dv.v
    v_htr = MACC(pW[:, 0:1], [(hin[:, :], identf[0:1, 0:1])], waits=[(dv, v_hin)])
    W("vector", pe, v_htr)
    v = EV(lambda q: q.tensor_copy(h0col[:, :], pW[:, 0:1]))
    W("vector", dv, v)
    v_scan2 = EV(lambda q: q.tensor_tensor_scan(hh, decay, vvn, h0col[:, 0:1],
                                                ALU.mult, ALU.subtract))
    W("vector", dv, v_scan2)
    v_hcast = EV(lambda q: q.tensor_copy(hcast[:, :], hh))
    if debug:
        W("sync", dv, v_scan2)
        DMA("sync", g_dbq, lambda q: q.dma_start(out=hh_dbg[:, :], in_=hh))

    silu4 = a1f[:, 2048:6144]
    v_z_evac = [0, 0]
    v_mul_last = 0
    for tcc in range(4):
        gate3 = arena3.rearrange("p (k c) -> p k c", c=2048)[:, :, 512 * tcc:512 * (tcc + 1)]
        W("scalar", dv, v_hcast if tcc == 0 else v_mul_last)
        v_sg = EA(lambda q, gate3=gate3: q.activation(
            silu4.rearrange("p (k c) -> p k c", c=512), gate3, AF.Silu))
        for k in range(8):
            pbuf = [p0, p1][k % 2]
            vy = MACC(pbuf[:, :],
                      [(wctb[:, 128 * k:128 * (k + 1)],
                        hcast[:, 512 * tcc:512 * (tcc + 1)])],
                      waits=[(dv, v_hcast), (dv, v_z_evac[k % 2])])
            W("vector", ac, v_sg)
            W("vector", pe, vy)
            v_mul_last = EV(lambda q, pbuf=pbuf, k=k: q.tensor_mul(
                dts2[:, :], pbuf[:, :], silu4[:, 512 * k:512 * (k + 1)]))
            v_z_evac[k % 2] = EV(lambda q, k=k, tcc=tcc: q.scalar_tensor_tensor(
                ubf[:, 2048 * k + 512 * tcc:2048 * k + 512 * (tcc + 1)],
                xnbf[:, 2048 * k + 512 * tcc:2048 * k + 512 * (tcc + 1)],
                dmat[:, k:k + 1], dts2[:, :], ALU.mult, ALU.add))
    V_Z = c_dv.v
    V_P3_PE = c_pe.v

    if stop_after < 3:
        _final_stub()
        return _finish()
    W("gpsimd", g_sc.s, V_SCIN)
    em("gpsimd", lambda q: q.collective_compute(
        "AllGather", ALU.bypass, [list(range(NCORE))],
        ins=[sc_in.ap()], outs=[sc_out.ap()]).then_inc(cc1, 1))
    W("gpsimd", cc1, 1)
    v_scg = DMA("gpsimd", g_scg, lambda q: q.dma_start(
        out=scg2.rearrange("p (tl e) -> p tl e", e=8),
        in_=sc_out.rearrange("(p tl) e -> p tl e", p=128)))
    emask = a1f[:, 0:1024]
    W("sync", dv, V_Z)
    v_eml = DMA("sync", g_em, lambda q: q.dma_start(out=emask, in_=emask_in[:, :]))
    W("vector", g_scg.s, v_scg)
    W("vector", g_em.s, v_eml)
    W("vector", dv, V_Z)
    v = EV(lambda q: q.tensor_mul(a1f[:, 1024:2048], scg2, emask))
    W("vector", dv, v)
    v_vE = EV(lambda q: q.tensor_reduce(
        vE[:, :], a1f[:, 1024:2048].rearrange("p (tl e) -> p tl e", e=8),
        axis=AX.X, op=ALU.add))
    locol = dts2[:, 0:1]
    stepcol = dts2[:, 1:2]
    scol = dts2[:, 2:3]
    tvec = dts2[:, 8:16]
    ge8 = dts2[:, 16:24]
    cnt8 = dts2[:, 24:32]
    W("vector", dv, v_vE)
    EV(lambda q: q.memset(locol, -64.0))
    EV(lambda q: q.memset(stepcol, 128.0 / 9.0))
    EV(lambda q: q.tensor_scalar(tvec, jmat[:, :], stepcol[:, 0:1],
                                 locol[:, 0:1], ALU.mult, ALU.add))
    for rnd in range(14):
        for j in range(8):
            EV(lambda q, j=j: q.tensor_scalar(cmpscr, vE[:, :], tvec[:, j:j + 1],
                                              0.0, ALU.is_ge, ALU.add,
                                              accum_out=cnt8[:, j:j + 1]))
        v_cmp = c_dv.v
        v_cnt = MACC(p2[:, 0:8], [(onestt[:, :], cnt8)], waits=[(dv, v_cmp)])
        W("vector", pe, v_cnt)
        EV(lambda q: q.tensor_scalar(ge8, p2[:, 0:8], KSEL - 0.5, None, ALU.is_ge))
        EV(lambda q: q.reduce_sum(scol, ge8, axis=AX.X))
        EV(lambda q: q.scalar_tensor_tensor(locol, stepcol, scol[:, 0:1],
                                            locol, ALU.mult, ALU.add))
        EV(lambda q: q.tensor_scalar(stepcol, stepcol, 1.0 / 9.0, None, ALU.mult))
        EV(lambda q: q.tensor_scalar(tvec, jmat[:, :], stepcol[:, 0:1],
                                     locol[:, 0:1], ALU.mult, ALU.add))
    v_mid = c_dv.v
    if debug:
        W("sync", dv, v_mid)
        DMA("sync", g_dbq, lambda q: q.dma_start(out=thr_dbg[0:1, :], in_=dts2[0:1, 0:8]))
    v_msk = EV(lambda q: q.tensor_scalar(msk, vE[:, :], locol[:, 0:1],
                                         None, ALU.is_ge))
    W("vector", dv, v_msk)
    v_ps = EV(lambda q: q.tensor_tensor_scan(pscan, onestt[:, :], msk,
                                             0.0, ALU.mult, ALU.add))
    v_exm = MACC(pW[:, 0:1], [(ltstrict[:, :], pscan[:, 127:128])],
                 waits=[(dv, v_ps)])
    v_bnm = MACC(pW[:, 1:2], [(lt16[:, :], pscan[:, 127:128])])
    W("vector", pe, v_bnm)
    EV(lambda q: q.tensor_copy(exc[:, :], pW[:, 0:1]))
    W("vector", dv, c_dv.v)
    EV(lambda q: q.tensor_copy(bnd[:, :], pW[:, 1:2]))
    W("vector", dv, c_dv.v)
    EV(lambda q: q.tensor_scalar(excm1[:, :], exc[:, :], -1.0, None, ALU.add))
    W("vector", dv, c_dv.v)
    EV(lambda q: q.tensor_scalar(slotf, pscan, excm1[:, 0:1],
                                 None, ALU.add))
    W("vector", dv, c_dv.v)
    EV(lambda q: q.tensor_scalar(fscr[:, :], slotf, bnd[:, 0:1],
                                 None, ALU.subtract))
    W("vector", dv, c_dv.v)
    EV(lambda q: q.tensor_scalar(fscr2, slotf, KSEL - 0.5, None, ALU.is_lt))
    W("vector", dv, c_dv.v)
    EV(lambda q: q.tensor_mul(fscr2, fscr2, msk))
    W("vector", dv, c_dv.v)
    EV(lambda q: q.tensor_scalar(gtv, fscr[:, :], CAP - 0.5, None, ALU.is_lt))
    W("vector", dv, c_dv.v)
    EV(lambda q: q.tensor_mul(gtv, gtv, msk))
    W("vector", dv, c_dv.v)
    EV(lambda q: q.tensor_copy(iscr, slotf))
    W("vector", dv, c_dv.v)
    EV(lambda q: q.tensor_scalar(iscr2, iscr, 127, None, ALU.bitwise_and))
    W("vector", dv, c_dv.v)
    EV(lambda q: q.tensor_scalar(iscr2, iscr2, 5, None, ALU.logical_shift_left))
    W("vector", dv, c_dv.v)
    EV(lambda q: q.tensor_scalar(iscr, iscr, 7, None, ALU.logical_shift_right))
    W("vector", dv, c_dv.v)
    EV(lambda q: q.tensor_add(iscr2, iscr2, iscr))
    W("vector", dv, c_dv.v)
    EV(lambda q: q.memset(desti, BIG))
    W("vector", dv, c_dv.v)
    EV(lambda q: q.tensor_copy(iscr, fscr2))
    W("vector", dv, c_dv.v)
    EV(lambda q: q.copy_predicated(desti, iscr, iscr2))
    W("vector", dv, c_dv.v)
    EV(lambda q: q.tensor_copy(iscr2, fscr[:, :]))
    W("vector", dv, c_dv.v)
    EV(lambda q: q.tensor_scalar(iscr2, iscr2, jbase[:, 0:1], None, ALU.add))
    W("vector", dv, c_dv.v)
    EV(lambda q: q.memset(pdesti, BIG))
    W("vector", dv, c_dv.v)
    EV(lambda q: q.tensor_copy(iscr, gtv))
    W("vector", dv, c_dv.v)
    EV(lambda q: q.copy_predicated(pdesti, iscr, iscr2))
    W("vector", dv, c_dv.v)
    W("scalar", dv, c_dv.v)
    v_gtv = EA(lambda q: q.activation(gtv, vE[:, :], AF.Sigmoid))
    EV(lambda q: q.tensor_scalar(iscr, tokid[:, :], 2047, None, ALU.bitwise_and))
    v_lid = c_dv.v
    W("vector", dv, v_lid)
    EV(lambda q: q.tensor_copy(prs5[:, 0:640:5], tokid[:, :]))
    W("vector", ac, v_gtv)
    EV(lambda q: q.tensor_copy(prs5[:, 1:640:5], gtv.bitcast(I32)))
    EV(lambda q: q.tensor_copy(prs5[:, 2:640:5], pdesti))
    EV(lambda q: q.tensor_copy(prs5[:, 3:640:5], iscr))
    EV(lambda q: q.tensor_copy(iscr2, fscr[:, :]))
    EV(lambda q: q.tensor_scalar(msk.bitcast(I32), iscr2, 7, None, ALU.logical_shift_right))
    EV(lambda q: q.tensor_scalar(iscr2, iscr2, 127, None, ALU.bitwise_and))
    EV(lambda q: q.tensor_scalar(pscan.bitcast(I32), iscr2, 2, None, ALU.logical_shift_left))
    EV(lambda q: q.tensor_add(iscr2, iscr2, pscan.bitcast(I32)))
    EV(lambda q: q.tensor_add(iscr2, iscr2, msk.bitcast(I32)))
    EV(lambda q: q.tensor_copy(fscr2, iscr2))
    EV(lambda q: q.tensor_scalar(fscr2, fscr2, jbase[:, 0:1], None, ALU.add))
    EV(lambda q: q.tensor_copy(iscr2, fscr2))
    EV(lambda q: q.memset(msk.bitcast(I32), BIG))
    EV(lambda q: q.tensor_copy(fscr2, pdesti))
    EV(lambda q: q.tensor_scalar(fscr2, fscr2, 1.0e8, None, ALU.is_lt))
    EV(lambda q: q.tensor_copy(iscr, fscr2))
    EV(lambda q: q.copy_predicated(msk.bitcast(I32), iscr, iscr2))
    v_quad = EV(lambda q: q.tensor_copy(prs5[:, 4:640:5], msk.bitcast(I32)))
    W("gpsimd", dv, v_quad)
    W("gpsimd", g_pref.s, V_PREFILL)
    def _mk_scat(tg):
        def f(q):
            q.indirect_dma_start(
                out=idxg[:, :],
                out_offset=IndirectOffsetOnAxis(desti[:, tg:tg + 1], axis=0),
                in_=prs5[:, 5 * tg:5 * (tg + 1)], in_offset=None,
                bounds_check=bc4095[0], oob_is_err=False).then_inc(g_sq1.s, 16)
        return f
    bc4095 = [None]
    def _set_regs(q):
        bc4095[0] = q.to_reg(KSEL - 1)
        bcNPR[0] = q.to_reg(NPR - 1)
        bcNTOK[0] = q.to_reg(NTOK - 1)
        bcT[0] = q.to_reg(T - 1)
        return q.engine_nop()
    bcNPR = [None]; bcNTOK = [None]; bcT = [None]
    em("gpsimd", lambda q: _set_regs(q))
    for tg in range(128):
        em("gpsimd", _mk_scat(tg))
        g_sq1.c.add(16)
    W("gpsimd", g_sq1.s, g_sq1.c.v)
    v_idxl = DMA("gpsimd", g_idxl, lambda q: q.dma_start(
        out=idxg_sb[:, :].rearrange("p (w c) -> p w c", c=5),
        in_=idxg.rearrange("(p w) c -> p w c", p=128)))
    W("vector", g_idxl.s, v_idxl)
    for g in range(32):
        EV(lambda q, g=g: q.tensor_copy(fscr[:, g:g + 1], idxg_sb[:, 5 * g:5 * g + 1]))
    W("vector", dv, c_dv.v)
    v = EV(lambda q: q.tensor_scalar(fscr2[:, 0:32], fscr[:, 0:32], NTOK - 0.5,
                                     None, ALU.is_lt))
    W("vector", dv, v)
    v_gm = EV(lambda q: q.tensor_mul(
        gmask[:, :], fscr2[:, 0:32],
        idxg_sb.bitcast(F32)[:, :].rearrange("p (w c) -> p w c", c=5)[:, :, 1]))
    W("gpsimd", dv, v_gm)
    def _mk_idscat(g):
        def f(q):
            q.indirect_dma_start(
                out=a2ais[:, :],
                out_offset=IndirectOffsetOnAxis(idxg_sb[:, 5 * g + 4:5 * g + 5], axis=0),
                in_=idxg_sb[:, 5 * g + 3:5 * g + 4], in_offset=None,
                bounds_check=bcNPR[0], oob_is_err=False).then_inc(g_sq2.s, 16)
        return f
    for g in range(32):
        em("gpsimd", _mk_idscat(g))
        g_sq2.c.add(16)
    v_idsf = g_sq2.c.v

    if stop_after < 4:
        _final_stub()
        return _finish()
    xg_bufs = [a2f[:, 0:4096], xnbf.bitcast(F32)[:, 0:4096]]
    exT_bufs = [arena2[:, 8192:12288], xnbf[:, 8192:12288]]
    xnr = a2f[:, 6656:7680]
    sqs = xnr
    xnrb4 = xnbf[:, 12288:16384]

    v_w1l = [0, 0]
    v_w2l = [0, 0]
    v_epi_dv = 0
    v_epi_psum = 0
    v_epi_half0 = 0
    v_tev = [0, 0]
    v_ext = [0, 0]
    v_xg_read = [0, 0]
    v_T_done = [0] * 9
    v_xnrb = {}
    v_gath_v = {}

    def emit_gather(rcn):
        buf = rcn % 2
        xgv = xg_bufs[buf].rearrange("p (b f) -> p b f", b=4)
        if rcn == 0:
            W("gpsimd", pe, V_P3_PE)
            W("gpsimd", g_sq2.s, v_idsf)
        elif rcn == 1:
            W("gpsimd", dv, V_Z)
        else:
            W("gpsimd", dv, v_xg_read[buf])
        for gg in range(4):
            g = 4 * rcn + gg
            DMA("gpsimd", g_gath, lambda q, g=g, gg=gg, xgv=xgv: q.indirect_dma_start(
                out=xgv[:, gg, :], out_offset=None, in_=xfull[:, :],
                in_offset=IndirectOffsetOnAxis(idxg_sb[:, 5 * g:5 * g + 1], axis=0),
                bounds_check=bcNTOK[0], oob_is_err=False))
        v_gath_v[rcn] = g_gath.c.v

    mcol4 = dts1[:, 0:4]
    s2col4 = dts1[:, 4:8]
    vcol4 = dts1[:, 8:12]
    ncol4 = dts1[:, 12:16]
    rcol4 = dts1[:, 16:20]

    def emit_ln(rcn):
        buf = rcn % 2
        v_gath = v_gath_v[rcn]
        xgv = xg_bufs[buf].rearrange("p (b f) -> p b f", b=4)
        if rcn >= 1:
            W("vector", pe, v_T_done[rcn - 1])
        W("vector", g_gath.s, v_gath)
        EV(lambda q, xgv=xgv: q.tensor_reduce(mcol4, xgv, axis=AX.X, op=ALU.add))
        for gg in range(4):
            EV(lambda q, gg=gg, xgv=xgv: q.scalar_tensor_tensor(
                sqs, xgv[:, gg, :], 1.0, xgv[:, gg, :], ALU.mult, ALU.mult,
                accum_out=s2col4[:, gg:gg + 1]))
        EV(lambda q: q.tensor_scalar(mcol4, mcol4, 1.0 / DM, None, ALU.mult))
        EV(lambda q: q.tensor_scalar(s2col4, s2col4, 1.0 / DM, None, ALU.mult))
        EV(lambda q: q.tensor_mul(vcol4, mcol4, mcol4))
        v = EV(lambda q: q.tensor_sub(vcol4, s2col4, vcol4))
        W("scalar", dv, v)
        if rcn == 0:
            W("scalar", pe, V_P3_PE)
        EA(lambda q: q.activation(ncol4, vcol4, AF.Ln, bias=epst[:, 0:1]))
        v_r0 = EA(lambda q: q.activation(rcol4, ncol4, AF.Exp, scale=mhalf[:, 0:1]))
        W("vector", ac, v_r0)
        EV(lambda q: q.tensor_scalar(vcol4, vcol4, LN_EPS, None, ALU.add))
        EV(lambda q: q.tensor_mul(ncol4, rcol4, rcol4))
        EV(lambda q: q.tensor_mul(ncol4, ncol4, vcol4))
        EV(lambda q: q.tensor_scalar(ncol4, ncol4, -0.5, 1.5, ALU.mult, ALU.add))
        EV(lambda q: q.tensor_mul(rcol4, rcol4, ncol4))
        v_last_read = 0
        for gg in range(4):
            v_last_read = EV(lambda q, gg=gg, xgv=xgv: q.tensor_scalar(
                xnr, xgv[:, gg, :], mcol4[:, gg:gg + 1], None, ALU.subtract))
            v = EV(lambda q, gg=gg: q.tensor_scalar(
                xnrb4[:, 1024 * gg:1024 * (gg + 1)], xnr, rcol4[:, gg:gg + 1],
                None, ALU.mult))
            v_xnrb[(rcn, gg)] = v
        v_xg_read[buf] = v_last_read

    def emit_T(rcn):
        buf = rcn % 2
        for gg in range(4):
            W("tensor", dv, v_xnrb[(rcn, gg)])
            for half in range(2):
                W("tensor", ac, v_tev[0])
                for k4 in range(4):
                    k = 4 * half + k4
                    EP(lambda q, gg=gg, k=k, k4=k4: q.transpose(
                        pT[:, 128 * k4:128 * (k4 + 1)],
                        xnrb4[:, 1024 * gg + 128 * k:1024 * gg + 128 * (k + 1)],
                        identbf[:, :]))
                v_tp = c_pe.v
                W("scalar", pe, v_tp)
                v_tev[0] = EA(lambda q, gg=gg, half=half, buf=buf: q.copy(
                    exT_bufs[buf].rearrange("p (k c) -> p k c", c=512)
                        [:, 4 * half:4 * half + 4, 128 * gg:128 * (gg + 1)],
                    pT[:, :].rearrange("p (k f) -> p k f", f=128)))
        v_ext[buf] = c_ac.v
        v_T_done[rcn] = c_pe.v

    v_epi_by_rc = {}

    def emit_scatter(rcn):
        obase = 4096 if rcn % 2 == 0 else 12288
        if rcn == 0:
            W("gpsimd", g_az.s, V_AZERO)
        for sub in range(4):
            g = 4 * rcn + sub
            osb = arena3[:, obase + 1024 * sub:obase + 1024 * (sub + 1)]
            W("gpsimd", dv, v_epi_by_rc[rcn])
            em("gpsimd", lambda q, g=g, osb=osb, rcn=rcn: q.indirect_dma_start(
                out=a2as[:, :],
                out_offset=IndirectOffsetOnAxis(idxg_sb[:, 5 * g + 2:5 * g + 3], axis=0),
                in_=osb, in_offset=None,
                bounds_check=bcNPR[0], oob_is_err=False).then_inc(g_osb[rcn].s, 16))
            g_osb[rcn].c.add(16)

    emit_gather(0)
    emit_ln(0)
    emit_T(0)
    for rc in range(8):
        if rc < 7:
            emit_gather(rc + 1)
            emit_ln(rc + 1)
        if rc >= 1:
            emit_scatter(rc - 1)
        exT = exT_bufs[rc % 2]
        v_gelu_pb = [0, 0, 0, 0]
        for ffc in range(32):
            hbuf = ffc % 2
            w1c = arena3[:, 1024 * hbuf:1024 * (hbuf + 1)]
            if rc == 0 and ffc < 2:
                W("sync", g_stgst.s, V_WBF)
                W("sync", pe, V_P3_PE)
            else:
                W("sync", pe, v_w1l[hbuf])
            vl = DMA("sync", g_w1[hbuf], lambda q, ffc=ffc, w1c=w1c: q.dma_start(
                out=w1c, in_=w1bfd[128 * ffc:128 * (ffc + 1), :]))
            pb = [p0, p1, p2, p3][ffc % 4]
            v_mm1 = MACC(pb[:, :],
                         [(w1c[:, 128 * k:128 * (k + 1)], exT[:, 512 * k:512 * (k + 1)])
                          for k in range(8)],
                         waits=[(g_w1[hbuf].s, vl), (ac, v_ext[rc % 2]),
                                (ac, v_gelu_pb[ffc % 4]),
                                (dv, v_epi_psum if ffc < 4 else 0)])
            v_w1l[hbuf] = v_mm1
            W("scalar", pe, v_mm1)
            v_gelu_pb[ffc % 4] = EA(lambda q, ffc=ffc, pb=pb: q.activation(
                arena1[:, 512 * ffc:512 * (ffc + 1)], pb[:, :], AF.Gelu,
                bias=b1m[:, ffc:ffc + 1]))
        v_gelu_all = c_ac.v
        if rc < 7:
            emit_T(rc + 1)
        for hh2 in range(2):
            for ffc in range(32):
                wbuf = ffc % 2
                w2c = arena3[:, 2048 + 1024 * wbuf:2048 + 1024 * wbuf + 512]
                W("sync", pe, v_w2l[wbuf])
                vl = DMA("sync", g_w2[wbuf], lambda q, ffc=ffc, hh2=hh2, w2c=w2c: q.dma_start(
                    out=w2c,
                    in_=w2bfd[4096 * hh2 + 128 * ffc:4096 * hh2 + 128 * (ffc + 1), :]))
                for sub in range(4):
                    pbs = [p0, p1, p2, p3][sub]
                    lastk = (ffc == 31)
                    wts = [(g_w2[wbuf].s, vl)]
                    if ffc == 0:
                        wts.append((ac, v_gelu_all))
                        if hh2 == 1:
                            wts.append((dv, v_epi_half0))
                    for sw, vw in wts:
                        W("tensor", sw, vw)
                    certify = (sub == 3) or lastk
                    em("tensor", lambda q, ffc=ffc, sub=sub, pbs=pbs, w2c=w2c,
                       lastk=lastk, certify=certify:
                       (q.matmul(pbs[:, :],
                                 arena1[:, 512 * ffc + 128 * sub:512 * ffc + 128 * (sub + 1)],
                                 w2c, start=(ffc == 0), stop=lastk).then_inc(pe, 1)
                        if certify else
                        q.matmul(pbs[:, :],
                                 arena1[:, 512 * ffc + 128 * sub:512 * ffc + 128 * (sub + 1)],
                                 w2c, start=(ffc == 0), stop=lastk)))
                    if certify:
                        c_pe.add(1)
                v_w2l[wbuf] = c_pe.v
            v_mm2h = c_pe.v
            obase = 4096 if rc % 2 == 0 else 12288
            W("vector", pe, v_mm2h)
            for sub in range(4):
                pbs = [p0, p1, p2, p3][sub]
                oev = a3f[:, 4096 + 512 * sub:4096 + 512 * (sub + 1)][:, 0:512]
                EV(lambda q, pbs=pbs, hh2=hh2, oev=oev: q.tensor_add(
                    oev, pbs[:, :], b2bc[:, 512 * hh2:512 * (hh2 + 1)]))
            v_epi_psum = c_dv.v
            if hh2 == 0:
                v_epi_half0 = v_epi_psum
            if rc > 1 and hh2 == 0:
                W("vector", g_osb[rc - 2].s, 64)
            for sub in range(4):
                g = 4 * rc + sub
                oev = a3f[:, 4096 + 512 * sub:4096 + 512 * (sub + 1)][:, 0:512]
                osb = arena3[:, obase + 1024 * sub:obase + 1024 * (sub + 1)]
                v_epi_dv = EV(lambda q, g=g, hh2=hh2, osb=osb, oev=oev: q.tensor_scalar(
                    osb[:, 512 * hh2:512 * (hh2 + 1)], oev, gmask[:, g:g + 1],
                    None, ALU.mult))
        v_epi_by_rc[rc] = v_epi_dv
    emit_scatter(7)
    if stop_after < 5:
        _final_stub()
        return _finish()
    for rc in range(8):
        W("gpsimd", g_osb[rc].s, 64)
    em("gpsimd", lambda q: q.collective_compute(
        "AllToAll", ALU.bypass, [list(range(NCORE))],
        ins=[a2as.ap()], outs=[a2ar.ap()]).then_inc(cc3, 1))
    em("gpsimd", lambda q: q.collective_compute(
        "AllToAll", ALU.bypass, [list(range(NCORE))],
        ins=[a2ais.ap()], outs=[a2air.ap()]).then_inc(cc4, 1))
    W("gpsimd", cc3, 1)
    W("gpsimd", cc4, 1)
    if stop_after < 4.3:
        _final_stub()
        return _finish()
    v_wol = DMA("gpsimd", g_wo, lambda q: q.dma_start(
        out=arena2[:, 0:8192].rearrange("p (k e) -> p k e", k=8),
        in_=w_outT.rearrange("(k p) e -> p k e", p=128)))
    W("gpsimd", g_gz.s, V_GZERO)
    v_ridl = DMA("gpsimd", g_mr, lambda q: q.dma_start(
        out=ridl[:, :].rearrange("p (j w) -> p j w", w=5),
        in_=a2air.rearrange("(j p w) o -> p j (w o)", p=128, w=5)))
    g_mg = [Q("g_mg0"), Q("g_mg1")]
    mgbuf = [arena3[:, 0:5120], arena3[:, 5120:10240]]
    vload = {}

    def mload(j):
        b = j % 2
        return DMA("gpsimd", g_mg[b], lambda q, j=j, b=b: q.dma_start(
            out=mgbuf[b].rearrange("p (c d) -> p c d", c=5),
            in_=a2ar[CAP * j:CAP * (j + 1), :].rearrange("(c p) d -> p c d", p=128)))

    W("gpsimd", g_mr.s, v_ridl)
    vload[0] = mload(0)
    vload[1] = mload(1)
    for j in range(8):
        b = j % 2
        W("gpsimd", g_mg[b].s, vload[j])
        if j > 0:
            W("gpsimd", g_ma[j - 1].s, 80)
        for c5 in range(5):
            m = 5 * j + c5
            em("gpsimd", lambda q, j=j, c5=c5, m=m, b=b: q.indirect_dma_start(
                out=gdense[:, :],
                out_offset=IndirectOffsetOnAxis(ridl[:, m:m + 1], axis=0),
                in_=mgbuf[b][:, 1024 * c5:1024 * c5 + 1024], in_offset=None,
                bounds_check=bcT[0], oob_is_err=False,
                compute_op=ALU.add).then_inc(g_ma[j].s, 16))
            g_ma[j].c.add(16)
        if j + 2 < 8:
            W("gpsimd", g_ma[j].s, 80)
            vload[j + 2] = mload(j + 2)
    if stop_after < 4.7:
        _final_stub()
        return _finish()
    gsil = a1f[:, 0:1024]
    xstile = a1f[:, 1024:2048]
    outtile = a1f[:, 2048:3072]
    v_sdbg_prev = [0]
    v_gl = 0
    v_out_ev = 0
    n_tt = 16 if stop_after >= 4.9 else (4 if stop_after >= 4.85 else 1)
    for tt in range(n_tt):
        if tt == 0:
            for jj in range(8):
                W("sync", g_ma[jj].s, 80)
        else:
            W("sync", dv, v_out_ev)
        vg = DMA("sync", g_gt, lambda q, tt=tt: q.dma_start(
            out=gtile[:, :], in_=gdense[128 * tt:128 * (tt + 1), :]))
        vx = DMA("sync", g_xs, lambda q, tt=tt: q.dma_start(
            out=xstile, in_=xs[128 * tt:128 * (tt + 1), :]))
        W("scalar", g_gt.s, vg)
        if debug:
            W("scalar", g_dbq.s, g_dbq.c.v)
        v_sil = EA(lambda q: q.activation(gsil, gtile[:, :], AF.Silu))
        for half in range(2):
            wts = []
            if half == 0:
                if tt == 0:
                    wts += [(g_wo.s, v_wol), (dv, V_Z)]
                else:
                    wts.append((dv, v_out_ev))
                if debug and tt > 0:
                    wts.append((ac, v_sdbg_prev[0]))
            v_mmo = MACC(pW[:, 512 * half:512 * (half + 1)],
                         [(ubf[:, 2048 * k + 128 * tt:2048 * k + 128 * (tt + 1)],
                           arena2[:, 1024 * k + 512 * half:1024 * k + 512 * (half + 1)])
                          for k in range(8)],
                         waits=wts)
        v_mmo2 = c_pe.v
        if debug:
            W("scalar", pe, v_mmo2)
            v_sdbg = EA(lambda q: q.copy(a1f[:, 3072:4096], pW[:, :]))
            v_sdbg_prev[0] = v_sdbg
            W("sync", ac, v_sdbg)
            DMA("sync", g_dbq, lambda q, tt=tt: q.dma_start(out=s_dbg[128 * tt:128 * (tt + 1), :],
                                                            in_=a1f[:, 3072:4096]))
        W("vector", pe, v_mmo2)
        W("vector", ac, v_sil)
        W("vector", g_xs.s, vx)
        W("vector", g_out.s, g_out.c.v)
        EV(lambda q: q.tensor_mul(outtile[:, 0:512], pW[:, 0:512], gsil[:, 0:512]))
        EV(lambda q: q.tensor_mul(outtile[:, 512:1024], pW[:, 512:1024], gsil[:, 512:1024]))
        EV(lambda q: q.tensor_add(outtile[:, 0:512], outtile[:, 0:512], xstile[:, 0:512]))
        v_out_ev = EV(lambda q: q.tensor_add(outtile[:, 512:1024], outtile[:, 512:1024], xstile[:, 512:1024]))
        W("sync", dv, v_out_ev)
        DMA("sync", g_out, lambda q, tt=tt: q.dma_start(out=out_t[128 * tt:128 * (tt + 1), :],
                                                        in_=outtile))
        if debug:
            W("sync", ac, v_sil)
            DMA("sync", g_dbq, lambda q, tt=tt: q.dma_start(out=g_dbg[128 * tt:128 * (tt + 1), :],
                                                            in_=gsil))
    if debug:
        W("sync", cc1, 1)
        DMA("sync", g_dbq, lambda q: q.dma_start(out=sc_dbg[:, :], in_=sc_in[:, :]))
        W("vector", dv, c_dv.v)
        v = EV(lambda q: q.tensor_copy(fscr[:, :], vE[:, :]))
        W("sync", dv, v)
        DMA("sync", g_dbq, lambda q: q.dma_start(out=ve_dbg[:, :], in_=fscr[:, :]))
    if n_tt < 16 or stop_after < 5:
        _final_stub()
        return _finish()
    W("sync", g_out.s, g_out.c.v)
    if debug:
        W("sync", g_dbq.s, g_dbq.c.v)
    return _finish()


def _host_inputs(inputs, debug=False):
    bf = ml_dtypes.bfloat16
    x = np.ascontiguousarray(np.asarray(inputs["L0_attn_x"], np.float32).reshape(NTOK, DM))
    W_in = np.asarray(inputs["L0_attn_W_in"], np.float32)
    W_state = np.asarray(inputs["L0_attn_W_state"], np.float32)
    W_dt = np.asarray(inputs["L0_attn_W_dt"], np.float32)
    b_dtv = np.asarray(inputs["L0_attn_b_dt"], np.float32)
    A_log = np.asarray(inputs["L0_attn_A_log"], np.float32)
    W_C = np.asarray(inputs["L0_attn_W_C"], np.float32)
    Dv = np.asarray(inputs["L0_attn_D"], np.float32)
    W_out = np.asarray(inputs["L0_attn_W_out"], np.float32)
    Wg = np.asarray(inputs["L0_ffn_Wg"], np.float32)
    W1 = np.asarray(inputs["L0_ffn_W1"], np.float32)
    b1 = np.asarray(inputs["L0_ffn_b1"], np.float32)
    W2 = np.asarray(inputs["L0_ffn_W2"], np.float32)
    b2 = np.asarray(inputs["L0_ffn_b2"], np.float32)

    com = dict(
        xfull=x,
        w_inT=np.ascontiguousarray(W_in.T),
        w_stT=np.ascontiguousarray(W_state.T),
        w_dtT=np.ascontiguousarray(W_dt.T),
        w_cT=np.ascontiguousarray(W_C.T),
        w_outT=np.ascontiguousarray(W_out.T),
        wg_in=np.ascontiguousarray(Wg),
        b_dt_in=np.ascontiguousarray(b_dtv.reshape(DS, 1)),
        a_log_in=np.ascontiguousarray(A_log.reshape(DS, 1)),
        dmat_in=np.ascontiguousarray(Dv.reshape(8, 128).T),
        identbf_in=np.eye(128, dtype=bf),
        identf_in=np.eye(128, dtype=np.float32),
        onescol_in=np.ones((128, 1), np.float32),
        onesrow_in=np.ones((1, 128), np.float32),
        onestt_in=np.ones((128, 128), np.float32),
        ltstrict_in=np.ascontiguousarray(
            (np.arange(128)[:, None] < np.arange(128)[None, :]).astype(np.float32)),
        lt16_in=np.ascontiguousarray(
            (np.arange(128)[:, None] < (16 * (np.arange(128)[None, :] // 16)))
            .astype(np.float32)),
        tokid_in=np.ascontiguousarray(
            (np.arange(128)[:, None] * 128 + np.arange(128)[None, :]).astype(np.int32)),
        jbase_in=np.ascontiguousarray(
            ((np.arange(128) // 16) * CAP).astype(np.float32).reshape(128, 1)),
        jmat_in=np.ascontiguousarray(
            np.tile(np.arange(1, 9, dtype=np.float32), (128, 1))),
    )
    in_maps = []
    for c in range(NCORE):
        m = dict(com)
        sl = slice(T * c, T * (c + 1))
        m["xs"] = np.ascontiguousarray(x[sl])
        m["xts"] = np.ascontiguousarray(x[sl].T)
        m["w1e"] = np.ascontiguousarray(W1[c])
        m["w2e"] = np.ascontiguousarray(W2[c])
        m["b1m_in"] = np.ascontiguousarray(b1[c].reshape(32, 128).T)
        m["b2bc_in"] = np.ascontiguousarray(np.broadcast_to(b2[c], (128, DM)))
        sv = np.zeros((1, 8), np.float32)
        for j in range(NCORE):
            if j // 4 == c // 4 and j < c:
                sv[0, j] = 1.0
        m["selv_in"] = sv
        em_ = np.zeros((128, 1024), np.float32)
        em_[:, c::8] = 1.0
        m["emask_in"] = em_
        in_maps.append(m)
    return in_maps


def kernel(**inputs):
    in_maps = _host_inputs(inputs)
    with allow_ici_all_to_all():
        nc = build_program(debug=False)
        res = run_bass_kernel_spmd(nc, in_maps, core_ids=list(range(NCORE)))
    out = np.concatenate([np.asarray(res.results[c]["out"]) for c in range(NCORE)], 0)
    return out.reshape(2, 8192, DM)


# version 10 (final)
# speedup vs baseline: 1.3980x; 1.0149x over previous
import numpy as np
import ml_dtypes
import concourse.bass as bass
import concourse.mybir as mybir
from concourse.bass import IndirectOffsetOnAxis
from concourse.bass_utils import run_bass_kernel_spmd
from concourse.aot_env import allow_ici_all_to_all

F32 = mybir.dt.float32
BF16 = mybir.dt.bfloat16
I32 = mybir.dt.int32
AF = mybir.ActivationFunctionType
ALU = mybir.AluOpType
AX = mybir.AxisListType

NCORE = 8
T = 2048
NTOK = 16384
DM = 1024
DS = 128
FF = 4096
KSEL = 4096
CAP = 640
NPR = NCORE * CAP
BIG = 1 << 28
NBIS = 42
RATE_EPS = 1e-4
LN_EPS = 1e-5


class Ctr:
    __slots__ = ("v",)
    def __init__(self):
        self.v = 0
    def add(self, n=1):
        self.v += n
        return self.v


def build_program(debug=False, stop_after=5):
    nc = bass.Bass("TRN2", target_bir_lowering=False, debug=False)
    D = lambda n, s, dt, k="ExternalInput": nc.dram_tensor(n, s, dt, kind=k)

    xts = D("xts", [DM, T], F32)
    xs = D("xs", [T, DM], F32)
    xfull = D("xfull", [NTOK, DM], F32)
    w_inT = D("w_inT", [DM, 2 * DM], F32)
    w_stT = D("w_stT", [DM, DS], F32)
    w_dtT = D("w_dtT", [DM, DS], F32)
    w_cT = D("w_cT", [DS, DM], F32)
    w_outT = D("w_outT", [DM, DM], F32)
    wg_in = D("wg_in", [DM, 8], F32)
    b_dt_in = D("b_dt_in", [DS, 1], F32)
    a_log_in = D("a_log_in", [DS, 1], F32)
    dmat_in = D("dmat_in", [128, 8], F32)
    w1e = D("w1e", [DM, FF], F32)
    w2e = D("w2e", [FF, DM], F32)
    b1m_in = D("b1m_in", [128, 32], F32)
    b2bc_in = D("b2bc_in", [128, DM], F32)
    identbf_in = D("identbf_in", [128, 128], BF16)
    identf_in = D("identf_in", [128, 128], F32)
    onescol_in = D("onescol_in", [128, 1], F32)
    onesrow_in = D("onesrow_in", [1, 128], F32)
    onestt_in = D("onestt_in", [128, 128], F32)
    ltstrict_in = D("ltstrict_in", [128, 128], F32)
    lt16_in = D("lt16_in", [128, 128], F32)
    tokid_in = D("tokid_in", [128, 128], I32)
    jbase_in = D("jbase_in", [128, 1], F32)
    selv_in = D("selv_in", [1, 8], F32)
    jmat_in = D("jmat_in", [128, 8], F32)
    emask_in = D("emask_in", [128, 1024], F32)

    sc_in = nc.dram_tensor("sc_in", [T, 8], F32)
    sc_out = nc.dram_tensor("sc_out", [NTOK, 8], F32)
    sum_in = nc.dram_tensor("sum_in", [2, 128], F32)
    sum_out = nc.dram_tensor("sum_out", [16, 128], F32)
    w1bfd = nc.dram_tensor("w1bfd", [FF, DM], BF16)
    w2bfd = nc.dram_tensor("w2bfd", [2 * FF, DM // 2], BF16)
    idxg = nc.dram_tensor("idxg", [KSEL, 5], I32)
    a2as = nc.dram_tensor("a2as", [NPR, DM], BF16)
    a2ar = nc.dram_tensor("a2ar", [NPR, DM], BF16)
    a2ais = nc.dram_tensor("a2ais", [NPR, 1], I32)
    a2air = nc.dram_tensor("a2air", [NPR, 1], I32)
    gdense = nc.dram_tensor("gdense", [T, DM], BF16)
    out_t = D("out", [T, DM], F32, "ExternalOutput")
    if debug:
        hh_dbg = D("hh_dbg", [128, T], F32, "ExternalOutput")
        thr_dbg = D("thr_dbg", [1, 8], F32, "ExternalOutput")
        s_dbg = D("s_dbg", [T, DM], F32, "ExternalOutput")
        g_dbg = D("g_dbg", [T, DM], F32, "ExternalOutput")
        sc_dbg = D("sc_dbg", [T, 8], F32, "ExternalOutput")
        ve_dbg = D("ve_dbg", [128, 128], F32, "ExternalOutput")

    from contextlib import ExitStack
    ctx = ExitStack()
    sb = lambda n, s, dt: ctx.enter_context(nc.sbuf_tensor(n, s, dt))
    ps = lambda n, s, dt: ctx.enter_context(nc.psum_tensor(n, s, dt))
    sem = lambda n: ctx.enter_context(nc.semaphore(n))

    xnbf = sb("xnbf", [128, 8 * T], BF16)
    ubf = sb("ubf", [128, 8 * T], BF16)
    arena1 = sb("arena1", [128, 8 * T], BF16)
    arena2 = sb("arena2", [128, 8 * T], BF16)
    arena3 = sb("arena3", [128, 8 * T], BF16)
    a1f = arena1.bitcast(F32)
    a2f = arena2.bitcast(F32)
    a3f = arena3.bitcast(F32)

    wstage = sb("wstage", [128, 4096], BF16)
    wg_sb = sb("wg_sb", [128, 64], F32)
    wstb = sb("wstb", [128, 1024], BF16)
    wdtb = sb("wdtb", [128, 1024], BF16)
    wctb = sb("wctb", [128, 1024], BF16)
    identbf = sb("identbf", [128, 128], BF16)
    identf = sb("identf", [128, 128], F32)
    onescol = sb("onescol", [128, 1], F32)
    onesrow = sb("onesrow", [1, 128], F32)
    onestt = sb("onestt", [128, 128], F32)
    ltstrict = sb("ltstrict", [128, 128], F32)
    lt16 = sb("lt16", [128, 128], F32)
    tokid = sb("tokid", [128, 128], I32)
    jbase = sb("jbase", [128, 1], F32)
    selv = sb("selv", [1, 8], F32)
    jmat = sb("jmat", [128, 8], F32)
    b_dt = sb("b_dt", [128, 1], F32)
    a_log = sb("a_log", [128, 1], F32)
    dmat = sb("dmat", [128, 8], F32)
    b1m = sb("b1m", [128, 32], F32)
    b2bc = sb("b2bc", [128, DM], F32)
    negrate = sb("negrate", [128, 1], F32)
    rates = sb("rates", [128, 4], F32)
    epst = sb("epst", [128, 1], F32)
    mhalf = sb("mhalf", [128, 1], F32)
    meanr = sb("meanr", [1, 512], F32)
    rstdr = sb("rstdr", [1, 512], F32)
    stat1 = sb("stat1", [1, 512], F32)
    stat3 = sb("stat3", [1, 512], F32)
    scsb = sb("scsb", [8, 512], F32)
    sctm = sb("sctm", [128, 32], F32)
    dts1 = sb("dts1", [128, 512], F32)
    dts2 = sb("dts2", [128, 512], F32)
    dts3 = sb("dts3", [128, 512], F32)
    dtsum = sb("dtsum", [128, 4], F32)
    hin = sb("hin", [1, 128], F32)
    hf1 = sb("hf1", [1, 128], F32)
    hf2 = sb("hf2", [1, 128], F32)
    h0col = sb("h0col", [128, 1], F32)
    vE = sb("vE", [128, 128], F32)
    bis = sb("bis", [1, 8], F32)
    thrcol = sb("thrcol", [128, 1], F32)
    cntcol = sb("cntcol", [128, 1], F32)
    exc = sb("exc", [128, 1], F32)
    excm1 = sb("excm1", [128, 1], F32)
    bnd = sb("bnd", [128, 1], F32)
    idxg_sb = sb("idxg_sb", [128, 160], I32)
    senti = sb("senti", [128, 40], I32)
    gmask = sb("gmask", [128, 32], F32)
    mcol = sb("mcol", [128, 1], F32)
    s2col = sb("s2col", [128, 1], F32)
    vcol = sb("vcol", [128, 1], F32)
    rcol = sb("rcol", [128, 1], F32)
    ncol = sb("ncol", [128, 1], F32)
    gtile = sb("gtile", [128, DM], BF16)
    recvd = sb("recvd", [128, DM], BF16)
    zerot = sb("zerot", [128, DM], BF16)
    ridl = sb("ridl", [128, 40], I32)

    p0 = ps("p0", [128, 512], F32)
    p1 = ps("p1", [128, 512], F32)
    p2 = ps("p2", [128, 512], F32)
    p3 = ps("p3", [128, 512], F32)
    pW = ps("pW", [128, 1024], F32)
    pS = ps("pS", [8, 512], F32)
    pT = ps("pT", [128, 512], BF16)

    pe = sem("pe"); c_pe = Ctr()
    dv = sem("dv"); c_dv = Ctr()
    ac = sem("ac"); c_ac = Ctr()
    cc1 = sem("cc1"); cc2 = sem("cc2"); cc3 = sem("cc3"); cc4 = sem("cc4")
    junk = sem("junk")

    class Q:
        def __init__(self, name, chained=False):
            self.s = sem(name)
            self.c = Ctr()
            self.chained = chained

    prog = {k: [] for k in ("sync", "scalar", "vector", "tensor", "gpsimd")}
    def em(engine, fn):
        prog[engine].append(fn)
    def W(engine, s, v):
        if v > 0 and not (engine == "tensor" and s is pe):
            em(engine, lambda q, s=s, v=v: q.wait_ge(s, v))
    def EP(fn):
        em("tensor", lambda q, fn=fn: fn(q).then_inc(pe, 1))
        return c_pe.add(1)
    def EV(fn):
        if c_dv.v > 0:
            em("vector", lambda q, v=c_dv.v: q.wait_ge(dv, v))
        em("vector", lambda q, fn=fn: fn(q).then_inc(dv, 1))
        return c_dv.add(1)
    def EA(fn):
        if c_ac.v > 0:
            em("scalar", lambda q, v=c_ac.v: q.wait_ge(ac, v))
        em("scalar", lambda q, fn=fn: fn(q).then_inc(ac, 1))
        return c_ac.add(1)
    def DMA(engine, grp, fn):
        if grp.chained and grp.c.v > 0:
            em(engine, lambda q, s=grp.s, v=grp.c.v: q.wait_ge(s, v))
        em(engine, lambda q, fn=fn, s=grp.s: fn(q).then_inc(s, 16))
        return grp.c.add(16)
    g_const = Q("g_const")
    g_wssm = Q("g_wssm")
    g_stgld = Q("g_stgld", chained=True)
    g_stgst = Q("g_stgst", chained=True)
    g_pref = Q("g_pref")
    g_gz = Q("g_gz")
    g_az = Q("g_az")
    g_x = Q("g_x", chained=True)
    g_sc = Q("g_sc")
    g_sum = Q("g_sum")
    g_sumg = Q("g_sumg")
    g_em = Q("g_em")
    g_scg = Q("g_scg")
    g_idxl = Q("g_idxl")
    g_sq1 = Q("g_sq1")
    g_sq2 = Q("g_sq2")
    g_osb = [Q(f"g_osb{i}") for i in range(8)]
    g_ma = [Q(f"g_ma{i}") for i in range(8)]
    g_gath = Q("g_gath", chained=True)
    g_w1 = [Q("g_w1a", chained=True), Q("g_w1b", chained=True)]
    g_w2 = [Q("g_w2a", chained=True), Q("g_w2b", chained=True)]
    g_wo = Q("g_wo")
    g_gt = Q("g_gt", chained=True)
    g_xs = Q("g_xs", chained=True)
    g_mr = Q("g_mr", chained=True)
    g_out = Q("g_out", chained=True)
    g_dbq = Q("g_dbq", chained=True)
    def MACC(psv, pairs, waits=()):
        for sw, vw in waits:
            W("tensor", sw, vw)
        n = len(pairs)
        for i, (lf, rf) in enumerate(pairs):
            if i < n - 1:
                em("tensor", lambda q, lf=lf, rf=rf, st=(i == 0), psv=psv:
                   q.matmul(psv, lf, rf, start=st, stop=False))
            else:
                em("tensor", lambda q, lf=lf, rf=rf, st=(i == 0), psv=psv:
                   q.matmul(psv, lf, rf, start=st, stop=True).then_inc(pe, 1))
        return c_pe.add(1)

    def _final_stub():
        W("sync", dv, c_dv.v)
        W("sync", ac, c_ac.v)
        W("sync", pe, c_pe.v)
        for tt in range(16):
            for hb in range(2):
                DMA("sync", g_out, lambda q, tt=tt, hb=hb: q.dma_start(
                    out=out_t[128 * tt:128 * (tt + 1), :].bitcast(BF16)[:, 1024 * hb:1024 * (hb + 1)],
                    in_=zerot[:, :]))
        W("sync", g_out.s, g_out.c.v)

    def _finish():
        with nc.Block() as block:
            @block.sync
            def _(q):
                for f in prog["sync"]:
                    f(q)
            @block.scalar
            def _(q):
                for f in prog["scalar"]:
                    f(q)
            @block.vector
            def _(q):
                for f in prog["vector"]:
                    f(q)
            @block.tensor
            def _(q):
                for f in prog["tensor"]:
                    f(q)
            @block.gpsimd
            def _(q):
                for f in prog["gpsimd"]:
                    f(q)
        ctx.close()
        return nc

    constpairs = [
        (wg_sb.rearrange("p (k e) -> p k e", e=8), wg_in.rearrange("(k p) e -> p k e", p=128)),
        (identbf, identbf_in[:, :]), (identf, identf_in[:, :]),
        (onescol, onescol_in[:, :]), (onesrow, onesrow_in[:, :]),
        (onestt, onestt_in[:, :]),
        (ltstrict, ltstrict_in[:, :]), (lt16, lt16_in[:, :]),
        (tokid, tokid_in[:, :]), (jbase, jbase_in[:, :]), (selv, selv_in[:, :]),
        (jmat, jmat_in[:, :]),
        (b_dt, b_dt_in[:, :]), (a_log, a_log_in[:, :]), (dmat, dmat_in[:, :]),
        (b1m, b1m_in[:, :]), (b2bc, b2bc_in[:, :]),
    ]
    for dst, src in constpairs:
        dap = dst if isinstance(dst, bass.AP) else dst[:, :]
        DMA("sync", g_const, lambda q, d=dap, s=src: q.dma_start(out=d, in_=s))
    V_CONST = g_const.c.v

    for dst, src in [
        (wstb.rearrange("p (uk n) -> p uk n", uk=8), w_stT.rearrange("(uk p) n -> p uk n", p=128)),
        (wdtb.rearrange("p (uk n) -> p uk n", uk=8), w_dtT.rearrange("(uk p) n -> p uk n", p=128)),
        (wctb[:, :], w_cT[:, :]),
    ]:
        DMA("gpsimd", g_wssm, lambda q, d=dst, s=src: q.dma_start(out=d, in_=s))
    DMA("gpsimd", g_wssm,
        lambda q: q.dma_start(out=arena2[:, :].rearrange("p (k u) -> p k u", k=8),
                              in_=w_inT.rearrange("(k p) u -> p k u", p=128)))
    V_WSSM = g_wssm.c.v

    V_SENTI = EV(lambda q: q.memset(senti[:, :], BIG))
    V_ZEROT = EV(lambda q: q.memset(zerot[:, :], 0))
    W("gpsimd", dv, V_ZEROT)
    for qd in range(4):
        DMA("gpsimd", g_pref, lambda q, qd=qd: q.dma_start(
            out=idxg[1024 * qd:1024 * (qd + 1), :].rearrange("(p w) c -> p w c", p=128),
            in_=senti[:, 0:40].rearrange("p (w c) -> p w c", c=5)))
    DMA("gpsimd", g_pref, lambda q: q.dma_start(
        out=a2ais[:, :].rearrange("(p w) c -> p w c", p=128),
        in_=senti[:, 0:40].rearrange("p (w c) -> p w c", c=1)))
    V_PREFILL = g_pref.c.v
    for tt in range(16):
        DMA("gpsimd", g_gz, lambda q, tt=tt: q.dma_start(
            out=gdense[128 * tt:128 * (tt + 1), :], in_=zerot[:, :]))
    V_GZERO = g_gz.c.v
    for m in range(40):
        DMA("gpsimd", g_az, lambda q, m=m: q.dma_start(
            out=a2as[128 * m:128 * (m + 1), :], in_=zerot[:, :]))
    V_AZERO = g_az.c.v

    wst_free = [0, 0]
    for i in range(16):
        h = i % 2
        r, cblk = divmod(i, 2)
        W("gpsimd", g_stgst.s, wst_free[h])
        v = DMA("gpsimd", g_stgld, lambda q, r=r, cblk=cblk, h=h: q.dma_start(
            out=wstage[:, h * 2048:h * 2048 + 2048],
            in_=w1e[128 * r:128 * (r + 1), 2048 * cblk:2048 * (cblk + 1)]))
        W("gpsimd", g_stgld.s, v)
        wst_free[h] = DMA("gpsimd", g_stgst, lambda q, r=r, cblk=cblk, h=h: q.dma_start(
            out=w1bfd.rearrange("(a p) (k f) -> p k a f", p=128, f=128)
                [:, r, 16 * cblk:16 * (cblk + 1), :],
            in_=wstage[:, h * 2048:h * 2048 + 2048].rearrange(
                "p (a f) -> p a f", f=128)))
    for i in range(32):
        h = i % 2
        W("gpsimd", g_stgst.s, wst_free[h])
        v = DMA("gpsimd", g_stgld, lambda q, i=i, h=h: q.dma_start(
            out=wstage[:, h * 2048:h * 2048 + 1024],
            in_=w2e[128 * i:128 * (i + 1), :]))
        W("gpsimd", g_stgld.s, v)
        wst_free[h] = DMA("gpsimd", g_stgst, lambda q, i=i, h=h: q.dma_start(
            out=w2bfd.rearrange("(h2 a p) c -> p h2 a c", h2=2, p=128)[:, :, i, :],
            in_=wstage[:, h * 2048:h * 2048 + 1024].rearrange(
                "p (h2 c) -> p h2 c", h2=2)))
    V_WBF = g_stgst.c.v

    W("scalar", g_const.s, V_CONST)
    EA(lambda q: q.activation(rates[:, 0:1], a_log[:, :], AF.Relu))
    v = EA(lambda q: q.activation(rates[:, 1:2], a_log[:, :], AF.Abs))
    W("scalar", ac, v)
    v = EA(lambda q: q.activation(rates[:, 2:3], rates[:, 1:2], AF.Exp, scale=-1.0))
    W("scalar", ac, v)
    v = EA(lambda q: q.activation(rates[:, 3:4], rates[:, 2:3], AF.Ln, bias=1.0))
    W("vector", ac, v)
    v = EV(lambda q: q.tensor_add(negrate[:, :], rates[:, 0:1], rates[:, 3:4]))
    W("vector", dv, v)
    V_NEGRATE = EV(lambda q: q.tensor_scalar(negrate[:, :], negrate[:, :],
                                             RATE_EPS, -1.0, ALU.add, ALU.mult))
    EV(lambda q: q.memset(epst[:, :], LN_EPS))
    V_EPST = EV(lambda q: q.memset(mhalf[:, :], -0.5))

    W("tensor", g_const.s, V_CONST)
    W("vector", g_const.s, V_CONST)
    W("gpsimd", g_const.s, V_CONST)

    decay = a1f[:, 0:2048]
    vvn = a1f[:, 2048:4096]
    hh = a1f[:, 4096:6144]
    sumg = a1f[0:1, 6144:8192]
    hcast = arena1[:, 0:2048]
    mb_sb = dts2[:, :]
    rb_sb = dts3[:, :]
    a2i = arena2.bitcast(I32)
    scg2 = a2f[:, 2048:3072]
    cmpscr = a2f[:, 0:128]
    msk = a2f[:, 128:256]
    pscan = a2f[:, 256:384]
    slotf = a2f[:, 384:512]
    fscr = a2f[:, 512:640]
    fscr2 = a2f[:, 640:768]
    gtv = a2f[:, 768:896]
    iscr = a2i[:, 896:1024]
    iscr2 = a2i[:, 1024:1152]
    desti = a2i[:, 1152:1280]
    pdesti = a2i[:, 1280:1408]
    prs5 = a2i[:, 1408:2048]
    xf_bufs = [a1f[:, 0:4096], a3f[:, 0:4096]]
    xnf_bufs = [a1f[:, 4096:8192], a3f[:, 4096:8192]]
    v_xncast = [0, 0, 0, 0]
    v_xn_by = [0, 0, 0, 0]
    v_sc_by = [0, 0, 0, 0]
    v_stat_rd = 0
    v_bc2_prev = 0
    v_sctm_prev = 0
    for tc in range(4):
        xf32 = xf_bufs[tc % 2]
        xnf = xnf_bufs[tc % 2]
        if tc > 1:
            W("sync", pe, v_sc_by[tc - 2])
            W("sync", ac, v_xncast[tc - 2])
            W("sync", dv, v_xn_by[tc - 2])
        v_xf = DMA("sync", g_x, lambda q, tc=tc, xf32=xf32: q.dma_start(
            out=xf32.rearrange("p (k t) -> p k t", k=8),
            in_=xts[:, 512 * tc:512 * (tc + 1)].rearrange("(k p) t -> p k t", p=128)))
        W("tensor", g_x.s, v_xf)
        W("tensor", dv, v_stat_rd)
        v_sum = MACC(p1[0:1, :],
                     [(onescol[:, :], xf32[:, 512 * k:512 * (k + 1)]) for k in range(8)])
        W("scalar", g_x.s, v_xf)
        for k in range(8):
            EA(lambda q, k=k, xnf=xnf, xf32=xf32: q.activation(
                xnf[:, 512 * k:512 * (k + 1)], xf32[:, 512 * k:512 * (k + 1)], AF.Square))
        v_sq = c_ac.v
        v_sumsq = MACC(p3[0:1, :],
                       [(onescol[:, :], xnf[:, 512 * k:512 * (k + 1)]) for k in range(8)],
                       waits=[(ac, v_sq)])
        mean_s = meanr[0:1, :]
        rstd_s = rstdr[0:1, :]
        var_s = stat1[0:1, :]
        sc3 = stat3[0:1, :]
        W("vector", pe, v_sumsq)
        W("vector", pe, v_bc2_prev)
        EV(lambda q, m=mean_s: q.tensor_scalar(m, p1[0:1, :], 1.0 / DM, None, ALU.mult))
        EV(lambda q, v_=var_s: q.tensor_scalar(v_, p3[0:1, :], 1.0 / DM, None, ALU.mult))
        v_stat_rd = c_dv.v
        v = EV(lambda q, m=mean_s, s3=sc3: q.tensor_mul(s3, m, m))
        W("vector", dv, v)
        v = EV(lambda q, v_=var_s, s3=sc3: q.tensor_sub(v_, v_, s3))
        W("scalar", dv, v)
        if tc == 0:
            W("scalar", dv, V_EPST)
        v = EA(lambda q, v_=var_s, s3=sc3: q.activation(s3, v_, AF.Ln, bias=epst[0:1, 0:1]))
        W("scalar", ac, v)
        v = EA(lambda q, r=rstd_s, s3=sc3: q.activation(r, s3, AF.Exp, scale=mhalf[0:1, 0:1]))
        W("vector", ac, v)
        v = EV(lambda q, v_=var_s: q.tensor_scalar(v_, v_, LN_EPS, None, ALU.add))
        W("vector", dv, v)
        v = EV(lambda q, r=rstd_s, s3=sc3: q.tensor_mul(s3, r, r))
        W("vector", dv, v)
        v = EV(lambda q, v_=var_s, s3=sc3: q.tensor_mul(s3, s3, v_))
        W("vector", dv, v)
        v = EV(lambda q, s3=sc3: q.tensor_scalar(s3, s3, -0.5, 1.5, ALU.mult, ALU.add))
        W("vector", dv, v)
        v_newt = EV(lambda q, r=rstd_s, s3=sc3: q.tensor_mul(r, r, s3))
        v_bc = MACC(pW[:, 0:512], [(onesrow[:, :], mean_s)],
                    waits=[(dv, v_newt), (ac, v_sctm_prev)])
        v_bc2 = MACC(pW[:, 512:1024], [(onesrow[:, :], rstd_s)])
        v_bc2_prev = v_bc2
        W("scalar", pe, v_bc2)
        EA(lambda q: q.copy(mb_sb, pW[:, 0:512]))
        v_ev = EA(lambda q: q.copy(rb_sb, pW[:, 512:1024]))
        W("vector", ac, v_ev)
        for k in range(8):
            EV(lambda q, k=k, xnf=xnf, xf32=xf32: q.tensor_sub(
                xnf[:, 512 * k:512 * (k + 1)], xf32[:, 512 * k:512 * (k + 1)], mb_sb))
            EV(lambda q, k=k, xnf=xnf: q.tensor_mul(
                xnf[:, 512 * k:512 * (k + 1)], xnf[:, 512 * k:512 * (k + 1)], rb_sb))
        v_xn = c_dv.v
        v_xn_by[tc] = v_xn
        W("scalar", dv, v_xn)
        for k in range(8):
            EA(lambda q, k=k, tc=tc, xnf=xnf: q.copy(
                xnbf[:, 2048 * k + 512 * tc:2048 * k + 512 * (tc + 1)],
                xnf[:, 512 * k:512 * (k + 1)]))
        v_xncast[tc] = c_ac.v
        v_sc = MACC(pS[0:8, :],
                    [(wg_sb[:, 8 * k:8 * (k + 1)], xnf[:, 512 * k:512 * (k + 1)])
                     for k in range(8)],
                    waits=[(dv, v_xn), (ac, v_sctm_prev)])
        v_sc_by[tc] = v_sc
        W("scalar", pe, v_sc)
        v_scsb = EA(lambda q: q.copy(scsb[:, :], pS[0:8, :]))
        W("tensor", ac, v_scsb)
        for j in range(4):
            EP(lambda q, j=j: q.transpose(
                pW[:, 8 * j:8 * (j + 1)], scsb[0:8, 128 * j:128 * (j + 1)], identf[0:8, 0:8]))
        v_tr = c_pe.v
        W("scalar", pe, v_tr)
        W("scalar", g_sc.s, g_sc.c.v)
        v_sctm = EA(lambda q: q.copy(sctm[:, :], pW[:, 0:32]))
        v_sctm_prev = v_sctm
        W("sync", ac, v_sctm)
        DMA("sync", g_sc, lambda q, tc=tc: q.dma_start(
            out=sc_in[512 * tc:512 * (tc + 1), :].rearrange("(j p) e -> p j e", p=128),
            in_=sctm[:, :].rearrange("p (j e) -> p j e", e=8)))
    V_SCIN = g_sc.c.v
    V_P1_PE = c_pe.v

    if stop_after < 2:
        _final_stub()
        return _finish()
    v_evac_ps = [0, 0]
    v_vvn_prev = 0
    v_dtabs_prev = 0
    W("tensor", g_wssm.s, V_WSSM)
    for tcc in range(4):
        W("tensor", ac, v_xncast[tcc])
        if tcc == 0:
            W("tensor", ac, v_xncast[3])
            W("tensor", g_sc.s, V_SCIN)
        for uk in range(16):
            pbuf = [p0, p1][uk % 2]
            W("tensor", ac, v_evac_ps[uk % 2])
            vm = MACC(pbuf[:, :],
                      [(arena2[:, 2048 * k + 128 * uk:2048 * k + 128 * (uk + 1)],
                        xnbf[:, 2048 * k + 512 * tcc:2048 * k + 512 * (tcc + 1)])
                       for k in range(8)])
            dstb = ubf if uk < 8 else arena3
            W("scalar", pe, vm)
            v_evac_ps[uk % 2] = EA(lambda q, d=dstb, uk=uk, tcc=tcc, pbuf=pbuf: q.copy(
                d[:, 2048 * (uk % 8) + 512 * tcc:2048 * (uk % 8) + 512 * (tcc + 1)],
                pbuf[:, :]))
        v_u_done = c_ac.v
        v_b = MACC(p2[:, :],
                   [(wstb[:, 128 * uk:128 * (uk + 1)],
                     ubf[:, 2048 * uk + 512 * tcc:2048 * uk + 512 * (tcc + 1)])
                    for uk in range(8)],
                   waits=[(ac, v_u_done), (dv, v_vvn_prev)])
        v_dt = MACC(p3[:, :],
                    [(wdtb[:, 128 * uk:128 * (uk + 1)],
                      ubf[:, 2048 * uk + 512 * tcc:2048 * uk + 512 * (tcc + 1)])
                     for uk in range(8)],
                    waits=[(ac, v_dtabs_prev)])
        W("scalar", pe, v_dt)
        W("scalar", dv, c_dv.v)
        EA(lambda q: q.activation(dts1[:, :], p3[:, :], AF.Relu, bias=b_dt[:, 0:1]))
        v = EA(lambda q: q.activation(dts2[:, :], p3[:, :], AF.Abs, bias=b_dt[:, 0:1]))
        v_dtabs_prev = v
        W("scalar", ac, v)
        v = EA(lambda q: q.activation(dts3[:, :], dts2[:, :], AF.Exp, scale=-1.0))
        W("scalar", ac, v)
        v_l = EA(lambda q: q.activation(dts2[:, :], dts3[:, :], AF.Ln, bias=1.0))
        W("vector", ac, v_l)
        v = EV(lambda q: q.tensor_add(dts1[:, :], dts1[:, :], dts2[:, :]))
        W("vector", dv, v)
        v_dtv = EV(lambda q: q.tensor_scalar_min(dts1[:, :], dts1[:, :], 10.0))
        W("vector", dv, v_dtv)
        EV(lambda q, tcc=tcc: q.reduce_sum(dtsum[:, tcc:tcc + 1], dts1[:, :], axis=AX.X))
        W("scalar", dv, v_dtv)
        if tcc == 0:
            W("scalar", dv, V_NEGRATE)
        v_dec = EA(lambda q, tcc=tcc: q.activation(
            decay[:, 512 * tcc:512 * (tcc + 1)], dts1[:, :], AF.Exp, scale=negrate[:, 0:1]))
        W("vector", ac, v_dec)
        W("vector", pe, v_b)
        v_vvn_prev = EV(lambda q, tcc=tcc: q.scalar_tensor_tensor(
            vvn[:, 512 * tcc:512 * (tcc + 1)], decay[:, 512 * tcc:512 * (tcc + 1)],
            1.0, p2[:, :], ALU.subtract, ALU.mult))
    W("vector", dv, c_dv.v)
    v = EV(lambda q: q.tensor_tensor_scan(hh, decay, vvn, 0.0, ALU.mult, ALU.subtract))
    W("vector", dv, v)
    v = EV(lambda q: q.tensor_add(dtsum[:, 0:1], dtsum[:, 0:1], dtsum[:, 1:2]))
    W("vector", dv, v)
    v = EV(lambda q: q.tensor_add(dtsum[:, 2:3], dtsum[:, 2:3], dtsum[:, 3:4]))
    W("vector", dv, v)
    v_dtt = EV(lambda q: q.tensor_add(dtsum[:, 0:1], dtsum[:, 0:1], dtsum[:, 2:3]))
    W("scalar", dv, v_dtt)
    v_at = EA(lambda q: q.activation(dtsum[:, 1:2], dtsum[:, 0:1], AF.Exp,
                                     scale=negrate[:, 0:1]))
    W("sync", ac, v_at)
    W("sync", dv, c_dv.v)
    DMA("sync", g_sum, lambda q: q.dma_start(out=sum_in[0:1, :].rearrange("o p -> p o"), in_=dtsum[:, 1:2]))
    v_sumdma = DMA("sync", g_sum, lambda q: q.dma_start(out=sum_in[1:2, :].rearrange("o p -> p o"), in_=hh[:, 2047:2048]))
    W("gpsimd", g_sum.s, v_sumdma)
    em("gpsimd", lambda q: q.collective_compute(
        "AllGather", ALU.bypass, [list(range(NCORE))],
        ins=[sum_in.ap()], outs=[sum_out.ap()]).then_inc(cc2, 1))
    W("gpsimd", cc2, 1)
    v_sumg = DMA("gpsimd", g_sumg, lambda q: q.dma_start(out=sumg[0:1, :], in_=sum_out.rearrange("a b -> (a b)")[None, :]))
    W("vector", g_sumg.s, v_sumg)
    EV(lambda q: q.memset(hin[:, :], 0.0))
    for j in range(NCORE):
        W("vector", dv, c_dv.v)
        EV(lambda q, j=j: q.tensor_scalar(hf1[:, :], sumg[0:1, 256 * j:256 * j + 128],
                                          -1.0, None, ALU.add))
        W("vector", dv, c_dv.v)
        EV(lambda q, j=j: q.tensor_scalar(hf1[:, :], hf1[:, :],
                                          selv[0:1, j:j + 1], 1.0, ALU.mult, ALU.add))
        W("vector", dv, c_dv.v)
        EV(lambda q: q.tensor_mul(hf2[:, :], hin[:, :], hf1[:, :]))
        W("vector", dv, c_dv.v)
        EV(lambda q, j=j: q.tensor_scalar(hf1[:, :], sumg[0:1, 256 * j + 128:256 * j + 256],
                                          selv[0:1, j:j + 1], None, ALU.mult))
        W("vector", dv, c_dv.v)
        EV(lambda q: q.tensor_add(hin[:, :], hf2[:, :], hf1[:, :]))
    v_hin = c_dv.v
    v_htr = MACC(pW[:, 0:1], [(hin[:, :], identf[0:1, 0:1])], waits=[(dv, v_hin)])
    W("vector", pe, v_htr)
    v = EV(lambda q: q.tensor_copy(h0col[:, :], pW[:, 0:1]))
    W("vector", dv, v)
    v_scan2 = EV(lambda q: q.tensor_tensor_scan(hh, decay, vvn, h0col[:, 0:1],
                                                ALU.mult, ALU.subtract))
    W("vector", dv, v_scan2)
    v_hcast = EV(lambda q: q.tensor_copy(hcast[:, :], hh))
    if debug:
        W("sync", dv, v_scan2)
        DMA("sync", g_dbq, lambda q: q.dma_start(out=hh_dbg[:, :], in_=hh))

    silu4 = a1f[:, 2048:6144]
    v_z_evac = [0, 0]
    v_mul_last = 0
    for tcc in range(4):
        gate3 = arena3.rearrange("p (k c) -> p k c", c=2048)[:, :, 512 * tcc:512 * (tcc + 1)]
        W("scalar", dv, v_hcast if tcc == 0 else v_mul_last)
        v_sg = EA(lambda q, gate3=gate3: q.activation(
            silu4.rearrange("p (k c) -> p k c", c=512), gate3, AF.Silu))
        for k in range(8):
            pbuf = [p0, p1][k % 2]
            vy = MACC(pbuf[:, :],
                      [(wctb[:, 128 * k:128 * (k + 1)],
                        hcast[:, 512 * tcc:512 * (tcc + 1)])],
                      waits=[(dv, v_hcast), (dv, v_z_evac[k % 2])])
            W("vector", ac, v_sg)
            W("vector", pe, vy)
            v_mul_last = EV(lambda q, pbuf=pbuf, k=k: q.tensor_mul(
                dts2[:, :], pbuf[:, :], silu4[:, 512 * k:512 * (k + 1)]))
            v_z_evac[k % 2] = EV(lambda q, k=k, tcc=tcc: q.scalar_tensor_tensor(
                ubf[:, 2048 * k + 512 * tcc:2048 * k + 512 * (tcc + 1)],
                xnbf[:, 2048 * k + 512 * tcc:2048 * k + 512 * (tcc + 1)],
                dmat[:, k:k + 1], dts2[:, :], ALU.mult, ALU.add))
    V_Z = c_dv.v
    V_P3_PE = c_pe.v

    if stop_after < 3:
        _final_stub()
        return _finish()
    W("gpsimd", g_sc.s, V_SCIN)
    em("gpsimd", lambda q: q.collective_compute(
        "AllGather", ALU.bypass, [list(range(NCORE))],
        ins=[sc_in.ap()], outs=[sc_out.ap()]).then_inc(cc1, 1))
    W("gpsimd", cc1, 1)
    v_scg = DMA("gpsimd", g_scg, lambda q: q.dma_start(
        out=scg2.rearrange("p (tl e) -> p tl e", e=8),
        in_=sc_out.rearrange("(p tl) e -> p tl e", p=128)))
    emask = a1f[:, 0:1024]
    W("sync", dv, V_Z)
    v_eml = DMA("sync", g_em, lambda q: q.dma_start(out=emask, in_=emask_in[:, :]))
    W("vector", g_scg.s, v_scg)
    W("vector", g_em.s, v_eml)
    W("vector", dv, V_Z)
    v = EV(lambda q: q.tensor_mul(a1f[:, 1024:2048], scg2, emask))
    W("vector", dv, v)
    v_vE = EV(lambda q: q.tensor_reduce(
        vE[:, :], a1f[:, 1024:2048].rearrange("p (tl e) -> p tl e", e=8),
        axis=AX.X, op=ALU.add))
    locol = dts2[:, 0:1]
    stepcol = dts2[:, 1:2]
    scol = dts2[:, 2:3]
    tvec = dts2[:, 8:16]
    ge8 = dts2[:, 16:24]
    cnt8 = dts2[:, 24:32]
    W("vector", dv, v_vE)
    EV(lambda q: q.memset(locol, -64.0))
    EV(lambda q: q.memset(stepcol, 128.0 / 9.0))
    EV(lambda q: q.tensor_scalar(tvec, jmat[:, :], stepcol[:, 0:1],
                                 locol[:, 0:1], ALU.mult, ALU.add))
    for rnd in range(14):
        for j in range(8):
            EV(lambda q, j=j: q.tensor_scalar(cmpscr, vE[:, :], tvec[:, j:j + 1],
                                              0.0, ALU.is_ge, ALU.add,
                                              accum_out=cnt8[:, j:j + 1]))
        v_cmp = c_dv.v
        v_cnt = MACC(p2[:, 0:8], [(onestt[:, :], cnt8)], waits=[(dv, v_cmp)])
        W("vector", pe, v_cnt)
        EV(lambda q: q.tensor_scalar(ge8, p2[:, 0:8], KSEL - 0.5, None, ALU.is_ge))
        EV(lambda q: q.reduce_sum(scol, ge8, axis=AX.X))
        EV(lambda q: q.scalar_tensor_tensor(locol, stepcol, scol[:, 0:1],
                                            locol, ALU.mult, ALU.add))
        EV(lambda q: q.tensor_scalar(stepcol, stepcol, 1.0 / 9.0, None, ALU.mult))
        EV(lambda q: q.tensor_scalar(tvec, jmat[:, :], stepcol[:, 0:1],
                                     locol[:, 0:1], ALU.mult, ALU.add))
    v_mid = c_dv.v
    if debug:
        W("sync", dv, v_mid)
        DMA("sync", g_dbq, lambda q: q.dma_start(out=thr_dbg[0:1, :], in_=dts2[0:1, 0:8]))
    v_msk = EV(lambda q: q.tensor_scalar(msk, vE[:, :], locol[:, 0:1],
                                         None, ALU.is_ge))
    W("vector", dv, v_msk)
    v_ps = EV(lambda q: q.tensor_tensor_scan(pscan, onestt[:, :], msk,
                                             0.0, ALU.mult, ALU.add))
    v_exm = MACC(pW[:, 0:1], [(ltstrict[:, :], pscan[:, 127:128])],
                 waits=[(dv, v_ps)])
    v_bnm = MACC(pW[:, 1:2], [(lt16[:, :], pscan[:, 127:128])])
    W("vector", pe, v_bnm)
    EV(lambda q: q.tensor_copy(exc[:, :], pW[:, 0:1]))
    W("vector", dv, c_dv.v)
    EV(lambda q: q.tensor_copy(bnd[:, :], pW[:, 1:2]))
    W("vector", dv, c_dv.v)
    EV(lambda q: q.tensor_scalar(excm1[:, :], exc[:, :], -1.0, None, ALU.add))
    W("vector", dv, c_dv.v)
    EV(lambda q: q.tensor_scalar(slotf, pscan, excm1[:, 0:1],
                                 None, ALU.add))
    W("vector", dv, c_dv.v)
    EV(lambda q: q.tensor_scalar(fscr[:, :], slotf, bnd[:, 0:1],
                                 None, ALU.subtract))
    W("vector", dv, c_dv.v)
    EV(lambda q: q.tensor_scalar(fscr2, slotf, KSEL - 0.5, None, ALU.is_lt))
    W("vector", dv, c_dv.v)
    EV(lambda q: q.tensor_mul(fscr2, fscr2, msk))
    W("vector", dv, c_dv.v)
    EV(lambda q: q.tensor_scalar(gtv, fscr[:, :], CAP - 0.5, None, ALU.is_lt))
    W("vector", dv, c_dv.v)
    EV(lambda q: q.tensor_mul(gtv, gtv, msk))
    W("vector", dv, c_dv.v)
    EV(lambda q: q.tensor_copy(iscr, slotf))
    W("vector", dv, c_dv.v)
    EV(lambda q: q.tensor_scalar(iscr2, iscr, 127, None, ALU.bitwise_and))
    W("vector", dv, c_dv.v)
    EV(lambda q: q.tensor_scalar(iscr2, iscr2, 5, None, ALU.logical_shift_left))
    W("vector", dv, c_dv.v)
    EV(lambda q: q.tensor_scalar(iscr, iscr, 7, None, ALU.logical_shift_right))
    W("vector", dv, c_dv.v)
    EV(lambda q: q.tensor_add(iscr2, iscr2, iscr))
    W("vector", dv, c_dv.v)
    EV(lambda q: q.memset(desti, BIG))
    W("vector", dv, c_dv.v)
    EV(lambda q: q.tensor_copy(iscr, fscr2))
    W("vector", dv, c_dv.v)
    EV(lambda q: q.copy_predicated(desti, iscr, iscr2))
    W("vector", dv, c_dv.v)
    EV(lambda q: q.tensor_copy(iscr2, fscr[:, :]))
    W("vector", dv, c_dv.v)
    EV(lambda q: q.tensor_scalar(iscr2, iscr2, jbase[:, 0:1], None, ALU.add))
    W("vector", dv, c_dv.v)
    EV(lambda q: q.memset(pdesti, BIG))
    W("vector", dv, c_dv.v)
    EV(lambda q: q.tensor_copy(iscr, gtv))
    W("vector", dv, c_dv.v)
    EV(lambda q: q.copy_predicated(pdesti, iscr, iscr2))
    W("vector", dv, c_dv.v)
    W("scalar", dv, c_dv.v)
    v_gtv = EA(lambda q: q.activation(gtv, vE[:, :], AF.Sigmoid))
    EV(lambda q: q.tensor_scalar(iscr, tokid[:, :], 2047, None, ALU.bitwise_and))
    v_lid = c_dv.v
    W("vector", dv, v_lid)
    EV(lambda q: q.tensor_copy(prs5[:, 0:640:5], tokid[:, :]))
    W("vector", ac, v_gtv)
    EV(lambda q: q.tensor_copy(prs5[:, 1:640:5], gtv.bitcast(I32)))
    EV(lambda q: q.tensor_copy(prs5[:, 2:640:5], pdesti))
    EV(lambda q: q.tensor_copy(prs5[:, 3:640:5], iscr))
    EV(lambda q: q.tensor_copy(iscr2, fscr[:, :]))
    EV(lambda q: q.tensor_scalar(msk.bitcast(I32), iscr2, 7, None, ALU.logical_shift_right))
    EV(lambda q: q.tensor_scalar(iscr2, iscr2, 127, None, ALU.bitwise_and))
    EV(lambda q: q.tensor_scalar(pscan.bitcast(I32), iscr2, 2, None, ALU.logical_shift_left))
    EV(lambda q: q.tensor_add(iscr2, iscr2, pscan.bitcast(I32)))
    EV(lambda q: q.tensor_add(iscr2, iscr2, msk.bitcast(I32)))
    EV(lambda q: q.tensor_copy(fscr2, iscr2))
    EV(lambda q: q.tensor_scalar(fscr2, fscr2, jbase[:, 0:1], None, ALU.add))
    EV(lambda q: q.tensor_copy(iscr2, fscr2))
    EV(lambda q: q.memset(msk.bitcast(I32), BIG))
    EV(lambda q: q.tensor_copy(fscr2, pdesti))
    EV(lambda q: q.tensor_scalar(fscr2, fscr2, 1.0e8, None, ALU.is_lt))
    EV(lambda q: q.tensor_copy(iscr, fscr2))
    EV(lambda q: q.copy_predicated(msk.bitcast(I32), iscr, iscr2))
    v_quad = EV(lambda q: q.tensor_copy(prs5[:, 4:640:5], msk.bitcast(I32)))
    W("gpsimd", dv, v_quad)
    W("gpsimd", g_pref.s, V_PREFILL)
    def _mk_scat(tg):
        def f(q):
            q.indirect_dma_start(
                out=idxg[:, :],
                out_offset=IndirectOffsetOnAxis(desti[:, tg:tg + 1], axis=0),
                in_=prs5[:, 5 * tg:5 * (tg + 1)], in_offset=None,
                bounds_check=bc4095[0], oob_is_err=False).then_inc(g_sq1.s, 16)
        return f
    bc4095 = [None]
    def _set_regs(q):
        bc4095[0] = q.to_reg(KSEL - 1)
        bcNPR[0] = q.to_reg(NPR - 1)
        bcNTOK[0] = q.to_reg(NTOK - 1)
        bcT[0] = q.to_reg(T - 1)
        return q.engine_nop()
    bcNPR = [None]; bcNTOK = [None]; bcT = [None]
    em("gpsimd", lambda q: _set_regs(q))
    for tg in range(128):
        em("gpsimd", _mk_scat(tg))
        g_sq1.c.add(16)
    W("gpsimd", g_sq1.s, g_sq1.c.v)
    v_idxl = DMA("gpsimd", g_idxl, lambda q: q.dma_start(
        out=idxg_sb[:, :].rearrange("p (w c) -> p w c", c=5),
        in_=idxg.rearrange("(p w) c -> p w c", p=128)))
    W("vector", g_idxl.s, v_idxl)
    for g in range(32):
        EV(lambda q, g=g: q.tensor_copy(fscr[:, g:g + 1], idxg_sb[:, 5 * g:5 * g + 1]))
    W("vector", dv, c_dv.v)
    v = EV(lambda q: q.tensor_scalar(fscr2[:, 0:32], fscr[:, 0:32], NTOK - 0.5,
                                     None, ALU.is_lt))
    W("vector", dv, v)
    v_gm = EV(lambda q: q.tensor_mul(
        gmask[:, :], fscr2[:, 0:32],
        idxg_sb.bitcast(F32)[:, :].rearrange("p (w c) -> p w c", c=5)[:, :, 1]))
    W("gpsimd", dv, v_gm)
    def _mk_idscat(g):
        def f(q):
            q.indirect_dma_start(
                out=a2ais[:, :],
                out_offset=IndirectOffsetOnAxis(idxg_sb[:, 5 * g + 4:5 * g + 5], axis=0),
                in_=idxg_sb[:, 5 * g + 3:5 * g + 4], in_offset=None,
                bounds_check=bcNPR[0], oob_is_err=False).then_inc(g_sq2.s, 16)
        return f
    for g in range(32):
        em("gpsimd", _mk_idscat(g))
        g_sq2.c.add(16)
    v_idsf = g_sq2.c.v

    if stop_after < 4:
        _final_stub()
        return _finish()
    xg_bufs = [a2f[:, 0:4096], xnbf.bitcast(F32)[:, 0:4096]]
    exT_bufs = [arena2[:, 8192:12288], xnbf[:, 8192:12288]]
    xnr = a2f[:, 6656:7680]
    sqs = xnr
    xnrb4 = xnbf[:, 12288:16384]

    v_w1l = [0, 0]
    v_w2l = [0, 0]
    v_epi_dv = 0
    v_epi_psum = 0
    v_epi_half0 = 0
    v_tev = [0, 0]
    v_ext = [0, 0]
    v_xg_read = [0, 0]
    v_T_done = [0] * 9
    v_xnrb = {}
    v_gath_v = {}

    def emit_gather(rcn):
        buf = rcn % 2
        xgv = xg_bufs[buf].rearrange("p (b f) -> p b f", b=4)
        if rcn == 0:
            W("gpsimd", pe, V_P3_PE)
            W("gpsimd", g_sq2.s, v_idsf)
        elif rcn == 1:
            W("gpsimd", dv, V_Z)
        else:
            W("gpsimd", dv, v_xg_read[buf])
        for gg in range(4):
            g = 4 * rcn + gg
            DMA("gpsimd", g_gath, lambda q, g=g, gg=gg, xgv=xgv: q.indirect_dma_start(
                out=xgv[:, gg, :], out_offset=None, in_=xfull[:, :],
                in_offset=IndirectOffsetOnAxis(idxg_sb[:, 5 * g:5 * g + 1], axis=0),
                bounds_check=bcNTOK[0], oob_is_err=False))
        v_gath_v[rcn] = g_gath.c.v

    mcol4 = dts1[:, 0:4]
    s2col4 = dts1[:, 4:8]
    vcol4 = dts1[:, 8:12]
    ncol4 = dts1[:, 12:16]
    rcol4 = dts1[:, 16:20]

    def emit_ln(rcn):
        buf = rcn % 2
        v_gath = v_gath_v[rcn]
        xgv = xg_bufs[buf].rearrange("p (b f) -> p b f", b=4)
        if rcn >= 1:
            W("vector", pe, v_T_done[rcn - 1])
        W("vector", g_gath.s, v_gath)
        EV(lambda q, xgv=xgv: q.tensor_reduce(mcol4, xgv, axis=AX.X, op=ALU.add))
        for gg in range(4):
            EV(lambda q, gg=gg, xgv=xgv: q.scalar_tensor_tensor(
                sqs, xgv[:, gg, :], 1.0, xgv[:, gg, :], ALU.mult, ALU.mult,
                accum_out=s2col4[:, gg:gg + 1]))
        EV(lambda q: q.tensor_scalar(mcol4, mcol4, 1.0 / DM, None, ALU.mult))
        EV(lambda q: q.tensor_scalar(s2col4, s2col4, 1.0 / DM, None, ALU.mult))
        EV(lambda q: q.tensor_mul(vcol4, mcol4, mcol4))
        v = EV(lambda q: q.tensor_sub(vcol4, s2col4, vcol4))
        W("scalar", dv, v)
        if rcn == 0:
            W("scalar", pe, V_P3_PE)
        EA(lambda q: q.activation(ncol4, vcol4, AF.Ln, bias=epst[:, 0:1]))
        v_r0 = EA(lambda q: q.activation(rcol4, ncol4, AF.Exp, scale=mhalf[:, 0:1]))
        W("vector", ac, v_r0)
        EV(lambda q: q.tensor_scalar(vcol4, vcol4, LN_EPS, None, ALU.add))
        EV(lambda q: q.tensor_mul(ncol4, rcol4, rcol4))
        EV(lambda q: q.tensor_mul(ncol4, ncol4, vcol4))
        EV(lambda q: q.tensor_scalar(ncol4, ncol4, -0.5, 1.5, ALU.mult, ALU.add))
        EV(lambda q: q.tensor_mul(rcol4, rcol4, ncol4))
        v_last_read = 0
        for gg in range(4):
            v_last_read = EV(lambda q, gg=gg, xgv=xgv: q.tensor_scalar(
                xnr, xgv[:, gg, :], mcol4[:, gg:gg + 1], None, ALU.subtract))
            v = EV(lambda q, gg=gg: q.tensor_scalar(
                xnrb4[:, 1024 * gg:1024 * (gg + 1)], xnr, rcol4[:, gg:gg + 1],
                None, ALU.mult))
            v_xnrb[(rcn, gg)] = v
        v_xg_read[buf] = v_last_read

    def emit_T(rcn):
        buf = rcn % 2
        for gg in range(4):
            W("tensor", dv, v_xnrb[(rcn, gg)])
            for half in range(2):
                W("tensor", ac, v_tev[0])
                for k4 in range(4):
                    k = 4 * half + k4
                    EP(lambda q, gg=gg, k=k, k4=k4: q.transpose(
                        pT[:, 128 * k4:128 * (k4 + 1)],
                        xnrb4[:, 1024 * gg + 128 * k:1024 * gg + 128 * (k + 1)],
                        identbf[:, :]))
                v_tp = c_pe.v
                W("scalar", pe, v_tp)
                v_tev[0] = EA(lambda q, gg=gg, half=half, buf=buf: q.copy(
                    exT_bufs[buf].rearrange("p (k c) -> p k c", c=512)
                        [:, 4 * half:4 * half + 4, 128 * gg:128 * (gg + 1)],
                    pT[:, :].rearrange("p (k f) -> p k f", f=128)))
        v_ext[buf] = c_ac.v
        v_T_done[rcn] = c_pe.v

    v_epi_by_rc = {}

    def emit_scatter(rcn):
        obase = 4096 if rcn % 2 == 0 else 12288
        if rcn == 0:
            W("gpsimd", g_az.s, V_AZERO)
        for sub in range(4):
            g = 4 * rcn + sub
            osb = arena3[:, obase + 1024 * sub:obase + 1024 * (sub + 1)]
            W("gpsimd", dv, v_epi_by_rc[rcn])
            em("gpsimd", lambda q, g=g, osb=osb, rcn=rcn: q.indirect_dma_start(
                out=a2as[:, :],
                out_offset=IndirectOffsetOnAxis(idxg_sb[:, 5 * g + 2:5 * g + 3], axis=0),
                in_=osb, in_offset=None,
                bounds_check=bcNPR[0], oob_is_err=False).then_inc(g_osb[rcn].s, 16))
            g_osb[rcn].c.add(16)

    emit_gather(0)
    emit_ln(0)
    emit_T(0)
    for rc in range(8):
        if rc < 7:
            emit_gather(rc + 1)
            emit_ln(rc + 1)
        if rc >= 1:
            emit_scatter(rc - 1)
        exT = exT_bufs[rc % 2]
        v_gelu_pb = [0, 0, 0, 0]
        for ffc in range(32):
            hbuf = ffc % 2
            w1c = arena3[:, 1024 * hbuf:1024 * (hbuf + 1)]
            if rc == 0 and ffc < 2:
                W("sync", g_stgst.s, V_WBF)
                W("sync", pe, V_P3_PE)
            else:
                W("sync", pe, v_w1l[hbuf])
            vl = DMA("sync", g_w1[hbuf], lambda q, ffc=ffc, w1c=w1c: q.dma_start(
                out=w1c, in_=w1bfd[128 * ffc:128 * (ffc + 1), :]))
            pb = [p0, p1, p2, p3][ffc % 4]
            v_mm1 = MACC(pb[:, :],
                         [(w1c[:, 128 * k:128 * (k + 1)], exT[:, 512 * k:512 * (k + 1)])
                          for k in range(8)],
                         waits=[(g_w1[hbuf].s, vl), (ac, v_ext[rc % 2]),
                                (ac, v_gelu_pb[ffc % 4]),
                                (dv, v_epi_psum if ffc < 4 else 0)])
            v_w1l[hbuf] = v_mm1
            W("scalar", pe, v_mm1)
            v_gelu_pb[ffc % 4] = EA(lambda q, ffc=ffc, pb=pb: q.activation(
                arena1[:, 512 * ffc:512 * (ffc + 1)], pb[:, :], AF.Gelu,
                bias=b1m[:, ffc:ffc + 1]))
        v_gelu_all = c_ac.v
        if rc < 7:
            emit_T(rc + 1)
        for hh2 in range(2):
            for ffc in range(32):
                wbuf = ffc % 2
                w2c = arena3[:, 2048 + 1024 * wbuf:2048 + 1024 * wbuf + 512]
                W("sync", pe, v_w2l[wbuf])
                vl = DMA("sync", g_w2[wbuf], lambda q, ffc=ffc, hh2=hh2, w2c=w2c: q.dma_start(
                    out=w2c,
                    in_=w2bfd[4096 * hh2 + 128 * ffc:4096 * hh2 + 128 * (ffc + 1), :]))
                for sub in range(4):
                    pbs = [p0, p1, p2, p3][sub]
                    lastk = (ffc == 31)
                    wts = [(g_w2[wbuf].s, vl)]
                    if ffc == 0:
                        wts.append((ac, v_gelu_all))
                        if hh2 == 1:
                            wts.append((dv, v_epi_half0))
                    for sw, vw in wts:
                        W("tensor", sw, vw)
                    certify = (sub == 3) or lastk
                    em("tensor", lambda q, ffc=ffc, sub=sub, pbs=pbs, w2c=w2c,
                       lastk=lastk, certify=certify:
                       (q.matmul(pbs[:, :],
                                 arena1[:, 512 * ffc + 128 * sub:512 * ffc + 128 * (sub + 1)],
                                 w2c, start=(ffc == 0), stop=lastk).then_inc(pe, 1)
                        if certify else
                        q.matmul(pbs[:, :],
                                 arena1[:, 512 * ffc + 128 * sub:512 * ffc + 128 * (sub + 1)],
                                 w2c, start=(ffc == 0), stop=lastk)))
                    if certify:
                        c_pe.add(1)
                v_w2l[wbuf] = c_pe.v
            v_mm2h = c_pe.v
            obase = 4096 if rc % 2 == 0 else 12288
            W("vector", pe, v_mm2h)
            for sub in range(4):
                pbs = [p0, p1, p2, p3][sub]
                oev = a3f[:, 4096 + 512 * sub:4096 + 512 * (sub + 1)][:, 0:512]
                EV(lambda q, pbs=pbs, hh2=hh2, oev=oev: q.tensor_add(
                    oev, pbs[:, :], b2bc[:, 512 * hh2:512 * (hh2 + 1)]))
            v_epi_psum = c_dv.v
            if hh2 == 0:
                v_epi_half0 = v_epi_psum
            if rc > 1 and hh2 == 0:
                W("vector", g_osb[rc - 2].s, 64)
            for sub in range(4):
                g = 4 * rc + sub
                oev = a3f[:, 4096 + 512 * sub:4096 + 512 * (sub + 1)][:, 0:512]
                osb = arena3[:, obase + 1024 * sub:obase + 1024 * (sub + 1)]
                v_epi_dv = EV(lambda q, g=g, hh2=hh2, osb=osb, oev=oev: q.tensor_scalar(
                    osb[:, 512 * hh2:512 * (hh2 + 1)], oev, gmask[:, g:g + 1],
                    None, ALU.mult))
        v_epi_by_rc[rc] = v_epi_dv
    emit_scatter(7)
    if stop_after < 5:
        _final_stub()
        return _finish()
    for rc in range(8):
        W("gpsimd", g_osb[rc].s, 64)
    em("gpsimd", lambda q: q.collective_compute(
        "AllToAll", ALU.bypass, [list(range(NCORE))],
        ins=[a2as.ap()], outs=[a2ar.ap()]).then_inc(cc3, 1))
    em("gpsimd", lambda q: q.collective_compute(
        "AllToAll", ALU.bypass, [list(range(NCORE))],
        ins=[a2ais.ap()], outs=[a2air.ap()]).then_inc(cc4, 1))
    W("gpsimd", cc3, 1)
    W("gpsimd", cc4, 1)
    if stop_after < 4.3:
        _final_stub()
        return _finish()
    v_wol = DMA("gpsimd", g_wo, lambda q: q.dma_start(
        out=arena2[:, 0:8192].rearrange("p (k e) -> p k e", k=8),
        in_=w_outT.rearrange("(k p) e -> p k e", p=128)))
    W("gpsimd", g_gz.s, V_GZERO)
    v_ridl = DMA("gpsimd", g_mr, lambda q: q.dma_start(
        out=ridl[:, :].rearrange("p (j w) -> p j w", w=5),
        in_=a2air.rearrange("(j p w) o -> p j (w o)", p=128, w=5)))
    g_mg = [Q("g_mg0"), Q("g_mg1")]
    mgbuf = [arena3[:, 0:5120], arena3[:, 5120:10240]]
    vload = {}

    def mload(j):
        b = j % 2
        return DMA("gpsimd", g_mg[b], lambda q, j=j, b=b: q.dma_start(
            out=mgbuf[b].rearrange("p (c d) -> p c d", c=5),
            in_=a2ar[CAP * j:CAP * (j + 1), :].rearrange("(c p) d -> p c d", p=128)))

    W("gpsimd", g_mr.s, v_ridl)
    vload[0] = mload(0)
    vload[1] = mload(1)
    for j in range(8):
        b = j % 2
        W("gpsimd", g_mg[b].s, vload[j])
        if j > 0:
            W("gpsimd", g_ma[j - 1].s, 80)
        for c5 in range(5):
            m = 5 * j + c5
            em("gpsimd", lambda q, j=j, c5=c5, m=m, b=b: q.indirect_dma_start(
                out=gdense[:, :],
                out_offset=IndirectOffsetOnAxis(ridl[:, m:m + 1], axis=0),
                in_=mgbuf[b][:, 1024 * c5:1024 * c5 + 1024], in_offset=None,
                bounds_check=bcT[0], oob_is_err=False,
                compute_op=ALU.add).then_inc(g_ma[j].s, 16))
            g_ma[j].c.add(16)
        if j + 2 < 8:
            W("gpsimd", g_ma[j].s, 80)
            vload[j + 2] = mload(j + 2)
    if stop_after < 4.7:
        _final_stub()
        return _finish()
    gsil = a1f[:, 0:1024]
    xstile = a1f[:, 1024:2048]
    outtile = a1f[:, 2048:3072]
    v_sdbg_prev = [0]
    v_gl = 0
    v_out_ev = 0
    n_tt = 16 if stop_after >= 4.9 else (4 if stop_after >= 4.85 else 1)
    for tt in range(n_tt):
        if tt == 0:
            for jj in range(8):
                W("sync", g_ma[jj].s, 80)
        else:
            W("sync", dv, v_out_ev)
        vg = DMA("sync", g_gt, lambda q, tt=tt: q.dma_start(
            out=gtile[:, :], in_=gdense[128 * tt:128 * (tt + 1), :]))
        vx = DMA("sync", g_xs, lambda q, tt=tt: q.dma_start(
            out=xstile, in_=xs[128 * tt:128 * (tt + 1), :]))
        W("scalar", g_gt.s, vg)
        if debug:
            W("scalar", g_dbq.s, g_dbq.c.v)
        v_sil = EA(lambda q: q.activation(gsil, gtile[:, :], AF.Silu))
        for half in range(2):
            wts = []
            if half == 0:
                if tt == 0:
                    wts += [(g_wo.s, v_wol), (dv, V_Z)]
                else:
                    wts.append((dv, v_out_ev))
                if debug and tt > 0:
                    wts.append((ac, v_sdbg_prev[0]))
            v_mmo = MACC(pW[:, 512 * half:512 * (half + 1)],
                         [(ubf[:, 2048 * k + 128 * tt:2048 * k + 128 * (tt + 1)],
                           arena2[:, 1024 * k + 512 * half:1024 * k + 512 * (half + 1)])
                          for k in range(8)],
                         waits=wts)
        v_mmo2 = c_pe.v
        if debug:
            W("scalar", pe, v_mmo2)
            v_sdbg = EA(lambda q: q.copy(a1f[:, 3072:4096], pW[:, :]))
            v_sdbg_prev[0] = v_sdbg
            W("sync", ac, v_sdbg)
            DMA("sync", g_dbq, lambda q, tt=tt: q.dma_start(out=s_dbg[128 * tt:128 * (tt + 1), :],
                                                            in_=a1f[:, 3072:4096]))
        W("vector", pe, v_mmo2)
        W("vector", ac, v_sil)
        W("vector", g_xs.s, vx)
        W("vector", g_out.s, g_out.c.v)
        EV(lambda q: q.tensor_mul(outtile[:, 0:512], pW[:, 0:512], gsil[:, 0:512]))
        EV(lambda q: q.tensor_mul(outtile[:, 512:1024], pW[:, 512:1024], gsil[:, 512:1024]))
        EV(lambda q: q.tensor_add(outtile[:, 0:512], outtile[:, 0:512], xstile[:, 0:512]))
        v_out_ev = EV(lambda q: q.tensor_add(outtile[:, 512:1024], outtile[:, 512:1024], xstile[:, 512:1024]))
        W("sync", dv, v_out_ev)
        DMA("sync", g_out, lambda q, tt=tt: q.dma_start(out=out_t[128 * tt:128 * (tt + 1), :],
                                                        in_=outtile))
        if debug:
            W("sync", ac, v_sil)
            DMA("sync", g_dbq, lambda q, tt=tt: q.dma_start(out=g_dbg[128 * tt:128 * (tt + 1), :],
                                                            in_=gsil))
    if debug:
        W("sync", cc1, 1)
        DMA("sync", g_dbq, lambda q: q.dma_start(out=sc_dbg[:, :], in_=sc_in[:, :]))
        W("vector", dv, c_dv.v)
        v = EV(lambda q: q.tensor_copy(fscr[:, :], vE[:, :]))
        W("sync", dv, v)
        DMA("sync", g_dbq, lambda q: q.dma_start(out=ve_dbg[:, :], in_=fscr[:, :]))
    if n_tt < 16 or stop_after < 5:
        _final_stub()
        return _finish()
    W("sync", g_out.s, g_out.c.v)
    if debug:
        W("sync", g_dbq.s, g_dbq.c.v)
    return _finish()


def _host_inputs(inputs, debug=False):
    bf = ml_dtypes.bfloat16
    x = np.ascontiguousarray(np.asarray(inputs["L0_attn_x"], np.float32).reshape(NTOK, DM))
    W_in = np.asarray(inputs["L0_attn_W_in"], np.float32)
    W_state = np.asarray(inputs["L0_attn_W_state"], np.float32)
    W_dt = np.asarray(inputs["L0_attn_W_dt"], np.float32)
    b_dtv = np.asarray(inputs["L0_attn_b_dt"], np.float32)
    A_log = np.asarray(inputs["L0_attn_A_log"], np.float32)
    W_C = np.asarray(inputs["L0_attn_W_C"], np.float32)
    Dv = np.asarray(inputs["L0_attn_D"], np.float32)
    W_out = np.asarray(inputs["L0_attn_W_out"], np.float32)
    Wg = np.asarray(inputs["L0_ffn_Wg"], np.float32)
    W1 = np.asarray(inputs["L0_ffn_W1"], np.float32)
    b1 = np.asarray(inputs["L0_ffn_b1"], np.float32)
    W2 = np.asarray(inputs["L0_ffn_W2"], np.float32)
    b2 = np.asarray(inputs["L0_ffn_b2"], np.float32)

    com = dict(
        xfull=x,
        w_inT=np.ascontiguousarray(W_in.T),
        w_stT=np.ascontiguousarray(W_state.T),
        w_dtT=np.ascontiguousarray(W_dt.T),
        w_cT=np.ascontiguousarray(W_C.T),
        w_outT=np.ascontiguousarray(W_out.T),
        wg_in=np.ascontiguousarray(Wg),
        b_dt_in=np.ascontiguousarray(b_dtv.reshape(DS, 1)),
        a_log_in=np.ascontiguousarray(A_log.reshape(DS, 1)),
        dmat_in=np.ascontiguousarray(Dv.reshape(8, 128).T),
        identbf_in=np.eye(128, dtype=bf),
        identf_in=np.eye(128, dtype=np.float32),
        onescol_in=np.ones((128, 1), np.float32),
        onesrow_in=np.ones((1, 128), np.float32),
        onestt_in=np.ones((128, 128), np.float32),
        ltstrict_in=np.ascontiguousarray(
            (np.arange(128)[:, None] < np.arange(128)[None, :]).astype(np.float32)),
        lt16_in=np.ascontiguousarray(
            (np.arange(128)[:, None] < (16 * (np.arange(128)[None, :] // 16)))
            .astype(np.float32)),
        tokid_in=np.ascontiguousarray(
            (np.arange(128)[:, None] * 128 + np.arange(128)[None, :]).astype(np.int32)),
        jbase_in=np.ascontiguousarray(
            ((np.arange(128) // 16) * CAP).astype(np.float32).reshape(128, 1)),
        jmat_in=np.ascontiguousarray(
            np.tile(np.arange(1, 9, dtype=np.float32), (128, 1))),
    )
    in_maps = []
    for c in range(NCORE):
        m = dict(com)
        sl = slice(T * c, T * (c + 1))
        m["xs"] = np.ascontiguousarray(x[sl])
        m["xts"] = np.ascontiguousarray(x[sl].T)
        m["w1e"] = np.ascontiguousarray(W1[c])
        m["w2e"] = np.ascontiguousarray(W2[c])
        m["b1m_in"] = np.ascontiguousarray(b1[c].reshape(32, 128).T)
        m["b2bc_in"] = np.ascontiguousarray(np.broadcast_to(b2[c], (128, DM)))
        sv = np.zeros((1, 8), np.float32)
        for j in range(NCORE):
            if j // 4 == c // 4 and j < c:
                sv[0, j] = 1.0
        m["selv_in"] = sv
        em_ = np.zeros((128, 1024), np.float32)
        em_[:, c::8] = 1.0
        m["emask_in"] = em_
        in_maps.append(m)
    return in_maps


def kernel(**inputs):
    in_maps = _host_inputs(inputs)
    with allow_ici_all_to_all():
        nc = build_program(debug=False)
        res = run_bass_kernel_spmd(nc, in_maps, core_ids=list(range(NCORE)))
    out = np.concatenate([np.asarray(res.results[c]["out"]) for c in range(NCORE)], 0)
    return out.reshape(2, 8192, DM)
